# version 03
# baseline (speedup 1.0000x reference)
import math
import numpy as np
import ml_dtypes
import concourse.bass as bass
import concourse.mybir as mybir
from concourse.bass_utils import run_bass_kernel_spmd

F32 = mybir.dt.float32
BF16 = mybir.dt.bfloat16
AF = mybir.ActivationFunctionType
ALU = mybir.AluOpType
AX = mybir.AxisListType

D, H, DH, L, NSEQ = 1024, 16, 64, 2048, 2
T = NSEQ * L
I = 3072
QB = 32
CD = D // 128
CI = I // 128
CMAG = float(3 * 2 ** 22)
QMA = 32767.0
EPS = 1e-5
ROPE_BASE = 10000.0
SAFE = 1.02 / 8.0

ENGS = ("pe", "act", "dve", "pool", "sp")
A_LIMIT = CI
M_LIMIT = 2
DBG_STAGE = None


class KB:
    def __init__(self, nc):
        self.nc = nc
        self.ops = {e: [] for e in ENGS}
        self.sems = {}
        self.cnt = {}
        self.waited = {e: {} for e in ENGS}

    def sem(self, name):
        if name not in self.sems:
            self.sems[name] = self.nc.alloc_semaphore(name)
            self.cnt[name] = 0
        return self.sems[name]

    def __init_dma(self):
        if not hasattr(self, "dma_sems"):
            self.dma_sems = set()

    def emit(self, eng, fn, waits=(), inc=None, inc_k=1):
        self.__init_dma()
        if eng in ("dve", "act", "pool"):
            waits = list(waits) + [self.tok(eng)]
        real = []
        for tokv in waits:
            if tokv is None:
                continue
            s, v = tokv
            if s in self.dma_sems:
                v = self.cnt[s]
            assert s in self.cnt and v <= self.cnt[s], f"future token {s}:{v}"
            if v > self.waited[eng].get(s, 0):
                real.append((self.sems[s], v))
                self.waited[eng][s] = v
        tok = None
        semh = None
        if inc is not None:
            self.sem(inc)
            if inc_k == 16:
                self.dma_sems.add(inc)
            self.cnt[inc] += inc_k
            tok = (inc, self.cnt[inc])
            semh = self.sems[inc]

        def run(e, real=real, fn=fn, semh=semh, k=inc_k):
            for (sh, v) in real:
                e.wait_ge(sh, v)
            ins = fn(e)
            if semh is not None:
                ins.then_inc(semh, k)
        self.ops[eng].append(run)
        return tok

    def tok(self, name):
        n = self.cnt.get(name, 0)
        return (name, n) if n > 0 else None

    def finalize(self, block):
        mp = {"pe": block.tensor, "act": block.scalar, "dve": block.vector,
              "pool": block.gpsimd, "sp": block.sync}
        for e in ENGS:
            ops = self.ops[e]

            def body(eng, ops=ops):
                for f in ops:
                    f(eng)
            mp[e](body)


def _rope_tables():
    pos = np.arange(L, dtype=np.float32)
    idx = np.arange(0, DH, 2, dtype=np.float32)
    freqs = np.exp(np.float32(-math.log(ROPE_BASE)) * idx / np.float32(DH)).astype(np.float32)
    ang = pos[:, None] * freqs[None, :]
    sin = np.sin(ang).astype(np.float32)
    cos = np.cos(ang).astype(np.float32)
    C = np.zeros((128, L), np.float32)
    S = np.zeros((128, L), np.float32)
    for p in range(128):
        i = p % 32
        C[p] = cos[:, i]
        S[p] = sin[:, i] * (-1.0 if (p % 64) < 32 else 1.0)
    return C, S


def build_program(dbg=None):
    nc = bass.Bass("TRN2", target_bir_lowering=False, debug=False)

    d_x = nc.dram_tensor("xT", [D, T], F32, kind="ExternalInput").ap()
    d_wq = nc.dram_tensor("wqT", [D, D], F32, kind="ExternalInput").ap()
    d_wk = nc.dram_tensor("wkT", [D, D], F32, kind="ExternalInput").ap()
    d_wv = nc.dram_tensor("wvT", [D, D], F32, kind="ExternalInput").ap()
    d_wo = nc.dram_tensor("woT", [D, D], F32, kind="ExternalInput").ap()
    d_wup = nc.dram_tensor("wupT", [D, I], F32, kind="ExternalInput").ap()
    d_wg = nc.dram_tensor("wgT", [D, I], F32, kind="ExternalInput").ap()
    d_wd = nc.dram_tensor("wdT", [I, D], F32, kind="ExternalInput").ap()
    d_wupN = nc.dram_tensor("wupN", [I, D], F32, kind="ExternalInput").ap()
    d_wgN = nc.dram_tensor("wgN", [I, D], F32, kind="ExternalInput").ap()
    d_wdN = nc.dram_tensor("wdN", [D, I], F32, kind="ExternalInput").ap()
    d_out = nc.dram_tensor("outT", [D, T], F32, kind="ExternalOutput").ap()

    d_xln = nc.dram_tensor("xlnd", [D, T], F32, kind="Internal").ap()
    d_up = nc.dram_tensor("upd", [I, T], BF16, kind="Internal").ap()
    d_gate = nc.dram_tensor("gated", [I, T], BF16, kind="Internal").ap()
    d_t = nc.dram_tensor("td", [I, T], BF16, kind="Internal").ap()
    d_P = nc.dram_tensor("Pd", [I, T], BF16, kind="Internal").ap()
    d_o = nc.dram_tensor("od", [I, T], BF16, kind="Internal").ap()
    d_srow = nc.dram_tensor("srowd", [1, H * L], F32, kind="Internal").ap()
    d_rrow = nc.dram_tensor("rrowd", [1, H * L], F32, kind="Internal").ap()
    cc_in = [nc.dram_tensor(f"ccin{i}", [1, 512], F32, kind="Internal").ap() for i in range(4)]
    cc_out = [nc.dram_tensor(f"ccout{i}", [1, 512], F32, kind="Internal").ap() for i in range(4)]

    Cnp, Snp = _rope_tables()
    i_tabC = nc.inline_tensor(Cnp.astype(ml_dtypes.bfloat16), "tabC").ap()
    i_tabS = nc.inline_tensor(Snp.astype(ml_dtypes.bfloat16), "tabS").ap()
    swap_np = np.zeros((128, 128), np.float32)
    for p in range(128):
        swap_np[p ^ 32, p] = 1.0
    i_swap = nc.inline_tensor(swap_np.astype(ml_dtypes.bfloat16), "swapM").ap()
    i_eyeF = nc.inline_tensor(np.eye(128, dtype=np.float32), "eyeF").ap()
    i_onescolB = nc.inline_tensor(np.ones((128, 1), ml_dtypes.bfloat16), "onescolB").ap()
    i_onesrowF = nc.inline_tensor(np.ones((1, 128), np.float32), "onesrowF").ap()
    constv_np = np.tile(np.array([[EPS, CMAG, -CMAG, 0.0]], np.float32), (128, 1))
    i_constv = nc.inline_tensor(constv_np, "constv").ap()
    ind_np = np.zeros((D, H), np.float32)
    for c in range(CD):
        ind_np[128 * c: 128 * c + 64, 2 * c] = 1.0
        ind_np[128 * c + 64: 128 * c + 128, 2 * c + 1] = 1.0
    i_ind = nc.inline_tensor(ind_np.astype(ml_dtypes.bfloat16), "indnq").ap()
    ind32_np = np.zeros((QB, CD * 128), np.float32)
    for c in range(CD):
        for p in range(128):
            ind32_np[4 * c + p // 32, 128 * c + p] = 1.0
    i_ind32 = nc.inline_tensor(ind32_np[:32].astype(np.float32), "ind32").ap()

    b = KB(nc)
    import contextlib
    with contextlib.ExitStack() as ctx:
        def sb(name, shape, dtype):
            return ctx.enter_context(nc.sbuf_tensor(name, shape, dtype))
        B_X = sb("B_X", [128, 16384], BF16)
        B_Q = sb("B_Q", [128, 16384], BF16)
        B_K = sb("B_K", [128, 16384], BF16)
        B_V = sb("B_V", [128, 16640], BF16)
        B_W = sb("B_W", [128, 16384], BF16)
        B_M = sb("B_M", [128, 12288], BF16)
        B_P = B_M
        V_S = sb("V_S", [128, 16], F32)
        V_C = sb("V_C", [128, 48], F32)
        V_R = sb("V_R", [1, 2048], F32)
        V_R2 = sb("V_R2", [1, 1024], F32)
        V_SC = sb("V_SC", [1, 32], F32)
        C_T = sb("C_T", [128, 2052], BF16)
        C_T2 = sb("C_T2", [128, 2048], BF16)
        C_SW = sb("C_SW", [128, 128], BF16)
        C_EYE = sb("C_EYE", [128, 128], F32)
        C_IND = sb("C_IND", [128, 128], BF16)
        C_OR = sb("C_OR", [1, 128], F32)
        CONSTV = sb("CONSTV", [128, 4], F32)
        nc.const_aps.register(F32, EPS, CONSTV[:, 0:1])
        nc.const_aps.register(F32, CMAG, CONSTV[:, 1:2])
        nc.const_aps.register(F32, -CMAG, CONSTV[:, 2:3])
        C_I32 = None
        PS = ctx.enter_context(nc.psum_tensor("PS", [128, 4096], F32))
        block = ctx.enter_context(nc.Block())

        B_Mf = B_M.ap().bitcast(F32)
        B_Wf = B_W.ap().bitcast(F32)
        B_Vf = B_V.ap().bitcast(F32)

        def bank(i, n=512):
            return PS[:, 512 * i: 512 * i + n]

        def dma(q, out_ap, in_ap, sem="ld", waits=()):
            return b.emit(q, lambda e: e.dma_start(out=out_ap, in_=in_ap),
                          waits=waits, inc=sem, inc_k=16)

        dma("sp", C_T[:, 0:2048], i_tabC, "ldc")
        dma("sp", C_T2[:, :], i_tabS, "ldc")
        dma("sp", C_T[:, 2048:2049], i_onescolB, "ldc")
        dma("sp", C_SW[:, :], i_swap, "ldc")
        dma("sp", C_EYE[:, :], i_eyeF, "ldc")
        dma("sp", C_IND.ap()[:, 0:128].rearrange("p (c h) -> p c h", c=CD),
            i_ind.rearrange("(c p) h -> p c h", p=128), "ldc")
        dma("sp", CONSTV[:, :], i_constv, "ldc")
        tok_consts = dma("sp", C_OR[:, :], i_onesrowF, "ldc")
        CCST = V_R[0:1, 0:512]
        GMX = V_R[0:1, 512:1024]

        def load_xseq(j, waits):
            tk = None
            for c in range(CD):
                tk = dma("pool", B_X[:, 2048 * c: 2048 * (c + 1)],
                         d_x[128 * c: 128 * (c + 1), L * j: L * (j + 1)],
                         "gx", waits)
            return tk

        def load_w(dr, slot, waits):
            return dma("pool",
                       B_W.ap()[:, 8192 * slot: 8192 * (slot + 1)].rearrange(
                           "p (c n) -> p c n", c=CD),
                       dr.rearrange("(c p) n -> p c n", p=128), f"gw{slot}", waits)

        tok_x = load_xseq(0, ())
        tok_wq = load_w(d_wq, 0, ())
        tok_wk = load_w(d_wk, 1, ())

        bank_free = [None] * 8
        tok_oproj_done = None
        tok_vproj_done = None
        pv_free = [None, None]
        tok_dbg_last = None

        if dbg == "simA":
            tok_rope = None
            tok_oproj_done = b.tok("pe")
            tok_vproj_done = b.tok("pe")
            b.emit("dve", lambda e: e.memset(V_S[:, :], 1.0), inc="dve")
            b.emit("dve", lambda e: e.memset(B_Q[:, :], 0.0), inc="dve")
            tz = b.emit("dve", lambda e: e.memset(B_K[:, :], 0.0), inc="dve")
            for c_ in range(CD):
                dma("sp", d_xln[128 * c_: 128 * (c_ + 1), :],
                    B_Q.ap().bitcast(F32)[:, 0:T], "sxln", waits=[tz])
        for j in range([] if dbg == "simA" else range(NSEQ) and NSEQ or NSEQ) if False else range(0 if dbg == "simA" else NSEQ):
            if j == 1:
                tok_x = load_xseq(1, [tok_oproj_done])
                tok_wq = load_w(d_wq, 0, [tok_vproj_done, b.tok("dve")])
                tok_wk = load_w(d_wk, 1, [tok_oproj_done])

            def proj_qk(wtok, slot, dst):
                last_mm = None
                last_ev = None
                for cq in range(CD):
                    for t4 in range(4):
                        bki = (cq * 4 + t4) % 4
                        w = [wtok, tok_x, tok_consts, bank_free[bki]]
                        for c in range(CD):
                            def mm(e, bki=bki, c=c, cq=cq, t4=t4, slot=slot):
                                return e.matmul(
                                    bank(bki),
                                    B_W[:, 8192 * slot + 1024 * c + 128 * cq:
                                        8192 * slot + 1024 * c + 128 * cq + 128],
                                    B_X[:, 2048 * c + 512 * t4: 2048 * c + 512 * (t4 + 1)],
                                    start=(c == 0), stop=(c == CD - 1))
                            last_mm = b.emit("pe", mm, waits=w if c == 0 else (),
                                             inc="pe" if c == CD - 1 else None)
                        def ev(e, bki=bki, cq=cq, t4=t4, dst=dst):
                            return e.copy(dst[:, 2048 * cq + 512 * t4: 2048 * cq + 512 * (t4 + 1)],
                                          bank(bki))
                        last_ev = b.emit("act", ev, waits=[last_mm], inc="act")
                        bank_free[bki] = last_ev
                return last_ev, last_mm

            tok_qe, tok_qmm = proj_qk(tok_wq, 0, B_Q)
            tok_ke, tok_kmm = proj_qk(tok_wk, 1, B_K)
            tok_wv = load_w(d_wv, 0, [tok_qmm])
            if dbg == "qraw" and j == 0:
                tdl = None
                for c in range(CD):
                    tdl = dma("pool", d_out[128 * c: 128 * (c + 1), 0:L],
                              B_Q[:, 2048 * c: 2048 * (c + 1)], "gst", waits=[tok_qe])
                b.emit("sp", lambda e: e.nop(), waits=[tdl])
                b.finalize(block)
                return nc

            prev_r1 = None
            prev_r3 = None
            for (src, etok) in ((B_Q, tok_qe), (B_K, tok_ke)):
                for cq in range(CD):
                    sw_last = None
                    for t4 in range(4):
                        bki = 4 + (t4 % 2)
                        w = [etok, tok_consts, bank_free[bki], prev_r1]
                        def mmsw(e, bki=bki, src=src, cq=cq, t4=t4):
                            return e.matmul(bank(bki), C_SW[:, :],
                                            src[:, 2048 * cq + 512 * t4: 2048 * cq + 512 * (t4 + 1)],
                                            start=True, stop=True)
                        tmm = b.emit("pe", mmsw, waits=w, inc="pe")
                        def evsw(e, bki=bki, t4=t4):
                            return e.copy(B_M[:, 512 * t4: 512 * (t4 + 1)], bank(bki))
                        sw_last = b.emit("act", evsw, waits=[tmm, prev_r1], inc="act")
                        bank_free[bki] = sw_last
                    def r1(e, src=src, cq=cq):
                        return e.tensor_tensor(B_M[:, 2048:4096],
                                               src[:, 2048 * cq: 2048 * (cq + 1)],
                                               C_T[:, 0:2048], ALU.mult)
                    b.emit("dve", r1, waits=[sw_last], inc="dve")
                    def r2(e):
                        return e.tensor_tensor(B_M[:, 4096:6144], B_M[:, 0:2048],
                                               C_T2[:, :], ALU.mult)
                    prev_r1 = b.emit("dve", r2, inc="dve")
                    def r3(e, src=src, cq=cq):
                        return e.tensor_tensor(src[:, 2048 * cq: 2048 * (cq + 1)],
                                               B_M[:, 2048:4096], B_M[:, 4096:6144],
                                               ALU.add)
                    prev_r3 = b.emit("dve", r3, inc="dve")
            tok_rope = prev_r3

            def vones(e):
                v = B_V.ap().rearrange("p (s w) -> p s w", w=65)
                return e.memset(v[:, :, 64:65], 1.0)
            tok_vones = b.emit("pool", vones,
                               waits=[b.tok("pe") if j == 1 else None], inc="pool")
            vlast_ev = None
            for st in range(16):
                for d2 in range(2):
                    bki = (st * 2 + d2) % 4
                    w = [tok_wv, bank_free[bki]]
                    for c in range(CD):
                        def mmv(e, bki=bki, c=c, st=st, d2=d2):
                            return e.matmul(
                                bank(bki),
                                B_X[:, 2048 * c + 128 * st: 2048 * c + 128 * (st + 1)],
                                B_W[:, 1024 * c + 512 * d2: 1024 * c + 512 * (d2 + 1)],
                                start=(c == 0), stop=(c == CD - 1))
                        last = b.emit("pe", mmv, waits=w if c == 0 else (),
                                      inc="pe" if c == CD - 1 else None)
                    def evv(e, bki=bki, st=st, d2=d2):
                        dst = B_V.ap()[:, 1040 * st: 1040 * (st + 1)].rearrange(
                            "p (h w) -> p h w", w=65)[:, 8 * d2: 8 * (d2 + 1), 0:64]
                        src = bank(bki).rearrange("p (h w) -> p h w", w=64)
                        return e.copy(dst, src)
                    vlast_ev = b.emit("act", evv, waits=[last, tok_vones], inc="act")
                    bank_free[bki] = vlast_ev
            tok_vproj_done = b.tok("pe")
            tok_wo = load_w(d_wo, 1, [tok_kmm])

            prev_mmn = None
            nsc = B_W[:, 0:512]
            for (src, base_col) in ((B_Q, 0), (B_K, 4)):
                for t4 in range(4):
                    w6 = [bank_free[6]]
                    last = None
                    for cq in range(CD):
                        def sqr(e, src=src, cq=cq, t4=t4):
                            return e.tensor_tensor(
                                nsc,
                                src[:, 2048 * cq + 512 * t4: 2048 * cq + 512 * (t4 + 1)],
                                src[:, 2048 * cq + 512 * t4: 2048 * cq + 512 * (t4 + 1)],
                                ALU.mult)
                        tsq = b.emit("dve", sqr, waits=[tok_rope, prev_mmn, tok_vproj_done], inc="dve")
                        def mmn(e, cq=cq):
                            return e.matmul(PS[0:16, 6 * 512: 6 * 512 + 512],
                                            C_IND[:, 16 * cq: 16 * (cq + 1)],
                                            nsc,
                                            start=(cq == 0), stop=(cq == CD - 1))
                        last = b.emit("pe", mmn, waits=[tsq] + (w6 if cq == 0 else []),
                                      inc="pe")
                        prev_mmn = last
                    col = base_col + t4
                    def red(e, col=col):
                        return e.tensor_reduce(V_S[0:16, col:col + 1],
                                               PS[0:16, 6 * 512: 6 * 512 + 512],
                                               AX.X, ALU.max, apply_absolute_value=True)
                    rt = b.emit("dve", red, waits=[last], inc="dve")
                    if dbg == "ind2" and j == 0 and src is B_Q and t4 == 0:
                        rt = b.emit("dve", lambda e: e.tensor_copy(
                            B_Mf[:, 512:1024][0:16, 0:512],
                            PS[0:16, 6 * 512: 6 * 512 + 512]), inc="dve")
                    bank_free[6] = rt
            tn1 = b.emit("dve", lambda e: e.tensor_reduce(V_S[0:16, 8:9], V_S[0:16, 0:4],
                                                          AX.X, ALU.max), inc="dve")
            tn2 = b.emit("dve", lambda e: e.tensor_reduce(V_S[0:16, 9:10], V_S[0:16, 4:8],
                                                          AX.X, ALU.max),
                         waits=[tn1], inc="dve")
            t3_ = b.emit("dve", lambda e: e.tensor_tensor(V_S[0:16, 10:11], V_S[0:16, 8:9],
                                                          V_S[0:16, 9:10], ALU.mult),
                         waits=[tn2], inc="dve")
            t4_ = b.emit("act", lambda e: e.activation(V_S[0:16, 11:12], V_S[0:16, 10:11],
                                                       AF.Sqrt), waits=[t3_], inc="act")
            for t4 in range(4):
                w6 = [bank_free[6]]
                last = None
                for cq in range(CD):
                    def dgm(e, cq=cq, t4=t4):
                        return e.tensor_tensor(
                            nsc,
                            B_Q[:, 2048 * cq + 512 * t4: 2048 * cq + 512 * (t4 + 1)],
                            B_K[:, 2048 * cq + 512 * t4: 2048 * cq + 512 * (t4 + 1)],
                            ALU.mult)
                    tsq = b.emit("dve", dgm, waits=[prev_mmn, tn2], inc="dve")
                    def mmn2(e, cq=cq):
                        return e.matmul(PS[0:16, 6 * 512: 6 * 512 + 512],
                                        C_IND[:, 16 * cq: 16 * (cq + 1)],
                                        nsc, start=(cq == 0), stop=(cq == CD - 1))
                    last = b.emit("pe", mmn2, waits=[tsq] + (w6 if cq == 0 else []),
                                  inc="pe")
                    prev_mmn = last
                def red2(e, t4=t4):
                    return e.tensor_reduce(V_S[0:16, t4:t4 + 1],
                                           PS[0:16, 6 * 512: 6 * 512 + 512],
                                           AX.X, ALU.max)
                rt = b.emit("dve", red2, waits=[last], inc="dve")
                bank_free[6] = rt
            tdg = b.emit("dve", lambda e: e.tensor_reduce(V_S[0:16, 13:14], V_S[0:16, 0:4],
                                                          AX.X, ALU.max),
                         waits=[bank_free[6]], inc="dve")
            tnA = b.emit("dve", lambda e: e.tensor_single_scalar(V_S[0:16, 14:15],
                         V_S[0:16, 13:14], 0.125, ALU.mult), waits=[tdg], inc="dve")
            tnB = b.emit("dve", lambda e: e.tensor_scalar(V_S[0:16, 15:16],
                         V_S[0:16, 11:12], SAFE, -85.0, ALU.mult, ALU.add),
                         waits=[t4_], inc="dve")
            tnC = b.emit("dve", lambda e: e.tensor_tensor(V_S[0:16, 10:11], V_S[0:16, 14:15],
                                                          V_S[0:16, 15:16], ALU.max),
                         waits=[tnA, tnB], inc="dve")
            t5_ = b.emit("dve", lambda e: e.tensor_single_scalar(V_S[0:16, 12:13],
                         V_S[0:16, 10:11], -1.0, ALU.mult),
                         waits=[tnC], inc="dve")
            w7 = [t5_, bank_free[7]]
            ttr = b.emit("pe", lambda e: e.transpose(PS[0:1, 7 * 512: 7 * 512 + 16],
                                                     V_S[0:16, 12:13], C_EYE[0:16, 0:16]),
                         waits=w7, inc="pe")
            tcp_ = b.emit("dve", lambda e: e.tensor_copy(V_SC[0:1, 0:16],
                                                         PS[0:1, 7 * 512: 7 * 512 + 16]),
                          waits=[ttr], inc="dve")
            tbc_ = b.emit("pe", lambda e: e.matmul(PS[:, 7 * 512: 7 * 512 + 16],
                                                   C_OR[:, :], V_SC[0:1, 0:16],
                                                   start=True, stop=True),
                          waits=[tcp_], inc="pe")
            tok_bias = b.emit("act", lambda e: e.copy(V_C[:, 0:16],
                                                      PS[:, 7 * 512: 7 * 512 + 16]),
                              waits=[tbc_], inc="act")
            bank_free[7] = tok_bias

            if dbg == "ind2" and j == 0:
                tc2 = dma("sp", d_out[0:16, 0:512], B_Mf[0:16, 512:1024], "st",
                          waits=[b.tok("dve")])
                tc3 = dma("sp", d_out[0:16, 512:528], V_S[0:16, 0:16], "st",
                          waits=[b.tok("dve")])
                b.emit("sp", lambda e: e.nop(), waits=[tc2, tc3])
                b.finalize(block)
                return nc

            if dbg == "ind" and j == 0:
                td1 = dma("pool", d_out[0:128, 0:128], C_IND[:, 0:128], "gst",
                          waits=[b.tok("dve"), b.tok("pe")])
                td2 = dma("pool", d_out[0:128, 128:640], B_M[:, 0:512], "gst",
                          waits=[b.tok("dve")])
                tcpy = b.emit("dve", lambda e: e.tensor_copy(
                    B_Mf[:, 512:1024][0:16, 0:512], PS[0:16, 6 * 512: 6 * 512 + 512]),
                    waits=[b.tok("pe")], inc="dve")
                td3 = dma("sp", d_out[0:16, 1024:1536], B_Mf[0:16, 512:1024], "st",
                          waits=[tcpy])
                b.emit("sp", lambda e: e.nop(), waits=[td1, td2, td3])
                b.finalize(block)
                return nc

            if dbg == "bias" and j == 0:
                tdbg1 = dma("sp", d_out[0:128, 0:48], V_C[:, :], "st", waits=[tok_bias])
                tdbg2 = dma("sp", d_out[0:16, 64:80], V_S[0:16, 0:16], "st",
                            waits=[tok_bias, b.tok("dve")])
                b.emit("sp", lambda e: e.nop(), waits=[tdbg1, tdbg2])
                b.finalize(block)
                return nc

            groups = [(0, 3), (3, 3), (6, 3), (9, 3), (12, 3), (15, 1)]
            prev_bc7 = tok_bias
            pv_free = [bank_free[6], None]
            sum_dma = [None, None]
            prev_rdist = [None, None]
            prev_rr = [None, None]
            rdist_done = {}
            for h in range(H):
                p0 = 64 * (h % 2)
                cq = h // 2
                qoff = 2048 * cq
                for lc in range(4):
                    pv_last = None
                    for gi, (s0, glen) in enumerate(groups):
                        gb = 0 if gi % 2 == 0 else 3
                        last_qk = None
                        for si in range(glen):
                            stile = s0 + si
                            wqk = [tok_rope, bank_free[gb]] if si == 0 else ()
                            def mmqk(e, gb=gb, si=si, stile=stile, p0=p0, qoff=qoff, lc=lc):
                                return e.matmul(
                                    PS[:, 512 * (gb + si): 512 * (gb + si) + 512],
                                    B_K[p0: p0 + 64, qoff + 128 * stile: qoff + 128 * (stile + 1)],
                                    B_Q[p0: p0 + 64, qoff + 512 * lc: qoff + 512 * (lc + 1)],
                                    start=True, stop=True)
                            last_qk = b.emit("pe", mmqk, waits=wqk,
                                             inc="pe" if si == glen - 1 else None)
                        pslot = (gi % 2) * 1536
                        def expg(e, gb=gb, glen=glen, pslot=pslot, h=h):
                            return e.activation(B_P[:, pslot: pslot + 512 * glen],
                                                PS[:, 512 * gb: 512 * gb + 512 * glen],
                                                AF.Exp, bias=V_C[:, h:h + 1], scale=0.125)
                        texp = b.emit("act", expg, waits=[last_qk, tok_bias], inc="act")
                        bank_free[gb] = texp
                        for si in range(glen):
                            stile = s0 + si
                            wpv = [texp]
                            if gi == 0 and si == 0:
                                wpv += [vlast_ev, tok_vones] + pv_free
                            def mmpv(e, stile=stile, si=si, pslot=pslot, h=h, gi=gi, glen=glen):
                                return e.matmul(
                                    PS[0:65, 6 * 512: 6 * 512 + 512],
                                    B_V[:, 1040 * stile + 65 * h: 1040 * stile + 65 * (h + 1)],
                                    B_P[:, pslot + 512 * si: pslot + 512 * (si + 1)],
                                    start=(gi == 0 and si == 0),
                                    stop=(gi == len(groups) - 1 and si == glen - 1))
                            pv_last = b.emit("pe", mmpv, waits=wpv,
                                             inc="pe" if (gi == len(groups) - 1 and si == glen - 1) else None)
                    def evpv(e, h=h, lc=lc, cq=cq, p0=p0):
                        return e.copy(B_X[p0: p0 + 64, 2048 * cq + 512 * lc: 2048 * cq + 512 * (lc + 1)],
                                      PS[0:64, 6 * 512: 6 * 512 + 512])
                    tev = b.emit("act", evpv, waits=[pv_last], inc="act")
                    sslot = 512 * (lc % 2)
                    def evsum(e, sslot=sslot):
                        return e.tensor_copy(V_R2[0:1, sslot: sslot + 512],
                                             PS[64:65, 6 * 512: 6 * 512 + 512])
                    tsum = b.emit("dve", evsum, waits=[pv_last, sum_dma[lc % 2]], inc="dve")
                    sum_dma[lc % 2] = dma("sp",
                                          d_srow[0:1, 2048 * h + 512 * lc: 2048 * h + 512 * (lc + 1)],
                                          V_R2[0:1, sslot: sslot + 512], f"ss{lc % 2}", waits=[tsum])
                    pv_free = [tev, tsum]
                    if dbg == "attn2":
                        trc2 = b.emit("dve", lambda e, sslot=sslot: e.reciprocal(
                            V_R[0:1, 1536:2048], V_R2[0:1, sslot: sslot + 512]),
                            waits=[tsum, prev_bc7], inc="dve")
                        wb2 = [trc2, bank_free[7]]
                        def bcr2(e):
                            return e.matmul(PS[0:64, 7 * 512: 7 * 512 + 512], C_OR[0:1, 0:64],
                                            V_R[0:1, 1536:2048], start=True, stop=True)
                        prev_bc7 = b.emit("pe", bcr2, waits=wb2, inc="pe")
                        def nrm2_(e, h=h, lc=lc, cq=cq, p0=p0):
                            sl = B_X[p0: p0 + 64, 2048 * cq + 512 * lc: 2048 * cq + 512 * (lc + 1)]
                            return e.tensor_tensor(sl, sl, PS[0:64, 7 * 512: 7 * 512 + 512],
                                                   ALU.mult)
                        norm_tt = b.emit("dve", nrm2_, waits=[prev_bc7, tev], inc="dve")
                        bank_free[7] = norm_tt
                        pv_free = [tev, norm_tt]
                if dbg in ("attnraw", "attn2"):
                    continue
                hp = h % 2
                tok_sr = [b.tok("ss0"), b.tok("ss1")]
                sd = B_Wf[:, 256 + 32 * hp: 272 + 32 * hp]
                sd2 = B_Wf[:, 272 + 32 * hp: 288 + 32 * hp]
                tok_d1 = dma("sp", sd,
                             d_srow[0:1, 2048 * h: 2048 * (h + 1)].rearrange(
                                 "o (p n) -> (o p) n", p=128), f"lsd{hp}",
                             waits=tok_sr + [prev_rdist[hp]])
                tok_rc = b.emit("dve", lambda e, sd=sd, sd2=sd2: e.reciprocal(sd2, sd),
                                waits=[tok_d1], inc="dve")
                tok_d2 = dma("sp",
                             d_rrow[0:1, 2048 * h: 2048 * (h + 1)].rearrange(
                                 "o (p n) -> (o p) n", p=128), sd2, f"sdist{hp}",
                             waits=[tok_rc])
                prev_rdist[hp] = tok_d2
                rdist_done[h] = tok_d2
                for hn in ([h - 1] if h > 0 else []) + ([h] if h == H - 1 else []):
                    cqn = hn // 2
                    p0n = 64 * (hn % 2)
                    norm_tt = None
                    for lc in range(4):
                        rsl = lc % 2
                        rstage = V_R[0:1, 1024 + 512 * rsl: 1024 + 512 * rsl + 512]
                        tok_d3 = dma("sp", rstage,
                                     d_rrow[0:1, 2048 * hn + 512 * lc: 2048 * hn + 512 * lc + 512],
                                     f"lrr{rsl}", waits=[rdist_done[hn], prev_rr[rsl]])
                        wb = [tok_d3, bank_free[7]]
                        def bcr(e, rstage=rstage):
                            return e.matmul(PS[0:64, 7 * 512: 7 * 512 + 512], C_OR[0:1, 0:64],
                                            rstage, start=True, stop=True)
                        prev_bc7 = b.emit("pe", bcr, waits=wb, inc="pe")
                        def nrm(e, hn=hn, lc=lc, cqn=cqn, p0n=p0n):
                            sl = B_X[p0n: p0n + 64, 2048 * cqn + 512 * lc: 2048 * cqn + 512 * (lc + 1)]
                            return e.tensor_tensor(sl, sl, PS[0:64, 7 * 512: 7 * 512 + 512],
                                                   ALU.mult)
                        norm_tt = b.emit("dve", nrm, waits=[prev_bc7, pv_free[0]], inc="dve")
                        bank_free[7] = norm_tt
                        prev_rr[rsl] = norm_tt
            tok_attn_norm = b.tok("dve")

            if dbg == "attnraw" and j == 0:
                for c in range(CD):
                    tok_dbg_last = dma("pool", d_out[128 * c: 128 * (c + 1), 0:L],
                                       B_X[:, 2048 * c: 2048 * (c + 1)], "gst",
                                       waits=[b.tok("act"), b.tok("ss0"), b.tok("ss1")])
                tok_dbg_last = dma("sp",
                                   d_out.rearrange("d t -> (d t)")[D * L: D * L + H * L].rearrange("(o n) -> o n", o=1),
                                   d_srow, "st", waits=[b.tok("st"), tok_dbg_last])
                b.emit("sp", lambda e: e.nop(), waits=[tok_dbg_last])
                b.finalize(block)
                return nc

            if dbg in ("q", "attn", "attn2"):
                srcbuf = B_Q if dbg == "q" else B_X
                wtok = tok_rope if dbg == "q" else tok_attn_norm
                for c in range(CD):
                    tok_dbg_last = dma("pool", d_out[128 * c: 128 * (c + 1), L * j: L * (j + 1)],
                                       srcbuf[:, 2048 * c: 2048 * (c + 1)], "gst", waits=[wtok])
                tok_oproj_done = b.tok("pe")
                if j == NSEQ - 1:
                    b.emit("sp", lambda e: e.nop(), waits=[tok_dbg_last])
                    b.finalize(block)
                    return nc
                continue

            s1ring = B_Wf
            xr_f = B_Mf[:, 0:512]
            t_f = B_Mf[:, 512:1024]
            xln_f = B_Mf[:, 1024:1536]
            s1b_b = B_M[:, 6144:6656]
            sq_b = B_M[:, 6656:7168]
            prev_xr = None
            prev_ap1 = None
            prev_xlndma = None
            lsum = lsq = None
            for t4 in range(4):
                ssum_w = [bank_free[4], bank_free[5]]
                ring_toks = []
                for dm in range(CD):
                    bki = dm % 3
                    w = [tok_wo, tok_attn_norm, bank_free[bki]]
                    for ca in range(CD):
                        def mmo(e, bki=bki, ca=ca, dm=dm, t4=t4):
                            return e.matmul(
                                bank(bki),
                                B_W[:, 8192 + 1024 * ca + 128 * dm:
                                    8192 + 1024 * ca + 128 * dm + 128],
                                B_X[:, 2048 * ca + 512 * t4: 2048 * ca + 512 * (t4 + 1)],
                                start=(ca == 0), stop=(ca == CD - 1))
                        last = b.emit("pe", mmo, waits=w if ca == 0 else (),
                                      inc="pe" if ca == CD - 1 else None)
                    tok_xr = dma("sp", xr_f,
                                 d_x[128 * dm: 128 * (dm + 1),
                                     L * j + 512 * t4: L * j + 512 * (t4 + 1)],
                                 "lxr", waits=[prev_xr])
                    def adds1(e, bki=bki, dm=dm):
                        return e.tensor_tensor(s1ring[:, 512 * dm: 512 * (dm + 1)],
                                               bank(bki), xr_f, ALU.add)
                    ts1 = b.emit("dve", adds1, waits=[last, tok_xr], inc="dve")
                    prev_xr = ts1
                    bank_free[bki] = ts1
                    ring_toks.append(ts1)
                    def cpbf(e, dm=dm):
                        return e.tensor_single_scalar(s1b_b, s1ring[:, 512 * dm: 512 * (dm + 1)],
                                                      1.0, ALU.mult)
                    tbf = b.emit("dve", cpbf, waits=[lsum], inc="dve")
                    def sqb(e):
                        return e.tensor_tensor(sq_b, s1b_b, s1b_b, ALU.mult)
                    tsq = b.emit("dve", sqb, waits=[lsq], inc="dve")
                    wst = [tbf] + (ssum_w if dm == 0 else [])
                    def mms(e, dm=dm):
                        return e.matmul(PS[0:1, 4 * 512: 4 * 512 + 512], C_T[:, 2048:2049],
                                        s1b_b, start=(dm == 0), stop=(dm == CD - 1))
                    lsum = b.emit("pe", mms, waits=wst, inc="pe")
                    def mms2(e, dm=dm):
                        return e.matmul(PS[0:1, 5 * 512: 5 * 512 + 512], C_T[:, 2048:2049],
                                        sq_b, start=(dm == 0), stop=(dm == CD - 1))
                    lsq = b.emit("pe", mms2, waits=[tsq], inc="pe")
                trm = b.emit("dve", lambda e: e.tensor_single_scalar(V_R[0:1, 0:512],
                             PS[0:1, 4 * 512: 4 * 512 + 512], 1.0 / D, ALU.mult),
                             waits=[lsum], inc="dve")
                trq = b.emit("dve", lambda e: e.tensor_single_scalar(V_R[0:1, 512:1024],
                             PS[0:1, 5 * 512: 5 * 512 + 512], 1.0 / D, ALU.mult),
                             waits=[lsq], inc="dve")
                bank_free[4] = trm
                bank_free[5] = trq
                trm2 = b.emit("dve", lambda e: e.tensor_tensor(V_R[0:1, 1024:1536], V_R[0:1, 0:512],
                                                        V_R[0:1, 0:512], ALU.mult),
                              waits=[trm], inc="dve")
                tvar = b.emit("dve", lambda e: e.tensor_tensor(V_R[0:1, 512:1024],
                                                               V_R[0:1, 512:1024],
                                                               V_R[0:1, 1024:1536],
                                                               ALU.subtract),
                              waits=[trq, trm2], inc="dve")
                tln = b.emit("act", lambda e: e.activation(V_R[0:1, 1024:1536],
                                                           V_R[0:1, 512:1024], AF.Ln,
                                                           bias=CONSTV[0:1, 0:1]),
                             waits=[tvar], inc="act")
                trstd = b.emit("act", lambda e: e.activation(V_R[0:1, 1024:1536],
                                                             V_R[0:1, 1024:1536], AF.Exp,
                                                             scale=-0.5), inc="act")
                tmr = b.emit("dve", lambda e: e.tensor_tensor(V_R[0:1, 1536:2048],
                                                              V_R[0:1, 0:512],
                                                              V_R[0:1, 1024:1536], ALU.mult),
                             waits=[trstd], inc="dve")
                tb6 = b.emit("pe", lambda e: e.matmul(PS[:, 6 * 512: 7 * 512], C_OR[:, :],
                                                      V_R[0:1, 1024:1536],
                                                      start=True, stop=True),
                             waits=[trstd, bank_free[6]], inc="pe")
                tb7 = b.emit("pe", lambda e: e.matmul(PS[:, 7 * 512: 8 * 512], C_OR[:, :],
                                                      V_R[0:1, 1536:2048],
                                                      start=True, stop=True),
                             waits=[tmr, bank_free[7]], inc="pe")
                last_ap = None
                for dm in range(CD):
                    def ap1(e, dm=dm):
                        return e.tensor_tensor(t_f, s1ring[:, 512 * dm: 512 * (dm + 1)],
                                               PS[:, 6 * 512: 7 * 512], ALU.mult)
                    ta1 = b.emit("dve", ap1, waits=[tb6, prev_ap1], inc="dve")
                    def ap2(e):
                        return e.tensor_tensor(xln_f, t_f, PS[:, 7 * 512: 8 * 512],
                                               ALU.subtract)
                    ta2 = b.emit("dve", ap2, waits=[tb7, prev_xlndma], inc="dve")
                    prev_xlndma = dma("sp", d_xln[128 * dm: 128 * (dm + 1),
                                      L * j + 512 * t4: L * j + 512 * (t4 + 1)],
                                      xln_f, "sxln", waits=[ta2])
                    first = (j == 0 and t4 == 0 and dm == 0)
                    col = 13 if first else 14
                    def redx(e, col=col):
                        return e.tensor_reduce(V_S[:, col:col + 1], xln_f,
                                               AX.X, ALU.max, apply_absolute_value=True)
                    trd = b.emit("dve", redx, inc="dve")
                    prev_ap1 = trd
                    if not first:
                        b.emit("dve", lambda e: e.tensor_tensor(V_S[:, 13:14], V_S[:, 13:14],
                                                                V_S[:, 14:15], ALU.max),
                               waits=[trd], inc="dve")
                    last_ap = trd
                bank_free[6] = last_ap
                bank_free[7] = last_ap
            tok_oproj_done = b.tok("pe")

        if dbg == "xln":
            tok = dma("sp", d_out, d_xln, "st", waits=[b.tok("sxln"), b.tok("dve")])
            b.emit("sp", lambda e: e.nop(), waits=[tok])
            b.finalize(block)
            return nc

        C_I32 = C_T.ap().bitcast(F32)[0:32, 0:1024]
        tok_i32 = dma("sp", C_I32, i_ind32, "ldc", waits=[b.tok("dve"), b.tok("act"), tok_consts])
        SN = {0: B_Mf[:, 2048:2816], 1: B_Mf[:, 2816:3584], 2: B_Mf[:, 3584:4352]}
        tok_vfree = b.tok("pe")
        tok_vfree_a = b.tok("act")
        mats = ((d_wupN, CI, 1), (d_wgN, CI, 1), (d_wdN, CD, 3))
        prev_nst = [None] * 4
        sn_tok = {}
        qrot = ("sp", "act", "pool")
        ldi = 0
        for mi, (dN, nt, nw) in enumerate(mats):
            tr_ = None
            for ot in range(nt):
                for w in range(nw):
                    sl = ldi % 4
                    nst = B_Vf[:, 1024 * sl: 1024 * sl + 1024]
                    tok_n = dma(qrot[ldi % 3], nst,
                                dN[128 * ot: 128 * (ot + 1), 1024 * w: 1024 * (w + 1)],
                                f"lsn{sl}", waits=[tok_vfree, tok_vfree_a, prev_nst[sl]])
                    ldi += 1
                    nb0 = (32 * ot) if nw == 1 else (96 * ot + 32 * w)
                    def redn(e, nst=nst, nb0=nb0, mi=mi):
                        return e.tensor_reduce(
                            SN[mi][:, nb0: nb0 + 32],
                            nst.rearrange("p (b w) -> p b w", w=QB),
                            AX.X, ALU.max, apply_absolute_value=True)
                    tr_ = b.emit("dve", redn, waits=[tok_n], inc="dve")
                    prev_nst[sl] = tr_
            ncols = 32 * nt * nw
            sn_tok[mi] = b.emit("dve", lambda e, mi=mi, ncols=ncols: e.tensor_scalar(
                SN[mi][:, 0:ncols], SN[mi][:, 0:ncols], 1.0 / 7.0, 1e-12,
                ALU.mult, ALU.max), waits=[tr_], inc="dve")
        tok_scales = sn_tok[2]

        def jit_scale_tiles(mi, src_slice, w_extra):
            sT_st = B_Vf[0:32, 6272:6400]
            rT_st = B_Vf[0:32, 6400:6528]
            wtr = [sn_tok[mi], bank_free[4]] + w_extra
            ttr2 = b.emit("pe", lambda e: e.transpose(
                PS[0:32, 4 * 512 + 256: 4 * 512 + 384], src_slice, C_EYE[:, :]),
                waits=wtr, inc="pe")
            tcp2 = b.emit("act", lambda e: e.copy(sT_st, PS[0:32, 4 * 512 + 256: 4 * 512 + 384]),
                          waits=[ttr2], inc="act")
            trc2 = b.emit("dve", lambda e: e.reciprocal(rT_st, sT_st),
                          waits=[tcp2], inc="dve")
            bank_free[4] = tcp2
            return sT_st, rT_st, trc2

        def barrier(idx, cols, after_toks):
            ctoks = []
            prev_gst = b.tok("gci")
            prev_gmx = b.tok("dve")
            tok_ccz = b.emit("dve", lambda e: e.memset(CCST, 0.0),
                             waits=after_toks + [prev_gst], inc="dve")
            for si, col in enumerate(cols):
                w = [bank_free[5]] + after_toks
                def trp(e, col=col):
                    return e.transpose(PS[0:1, 5 * 512: 5 * 512 + 128],
                                       V_S[:, col:col + 1], C_EYE[:, :])
                tt = b.emit("pe", trp, waits=w, inc="pe")
                def rd(e, si=si, idx=idx):
                    return e.tensor_reduce(CCST[0:1, si:si + 1],
                                           PS[0:1, 5 * 512: 5 * 512 + 128],
                                           AX.X, ALU.max)
                tr_ = b.emit("dve", rd, waits=[tt, tok_ccz], inc="dve")
                bank_free[5] = tr_
                ctoks.append(tr_)
            tin = dma("pool", cc_in[idx], CCST, "gci", waits=ctoks)
            tcc = b.emit("pool", lambda e, idx=idx: e.collective_compute(
                "AllReduce", ALU.max, replica_groups=[list(range(8))],
                ins=[cc_in[idx]], outs=[cc_out[idx]]), waits=[tin], inc="cc")
            tg = dma("pool", GMX, cc_out[idx], "gco", waits=[tcc, prev_gmx])
            return tg

        def bcast_consts(row_lo, n, vc_lo, wtok):
            w = [wtok, bank_free[5]]
            tb = b.emit("pe", lambda e: e.matmul(PS[:, 5 * 512: 5 * 512 + n], C_OR[:, :],
                                                 V_SC[0:1, row_lo: row_lo + n],
                                                 start=True, stop=True),
                        waits=w, inc="pe")
            tc = b.emit("act", lambda e: e.copy(V_C[:, vc_lo: vc_lo + n],
                                                PS[:, 5 * 512: 5 * 512 + n]),
                        waits=[tb], inc="act")
            bank_free[5] = tc
            return tc

        tg1 = barrier(0, [13], [b.tok("dve")])
        t_a = b.emit("dve", lambda e: e.tensor_scalar(V_SC[0:1, 0:1], GMX[:, 0:1],
                                                      1.0 / QMA, 1e-12, ALU.mult, ALU.max),
                     waits=[tg1], inc="dve")
        t_b = b.emit("dve", lambda e: e.reciprocal(V_SC[0:1, 1:2], V_SC[0:1, 0:1]),
                     waits=[t_a], inc="dve")
        tok_c1 = bcast_consts(0, 2, 32, t_b)

        def xq_slab(c, t0, n):
            if c < 4:
                return B_Q[:, 4096 * c + t0: 4096 * c + t0 + n]
            return B_K[:, 4096 * (c - 4) + t0: 4096 * (c - 4) + t0 + n]

        xq_tok = None
        prev_ld = [None] * 4
        tok_ln1_done = (b.tok("dve"), b.tok("pe"))
        for c in range(CD):
            for t2 in range(2):
                sl = (c * 2 + t2) % 4
                stage = B_Wf[:, 2048 * sl: 2048 * sl + 2048]
                tok_l = dma("sp" if sl % 2 == 0 else "act", stage,
                            d_xln[128 * c: 128 * (c + 1), 2048 * t2: 2048 * (t2 + 1)],
                            f"lxq{sl}", waits=[prev_ld[sl], tok_scales, b.tok("sxln"), b.tok("gw0"), b.tok("gw1"), b.tok("gx"),
                                          tok_ln1_done[0], tok_ln1_done[1]])
                def q1(e, stage=stage):
                    return e.tensor_scalar(stage, stage, V_C[:, 33:34], CMAG,
                                           ALU.mult, ALU.add)
                tq1 = b.emit("dve", q1, waits=[tok_l, tok_c1], inc="dve")
                def q2(e, stage=stage, c=c, t2=t2):
                    return e.tensor_scalar(xq_slab(c, 2048 * t2, 2048), stage,
                                           -CMAG, V_C[:, 32:33], ALU.add, ALU.mult)
                xq_tok = b.emit("dve", q2, waits=[tq1], inc="dve")
                prev_ld[sl] = xq_tok

        if dbg == "xq":
            tdl = None
            for c in range(CD):
                buf = B_Q if c < 4 else B_K
                off = 4096 * (c % 4)
                tdl = dma("pool", d_out[128 * c: 128 * (c + 1), :],
                          buf[:, off: off + 4096], "gst", waits=[xq_tok])
            b.emit("sp", lambda e: e.nop(), waits=[tdl])
            b.finalize(block)
            return nc

        dq_base = 4096
        sp_base = 8192
        run_up, run_gate = 1, 2
        first_red = {1: True, 2: True}
        prev_w = [None, None]
        prev_dq = [None, None]
        prev_sp = [None] * 4
        prev_rd = [None] * 4
        spill_n = 0
        gtoks = {}
        tok_dq_done = {}
        a_last_mm = None
        for it in range(A_LIMIT):
            for mi, (dT, ddst, strow) in list(enumerate(((d_wup, d_up, 0), (d_wg, d_gate, 32))))[:M_LIMIT]:
                sl = (it * 2 + mi) % 2
                wst = B_Vf[:, 1024 * sl: 1024 * sl + 1024]
                tok_w = dma("sp", wst.rearrange("p (c n) -> p c n", c=CD),
                            dT[:, 128 * it: 128 * (it + 1)].rearrange(
                                "(c p) n -> p c n", p=128),
                            f"lwa{sl}", waits=[prev_w[sl], xq_tok])
                sT_st, rT_st, tokjit = jit_scale_tiles(
                    mi, SN[mi][:, 32 * it: 32 * (it + 1)], [tok_i32, tok_w])
                dq_toks = []
                srT = B_Vf[0:32, 6272:6528]
                for c in range(CD):
                    wdq = [tokjit, bank_free[4], prev_dq[sl]]
                    def bs1(e, c=c, srT=srT):
                        return e.matmul(PS[:, 4 * 512: 4 * 512 + 256],
                                        C_I32[:, 128 * c: 128 * (c + 1)],
                                        srT, start=True, stop=True)
                    tb2 = b.emit("pe", bs1, waits=wdq, inc="pe")
                    scr = B_Vf[:, 6144: 6272]
                    def dq1(e, wst=wst, c=c, scr=scr):
                        return e.tensor_tensor(scr, wst[:, 128 * c: 128 * (c + 1)],
                                               PS[:, 4 * 512 + 128: 4 * 512 + 256], ALU.mult)
                    td1 = b.emit("dve", dq1, waits=[tb2], inc="dve")
                    def dq2(e, scr=scr):
                        return e.tensor_scalar(scr, scr, CMAG, -CMAG, ALU.add, ALU.add)
                    td2 = b.emit("dve", dq2, inc="dve")
                    def dq3(e, c=c, sl=sl, scr=scr):
                        return e.tensor_tensor(B_V[:, dq_base + 2048 * sl + 128 * c:
                                                    dq_base + 2048 * sl + 128 * (c + 1)],
                                               scr, PS[:, 4 * 512: 4 * 512 + 128], ALU.mult)
                    td3 = b.emit("dve", dq3, inc="dve")
                    bank_free[4] = td3
                    dq_toks.append(td3)
                prev_w[sl] = dq_toks[-1]
                prev_dq[sl] = None
                for tp in range(4):
                    pb = 0 if (it * 8 + mi * 4 + tp) % 2 == 0 else 2
                    wa = [dq_toks[-1], bank_free[pb], bank_free[pb + 1]]
                    for c in range(CD):
                        for half in range(2):
                            def mma(e, pb=pb, c=c, half=half, sl=sl, tp=tp):
                                return e.matmul(
                                    PS[:, 512 * (pb + half): 512 * (pb + half) + 512],
                                    B_V[:, dq_base + 2048 * sl + 128 * c:
                                        dq_base + 2048 * sl + 128 * (c + 1)],
                                    xq_slab(c, 1024 * tp + 512 * half, 512),
                                    start=(c == 0), stop=(c == CD - 1))
                            a_last_mm = b.emit("pe", mma, waits=wa if (c == 0 and half == 0) else (),
                                               inc="pe" if (c == CD - 1 and half == 1) else None)
                    ssl = spill_n % 4
                    spill_n += 1
                    spst = B_V[:, sp_base + 1024 * ssl: sp_base + 1024 * (ssl + 1)]
                    def eva(e, pb=pb, spst=spst):
                        return e.copy(spst, PS[:, 512 * pb: 512 * pb + 1024])
                    tev = b.emit("act", eva, waits=[a_last_mm, prev_sp[ssl], prev_rd[ssl]],
                                 inc="act")
                    run = run_up if mi == 0 else run_gate
                    if first_red[run]:
                        def redA(e, spst=spst, run=run):
                            return e.tensor_reduce(V_S[:, run:run + 1], spst,
                                                   AX.X, ALU.max, apply_absolute_value=True)
                        trd = b.emit("dve", redA, waits=[tev], inc="dve")
                        first_red[run] = False
                    else:
                        def redA(e, spst=spst):
                            return e.tensor_reduce(V_S[:, 15:16], spst,
                                                   AX.X, ALU.max, apply_absolute_value=True)
                        trd0 = b.emit("dve", redA, waits=[tev], inc="dve")
                        def acc(e, run=run):
                            return e.tensor_tensor(V_S[:, run:run + 1], V_S[:, run:run + 1],
                                                   V_S[:, 15:16], ALU.max)
                        trd = b.emit("dve", acc, waits=[trd0], inc="dve")
                    prev_rd[ssl] = trd
                    bank_free[pb] = tev
                    bank_free[pb + 1] = tev
                    prev_sp[ssl] = dma("sp", ddst[128 * it: 128 * (it + 1),
                                                1024 * tp: 1024 * (tp + 1)],
                                       spst, f"ssp{ssl}", waits=[tev])

        if dbg == "ffnA":
            dob = d_out.bitcast(BF16)
            t1d = dma("sp", dob[0:512, 0:T], d_up[0:512, :], "st",
                      waits=[b.tok(f"ssp{0}"), b.tok("ssp1"), b.tok("ssp2"), b.tok("ssp3"), b.tok("dve")])
            t2d = dma("sp", dob[512:1024, 0:T], d_gate[0:512, :], "st",
                      waits=[b.tok(f"ssp{0}"), b.tok("ssp1"), b.tok("ssp2"), b.tok("ssp3"), b.tok("dve")])
            b.emit("sp", lambda e: e.nop(), waits=[t1d, t2d])
            b.finalize(block)
            return nc

        tg2 = barrier(1, [run_up, run_gate], [b.tok("dve"), b.tok("act")])
        tq1 = b.emit("dve", lambda e: e.tensor_scalar(V_SC[0:1, 2:3], GMX[:, 0:1],
                                                1.0 / QMA, 1e-12, ALU.mult, ALU.max),
               waits=[tg2], inc="dve")
        tq2 = b.emit("dve", lambda e: e.reciprocal(V_SC[0:1, 3:4], V_SC[0:1, 2:3]),
                     waits=[tq1], inc="dve")
        tq3 = b.emit("dve", lambda e: e.tensor_scalar(V_SC[0:1, 4:5], GMX[:, 1:2],
                                                1.0 / QMA, 1e-12, ALU.mult, ALU.max),
               waits=[tq2], inc="dve")
        tq4 = b.emit("dve", lambda e: e.reciprocal(V_SC[0:1, 5:6], V_SC[0:1, 4:5]),
                     waits=[tq3], inc="dve")
        t_ct = b.emit("dve", lambda e: e.tensor_single_scalar(V_SC[0:1, 6:7], V_SC[0:1, 4:5],
                                                       float(2 ** -16), ALU.mult),
                      waits=[tq4], inc="dve")
        tok_c2 = bcast_consts(2, 5, 34, t_ct)

        def ddq_slab(it, dm):
            base = 1024 * it + 128 * dm
            if it < 16:
                return B_Q[:, base: base + 128]
            return B_K[:, base - 16384: base - 16384 + 128]

        def ddq_tile(it):
            if it < 16:
                return B_Q[:, 1024 * it: 1024 * (it + 1)]
            return B_K[:, 1024 * (it - 16): 1024 * (it - 15)]

        dscr = B_Vf[:, 4352:5376]
        prev_wd = [b.tok("dve"), b.tok("dve")]
        prev_dd = None
        sb_prev = [tok_c2, tok_c2]
        tr_ev = {}
        ddq_last = None
        for g in range(3):
            sTg = B_Vf[0:32, 5376 + 0: 5376 + 1024]
            rTg = B_Vf[0:32, 6400 + 0: 6400 + 1024]
            ttr = None
            for dm in range(CD):
                def trg(e, dm=dm, g=g):
                    return e.transpose(
                        PS[0:32, 6 * 512 + 128 * dm: 6 * 512 + 128 * (dm + 1)],
                        SN[2][:, 96 * dm + 32 * g: 96 * dm + 32 * g + 32],
                        C_EYE[:, :])
                w0 = [tok_scales, sb_prev[1], prev_dd]
                if g == 0 and dm == 0:
                    w0 += [b.tok(f"ssp{i}") for i in range(4)]
                ttr = b.emit("pe", trg,
                             waits=w0 if dm == 0 else (),
                             inc="pe" if dm == CD - 1 else None)
            tev = b.emit("act", lambda e, sTg=sTg: e.copy(sTg, PS[0:32, 6 * 512: 6 * 512 + 1024]),
                         waits=[ttr], inc="act")
            trc = b.emit("dve", lambda e, sTg=sTg, rTg=rTg: e.reciprocal(rTg, sTg),
                         waits=[tev], inc="dve")
            for li in range(8):
                it = 8 * g + li
                sl = it % 2
                crow = it % 8
                wst = B_Vf[:, 1024 * sl: 1024 * sl + 1024]
                tok_w = dma("sp", wst, d_wd[128 * it: 128 * (it + 1), :],
                            f"lwd{sl}", waits=[prev_wd[sl]])
                pbase = 0 if it % 2 == 0 else 4
                s_b = PS[:, 512 * pbase: 512 * pbase + 1024]
                r_b = PS[:, 512 * pbase + 1024: 512 * pbase + 2048]
                wmm = [trc, sb_prev[0 if pbase == 0 else 1]]
                tmm = None
                for h2 in range(2):
                    def bs(e, crow=crow, h2=h2, pbase=pbase, sTg=sTg):
                        return e.matmul(PS[:, 512 * pbase + 512 * h2: 512 * pbase + 512 * (h2 + 1)],
                                        C_I32[:, 128 * crow: 128 * (crow + 1)],
                                        sTg[0:32, 512 * h2: 512 * (h2 + 1)],
                                        start=True, stop=True)
                    tmm = b.emit("pe", bs, waits=wmm if h2 == 0 else (), inc="pe")
                for h2 in range(2):
                    def br(e, crow=crow, h2=h2, pbase=pbase, rTg=rTg):
                        return e.matmul(PS[:, 512 * pbase + 1024 + 512 * h2: 512 * pbase + 1024 + 512 * (h2 + 1)],
                                        C_I32[:, 128 * crow: 128 * (crow + 1)],
                                        rTg[0:32, 512 * h2: 512 * (h2 + 1)],
                                        start=True, stop=True)
                    tmm = b.emit("pe", br, inc="pe")
                d1 = b.emit("dve", lambda e, wst=wst, r_b=r_b: e.tensor_tensor(
                    dscr, wst, r_b, ALU.mult), waits=[tmm, tok_w], inc="dve")
                d2 = b.emit("dve", lambda e: e.tensor_scalar(
                    dscr, dscr, CMAG, -CMAG, ALU.add, ALU.add), inc="dve")
                d3 = b.emit("dve", lambda e, it=it, s_b=s_b: e.tensor_tensor(
                    ddq_tile(it), dscr, s_b, ALU.mult), inc="dve")
                sb_prev[0 if pbase == 0 else 1] = d3
                prev_wd[sl] = d1
                prev_dd = d3
                ddq_last = d3
        for _bi in range(8):
            bank_free[_bi] = sb_prev[0] if _bi < 4 else sb_prev[1]

        B_Xf = B_X.ap().bitcast(F32)
        prev_g = [None] * 4
        prev_fa = [None] * 4
        prev_fb = [None] * 4
        prev_fc = [None] * 4
        prev_tst = [None] * 4
        tmax_tok = None
        NTB = I // 128 * (T // 1024)
        for k in range(NTB):
            it, th = k // 4, k % 4
            sst = k % 4
            gst = B_W[:, 1024 * sst: 1024 * (sst + 1)]
            tst = B_M[:, 1024 * sst: 1024 * (sst + 1)]
            fA = B_Wf[:, 4096 + 1024 * sst: 4096 + 1024 * sst + 1024]
            fB = B_Xf[:, 1024 * sst: 1024 * sst + 1024]
            fC = B_Xf[:, 4096 + 1024 * sst: 4096 + 1024 * sst + 1024]
            tok_g = dma("sp", gst, d_gate[128 * it: 128 * (it + 1),
                                          1024 * th: 1024 * (th + 1)],
                        f"lgb{sst}",
                        waits=[prev_g[sst]] + ([b.tok(f"ssp{i}") for i in range(4)] if k == 0 else []))
            a1 = b.emit("act", lambda e, gst=gst, fA=fA: e.activation(
                fA, gst, AF.Copy, bias=CMAG, scale=V_C[:, 37:38]),
                waits=[tok_g, tok_c2, prev_fa[sst]], inc="act")
            v1 = b.emit("dve", lambda e, fA=fA, fB=fB: e.tensor_single_scalar(
                fB, fA, -CMAG, ALU.add), waits=[a1, prev_fb[sst]], inc="dve")
            a3 = b.emit("act", lambda e, fB=fB, fC=fC: e.activation(
                fC, fB, AF.Sigmoid, scale=V_C[:, 36:37]),
                waits=[v1, prev_fc[sst]], inc="act")
            v2 = b.emit("dve", lambda e, fC=fC, fA=fA: e.tensor_scalar(
                fA, fC, 65536.0, CMAG, ALU.mult, ALU.add), waits=[a3], inc="dve")
            v3 = b.emit("dve", lambda e, fA=fA: e.tensor_scalar(
                fA, fA, -CMAG, 65535.0, ALU.add, ALU.min), inc="dve")
            v4 = b.emit("dve", lambda e, fA=fA, fB=fB, tst=tst: e.tensor_tensor(
                tst, fB, fA, ALU.mult), waits=[prev_tst[sst]], inc="dve")
            if tmax_tok is None:
                v5 = b.emit("dve", lambda e, tst=tst: e.tensor_reduce(
                    V_S[:, 3:4], tst, AX.X, ALU.max, apply_absolute_value=True), inc="dve")
            else:
                tv5 = b.emit("dve", lambda e, tst=tst: e.tensor_reduce(
                    V_S[:, 15:16], tst, AX.X, ALU.max, apply_absolute_value=True), inc="dve")
                v5 = b.emit("dve", lambda e: e.tensor_tensor(
                    V_S[:, 3:4], V_S[:, 3:4], V_S[:, 15:16], ALU.max),
                    waits=[tv5], inc="dve")
            tmax_tok = v5
            prev_g[sst] = a1
            prev_fa[sst] = v4
            prev_fb[sst] = v4
            prev_fc[sst] = v2
            prev_tst[sst] = dma("pool", d_t[128 * it: 128 * (it + 1),
                                            1024 * th: 1024 * (th + 1)],
                                tst, f"stb{sst}", waits=[v5])

        tg3 = barrier(2, [3], [b.tok("dve")])
        tr1 = b.emit("dve", lambda e: e.tensor_tensor(V_SC[0:1, 7:8], GMX[:, 0:1],
                                                V_SC[0:1, 6:7], ALU.mult),
               waits=[tg3], inc="dve")
        tr2 = b.emit("dve", lambda e: e.tensor_scalar(V_SC[0:1, 8:9], V_SC[0:1, 7:8],
                                                1.0 / QMA, 1e-12, ALU.mult, ALU.max),
               waits=[tr1], inc="dve")
        t_rt = b.emit("dve", lambda e: e.reciprocal(V_SC[0:1, 9:10], V_SC[0:1, 8:9]),
                      waits=[tr2], inc="dve")
        t_rtc = b.emit("dve", lambda e: e.tensor_tensor(V_SC[0:1, 16:17], V_SC[0:1, 9:10],
                                                        V_SC[0:1, 6:7], ALU.mult),
                       waits=[t_rt], inc="dve")
        tok_c3 = bcast_consts(16, 1, 39, t_rtc)

        prev_tl = [None] * 4
        prev_ul = [None] * 4
        prev_fa = [None] * 4
        prev_fb = [None] * 4
        prev_pst = [None] * 4
        pmax_tok = None
        for k in range(NTB):
            it, th = k // 4, k % 4
            sst = k % 4
            tstg = B_W[:, 1024 * sst: 1024 * (sst + 1)]
            ustg = B_W[:, 4096 + 1024 * sst: 4096 + 1024 * (sst + 1)]
            fA = B_Wf[:, 4096 + 1024 * sst: 4096 + 1024 * sst + 1024]
            fB = B_Xf[:, 1024 * sst: 1024 * sst + 1024]
            pst = B_M[:, 1024 * sst: 1024 * (sst + 1)]
            tok_t = dma("sp", tstg, d_t[128 * it: 128 * (it + 1),
                                        1024 * th: 1024 * (th + 1)],
                        f"ltc{sst}",
                        waits=[prev_tl[sst]] + ([b.tok(f"stb{i}") for i in range(4)] if k == 0 else []))
            tok_u = dma("act", ustg, d_up[128 * it: 128 * (it + 1),
                                          1024 * th: 1024 * (th + 1)],
                        f"luc{sst}", waits=[prev_ul[sst]])
            a1 = b.emit("act", lambda e, tstg=tstg, fA=fA: e.activation(
                fA, tstg, AF.Copy, bias=CMAG, scale=V_C[:, 39:40]),
                waits=[tok_t, tok_c3, prev_fa[sst]], inc="act")
            v1 = b.emit("dve", lambda e, fA=fA: e.tensor_single_scalar(
                fA, fA, -CMAG, ALU.add), waits=[a1], inc="dve")
            a2 = b.emit("act", lambda e, ustg=ustg, fB=fB: e.activation(
                fB, ustg, AF.Copy, bias=CMAG, scale=V_C[:, 35:36]),
                waits=[tok_u, tok_c3, prev_fb[sst]], inc="act")
            v2 = b.emit("dve", lambda e, fB=fB: e.tensor_single_scalar(
                fB, fB, -CMAG, ALU.add), waits=[a2], inc="dve")
            v3 = b.emit("dve", lambda e, fA=fA, fB=fB, pst=pst: e.tensor_tensor(
                pst, fA, fB, ALU.mult), waits=[prev_pst[sst]], inc="dve")
            if pmax_tok is None:
                v4 = b.emit("dve", lambda e, pst=pst: e.tensor_reduce(
                    V_S[:, 4:5], pst, AX.X, ALU.max, apply_absolute_value=True), inc="dve")
            else:
                tv4 = b.emit("dve", lambda e, pst=pst: e.tensor_reduce(
                    V_S[:, 15:16], pst, AX.X, ALU.max, apply_absolute_value=True), inc="dve")
                v4 = b.emit("dve", lambda e: e.tensor_tensor(
                    V_S[:, 4:5], V_S[:, 4:5], V_S[:, 15:16], ALU.max),
                    waits=[tv4], inc="dve")
            pmax_tok = v4
            prev_tl[sst] = a1
            prev_ul[sst] = a2
            prev_fa[sst] = v3
            prev_fb[sst] = v3
            prev_pst[sst] = dma("pool", d_P[128 * it: 128 * (it + 1),
                                            1024 * th: 1024 * (th + 1)],
                                pst, f"stp{sst}", waits=[v4])

        tg4 = barrier(3, [4], [b.tok("dve")])
        ts1_ = b.emit("dve", lambda e: e.tensor_tensor(V_SC[0:1, 10:11], V_SC[0:1, 8:9],
                                                V_SC[0:1, 2:3], ALU.mult),
               waits=[tg4], inc="dve")
        ts2_ = b.emit("dve", lambda e: e.tensor_tensor(V_SC[0:1, 11:12], V_SC[0:1, 10:11],
                                                GMX[:, 0:1], ALU.mult),
               waits=[ts1_], inc="dve")
        ts3_ = b.emit("dve", lambda e: e.tensor_scalar(V_SC[0:1, 12:13], V_SC[0:1, 11:12],
                                                1.0 / QMA, 1e-12, ALU.mult, ALU.max),
               waits=[ts2_], inc="dve")
        ts4_ = b.emit("dve", lambda e: e.reciprocal(V_SC[0:1, 13:14], V_SC[0:1, 12:13]),
               waits=[ts3_], inc="dve")
        t_co = b.emit("dve", lambda e: e.tensor_tensor(V_SC[0:1, 14:15], V_SC[0:1, 10:11],
                                                       V_SC[0:1, 13:14], ALU.mult),
                      waits=[ts4_], inc="dve")
        tok_c4 = bcast_consts(12, 3, 40, t_co)

        def oq_slab(set_, it2):
            if set_ == 0:
                return B_X[:, 512 * it2: 512 * (it2 + 1)]
            if it2 < 8:
                return B_X[:, 12288 + 512 * it2: 12288 + 512 * (it2 + 1)]
            return B_M[:, 512 * (it2 - 8): 512 * (it2 - 7)]

        s2ring = B_Wf[:, 0:4096]
        prev_ost = [None] * 4
        prev_fd = [None] * 4
        prev_oq = [None, None]
        prep_done = [None, None]
        prev_xr = [None] * 4
        prev_ap = [None, None]
        out_toks = [None, None]
        lsum_d = lsq_d = None

        def prep_oq(tc, extra_w=()):
            set_ = tc % 2
            last = None
            for it2 in range(CI):
                r = it2 % 4
                ost = B_W[:, 8192 + 512 * r: 8192 + 512 * (r + 1)]
                fD = B_Wf[:, 5120 + 512 * r: 5120 + 512 * (r + 1)]
                tl = dma("sp", ost, d_P[128 * it2: 128 * (it2 + 1),
                                        512 * tc: 512 * (tc + 1)],
                         f"lpd{r}", waits=[prev_ost[r]] + list(extra_w))
                pa = b.emit("act", lambda e, ost=ost, fD=fD: e.activation(
                    fD, ost, AF.Copy, bias=CMAG, scale=V_C[:, 42:43]),
                    waits=[tl, tok_c4, prev_fd[r]], inc="act")
                pv = b.emit("dve", lambda e, fD=fD, set_=set_, it2=it2: e.tensor_scalar(
                    oq_slab(set_, it2), fD, -CMAG, V_C[:, 40:41], ALU.add, ALU.mult),
                    waits=[pa, prev_oq[set_]], inc="dve")
                prev_ost[r] = pa
                prev_fd[r] = pv
                last = pv
            prep_done[set_] = last

        prep_oq(0, extra_w=[b.tok(f"stp{i}") for i in range(4)] + [b.tok("dve"), ddq_last])
        for tc in range(8):
            set_ = tc % 2
            ring_toks = []
            lsum_d = lsq_d = None
            for dm in range(CD):
                bki = dm % 4
                w = [ddq_last, prep_done[set_], bank_free[bki]]
                for it2 in range(CI):
                    def mmd(e, bki=bki, it2=it2, dm=dm, set_=set_):
                        return e.matmul(bank(bki), ddq_slab(it2, dm),
                                        oq_slab(set_, it2),
                                        start=(it2 == 0), stop=(it2 == CI - 1))
                    last = b.emit("pe", mmd, waits=w if it2 == 0 else (),
                                  inc="pe" if it2 == CI - 1 else None)
                xr = B_Vf[:, 512 * (dm % 4): 512 * (dm % 4) + 512]
                tok_xr = dma("act", xr,
                             d_xln[128 * dm: 128 * (dm + 1), 512 * tc: 512 * (tc + 1)],
                             f"lx2{dm % 4}",
                             waits=[prev_xr[dm % 4]] + ([ddq_last] if tc == 0 and dm < 4 else []))
                def adds2(e, bki=bki, dm=dm, xr=xr):
                    return e.tensor_tensor(s2ring[:, 512 * dm: 512 * (dm + 1)],
                                           bank(bki), xr, ALU.add)
                ts2 = b.emit("dve", adds2, waits=[last, tok_xr], inc="dve")
                prev_xr[dm % 4] = ts2
                bank_free[bki] = ts2
                ring_toks.append(ts2)
                def cpbf(e, dm=dm):
                    return e.tensor_single_scalar(B_M[:, 8192:8704],
                                                  s2ring[:, 512 * dm: 512 * (dm + 1)],
                                                  1.0, ALU.mult)
                tbf = b.emit("dve", cpbf, waits=[lsum_d], inc="dve")
                tsq = b.emit("dve", lambda e: e.tensor_tensor(B_M[:, 8704:9216],
                                                              B_M[:, 8192:8704],
                                                              B_M[:, 8192:8704], ALU.mult),
                             waits=[lsq_d], inc="dve")
                wst2 = [tbf] + ([bank_free[4], bank_free[5]] if dm == 0 else [])
                lsum_d = b.emit("pe", lambda e, dm=dm: e.matmul(
                    PS[0:1, 4 * 512: 4 * 512 + 512], C_T[:, 2048:2049], B_M[:, 8192:8704],
                    start=(dm == 0), stop=(dm == CD - 1)),
                    waits=wst2, inc="pe")
                lsq_d = b.emit("pe", lambda e, dm=dm: e.matmul(
                    PS[0:1, 5 * 512: 5 * 512 + 512], C_T[:, 2048:2049], B_M[:, 8704:9216],
                    start=(dm == 0), stop=(dm == CD - 1)),
                    waits=[tsq], inc="pe")
            if tc < 7:
                prep_oq(tc + 1)
            trm = b.emit("dve", lambda e: e.tensor_single_scalar(V_R[0:1, 0:512],
                         PS[0:1, 4 * 512: 4 * 512 + 512], 1.0 / D, ALU.mult),
                         waits=[lsum_d], inc="dve")
            trq = b.emit("dve", lambda e: e.tensor_single_scalar(V_R[0:1, 512:1024],
                         PS[0:1, 5 * 512: 5 * 512 + 512], 1.0 / D, ALU.mult),
                         waits=[lsq_d], inc="dve")
            bank_free[4] = trm
            bank_free[5] = trq
            trm2 = b.emit("dve", lambda e: e.tensor_tensor(V_R[0:1, 1024:1536], V_R[0:1, 0:512],
                                                    V_R[0:1, 0:512], ALU.mult),
                          waits=[trm], inc="dve")
            tvar = b.emit("dve", lambda e: e.tensor_tensor(V_R[0:1, 512:1024],
                                                           V_R[0:1, 512:1024],
                                                           V_R[0:1, 1024:1536], ALU.subtract),
                          waits=[trq, trm2], inc="dve")
            tln = b.emit("act", lambda e: e.activation(V_R[0:1, 1024:1536],
                                                       V_R[0:1, 512:1024], AF.Ln, bias=CONSTV[0:1, 0:1]),
                         waits=[tvar], inc="act")
            trstd = b.emit("act", lambda e: e.activation(V_R[0:1, 1024:1536],
                                                         V_R[0:1, 1024:1536], AF.Exp,
                                                         scale=-0.5), inc="act")
            tmr = b.emit("dve", lambda e: e.tensor_tensor(V_R[0:1, 1536:2048],
                                                          V_R[0:1, 0:512],
                                                          V_R[0:1, 1024:1536], ALU.mult),
                         waits=[trstd], inc="dve")
            tb6 = b.emit("pe", lambda e: e.matmul(PS[:, 6 * 512: 7 * 512], C_OR[:, :],
                                                  V_R[0:1, 1024:1536], start=True, stop=True),
                         waits=[trstd, bank_free[6]], inc="pe")
            tb7 = b.emit("pe", lambda e: e.matmul(PS[:, 7 * 512: 8 * 512], C_OR[:, :],
                                                  V_R[0:1, 1536:2048], start=True, stop=True),
                         waits=[tmr, bank_free[7]], inc="pe")
            last_ap = None
            for dm in range(CD):
                pp = dm % 2
                ap1 = B_Vf[:, 2048 + 512 * pp: 2048 + 512 * (pp + 1)]
                ap2 = B_Vf[:, 3072 + 512 * pp: 3072 + 512 * (pp + 1)]
                ta1 = b.emit("dve", lambda e, dm=dm, ap1=ap1: e.tensor_tensor(
                    ap1, s2ring[:, 512 * dm: 512 * (dm + 1)],
                    PS[:, 6 * 512: 7 * 512], ALU.mult),
                    waits=[tb6, ring_toks[dm], prev_ap[pp]], inc="dve")
                ta2 = b.emit("dve", lambda e, ap1=ap1, ap2=ap2: e.tensor_tensor(
                    ap2, ap1, PS[:, 7 * 512: 8 * 512],
                    ALU.subtract), waits=[tb7, out_toks[pp]], inc="dve")
                out_toks[pp] = dma("pool", d_out[128 * dm: 128 * (dm + 1),
                                                 512 * tc: 512 * (tc + 1)],
                                   ap2, f"sout{pp}", waits=[ta2])
                prev_ap[pp] = ta2
                last_ap = ta2
            bank_free[6] = last_ap
            bank_free[7] = last_ap

        b.emit("sp", lambda e: e.nop(), waits=[out_toks[0], out_toks[1]])
        b.finalize(block)
        return nc


def _prep_core_inputs(inputs, c):
    x = np.asarray(inputs["L0_attn_x"])
    Wq = np.asarray(inputs["L0_attn_W_q"])
    Wk = np.asarray(inputs["L0_attn_W_k"])
    Wv = np.asarray(inputs["L0_attn_W_v"])
    Wo = np.asarray(inputs["L0_attn_W_o"])
    wg = np.asarray(inputs["L0_ffn_w_gate"])
    wu = np.asarray(inputs["L0_ffn_w_up"])
    wd = np.asarray(inputs["L0_ffn_w_down"])
    perm = np.zeros(D, np.int64)
    for h in range(H):
        perm[DH * h: DH * h + 32] = DH * h + 2 * np.arange(32)
        perm[DH * h + 32: DH * h + 64] = DH * h + 2 * np.arange(32) + 1
    xc = x[2 * c: 2 * c + 2].reshape(T, D)
    return {
        "xT": np.ascontiguousarray(xc.T),
        "wqT": np.ascontiguousarray(Wq[perm].T),
        "wkT": np.ascontiguousarray(Wk[perm].T),
        "wvT": np.ascontiguousarray(Wv.T),
        "woT": np.ascontiguousarray(Wo.T),
        "wupT": np.ascontiguousarray(wu.T),
        "wgT": np.ascontiguousarray(wg.T),
        "wdT": np.ascontiguousarray(wd.T),
        "wupN": np.ascontiguousarray(wu),
        "wgN": np.ascontiguousarray(wg),
        "wdN": np.ascontiguousarray(wd),
    }


def kernel(**inputs):
    nc = build_program(DBG_STAGE)
    in_maps = [_prep_core_inputs(inputs, c) for c in range(8)]
    res = run_bass_kernel_spmd(nc, in_maps, core_ids=list(range(8)))
    outs = []
    for c in range(8):
        o = np.asarray(res.results[c]["outT"])
        outs.append(o.T.reshape(2, L, D))
    return np.concatenate(outs, axis=0).astype(np.float32)


# version 04
# speedup vs baseline: 1.1470x; 1.1470x over previous
import math
import numpy as np
import ml_dtypes
import concourse.bass as bass
import concourse.mybir as mybir
from concourse.bass_utils import run_bass_kernel_spmd

F32 = mybir.dt.float32
BF16 = mybir.dt.bfloat16
AF = mybir.ActivationFunctionType
ALU = mybir.AluOpType
AX = mybir.AxisListType

D, H, DH, L, NSEQ = 1024, 16, 64, 2048, 2
T = NSEQ * L
I = 3072
QB = 32
CD = D // 128
CI = I // 128
CMAG = float(3 * 2 ** 22)
QMA = 32767.0
EPS = 1e-5
ROPE_BASE = 10000.0
SAFE = 1.02 / 8.0

ENGS = ("pe", "act", "dve", "pool", "sp")
A_LIMIT = CI
M_LIMIT = 2
DBG_STAGE = None


class KB:
    def __init__(self, nc):
        self.nc = nc
        self.ops = {e: [] for e in ENGS}
        self.sems = {}
        self.cnt = {}
        self.waited = {e: {} for e in ENGS}

    def sem(self, name):
        if name not in self.sems:
            self.sems[name] = self.nc.alloc_semaphore(name)
            self.cnt[name] = 0
        return self.sems[name]

    def __init_dma(self):
        if not hasattr(self, "dma_sems"):
            self.dma_sems = set()

    def emit(self, eng, fn, waits=(), inc=None, inc_k=1):
        self.__init_dma()
        if eng in ("dve", "act", "pool"):
            waits = list(waits) + [self.tok(eng)]
        real = []
        for tokv in waits:
            if tokv is None:
                continue
            s, v = tokv
            if s in self.dma_sems:
                v = self.cnt[s]
            assert s in self.cnt and v <= self.cnt[s], f"future token {s}:{v}"
            if v > self.waited[eng].get(s, 0):
                real.append((self.sems[s], v))
                self.waited[eng][s] = v
        tok = None
        semh = None
        if inc is not None:
            self.sem(inc)
            if inc_k == 16:
                self.dma_sems.add(inc)
            self.cnt[inc] += inc_k
            tok = (inc, self.cnt[inc])
            semh = self.sems[inc]

        def run(e, real=real, fn=fn, semh=semh, k=inc_k):
            for (sh, v) in real:
                e.wait_ge(sh, v)
            ins = fn(e)
            if semh is not None:
                ins.then_inc(semh, k)
        self.ops[eng].append(run)
        return tok

    def tok(self, name):
        n = self.cnt.get(name, 0)
        return (name, n) if n > 0 else None

    def finalize(self, block):
        mp = {"pe": block.tensor, "act": block.scalar, "dve": block.vector,
              "pool": block.gpsimd, "sp": block.sync}
        for e in ENGS:
            ops = self.ops[e]

            def body(eng, ops=ops):
                for f in ops:
                    f(eng)
            mp[e](body)


def _rope_tables():
    pos = np.arange(L, dtype=np.float32)
    idx = np.arange(0, DH, 2, dtype=np.float32)
    freqs = np.exp(np.float32(-math.log(ROPE_BASE)) * idx / np.float32(DH)).astype(np.float32)
    ang = pos[:, None] * freqs[None, :]
    sin = np.sin(ang).astype(np.float32)
    cos = np.cos(ang).astype(np.float32)
    C = np.zeros((128, L), np.float32)
    S = np.zeros((128, L), np.float32)
    for p in range(128):
        i = p % 32
        C[p] = cos[:, i]
        S[p] = sin[:, i] * (-1.0 if (p % 64) < 32 else 1.0)
    return C, S


def build_program(dbg=None):
    nc = bass.Bass("TRN2", target_bir_lowering=False, debug=False)

    d_x = nc.dram_tensor("xT", [D, T], F32, kind="ExternalInput").ap()
    d_wq = nc.dram_tensor("wqT", [D, D], F32, kind="ExternalInput").ap()
    d_wk = nc.dram_tensor("wkT", [D, D], F32, kind="ExternalInput").ap()
    d_wv = nc.dram_tensor("wvT", [D, D], F32, kind="ExternalInput").ap()
    d_wo = nc.dram_tensor("woT", [D, D], F32, kind="ExternalInput").ap()
    d_wup = nc.dram_tensor("wupT", [D, I], F32, kind="ExternalInput").ap()
    d_wg = nc.dram_tensor("wgT", [D, I], F32, kind="ExternalInput").ap()
    d_wd = nc.dram_tensor("wdT", [I, D], F32, kind="ExternalInput").ap()
    d_wupN = nc.dram_tensor("wupN", [I, D], F32, kind="ExternalInput").ap()
    d_wgN = nc.dram_tensor("wgN", [I, D], F32, kind="ExternalInput").ap()
    d_wdN = nc.dram_tensor("wdN", [D, I], F32, kind="ExternalInput").ap()
    d_out = nc.dram_tensor("outT", [D, T], F32, kind="ExternalOutput").ap()

    d_xln = nc.dram_tensor("xlnd", [D, T], F32, kind="Internal").ap()
    d_up = nc.dram_tensor("upd", [I, T], BF16, kind="Internal").ap()
    d_gate = nc.dram_tensor("gated", [I, T], BF16, kind="Internal").ap()
    d_t = nc.dram_tensor("td", [I, T], BF16, kind="Internal").ap()
    d_P = nc.dram_tensor("Pd", [I, T], BF16, kind="Internal").ap()
    d_o = nc.dram_tensor("od", [I, T], BF16, kind="Internal").ap()
    d_srow = nc.dram_tensor("srowd", [1, H * L], F32, kind="Internal").ap()
    d_rrow = nc.dram_tensor("rrowd", [1, H * L], F32, kind="Internal").ap()
    cc_in = [nc.dram_tensor(f"ccin{i}", [1, 512], F32, kind="Internal").ap() for i in range(4)]
    cc_out = [nc.dram_tensor(f"ccout{i}", [1, 512], F32, kind="Internal").ap() for i in range(4)]

    Cnp, Snp = _rope_tables()
    i_tabC = nc.inline_tensor(Cnp.astype(ml_dtypes.bfloat16), "tabC").ap()
    i_tabS = nc.inline_tensor(Snp.astype(ml_dtypes.bfloat16), "tabS").ap()
    swap_np = np.zeros((128, 128), np.float32)
    for p in range(128):
        swap_np[p ^ 32, p] = 1.0
    i_swap = nc.inline_tensor(swap_np.astype(ml_dtypes.bfloat16), "swapM").ap()
    i_eyeF = nc.inline_tensor(np.eye(128, dtype=np.float32), "eyeF").ap()
    i_onescolB = nc.inline_tensor(np.ones((128, 1), ml_dtypes.bfloat16), "onescolB").ap()
    i_onesrowF = nc.inline_tensor(np.ones((1, 128), np.float32), "onesrowF").ap()
    constv_np = np.tile(np.array([[EPS, CMAG, -CMAG, 0.0]], np.float32), (128, 1))
    i_constv = nc.inline_tensor(constv_np, "constv").ap()
    ind_np = np.zeros((D, H), np.float32)
    for c in range(CD):
        ind_np[128 * c: 128 * c + 64, 2 * c] = 1.0
        ind_np[128 * c + 64: 128 * c + 128, 2 * c + 1] = 1.0
    i_ind = nc.inline_tensor(ind_np.astype(ml_dtypes.bfloat16), "indnq").ap()
    ind32_np = np.zeros((QB, CD * 128), np.float32)
    for c in range(CD):
        for p in range(128):
            ind32_np[4 * c + p // 32, 128 * c + p] = 1.0
    i_ind32 = nc.inline_tensor(ind32_np[:32].astype(np.float32), "ind32").ap()

    b = KB(nc)
    import contextlib
    with contextlib.ExitStack() as ctx:
        def sb(name, shape, dtype):
            return ctx.enter_context(nc.sbuf_tensor(name, shape, dtype))
        B_X = sb("B_X", [128, 16384], BF16)
        B_Q = sb("B_Q", [128, 16384], BF16)
        B_K = sb("B_K", [128, 16384], BF16)
        B_V = sb("B_V", [128, 16640], BF16)
        B_W = sb("B_W", [128, 16384], BF16)
        B_M = sb("B_M", [128, 12288], BF16)
        B_P = B_M
        V_S = sb("V_S", [128, 16], F32)
        V_C = sb("V_C", [128, 48], F32)
        V_R = sb("V_R", [1, 2048], F32)
        V_R2 = sb("V_R2", [1, 1024], F32)
        V_SC = sb("V_SC", [1, 32], F32)
        C_T = sb("C_T", [128, 2052], BF16)
        C_T2 = sb("C_T2", [128, 2048], BF16)
        C_SW = sb("C_SW", [128, 128], BF16)
        C_EYE = sb("C_EYE", [128, 128], F32)
        C_IND = sb("C_IND", [128, 128], BF16)
        C_OR = sb("C_OR", [1, 128], F32)
        CONSTV = sb("CONSTV", [128, 4], F32)
        nc.const_aps.register(F32, EPS, CONSTV[:, 0:1])
        nc.const_aps.register(F32, CMAG, CONSTV[:, 1:2])
        nc.const_aps.register(F32, -CMAG, CONSTV[:, 2:3])
        C_I32 = None
        PS = ctx.enter_context(nc.psum_tensor("PS", [128, 4096], F32))
        block = ctx.enter_context(nc.Block())

        B_Mf = B_M.ap().bitcast(F32)
        B_Wf = B_W.ap().bitcast(F32)
        B_Vf = B_V.ap().bitcast(F32)

        def bank(i, n=512):
            return PS[:, 512 * i: 512 * i + n]

        def dma(q, out_ap, in_ap, sem="ld", waits=()):
            return b.emit(q, lambda e: e.dma_start(out=out_ap, in_=in_ap),
                          waits=waits, inc=sem, inc_k=16)

        dma("sp", C_T[:, 0:2048], i_tabC, "ldc")
        dma("sp", C_T2[:, :], i_tabS, "ldc")
        dma("sp", C_T[:, 2048:2049], i_onescolB, "ldc")
        dma("sp", C_SW[:, :], i_swap, "ldc")
        dma("sp", C_EYE[:, :], i_eyeF, "ldc")
        dma("sp", C_IND.ap()[:, 0:128].rearrange("p (c h) -> p c h", c=CD),
            i_ind.rearrange("(c p) h -> p c h", p=128), "ldc")
        dma("sp", CONSTV[:, :], i_constv, "ldc")
        tok_consts = dma("sp", C_OR[:, :], i_onesrowF, "ldc")
        CCST = V_R[0:1, 0:512]
        GMX = V_R[0:1, 512:1024]

        def load_xseq(j, waits):
            tk = None
            for c in range(CD):
                tk = dma("pool", B_X[:, 2048 * c: 2048 * (c + 1)],
                         d_x[128 * c: 128 * (c + 1), L * j: L * (j + 1)],
                         "gx", waits)
            return tk

        def load_w(dr, slot, waits):
            return dma("pool",
                       B_W.ap()[:, 8192 * slot: 8192 * (slot + 1)].rearrange(
                           "p (c n) -> p c n", c=CD),
                       dr.rearrange("(c p) n -> p c n", p=128), f"gw{slot}", waits)

        tok_x = load_xseq(0, ())
        tok_wq = load_w(d_wq, 0, ())
        tok_wk = load_w(d_wk, 1, ())

        bank_free = [None] * 8
        tok_oproj_done = None
        tok_vproj_done = None
        pv_free = [None, None]
        tok_dbg_last = None

        if dbg == "simA":
            tok_rope = None
            tok_oproj_done = b.tok("pe")
            tok_vproj_done = b.tok("pe")
            b.emit("dve", lambda e: e.memset(V_S[:, :], 1.0), inc="dve")
            b.emit("dve", lambda e: e.memset(B_Q[:, :], 0.0), inc="dve")
            tz = b.emit("dve", lambda e: e.memset(B_K[:, :], 0.0), inc="dve")
            for c_ in range(CD):
                dma("sp", d_xln[128 * c_: 128 * (c_ + 1), :],
                    B_Q.ap().bitcast(F32)[:, 0:T], "sxln", waits=[tz])
        for j in range([] if dbg == "simA" else range(NSEQ) and NSEQ or NSEQ) if False else range(0 if dbg == "simA" else NSEQ):
            if j == 1:
                tok_x = load_xseq(1, [tok_oproj_done])
                tok_wq = load_w(d_wq, 0, [tok_vproj_done, b.tok("dve")])
                tok_wk = load_w(d_wk, 1, [tok_oproj_done])

            def proj_qk(wtok, slot, dst):
                last_mm = None
                last_ev = None
                for cq in range(CD):
                    for t4 in range(4):
                        bki = (cq * 4 + t4) % 4
                        w = [wtok, tok_x, tok_consts, bank_free[bki]]
                        for c in range(CD):
                            def mm(e, bki=bki, c=c, cq=cq, t4=t4, slot=slot):
                                return e.matmul(
                                    bank(bki),
                                    B_W[:, 8192 * slot + 1024 * c + 128 * cq:
                                        8192 * slot + 1024 * c + 128 * cq + 128],
                                    B_X[:, 2048 * c + 512 * t4: 2048 * c + 512 * (t4 + 1)],
                                    start=(c == 0), stop=(c == CD - 1))
                            last_mm = b.emit("pe", mm, waits=w if c == 0 else (),
                                             inc="pe" if c == CD - 1 else None)
                        def ev(e, bki=bki, cq=cq, t4=t4, dst=dst):
                            return e.copy(dst[:, 2048 * cq + 512 * t4: 2048 * cq + 512 * (t4 + 1)],
                                          bank(bki))
                        last_ev = b.emit("act", ev, waits=[last_mm], inc="act")
                        bank_free[bki] = last_ev
                return last_ev, last_mm

            tok_qe, tok_qmm = proj_qk(tok_wq, 0, B_Q)
            tok_ke, tok_kmm = proj_qk(tok_wk, 1, B_K)
            tok_wv = load_w(d_wv, 0, [tok_qmm])
            if dbg == "qraw" and j == 0:
                tdl = None
                for c in range(CD):
                    tdl = dma("pool", d_out[128 * c: 128 * (c + 1), 0:L],
                              B_Q[:, 2048 * c: 2048 * (c + 1)], "gst", waits=[tok_qe])
                b.emit("sp", lambda e: e.nop(), waits=[tdl])
                b.finalize(block)
                return nc

            prev_r1 = None
            prev_r3 = None
            for (src, etok) in ((B_Q, tok_qe), (B_K, tok_ke)):
                for cq in range(CD):
                    sw_last = None
                    for t4 in range(4):
                        bki = 4 + (t4 % 2)
                        w = [etok, tok_consts, bank_free[bki], prev_r1]
                        def mmsw(e, bki=bki, src=src, cq=cq, t4=t4):
                            return e.matmul(bank(bki), C_SW[:, :],
                                            src[:, 2048 * cq + 512 * t4: 2048 * cq + 512 * (t4 + 1)],
                                            start=True, stop=True)
                        tmm = b.emit("pe", mmsw, waits=w, inc="pe")
                        def evsw(e, bki=bki, t4=t4):
                            return e.copy(B_M[:, 512 * t4: 512 * (t4 + 1)], bank(bki))
                        sw_last = b.emit("act", evsw, waits=[tmm, prev_r1], inc="act")
                        bank_free[bki] = sw_last
                    def r1(e, src=src, cq=cq):
                        return e.tensor_tensor(B_M[:, 2048:4096],
                                               src[:, 2048 * cq: 2048 * (cq + 1)],
                                               C_T[:, 0:2048], ALU.mult)
                    b.emit("dve", r1, waits=[sw_last], inc="dve")
                    def r2(e):
                        return e.tensor_tensor(B_M[:, 4096:6144], B_M[:, 0:2048],
                                               C_T2[:, :], ALU.mult)
                    prev_r1 = b.emit("dve", r2, inc="dve")
                    def r3(e, src=src, cq=cq):
                        return e.tensor_tensor(src[:, 2048 * cq: 2048 * (cq + 1)],
                                               B_M[:, 2048:4096], B_M[:, 4096:6144],
                                               ALU.add)
                    prev_r3 = b.emit("dve", r3, inc="dve")
            tok_rope = prev_r3

            def vones(e):
                v = B_V.ap().rearrange("p (s w) -> p s w", w=65)
                return e.memset(v[:, :, 64:65], 1.0)
            tok_vones = b.emit("pool", vones,
                               waits=[b.tok("pe") if j == 1 else None], inc="pool")
            vlast_ev = None
            for st in range(16):
                for d2 in range(2):
                    bki = (st * 2 + d2) % 4
                    w = [tok_wv, bank_free[bki]]
                    for c in range(CD):
                        def mmv(e, bki=bki, c=c, st=st, d2=d2):
                            return e.matmul(
                                bank(bki),
                                B_X[:, 2048 * c + 128 * st: 2048 * c + 128 * (st + 1)],
                                B_W[:, 1024 * c + 512 * d2: 1024 * c + 512 * (d2 + 1)],
                                start=(c == 0), stop=(c == CD - 1))
                        last = b.emit("pe", mmv, waits=w if c == 0 else (),
                                      inc="pe" if c == CD - 1 else None)
                    def evv(e, bki=bki, st=st, d2=d2):
                        dst = B_V.ap()[:, 1040 * st: 1040 * (st + 1)].rearrange(
                            "p (h w) -> p h w", w=65)[:, 8 * d2: 8 * (d2 + 1), 0:64]
                        src = bank(bki).rearrange("p (h w) -> p h w", w=64)
                        return e.copy(dst, src)
                    vlast_ev = b.emit("act", evv, waits=[last, tok_vones], inc="act")
                    bank_free[bki] = vlast_ev
            tok_vproj_done = b.tok("pe")
            tok_wo = load_w(d_wo, 1, [tok_kmm])

            prev_mmn = None
            nsc = B_W[:, 0:512]
            for (src, base_col) in ((B_Q, 0), (B_K, 4)):
                for t4 in range(4):
                    w6 = [bank_free[6]]
                    last = None
                    for cq in range(CD):
                        def sqr(e, src=src, cq=cq, t4=t4):
                            return e.tensor_tensor(
                                nsc,
                                src[:, 2048 * cq + 512 * t4: 2048 * cq + 512 * (t4 + 1)],
                                src[:, 2048 * cq + 512 * t4: 2048 * cq + 512 * (t4 + 1)],
                                ALU.mult)
                        tsq = b.emit("dve", sqr, waits=[tok_rope, prev_mmn, tok_vproj_done], inc="dve")
                        def mmn(e, cq=cq):
                            return e.matmul(PS[0:16, 6 * 512: 6 * 512 + 512],
                                            C_IND[:, 16 * cq: 16 * (cq + 1)],
                                            nsc,
                                            start=(cq == 0), stop=(cq == CD - 1))
                        last = b.emit("pe", mmn, waits=[tsq] + (w6 if cq == 0 else []),
                                      inc="pe")
                        prev_mmn = last
                    col = base_col + t4
                    def red(e, col=col):
                        return e.tensor_reduce(V_S[0:16, col:col + 1],
                                               PS[0:16, 6 * 512: 6 * 512 + 512],
                                               AX.X, ALU.max, apply_absolute_value=True)
                    rt = b.emit("dve", red, waits=[last], inc="dve")
                    if dbg == "ind2" and j == 0 and src is B_Q and t4 == 0:
                        rt = b.emit("dve", lambda e: e.tensor_copy(
                            B_Mf[:, 512:1024][0:16, 0:512],
                            PS[0:16, 6 * 512: 6 * 512 + 512]), inc="dve")
                    bank_free[6] = rt
            tn1 = b.emit("dve", lambda e: e.tensor_reduce(V_S[0:16, 8:9], V_S[0:16, 0:4],
                                                          AX.X, ALU.max), inc="dve")
            tn2 = b.emit("dve", lambda e: e.tensor_reduce(V_S[0:16, 9:10], V_S[0:16, 4:8],
                                                          AX.X, ALU.max),
                         waits=[tn1], inc="dve")
            t3_ = b.emit("dve", lambda e: e.tensor_tensor(V_S[0:16, 10:11], V_S[0:16, 8:9],
                                                          V_S[0:16, 9:10], ALU.mult),
                         waits=[tn2], inc="dve")
            t4_ = b.emit("act", lambda e: e.activation(V_S[0:16, 11:12], V_S[0:16, 10:11],
                                                       AF.Sqrt), waits=[t3_], inc="act")
            for t4 in range(4):
                w6 = [bank_free[6]]
                last = None
                for cq in range(CD):
                    def dgm(e, cq=cq, t4=t4):
                        return e.tensor_tensor(
                            nsc,
                            B_Q[:, 2048 * cq + 512 * t4: 2048 * cq + 512 * (t4 + 1)],
                            B_K[:, 2048 * cq + 512 * t4: 2048 * cq + 512 * (t4 + 1)],
                            ALU.mult)
                    tsq = b.emit("dve", dgm, waits=[prev_mmn, tn2], inc="dve")
                    def mmn2(e, cq=cq):
                        return e.matmul(PS[0:16, 6 * 512: 6 * 512 + 512],
                                        C_IND[:, 16 * cq: 16 * (cq + 1)],
                                        nsc, start=(cq == 0), stop=(cq == CD - 1))
                    last = b.emit("pe", mmn2, waits=[tsq] + (w6 if cq == 0 else []),
                                  inc="pe")
                    prev_mmn = last
                def red2(e, t4=t4):
                    return e.tensor_reduce(V_S[0:16, t4:t4 + 1],
                                           PS[0:16, 6 * 512: 6 * 512 + 512],
                                           AX.X, ALU.max)
                rt = b.emit("dve", red2, waits=[last], inc="dve")
                bank_free[6] = rt
            tdg = b.emit("dve", lambda e: e.tensor_reduce(V_S[0:16, 13:14], V_S[0:16, 0:4],
                                                          AX.X, ALU.max),
                         waits=[bank_free[6]], inc="dve")
            tnA = b.emit("dve", lambda e: e.tensor_single_scalar(V_S[0:16, 14:15],
                         V_S[0:16, 13:14], 0.125, ALU.mult), waits=[tdg], inc="dve")
            tnB = b.emit("dve", lambda e: e.tensor_scalar(V_S[0:16, 15:16],
                         V_S[0:16, 11:12], SAFE, -85.0, ALU.mult, ALU.add),
                         waits=[t4_], inc="dve")
            tnC = b.emit("dve", lambda e: e.tensor_tensor(V_S[0:16, 10:11], V_S[0:16, 14:15],
                                                          V_S[0:16, 15:16], ALU.max),
                         waits=[tnA, tnB], inc="dve")
            t5_ = b.emit("dve", lambda e: e.tensor_single_scalar(V_S[0:16, 12:13],
                         V_S[0:16, 10:11], -1.0, ALU.mult),
                         waits=[tnC], inc="dve")
            w7 = [t5_, bank_free[7]]
            ttr = b.emit("pe", lambda e: e.transpose(PS[0:1, 7 * 512: 7 * 512 + 16],
                                                     V_S[0:16, 12:13], C_EYE[0:16, 0:16]),
                         waits=w7, inc="pe")
            tcp_ = b.emit("dve", lambda e: e.tensor_copy(V_SC[0:1, 0:16],
                                                         PS[0:1, 7 * 512: 7 * 512 + 16]),
                          waits=[ttr], inc="dve")
            tbc_ = b.emit("pe", lambda e: e.matmul(PS[:, 7 * 512: 7 * 512 + 16],
                                                   C_OR[:, :], V_SC[0:1, 0:16],
                                                   start=True, stop=True),
                          waits=[tcp_], inc="pe")
            tok_bias = b.emit("act", lambda e: e.copy(V_C[:, 0:16],
                                                      PS[:, 7 * 512: 7 * 512 + 16]),
                              waits=[tbc_], inc="act")
            bank_free[7] = tok_bias

            if dbg == "ind2" and j == 0:
                tc2 = dma("sp", d_out[0:16, 0:512], B_Mf[0:16, 512:1024], "st",
                          waits=[b.tok("dve")])
                tc3 = dma("sp", d_out[0:16, 512:528], V_S[0:16, 0:16], "st",
                          waits=[b.tok("dve")])
                b.emit("sp", lambda e: e.nop(), waits=[tc2, tc3])
                b.finalize(block)
                return nc

            if dbg == "ind" and j == 0:
                td1 = dma("pool", d_out[0:128, 0:128], C_IND[:, 0:128], "gst",
                          waits=[b.tok("dve"), b.tok("pe")])
                td2 = dma("pool", d_out[0:128, 128:640], B_M[:, 0:512], "gst",
                          waits=[b.tok("dve")])
                tcpy = b.emit("dve", lambda e: e.tensor_copy(
                    B_Mf[:, 512:1024][0:16, 0:512], PS[0:16, 6 * 512: 6 * 512 + 512]),
                    waits=[b.tok("pe")], inc="dve")
                td3 = dma("sp", d_out[0:16, 1024:1536], B_Mf[0:16, 512:1024], "st",
                          waits=[tcpy])
                b.emit("sp", lambda e: e.nop(), waits=[td1, td2, td3])
                b.finalize(block)
                return nc

            if dbg == "bias" and j == 0:
                tdbg1 = dma("sp", d_out[0:128, 0:48], V_C[:, :], "st", waits=[tok_bias])
                tdbg2 = dma("sp", d_out[0:16, 64:80], V_S[0:16, 0:16], "st",
                            waits=[tok_bias, b.tok("dve")])
                b.emit("sp", lambda e: e.nop(), waits=[tdbg1, tdbg2])
                b.finalize(block)
                return nc

            groups = [(0, 3), (3, 3), (6, 3), (9, 3), (12, 3), (15, 1)]
            prev_bc7 = tok_bias
            pv_free = [bank_free[6], None]
            sum_dma = [None, None]
            prev_rdist = [None, None]
            prev_rr = [None] * 4
            rdist_done = {}
            for h in range(H):
                p0 = 64 * (h % 2)
                cq = h // 2
                qoff = 2048 * cq
                for lc in range(4):
                    pv_last = None
                    pending = None

                    def emit_pv(pend):
                        gi, s0, glen, pslot, texp = pend
                        pv = None
                        for si in range(glen):
                            stile = s0 + si
                            wpv = [texp]
                            if gi == 0 and si == 0:
                                wpv += [vlast_ev, tok_vones] + pv_free
                            def mmpv(e, stile=stile, si=si, pslot=pslot, h=h, gi=gi, glen=glen):
                                return e.matmul(
                                    PS[0:65, 6 * 512: 6 * 512 + 512],
                                    B_V[:, 1040 * stile + 65 * h: 1040 * stile + 65 * (h + 1)],
                                    B_P[:, pslot + 512 * si: pslot + 512 * (si + 1)],
                                    start=(gi == 0 and si == 0),
                                    stop=(gi == len(groups) - 1 and si == glen - 1))
                            pv = b.emit("pe", mmpv, waits=wpv,
                                        inc="pe" if (gi == len(groups) - 1 and si == glen - 1) else None)
                        return pv

                    for gi, (s0, glen) in enumerate(groups):
                        gb = 0 if gi % 2 == 0 else 3
                        last_qk = None
                        for si in range(glen):
                            stile = s0 + si
                            wqk = [tok_rope, bank_free[gb]] if si == 0 else ()
                            def mmqk(e, gb=gb, si=si, stile=stile, p0=p0, qoff=qoff, lc=lc):
                                return e.matmul(
                                    PS[:, 512 * (gb + si): 512 * (gb + si) + 512],
                                    B_K[p0: p0 + 64, qoff + 128 * stile: qoff + 128 * (stile + 1)],
                                    B_Q[p0: p0 + 64, qoff + 512 * lc: qoff + 512 * (lc + 1)],
                                    start=True, stop=True)
                            last_qk = b.emit("pe", mmqk, waits=wqk,
                                             inc="pe" if si == glen - 1 else None)
                        pslot = (gi % 2) * 1536
                        def expg(e, gb=gb, glen=glen, pslot=pslot, h=h):
                            return e.activation(B_P[:, pslot: pslot + 512 * glen],
                                                PS[:, 512 * gb: 512 * gb + 512 * glen],
                                                AF.Exp, bias=V_C[:, h:h + 1], scale=0.125)
                        texp = b.emit("act", expg, waits=[last_qk, tok_bias], inc="act")
                        bank_free[gb] = texp
                        if pending is not None:
                            emit_pv(pending)
                        pending = (gi, s0, glen, pslot, texp)
                    pv_last = emit_pv(pending)
                    def evpv(e, h=h, lc=lc, cq=cq, p0=p0):
                        return e.copy(B_X[p0: p0 + 64, 2048 * cq + 512 * lc: 2048 * cq + 512 * (lc + 1)],
                                      PS[0:64, 6 * 512: 6 * 512 + 512])
                    tev = b.emit("act", evpv, waits=[pv_last], inc="act")
                    sslot = 512 * (lc % 2)
                    def evsum(e, sslot=sslot):
                        return e.tensor_copy(V_R2[0:1, sslot: sslot + 512],
                                             PS[64:65, 6 * 512: 6 * 512 + 512])
                    tsum = b.emit("dve", evsum, waits=[pv_last, sum_dma[lc % 2]], inc="dve")
                    sum_dma[lc % 2] = dma("sp",
                                          d_srow[0:1, 2048 * h + 512 * lc: 2048 * h + 512 * (lc + 1)],
                                          V_R2[0:1, sslot: sslot + 512], f"ss{lc % 2}", waits=[tsum])
                    pv_free = [tev, tsum]
                    if dbg == "attn2":
                        trc2 = b.emit("dve", lambda e, sslot=sslot: e.reciprocal(
                            V_R[0:1, 1536:2048], V_R2[0:1, sslot: sslot + 512]),
                            waits=[tsum, prev_bc7], inc="dve")
                        wb2 = [trc2, bank_free[7]]
                        def bcr2(e):
                            return e.matmul(PS[0:64, 7 * 512: 7 * 512 + 512], C_OR[0:1, 0:64],
                                            V_R[0:1, 1536:2048], start=True, stop=True)
                        prev_bc7 = b.emit("pe", bcr2, waits=wb2, inc="pe")
                        def nrm2_(e, h=h, lc=lc, cq=cq, p0=p0):
                            sl = B_X[p0: p0 + 64, 2048 * cq + 512 * lc: 2048 * cq + 512 * (lc + 1)]
                            return e.tensor_tensor(sl, sl, PS[0:64, 7 * 512: 7 * 512 + 512],
                                                   ALU.mult)
                        norm_tt = b.emit("dve", nrm2_, waits=[prev_bc7, tev], inc="dve")
                        bank_free[7] = norm_tt
                        pv_free = [tev, norm_tt]
                if dbg in ("attnraw", "attn2"):
                    continue
                hp = h % 2
                tok_sr = [b.tok("ss0"), b.tok("ss1")]
                sd = B_Wf[:, 256 + 32 * hp: 272 + 32 * hp]
                sd2 = B_Wf[:, 272 + 32 * hp: 288 + 32 * hp]
                tok_d1 = dma("sp", sd,
                             d_srow[0:1, 2048 * h: 2048 * (h + 1)].rearrange(
                                 "o (p n) -> (o p) n", p=128), f"lsd{hp}",
                             waits=tok_sr + [prev_rdist[hp]])
                tok_rc = b.emit("dve", lambda e, sd=sd, sd2=sd2: e.reciprocal(sd2, sd),
                                waits=[tok_d1], inc="dve")
                tok_d2 = dma("sp",
                             d_rrow[0:1, 2048 * h: 2048 * (h + 1)].rearrange(
                                 "o (p n) -> (o p) n", p=128), sd2, f"sdist{hp}",
                             waits=[tok_rc])
                prev_rdist[hp] = tok_d2
                rdist_done[h] = tok_d2
                for hn in ([h - 1] if h > 0 else []) + ([h] if h == H - 1 else []):
                    cqn = hn // 2
                    p0n = 64 * (hn % 2)
                    norm_tt = None
                    d3toks = []
                    for lc in range(4):
                        rstage = B_Wf[0:1, 512 + 512 * lc: 1024 + 512 * lc]
                        d3toks.append(dma("sp", rstage,
                                     d_rrow[0:1, 2048 * hn + 512 * lc: 2048 * hn + 512 * lc + 512],
                                     f"lrr{lc}", waits=[rdist_done[hn], prev_rr[lc]]))
                    for lc in range(4):
                        rstage = B_Wf[0:1, 512 + 512 * lc: 1024 + 512 * lc]
                        wb = [d3toks[lc], bank_free[7]]
                        def bcr(e, rstage=rstage):
                            return e.matmul(PS[0:64, 7 * 512: 7 * 512 + 512], C_OR[0:1, 0:64],
                                            rstage, start=True, stop=True)
                        prev_bc7 = b.emit("pe", bcr, waits=wb, inc="pe")
                        def nrm(e, hn=hn, lc=lc, cqn=cqn, p0n=p0n):
                            sl = B_X[p0n: p0n + 64, 2048 * cqn + 512 * lc: 2048 * cqn + 512 * (lc + 1)]
                            return e.tensor_tensor(sl, sl, PS[0:64, 7 * 512: 7 * 512 + 512],
                                                   ALU.mult)
                        norm_tt = b.emit("dve", nrm, waits=[prev_bc7, pv_free[0]], inc="dve")
                        bank_free[7] = norm_tt
                        prev_rr[lc] = norm_tt
            tok_attn_norm = b.tok("dve")

            if dbg == "attnraw" and j == 0:
                for c in range(CD):
                    tok_dbg_last = dma("pool", d_out[128 * c: 128 * (c + 1), 0:L],
                                       B_X[:, 2048 * c: 2048 * (c + 1)], "gst",
                                       waits=[b.tok("act"), b.tok("ss0"), b.tok("ss1")])
                tok_dbg_last = dma("sp",
                                   d_out.rearrange("d t -> (d t)")[D * L: D * L + H * L].rearrange("(o n) -> o n", o=1),
                                   d_srow, "st", waits=[b.tok("st"), tok_dbg_last])
                b.emit("sp", lambda e: e.nop(), waits=[tok_dbg_last])
                b.finalize(block)
                return nc

            if dbg in ("q", "attn", "attn2"):
                srcbuf = B_Q if dbg == "q" else B_X
                wtok = tok_rope if dbg == "q" else tok_attn_norm
                for c in range(CD):
                    tok_dbg_last = dma("pool", d_out[128 * c: 128 * (c + 1), L * j: L * (j + 1)],
                                       srcbuf[:, 2048 * c: 2048 * (c + 1)], "gst", waits=[wtok])
                tok_oproj_done = b.tok("pe")
                if j == NSEQ - 1:
                    b.emit("sp", lambda e: e.nop(), waits=[tok_dbg_last])
                    b.finalize(block)
                    return nc
                continue

            s1ring = B_Wf
            xr_f = B_Mf[:, 0:512]
            t_f = B_Mf[:, 512:1024]
            xln_f = B_Mf[:, 1024:1536]
            s1b_b = B_M[:, 6144:6656]
            sq_b = B_M[:, 6656:7168]
            prev_xr = None
            prev_ap1 = None
            prev_xlndma = None
            lsum = lsq = None
            for t4 in range(4):
                ssum_w = [bank_free[4], bank_free[5]]
                ring_toks = []
                for dm in range(CD):
                    bki = dm % 3
                    w = [tok_wo, tok_attn_norm, bank_free[bki]]
                    for ca in range(CD):
                        def mmo(e, bki=bki, ca=ca, dm=dm, t4=t4):
                            return e.matmul(
                                bank(bki),
                                B_W[:, 8192 + 1024 * ca + 128 * dm:
                                    8192 + 1024 * ca + 128 * dm + 128],
                                B_X[:, 2048 * ca + 512 * t4: 2048 * ca + 512 * (t4 + 1)],
                                start=(ca == 0), stop=(ca == CD - 1))
                        last = b.emit("pe", mmo, waits=w if ca == 0 else (),
                                      inc="pe" if ca == CD - 1 else None)
                    tok_xr = dma("sp", xr_f,
                                 d_x[128 * dm: 128 * (dm + 1),
                                     L * j + 512 * t4: L * j + 512 * (t4 + 1)],
                                 "lxr", waits=[prev_xr])
                    def adds1(e, bki=bki, dm=dm):
                        return e.tensor_tensor(s1ring[:, 512 * dm: 512 * (dm + 1)],
                                               bank(bki), xr_f, ALU.add)
                    ts1 = b.emit("dve", adds1, waits=[last, tok_xr], inc="dve")
                    prev_xr = ts1
                    bank_free[bki] = ts1
                    ring_toks.append(ts1)
                    def cpbf(e, dm=dm):
                        return e.tensor_single_scalar(s1b_b, s1ring[:, 512 * dm: 512 * (dm + 1)],
                                                      1.0, ALU.mult)
                    tbf = b.emit("dve", cpbf, waits=[lsum], inc="dve")
                    def sqb(e):
                        return e.tensor_tensor(sq_b, s1b_b, s1b_b, ALU.mult)
                    tsq = b.emit("dve", sqb, waits=[lsq], inc="dve")
                    wst = [tbf] + (ssum_w if dm == 0 else [])
                    def mms(e, dm=dm):
                        return e.matmul(PS[0:1, 4 * 512: 4 * 512 + 512], C_T[:, 2048:2049],
                                        s1b_b, start=(dm == 0), stop=(dm == CD - 1))
                    lsum = b.emit("pe", mms, waits=wst, inc="pe")
                    def mms2(e, dm=dm):
                        return e.matmul(PS[0:1, 5 * 512: 5 * 512 + 512], C_T[:, 2048:2049],
                                        sq_b, start=(dm == 0), stop=(dm == CD - 1))
                    lsq = b.emit("pe", mms2, waits=[tsq], inc="pe")
                trm = b.emit("dve", lambda e: e.tensor_single_scalar(V_R[0:1, 0:512],
                             PS[0:1, 4 * 512: 4 * 512 + 512], 1.0 / D, ALU.mult),
                             waits=[lsum], inc="dve")
                trq = b.emit("dve", lambda e: e.tensor_single_scalar(V_R[0:1, 512:1024],
                             PS[0:1, 5 * 512: 5 * 512 + 512], 1.0 / D, ALU.mult),
                             waits=[lsq], inc="dve")
                bank_free[4] = trm
                bank_free[5] = trq
                trm2 = b.emit("dve", lambda e: e.tensor_tensor(V_R[0:1, 1024:1536], V_R[0:1, 0:512],
                                                        V_R[0:1, 0:512], ALU.mult),
                              waits=[trm], inc="dve")
                tvar = b.emit("dve", lambda e: e.tensor_tensor(V_R[0:1, 512:1024],
                                                               V_R[0:1, 512:1024],
                                                               V_R[0:1, 1024:1536],
                                                               ALU.subtract),
                              waits=[trq, trm2], inc="dve")
                tln = b.emit("act", lambda e: e.activation(V_R[0:1, 1024:1536],
                                                           V_R[0:1, 512:1024], AF.Ln,
                                                           bias=CONSTV[0:1, 0:1]),
                             waits=[tvar], inc="act")
                trstd = b.emit("act", lambda e: e.activation(V_R[0:1, 1024:1536],
                                                             V_R[0:1, 1024:1536], AF.Exp,
                                                             scale=-0.5), inc="act")
                tmr = b.emit("dve", lambda e: e.tensor_tensor(V_R[0:1, 1536:2048],
                                                              V_R[0:1, 0:512],
                                                              V_R[0:1, 1024:1536], ALU.mult),
                             waits=[trstd], inc="dve")
                tb6 = b.emit("pe", lambda e: e.matmul(PS[:, 6 * 512: 7 * 512], C_OR[:, :],
                                                      V_R[0:1, 1024:1536],
                                                      start=True, stop=True),
                             waits=[trstd, bank_free[6]], inc="pe")
                tb7 = b.emit("pe", lambda e: e.matmul(PS[:, 7 * 512: 8 * 512], C_OR[:, :],
                                                      V_R[0:1, 1536:2048],
                                                      start=True, stop=True),
                             waits=[tmr, bank_free[7]], inc="pe")
                last_ap = None
                for dm in range(CD):
                    def ap1(e, dm=dm):
                        return e.tensor_tensor(t_f, s1ring[:, 512 * dm: 512 * (dm + 1)],
                                               PS[:, 6 * 512: 7 * 512], ALU.mult)
                    ta1 = b.emit("dve", ap1, waits=[tb6, prev_ap1], inc="dve")
                    def ap2(e):
                        return e.tensor_tensor(xln_f, t_f, PS[:, 7 * 512: 8 * 512],
                                               ALU.subtract)
                    ta2 = b.emit("dve", ap2, waits=[tb7, prev_xlndma], inc="dve")
                    prev_xlndma = dma("sp", d_xln[128 * dm: 128 * (dm + 1),
                                      L * j + 512 * t4: L * j + 512 * (t4 + 1)],
                                      xln_f, "sxln", waits=[ta2])
                    first = (j == 0 and t4 == 0 and dm == 0)
                    col = 13 if first else 14
                    def redx(e, col=col):
                        return e.tensor_reduce(V_S[:, col:col + 1], xln_f,
                                               AX.X, ALU.max, apply_absolute_value=True)
                    trd = b.emit("dve", redx, inc="dve")
                    prev_ap1 = trd
                    if not first:
                        b.emit("dve", lambda e: e.tensor_tensor(V_S[:, 13:14], V_S[:, 13:14],
                                                                V_S[:, 14:15], ALU.max),
                               waits=[trd], inc="dve")
                    last_ap = trd
                bank_free[6] = last_ap
                bank_free[7] = last_ap
            tok_oproj_done = b.tok("pe")

        if dbg == "xln":
            tok = dma("sp", d_out, d_xln, "st", waits=[b.tok("sxln"), b.tok("dve")])
            b.emit("sp", lambda e: e.nop(), waits=[tok])
            b.finalize(block)
            return nc

        C_I32 = C_T.ap().bitcast(F32)[0:32, 0:1024]
        tok_i32 = dma("sp", C_I32, i_ind32, "ldc", waits=[b.tok("dve"), b.tok("act"), tok_consts])
        SN = {0: B_Mf[:, 2048:2816], 1: B_Mf[:, 2816:3584], 2: B_Mf[:, 3584:4352]}
        tok_vfree = b.tok("pe")
        tok_vfree_a = b.tok("act")
        mats = ((d_wupN, CI, 1), (d_wgN, CI, 1), (d_wdN, CD, 3))
        prev_nst = [None] * 4
        sn_tok = {}
        qrot = ("sp", "act", "pool")
        ldi = 0
        for mi, (dN, nt, nw) in enumerate(mats):
            tr_ = None
            for ot in range(nt):
                for w in range(nw):
                    sl = ldi % 4
                    nst = B_Vf[:, 1024 * sl: 1024 * sl + 1024]
                    tok_n = dma(qrot[ldi % 3], nst,
                                dN[128 * ot: 128 * (ot + 1), 1024 * w: 1024 * (w + 1)],
                                f"lsn{sl}", waits=[tok_vfree, tok_vfree_a, prev_nst[sl]])
                    ldi += 1
                    nb0 = (32 * ot) if nw == 1 else (96 * ot + 32 * w)
                    def redn(e, nst=nst, nb0=nb0, mi=mi):
                        return e.tensor_reduce(
                            SN[mi][:, nb0: nb0 + 32],
                            nst.rearrange("p (b w) -> p b w", w=QB),
                            AX.X, ALU.max, apply_absolute_value=True)
                    tr_ = b.emit("dve", redn, waits=[tok_n], inc="dve")
                    prev_nst[sl] = tr_
            ncols = 32 * nt * nw
            sn_tok[mi] = b.emit("dve", lambda e, mi=mi, ncols=ncols: e.tensor_scalar(
                SN[mi][:, 0:ncols], SN[mi][:, 0:ncols], 1.0 / 7.0, 1e-12,
                ALU.mult, ALU.max), waits=[tr_], inc="dve")
        tok_scales = sn_tok[2]

        def jit_scale_tiles(mi, src_slice, w_extra):
            sT_st = B_Vf[0:32, 6272:6400]
            rT_st = B_Vf[0:32, 6400:6528]
            wtr = [sn_tok[mi], bank_free[4]] + w_extra
            ttr2 = b.emit("pe", lambda e: e.transpose(
                PS[0:32, 4 * 512 + 256: 4 * 512 + 384], src_slice, C_EYE[:, :]),
                waits=wtr, inc="pe")
            tcp2 = b.emit("act", lambda e: e.copy(sT_st, PS[0:32, 4 * 512 + 256: 4 * 512 + 384]),
                          waits=[ttr2], inc="act")
            trc2 = b.emit("dve", lambda e: e.reciprocal(rT_st, sT_st),
                          waits=[tcp2], inc="dve")
            bank_free[4] = tcp2
            return sT_st, rT_st, trc2

        def barrier(idx, cols, after_toks):
            ctoks = []
            prev_gst = b.tok("gci")
            prev_gmx = b.tok("dve")
            tok_ccz = b.emit("dve", lambda e: e.memset(CCST, 0.0),
                             waits=after_toks + [prev_gst], inc="dve")
            for si, col in enumerate(cols):
                w = [bank_free[5]] + after_toks
                def trp(e, col=col):
                    return e.transpose(PS[0:1, 5 * 512: 5 * 512 + 128],
                                       V_S[:, col:col + 1], C_EYE[:, :])
                tt = b.emit("pe", trp, waits=w, inc="pe")
                def rd(e, si=si, idx=idx):
                    return e.tensor_reduce(CCST[0:1, si:si + 1],
                                           PS[0:1, 5 * 512: 5 * 512 + 128],
                                           AX.X, ALU.max)
                tr_ = b.emit("dve", rd, waits=[tt, tok_ccz], inc="dve")
                bank_free[5] = tr_
                ctoks.append(tr_)
            tin = dma("pool", cc_in[idx], CCST, "gci", waits=ctoks)
            tcc = b.emit("pool", lambda e, idx=idx: e.collective_compute(
                "AllReduce", ALU.max, replica_groups=[list(range(8))],
                ins=[cc_in[idx]], outs=[cc_out[idx]]), waits=[tin], inc="cc")
            tg = dma("pool", GMX, cc_out[idx], "gco", waits=[tcc, prev_gmx])
            return tg

        def bcast_consts(row_lo, n, vc_lo, wtok):
            w = [wtok, bank_free[5]]
            tb = b.emit("pe", lambda e: e.matmul(PS[:, 5 * 512: 5 * 512 + n], C_OR[:, :],
                                                 V_SC[0:1, row_lo: row_lo + n],
                                                 start=True, stop=True),
                        waits=w, inc="pe")
            tc = b.emit("act", lambda e: e.copy(V_C[:, vc_lo: vc_lo + n],
                                                PS[:, 5 * 512: 5 * 512 + n]),
                        waits=[tb], inc="act")
            bank_free[5] = tc
            return tc

        tg1 = barrier(0, [13], [b.tok("dve")])
        t_a = b.emit("dve", lambda e: e.tensor_scalar(V_SC[0:1, 0:1], GMX[:, 0:1],
                                                      1.0 / QMA, 1e-12, ALU.mult, ALU.max),
                     waits=[tg1], inc="dve")
        t_b = b.emit("dve", lambda e: e.reciprocal(V_SC[0:1, 1:2], V_SC[0:1, 0:1]),
                     waits=[t_a], inc="dve")
        tok_c1 = bcast_consts(0, 2, 32, t_b)

        def xq_slab(c, t0, n):
            if c < 4:
                return B_Q[:, 4096 * c + t0: 4096 * c + t0 + n]
            return B_K[:, 4096 * (c - 4) + t0: 4096 * (c - 4) + t0 + n]

        xq_tok = None
        prev_ld = [None] * 4
        tok_ln1_done = (b.tok("dve"), b.tok("pe"))
        for c in range(CD):
            for t2 in range(2):
                sl = (c * 2 + t2) % 4
                stage = B_Wf[:, 2048 * sl: 2048 * sl + 2048]
                tok_l = dma("sp" if sl % 2 == 0 else "act", stage,
                            d_xln[128 * c: 128 * (c + 1), 2048 * t2: 2048 * (t2 + 1)],
                            f"lxq{sl}", waits=[prev_ld[sl], tok_scales, b.tok("sxln"), b.tok("gw0"), b.tok("gw1"), b.tok("gx"),
                                          tok_ln1_done[0], tok_ln1_done[1]])
                def q1(e, stage=stage):
                    return e.tensor_scalar(stage, stage, V_C[:, 33:34], CMAG,
                                           ALU.mult, ALU.add)
                tq1 = b.emit("dve", q1, waits=[tok_l, tok_c1], inc="dve")
                def q2(e, stage=stage, c=c, t2=t2):
                    return e.tensor_scalar(xq_slab(c, 2048 * t2, 2048), stage,
                                           -CMAG, V_C[:, 32:33], ALU.add, ALU.mult)
                xq_tok = b.emit("dve", q2, waits=[tq1], inc="dve")
                prev_ld[sl] = xq_tok

        if dbg == "xq":
            tdl = None
            for c in range(CD):
                buf = B_Q if c < 4 else B_K
                off = 4096 * (c % 4)
                tdl = dma("pool", d_out[128 * c: 128 * (c + 1), :],
                          buf[:, off: off + 4096], "gst", waits=[xq_tok])
            b.emit("sp", lambda e: e.nop(), waits=[tdl])
            b.finalize(block)
            return nc

        dq_base = 4096
        sp_base = 8192
        run_up, run_gate = 1, 2
        first_red = {1: True, 2: True}
        prev_w = [None, None]
        prev_dq = [None, None]
        prev_sp = [None] * 4
        prev_rd = [None] * 4
        spill_n = 0
        gtoks = {}
        tok_dq_done = {}
        a_last_mm = None
        for it in range(A_LIMIT):
            for mi, (dT, ddst, strow) in list(enumerate(((d_wup, d_up, 0), (d_wg, d_gate, 32))))[:M_LIMIT]:
                sl = (it * 2 + mi) % 2
                wst = B_Vf[:, 1024 * sl: 1024 * sl + 1024]
                tok_w = dma("sp", wst.rearrange("p (c n) -> p c n", c=CD),
                            dT[:, 128 * it: 128 * (it + 1)].rearrange(
                                "(c p) n -> p c n", p=128),
                            f"lwa{sl}", waits=[prev_w[sl], xq_tok])
                sT_st, rT_st, tokjit = jit_scale_tiles(
                    mi, SN[mi][:, 32 * it: 32 * (it + 1)], [tok_i32, tok_w])
                dq_toks = []
                srT = B_Vf[0:32, 6272:6528]
                for c in range(CD):
                    wdq = [tokjit, bank_free[4], prev_dq[sl]]
                    def bs1(e, c=c, srT=srT):
                        return e.matmul(PS[:, 4 * 512: 4 * 512 + 256],
                                        C_I32[:, 128 * c: 128 * (c + 1)],
                                        srT, start=True, stop=True)
                    tb2 = b.emit("pe", bs1, waits=wdq, inc="pe")
                    scr = B_Vf[:, 6144: 6272]
                    def dq1(e, wst=wst, c=c, scr=scr):
                        return e.tensor_tensor(scr, wst[:, 128 * c: 128 * (c + 1)],
                                               PS[:, 4 * 512 + 128: 4 * 512 + 256], ALU.mult)
                    td1 = b.emit("dve", dq1, waits=[tb2], inc="dve")
                    def dq2(e, scr=scr):
                        return e.tensor_scalar(scr, scr, CMAG, -CMAG, ALU.add, ALU.add)
                    td2 = b.emit("dve", dq2, inc="dve")
                    def dq3(e, c=c, sl=sl, scr=scr):
                        return e.tensor_tensor(B_V[:, dq_base + 2048 * sl + 128 * c:
                                                    dq_base + 2048 * sl + 128 * (c + 1)],
                                               scr, PS[:, 4 * 512: 4 * 512 + 128], ALU.mult)
                    td3 = b.emit("dve", dq3, inc="dve")
                    bank_free[4] = td3
                    dq_toks.append(td3)
                prev_w[sl] = dq_toks[-1]
                prev_dq[sl] = None
                for tp in range(4):
                    pb = 0 if (it * 8 + mi * 4 + tp) % 2 == 0 else 2
                    wa = [dq_toks[-1], bank_free[pb], bank_free[pb + 1]]
                    for c in range(CD):
                        for half in range(2):
                            def mma(e, pb=pb, c=c, half=half, sl=sl, tp=tp):
                                return e.matmul(
                                    PS[:, 512 * (pb + half): 512 * (pb + half) + 512],
                                    B_V[:, dq_base + 2048 * sl + 128 * c:
                                        dq_base + 2048 * sl + 128 * (c + 1)],
                                    xq_slab(c, 1024 * tp + 512 * half, 512),
                                    start=(c == 0), stop=(c == CD - 1))
                            a_last_mm = b.emit("pe", mma, waits=wa if (c == 0 and half == 0) else (),
                                               inc="pe" if (c == CD - 1 and half == 1) else None)
                    ssl = spill_n % 4
                    spill_n += 1
                    spst = B_V[:, sp_base + 1024 * ssl: sp_base + 1024 * (ssl + 1)]
                    def eva(e, pb=pb, spst=spst):
                        return e.copy(spst, PS[:, 512 * pb: 512 * pb + 1024])
                    tev = b.emit("act", eva, waits=[a_last_mm, prev_sp[ssl], prev_rd[ssl]],
                                 inc="act")
                    run = run_up if mi == 0 else run_gate
                    if first_red[run]:
                        def redA(e, spst=spst, run=run):
                            return e.tensor_reduce(V_S[:, run:run + 1], spst,
                                                   AX.X, ALU.max, apply_absolute_value=True)
                        trd = b.emit("dve", redA, waits=[tev], inc="dve")
                        first_red[run] = False
                    else:
                        def redA(e, spst=spst):
                            return e.tensor_reduce(V_S[:, 15:16], spst,
                                                   AX.X, ALU.max, apply_absolute_value=True)
                        trd0 = b.emit("dve", redA, waits=[tev], inc="dve")
                        def acc(e, run=run):
                            return e.tensor_tensor(V_S[:, run:run + 1], V_S[:, run:run + 1],
                                                   V_S[:, 15:16], ALU.max)
                        trd = b.emit("dve", acc, waits=[trd0], inc="dve")
                    prev_rd[ssl] = trd
                    bank_free[pb] = tev
                    bank_free[pb + 1] = tev
                    prev_sp[ssl] = dma("sp", ddst[128 * it: 128 * (it + 1),
                                                1024 * tp: 1024 * (tp + 1)],
                                       spst, f"ssp{ssl}", waits=[tev])

        if dbg == "ffnA":
            dob = d_out.bitcast(BF16)
            t1d = dma("sp", dob[0:512, 0:T], d_up[0:512, :], "st",
                      waits=[b.tok(f"ssp{0}"), b.tok("ssp1"), b.tok("ssp2"), b.tok("ssp3"), b.tok("dve")])
            t2d = dma("sp", dob[512:1024, 0:T], d_gate[0:512, :], "st",
                      waits=[b.tok(f"ssp{0}"), b.tok("ssp1"), b.tok("ssp2"), b.tok("ssp3"), b.tok("dve")])
            b.emit("sp", lambda e: e.nop(), waits=[t1d, t2d])
            b.finalize(block)
            return nc

        tg2 = barrier(1, [run_up, run_gate], [b.tok("dve"), b.tok("act")])
        tq1 = b.emit("dve", lambda e: e.tensor_scalar(V_SC[0:1, 2:3], GMX[:, 0:1],
                                                1.0 / QMA, 1e-12, ALU.mult, ALU.max),
               waits=[tg2], inc="dve")
        tq2 = b.emit("dve", lambda e: e.reciprocal(V_SC[0:1, 3:4], V_SC[0:1, 2:3]),
                     waits=[tq1], inc="dve")
        tq3 = b.emit("dve", lambda e: e.tensor_scalar(V_SC[0:1, 4:5], GMX[:, 1:2],
                                                1.0 / QMA, 1e-12, ALU.mult, ALU.max),
               waits=[tq2], inc="dve")
        tq4 = b.emit("dve", lambda e: e.reciprocal(V_SC[0:1, 5:6], V_SC[0:1, 4:5]),
                     waits=[tq3], inc="dve")
        t_ct = b.emit("dve", lambda e: e.tensor_single_scalar(V_SC[0:1, 6:7], V_SC[0:1, 4:5],
                                                       float(2 ** -16), ALU.mult),
                      waits=[tq4], inc="dve")
        tok_c2 = bcast_consts(2, 5, 34, t_ct)

        def ddq_slab(it, dm):
            base = 1024 * it + 128 * dm
            if it < 16:
                return B_Q[:, base: base + 128]
            return B_K[:, base - 16384: base - 16384 + 128]

        def ddq_tile(it):
            if it < 16:
                return B_Q[:, 1024 * it: 1024 * (it + 1)]
            return B_K[:, 1024 * (it - 16): 1024 * (it - 15)]

        dscr = B_Vf[:, 4352:5376]
        prev_wd = [b.tok("dve"), b.tok("dve")]
        prev_dd = None
        sb_prev = [tok_c2, tok_c2]
        tr_ev = {}
        ddq_last = None
        for g in range(3):
            sTg = B_Vf[0:32, 5376 + 0: 5376 + 1024]
            rTg = B_Vf[0:32, 6400 + 0: 6400 + 1024]
            ttr = None
            for dm in range(CD):
                def trg(e, dm=dm, g=g):
                    return e.transpose(
                        PS[0:32, 6 * 512 + 128 * dm: 6 * 512 + 128 * (dm + 1)],
                        SN[2][:, 96 * dm + 32 * g: 96 * dm + 32 * g + 32],
                        C_EYE[:, :])
                w0 = [tok_scales, sb_prev[1], prev_dd]
                if g == 0 and dm == 0:
                    w0 += [b.tok(f"ssp{i}") for i in range(4)]
                ttr = b.emit("pe", trg,
                             waits=w0 if dm == 0 else (),
                             inc="pe" if dm == CD - 1 else None)
            tev = b.emit("act", lambda e, sTg=sTg: e.copy(sTg, PS[0:32, 6 * 512: 6 * 512 + 1024]),
                         waits=[ttr], inc="act")
            trc = b.emit("dve", lambda e, sTg=sTg, rTg=rTg: e.reciprocal(rTg, sTg),
                         waits=[tev], inc="dve")
            for li in range(8):
                it = 8 * g + li
                sl = it % 2
                crow = it % 8
                wst = B_Vf[:, 1024 * sl: 1024 * sl + 1024]
                tok_w = dma("sp", wst, d_wd[128 * it: 128 * (it + 1), :],
                            f"lwd{sl}", waits=[prev_wd[sl]])
                pbase = 0 if it % 2 == 0 else 4
                s_b = PS[:, 512 * pbase: 512 * pbase + 1024]
                r_b = PS[:, 512 * pbase + 1024: 512 * pbase + 2048]
                wmm = [trc, sb_prev[0 if pbase == 0 else 1]]
                tmm = None
                for h2 in range(2):
                    def bs(e, crow=crow, h2=h2, pbase=pbase, sTg=sTg):
                        return e.matmul(PS[:, 512 * pbase + 512 * h2: 512 * pbase + 512 * (h2 + 1)],
                                        C_I32[:, 128 * crow: 128 * (crow + 1)],
                                        sTg[0:32, 512 * h2: 512 * (h2 + 1)],
                                        start=True, stop=True)
                    tmm = b.emit("pe", bs, waits=wmm if h2 == 0 else (), inc="pe")
                for h2 in range(2):
                    def br(e, crow=crow, h2=h2, pbase=pbase, rTg=rTg):
                        return e.matmul(PS[:, 512 * pbase + 1024 + 512 * h2: 512 * pbase + 1024 + 512 * (h2 + 1)],
                                        C_I32[:, 128 * crow: 128 * (crow + 1)],
                                        rTg[0:32, 512 * h2: 512 * (h2 + 1)],
                                        start=True, stop=True)
                    tmm = b.emit("pe", br, inc="pe")
                d1 = b.emit("dve", lambda e, wst=wst, r_b=r_b: e.tensor_tensor(
                    dscr, wst, r_b, ALU.mult), waits=[tmm, tok_w], inc="dve")
                d2 = b.emit("dve", lambda e: e.tensor_scalar(
                    dscr, dscr, CMAG, -CMAG, ALU.add, ALU.add), inc="dve")
                d3 = b.emit("dve", lambda e, it=it, s_b=s_b: e.tensor_tensor(
                    ddq_tile(it), dscr, s_b, ALU.mult), inc="dve")
                sb_prev[0 if pbase == 0 else 1] = d3
                prev_wd[sl] = d1
                prev_dd = d3
                ddq_last = d3
        for _bi in range(8):
            bank_free[_bi] = sb_prev[0] if _bi < 4 else sb_prev[1]

        B_Xf = B_X.ap().bitcast(F32)
        prev_g = [None] * 4
        prev_fa = [None] * 4
        prev_fb = [None] * 4
        prev_fc = [None] * 4
        prev_tst = [None] * 4
        tmax_tok = None
        NTB = I // 128 * (T // 1024)
        for k in range(NTB):
            it, th = k // 4, k % 4
            sst = k % 4
            gst = B_W[:, 1024 * sst: 1024 * (sst + 1)]
            tst = B_M[:, 1024 * sst: 1024 * (sst + 1)]
            fA = B_Wf[:, 4096 + 1024 * sst: 4096 + 1024 * sst + 1024]
            fB = B_Xf[:, 1024 * sst: 1024 * sst + 1024]
            fC = B_Xf[:, 4096 + 1024 * sst: 4096 + 1024 * sst + 1024]
            tok_g = dma("sp", gst, d_gate[128 * it: 128 * (it + 1),
                                          1024 * th: 1024 * (th + 1)],
                        f"lgb{sst}",
                        waits=[prev_g[sst]] + ([b.tok(f"ssp{i}") for i in range(4)] if k == 0 else []))
            a1 = b.emit("act", lambda e, gst=gst, fA=fA: e.activation(
                fA, gst, AF.Copy, bias=CMAG, scale=V_C[:, 37:38]),
                waits=[tok_g, tok_c2, prev_fa[sst]], inc="act")
            v1 = b.emit("dve", lambda e, fA=fA, fB=fB: e.tensor_single_scalar(
                fB, fA, -CMAG, ALU.add), waits=[a1, prev_fb[sst]], inc="dve")
            a3 = b.emit("act", lambda e, fB=fB, fC=fC: e.activation(
                fC, fB, AF.Sigmoid, scale=V_C[:, 36:37]),
                waits=[v1, prev_fc[sst]], inc="act")
            v2 = b.emit("dve", lambda e, fC=fC, fA=fA: e.tensor_scalar(
                fA, fC, 65536.0, CMAG, ALU.mult, ALU.add), waits=[a3], inc="dve")
            v3 = b.emit("dve", lambda e, fA=fA: e.tensor_scalar(
                fA, fA, -CMAG, 65535.0, ALU.add, ALU.min), inc="dve")
            v4 = b.emit("dve", lambda e, fA=fA, fB=fB, tst=tst: e.tensor_tensor(
                tst, fB, fA, ALU.mult), waits=[prev_tst[sst]], inc="dve")
            if tmax_tok is None:
                v5 = b.emit("dve", lambda e, tst=tst: e.tensor_reduce(
                    V_S[:, 3:4], tst, AX.X, ALU.max, apply_absolute_value=True), inc="dve")
            else:
                tv5 = b.emit("dve", lambda e, tst=tst: e.tensor_reduce(
                    V_S[:, 15:16], tst, AX.X, ALU.max, apply_absolute_value=True), inc="dve")
                v5 = b.emit("dve", lambda e: e.tensor_tensor(
                    V_S[:, 3:4], V_S[:, 3:4], V_S[:, 15:16], ALU.max),
                    waits=[tv5], inc="dve")
            tmax_tok = v5
            prev_g[sst] = a1
            prev_fa[sst] = v4
            prev_fb[sst] = v4
            prev_fc[sst] = v2
            prev_tst[sst] = dma("pool", d_t[128 * it: 128 * (it + 1),
                                            1024 * th: 1024 * (th + 1)],
                                tst, f"stb{sst}", waits=[v5])

        tg3 = barrier(2, [3], [b.tok("dve")])
        tr1 = b.emit("dve", lambda e: e.tensor_tensor(V_SC[0:1, 7:8], GMX[:, 0:1],
                                                V_SC[0:1, 6:7], ALU.mult),
               waits=[tg3], inc="dve")
        tr2 = b.emit("dve", lambda e: e.tensor_scalar(V_SC[0:1, 8:9], V_SC[0:1, 7:8],
                                                1.0 / QMA, 1e-12, ALU.mult, ALU.max),
               waits=[tr1], inc="dve")
        t_rt = b.emit("dve", lambda e: e.reciprocal(V_SC[0:1, 9:10], V_SC[0:1, 8:9]),
                      waits=[tr2], inc="dve")
        t_rtc = b.emit("dve", lambda e: e.tensor_tensor(V_SC[0:1, 16:17], V_SC[0:1, 9:10],
                                                        V_SC[0:1, 6:7], ALU.mult),
                       waits=[t_rt], inc="dve")
        tok_c3 = bcast_consts(16, 1, 39, t_rtc)

        prev_tl = [None] * 4
        prev_ul = [None] * 4
        prev_fa = [None] * 4
        prev_fb = [None] * 4
        prev_pst = [None] * 4
        pmax_tok = None
        for k in range(NTB):
            it, th = k // 4, k % 4
            sst = k % 4
            tstg = B_W[:, 1024 * sst: 1024 * (sst + 1)]
            ustg = B_W[:, 4096 + 1024 * sst: 4096 + 1024 * (sst + 1)]
            fA = B_Wf[:, 4096 + 1024 * sst: 4096 + 1024 * sst + 1024]
            fB = B_Xf[:, 1024 * sst: 1024 * sst + 1024]
            pst = B_M[:, 1024 * sst: 1024 * (sst + 1)]
            tok_t = dma("sp", tstg, d_t[128 * it: 128 * (it + 1),
                                        1024 * th: 1024 * (th + 1)],
                        f"ltc{sst}",
                        waits=[prev_tl[sst]] + ([b.tok(f"stb{i}") for i in range(4)] if k == 0 else []))
            tok_u = dma("act", ustg, d_up[128 * it: 128 * (it + 1),
                                          1024 * th: 1024 * (th + 1)],
                        f"luc{sst}", waits=[prev_ul[sst]])
            a1 = b.emit("act", lambda e, tstg=tstg, fA=fA: e.activation(
                fA, tstg, AF.Copy, bias=CMAG, scale=V_C[:, 39:40]),
                waits=[tok_t, tok_c3, prev_fa[sst]], inc="act")
            v1 = b.emit("dve", lambda e, fA=fA: e.tensor_single_scalar(
                fA, fA, -CMAG, ALU.add), waits=[a1], inc="dve")
            a2 = b.emit("act", lambda e, ustg=ustg, fB=fB: e.activation(
                fB, ustg, AF.Copy, bias=CMAG, scale=V_C[:, 35:36]),
                waits=[tok_u, tok_c3, prev_fb[sst]], inc="act")
            v2 = b.emit("dve", lambda e, fB=fB: e.tensor_single_scalar(
                fB, fB, -CMAG, ALU.add), waits=[a2], inc="dve")
            v3 = b.emit("dve", lambda e, fA=fA, fB=fB, pst=pst: e.tensor_tensor(
                pst, fA, fB, ALU.mult), waits=[prev_pst[sst]], inc="dve")
            if pmax_tok is None:
                v4 = b.emit("dve", lambda e, pst=pst: e.tensor_reduce(
                    V_S[:, 4:5], pst, AX.X, ALU.max, apply_absolute_value=True), inc="dve")
            else:
                tv4 = b.emit("dve", lambda e, pst=pst: e.tensor_reduce(
                    V_S[:, 15:16], pst, AX.X, ALU.max, apply_absolute_value=True), inc="dve")
                v4 = b.emit("dve", lambda e: e.tensor_tensor(
                    V_S[:, 4:5], V_S[:, 4:5], V_S[:, 15:16], ALU.max),
                    waits=[tv4], inc="dve")
            pmax_tok = v4
            prev_tl[sst] = a1
            prev_ul[sst] = a2
            prev_fa[sst] = v3
            prev_fb[sst] = v3
            prev_pst[sst] = dma("pool", d_P[128 * it: 128 * (it + 1),
                                            1024 * th: 1024 * (th + 1)],
                                pst, f"stp{sst}", waits=[v4])

        tg4 = barrier(3, [4], [b.tok("dve")])
        ts1_ = b.emit("dve", lambda e: e.tensor_tensor(V_SC[0:1, 10:11], V_SC[0:1, 8:9],
                                                V_SC[0:1, 2:3], ALU.mult),
               waits=[tg4], inc="dve")
        ts2_ = b.emit("dve", lambda e: e.tensor_tensor(V_SC[0:1, 11:12], V_SC[0:1, 10:11],
                                                GMX[:, 0:1], ALU.mult),
               waits=[ts1_], inc="dve")
        ts3_ = b.emit("dve", lambda e: e.tensor_scalar(V_SC[0:1, 12:13], V_SC[0:1, 11:12],
                                                1.0 / QMA, 1e-12, ALU.mult, ALU.max),
               waits=[ts2_], inc="dve")
        ts4_ = b.emit("dve", lambda e: e.reciprocal(V_SC[0:1, 13:14], V_SC[0:1, 12:13]),
               waits=[ts3_], inc="dve")
        t_co = b.emit("dve", lambda e: e.tensor_tensor(V_SC[0:1, 14:15], V_SC[0:1, 10:11],
                                                       V_SC[0:1, 13:14], ALU.mult),
                      waits=[ts4_], inc="dve")
        tok_c4 = bcast_consts(12, 3, 40, t_co)

        def oq_slab(set_, it2):
            if set_ == 0:
                return B_X[:, 512 * it2: 512 * (it2 + 1)]
            if it2 < 8:
                return B_X[:, 12288 + 512 * it2: 12288 + 512 * (it2 + 1)]
            return B_M[:, 512 * (it2 - 8): 512 * (it2 - 7)]

        s2ring = B_Wf[:, 0:4096]
        prev_ost = [None] * 4
        prev_fd = [None] * 4
        prev_oq = [None, None]
        prep_done = [None, None]
        prev_xr = [None] * 4
        prev_ap = [None, None]
        out_toks = [None, None]
        lsum_d = lsq_d = None

        def prep_oq(tc, extra_w=()):
            set_ = tc % 2
            last = None
            for it2 in range(CI):
                r = it2 % 4
                ost = B_W[:, 8192 + 512 * r: 8192 + 512 * (r + 1)]
                fD = B_Wf[:, 5120 + 512 * r: 5120 + 512 * (r + 1)]
                tl = dma("sp", ost, d_P[128 * it2: 128 * (it2 + 1),
                                        512 * tc: 512 * (tc + 1)],
                         f"lpd{r}", waits=[prev_ost[r]] + list(extra_w))
                pa = b.emit("act", lambda e, ost=ost, fD=fD: e.activation(
                    fD, ost, AF.Copy, bias=CMAG, scale=V_C[:, 42:43]),
                    waits=[tl, tok_c4, prev_fd[r]], inc="act")
                pv = b.emit("dve", lambda e, fD=fD, set_=set_, it2=it2: e.tensor_scalar(
                    oq_slab(set_, it2), fD, -CMAG, V_C[:, 40:41], ALU.add, ALU.mult),
                    waits=[pa, prev_oq[set_]], inc="dve")
                prev_ost[r] = pa
                prev_fd[r] = pv
                last = pv
            prep_done[set_] = last

        prep_oq(0, extra_w=[b.tok(f"stp{i}") for i in range(4)] + [b.tok("dve"), ddq_last])
        for tc in range(8):
            set_ = tc % 2
            ring_toks = []
            lsum_d = lsq_d = None
            for dm in range(CD):
                bki = dm % 4
                w = [ddq_last, prep_done[set_], bank_free[bki]]
                for it2 in range(CI):
                    def mmd(e, bki=bki, it2=it2, dm=dm, set_=set_):
                        return e.matmul(bank(bki), ddq_slab(it2, dm),
                                        oq_slab(set_, it2),
                                        start=(it2 == 0), stop=(it2 == CI - 1))
                    last = b.emit("pe", mmd, waits=w if it2 == 0 else (),
                                  inc="pe" if it2 == CI - 1 else None)
                xr = B_Vf[:, 512 * (dm % 4): 512 * (dm % 4) + 512]
                tok_xr = dma("act", xr,
                             d_xln[128 * dm: 128 * (dm + 1), 512 * tc: 512 * (tc + 1)],
                             f"lx2{dm % 4}",
                             waits=[prev_xr[dm % 4]] + ([ddq_last] if tc == 0 and dm < 4 else []))
                def adds2(e, bki=bki, dm=dm, xr=xr):
                    return e.tensor_tensor(s2ring[:, 512 * dm: 512 * (dm + 1)],
                                           bank(bki), xr, ALU.add)
                ts2 = b.emit("dve", adds2, waits=[last, tok_xr], inc="dve")
                prev_xr[dm % 4] = ts2
                bank_free[bki] = ts2
                ring_toks.append(ts2)
                def cpbf(e, dm=dm):
                    return e.tensor_single_scalar(B_M[:, 8192:8704],
                                                  s2ring[:, 512 * dm: 512 * (dm + 1)],
                                                  1.0, ALU.mult)
                tbf = b.emit("dve", cpbf, waits=[lsum_d], inc="dve")
                tsq = b.emit("dve", lambda e: e.tensor_tensor(B_M[:, 8704:9216],
                                                              B_M[:, 8192:8704],
                                                              B_M[:, 8192:8704], ALU.mult),
                             waits=[lsq_d], inc="dve")
                wst2 = [tbf] + ([bank_free[4], bank_free[5]] if dm == 0 else [])
                lsum_d = b.emit("pe", lambda e, dm=dm: e.matmul(
                    PS[0:1, 4 * 512: 4 * 512 + 512], C_T[:, 2048:2049], B_M[:, 8192:8704],
                    start=(dm == 0), stop=(dm == CD - 1)),
                    waits=wst2, inc="pe")
                lsq_d = b.emit("pe", lambda e, dm=dm: e.matmul(
                    PS[0:1, 5 * 512: 5 * 512 + 512], C_T[:, 2048:2049], B_M[:, 8704:9216],
                    start=(dm == 0), stop=(dm == CD - 1)),
                    waits=[tsq], inc="pe")
            if tc < 7:
                prep_oq(tc + 1)
            trm = b.emit("dve", lambda e: e.tensor_single_scalar(V_R[0:1, 0:512],
                         PS[0:1, 4 * 512: 4 * 512 + 512], 1.0 / D, ALU.mult),
                         waits=[lsum_d], inc="dve")
            trq = b.emit("dve", lambda e: e.tensor_single_scalar(V_R[0:1, 512:1024],
                         PS[0:1, 5 * 512: 5 * 512 + 512], 1.0 / D, ALU.mult),
                         waits=[lsq_d], inc="dve")
            bank_free[4] = trm
            bank_free[5] = trq
            trm2 = b.emit("dve", lambda e: e.tensor_tensor(V_R[0:1, 1024:1536], V_R[0:1, 0:512],
                                                    V_R[0:1, 0:512], ALU.mult),
                          waits=[trm], inc="dve")
            tvar = b.emit("dve", lambda e: e.tensor_tensor(V_R[0:1, 512:1024],
                                                           V_R[0:1, 512:1024],
                                                           V_R[0:1, 1024:1536], ALU.subtract),
                          waits=[trq, trm2], inc="dve")
            tln = b.emit("act", lambda e: e.activation(V_R[0:1, 1024:1536],
                                                       V_R[0:1, 512:1024], AF.Ln, bias=CONSTV[0:1, 0:1]),
                         waits=[tvar], inc="act")
            trstd = b.emit("act", lambda e: e.activation(V_R[0:1, 1024:1536],
                                                         V_R[0:1, 1024:1536], AF.Exp,
                                                         scale=-0.5), inc="act")
            tmr = b.emit("dve", lambda e: e.tensor_tensor(V_R[0:1, 1536:2048],
                                                          V_R[0:1, 0:512],
                                                          V_R[0:1, 1024:1536], ALU.mult),
                         waits=[trstd], inc="dve")
            tb6 = b.emit("pe", lambda e: e.matmul(PS[:, 6 * 512: 7 * 512], C_OR[:, :],
                                                  V_R[0:1, 1024:1536], start=True, stop=True),
                         waits=[trstd, bank_free[6]], inc="pe")
            tb7 = b.emit("pe", lambda e: e.matmul(PS[:, 7 * 512: 8 * 512], C_OR[:, :],
                                                  V_R[0:1, 1536:2048], start=True, stop=True),
                         waits=[tmr, bank_free[7]], inc="pe")
            last_ap = None
            for dm in range(CD):
                pp = dm % 2
                ap1 = B_Vf[:, 2048 + 512 * pp: 2048 + 512 * (pp + 1)]
                ap2 = B_Vf[:, 3072 + 512 * pp: 3072 + 512 * (pp + 1)]
                ta1 = b.emit("dve", lambda e, dm=dm, ap1=ap1: e.tensor_tensor(
                    ap1, s2ring[:, 512 * dm: 512 * (dm + 1)],
                    PS[:, 6 * 512: 7 * 512], ALU.mult),
                    waits=[tb6, ring_toks[dm], prev_ap[pp]], inc="dve")
                ta2 = b.emit("dve", lambda e, ap1=ap1, ap2=ap2: e.tensor_tensor(
                    ap2, ap1, PS[:, 7 * 512: 8 * 512],
                    ALU.subtract), waits=[tb7, out_toks[pp]], inc="dve")
                out_toks[pp] = dma("pool", d_out[128 * dm: 128 * (dm + 1),
                                                 512 * tc: 512 * (tc + 1)],
                                   ap2, f"sout{pp}", waits=[ta2])
                prev_ap[pp] = ta2
                last_ap = ta2
            bank_free[6] = last_ap
            bank_free[7] = last_ap

        b.emit("sp", lambda e: e.nop(), waits=[out_toks[0], out_toks[1]])
        b.finalize(block)
        return nc


def _prep_core_inputs(inputs, c):
    x = np.asarray(inputs["L0_attn_x"])
    Wq = np.asarray(inputs["L0_attn_W_q"])
    Wk = np.asarray(inputs["L0_attn_W_k"])
    Wv = np.asarray(inputs["L0_attn_W_v"])
    Wo = np.asarray(inputs["L0_attn_W_o"])
    wg = np.asarray(inputs["L0_ffn_w_gate"])
    wu = np.asarray(inputs["L0_ffn_w_up"])
    wd = np.asarray(inputs["L0_ffn_w_down"])
    perm = np.zeros(D, np.int64)
    for h in range(H):
        perm[DH * h: DH * h + 32] = DH * h + 2 * np.arange(32)
        perm[DH * h + 32: DH * h + 64] = DH * h + 2 * np.arange(32) + 1
    xc = x[2 * c: 2 * c + 2].reshape(T, D)
    return {
        "xT": np.ascontiguousarray(xc.T),
        "wqT": np.ascontiguousarray(Wq[perm].T),
        "wkT": np.ascontiguousarray(Wk[perm].T),
        "wvT": np.ascontiguousarray(Wv.T),
        "woT": np.ascontiguousarray(Wo.T),
        "wupT": np.ascontiguousarray(wu.T),
        "wgT": np.ascontiguousarray(wg.T),
        "wdT": np.ascontiguousarray(wd.T),
        "wupN": np.ascontiguousarray(wu),
        "wgN": np.ascontiguousarray(wg),
        "wdN": np.ascontiguousarray(wd),
    }


def kernel(**inputs):
    nc = build_program(DBG_STAGE)
    in_maps = [_prep_core_inputs(inputs, c) for c in range(8)]
    res = run_bass_kernel_spmd(nc, in_maps, core_ids=list(range(8)))
    outs = []
    for c in range(8):
        o = np.asarray(res.results[c]["outT"])
        outs.append(o.T.reshape(2, L, D))
    return np.concatenate(outs, axis=0).astype(np.float32)


# version 05
# speedup vs baseline: 1.3560x; 1.1822x over previous
import math
import numpy as np
import ml_dtypes
import concourse.bass as bass
import concourse.mybir as mybir
from concourse.bass_utils import run_bass_kernel_spmd

F32 = mybir.dt.float32
BF16 = mybir.dt.bfloat16
AF = mybir.ActivationFunctionType
ALU = mybir.AluOpType
AX = mybir.AxisListType

D, H, DH, L, NSEQ = 1024, 16, 64, 2048, 2
T = NSEQ * L
I = 3072
QB = 32
CD = D // 128
CI = I // 128
CMAG = float(3 * 2 ** 22)
QMA = 32767.0
EPS = 1e-5
ROPE_BASE = 10000.0
SAFE = 1.02 / 8.0

ENGS = ("pe", "act", "dve", "pool", "sp")
A_LIMIT = CI
M_LIMIT = 2
DBG_STAGE = None


class KB:
    def __init__(self, nc):
        self.nc = nc
        self.ops = {e: [] for e in ENGS}
        self.sems = {}
        self.cnt = {}
        self.waited = {e: {} for e in ENGS}

    def sem(self, name):
        if name not in self.sems:
            self.sems[name] = self.nc.alloc_semaphore(name)
            self.cnt[name] = 0
        return self.sems[name]

    def __init_dma(self):
        if not hasattr(self, "dma_sems"):
            self.dma_sems = set()

    def emit(self, eng, fn, waits=(), inc=None, inc_k=1):
        self.__init_dma()
        if eng in ("dve", "act", "pool"):
            waits = list(waits) + [self.tok(eng)]
        real = []
        for tokv in waits:
            if tokv is None:
                continue
            s, v = tokv
            if s in self.dma_sems:
                v = self.cnt[s]
            assert s in self.cnt and v <= self.cnt[s], f"future token {s}:{v}"
            if v > self.waited[eng].get(s, 0):
                real.append((self.sems[s], v))
                self.waited[eng][s] = v
        tok = None
        semh = None
        if inc is not None:
            self.sem(inc)
            if inc_k == 16:
                self.dma_sems.add(inc)
            self.cnt[inc] += inc_k
            tok = (inc, self.cnt[inc])
            semh = self.sems[inc]

        def run(e, real=real, fn=fn, semh=semh, k=inc_k):
            for (sh, v) in real:
                e.wait_ge(sh, v)
            ins = fn(e)
            if semh is not None:
                ins.then_inc(semh, k)
        self.ops[eng].append(run)
        return tok

    def tok(self, name):
        n = self.cnt.get(name, 0)
        return (name, n) if n > 0 else None

    def finalize(self, block):
        mp = {"pe": block.tensor, "act": block.scalar, "dve": block.vector,
              "pool": block.gpsimd, "sp": block.sync}
        for e in ENGS:
            ops = self.ops[e]

            def body(eng, ops=ops):
                for f in ops:
                    f(eng)
            mp[e](body)


def _rope_tables():
    pos = np.arange(L, dtype=np.float32)
    idx = np.arange(0, DH, 2, dtype=np.float32)
    freqs = np.exp(np.float32(-math.log(ROPE_BASE)) * idx / np.float32(DH)).astype(np.float32)
    ang = pos[:, None] * freqs[None, :]
    sin = np.sin(ang).astype(np.float32)
    cos = np.cos(ang).astype(np.float32)
    C = np.zeros((128, L), np.float32)
    S = np.zeros((128, L), np.float32)
    for p in range(128):
        i = p % 32
        C[p] = cos[:, i]
        S[p] = sin[:, i] * (-1.0 if (p % 64) < 32 else 1.0)
    return C, S


def build_program(dbg=None):
    nc = bass.Bass("TRN2", target_bir_lowering=False, debug=False)

    d_x = nc.dram_tensor("xT", [D, T], F32, kind="ExternalInput").ap()
    d_wq = nc.dram_tensor("wqT", [D, D], F32, kind="ExternalInput").ap()
    d_wk = nc.dram_tensor("wkT", [D, D], F32, kind="ExternalInput").ap()
    d_wv = nc.dram_tensor("wvT", [D, D], F32, kind="ExternalInput").ap()
    d_wo = nc.dram_tensor("woT", [D, D], F32, kind="ExternalInput").ap()
    d_wup = nc.dram_tensor("wupT", [D, I], F32, kind="ExternalInput").ap()
    d_wg = nc.dram_tensor("wgT", [D, I], F32, kind="ExternalInput").ap()
    d_wd = nc.dram_tensor("wdT", [I, D], F32, kind="ExternalInput").ap()
    d_wupN = nc.dram_tensor("wupN", [I, D], F32, kind="ExternalInput").ap()
    d_wgN = nc.dram_tensor("wgN", [I, D], F32, kind="ExternalInput").ap()
    d_wdN = nc.dram_tensor("wdN", [D, I], F32, kind="ExternalInput").ap()
    d_out = nc.dram_tensor("outT", [D, T], F32, kind="ExternalOutput").ap()

    d_xln = nc.dram_tensor("xlnd", [D, T], F32, kind="Internal").ap()
    d_up = nc.dram_tensor("upd", [I, T], BF16, kind="Internal").ap()
    d_gate = nc.dram_tensor("gated", [I, T], BF16, kind="Internal").ap()
    d_t = nc.dram_tensor("td", [I, T], BF16, kind="Internal").ap()
    d_P = nc.dram_tensor("Pd", [I, T], BF16, kind="Internal").ap()
    d_o = nc.dram_tensor("od", [I, T], BF16, kind="Internal").ap()
    d_srow = nc.dram_tensor("srowd", [1, H * L], F32, kind="Internal").ap()
    d_rrow = nc.dram_tensor("rrowd", [1, H * L], F32, kind="Internal").ap()
    cc_in = [nc.dram_tensor(f"ccin{i}", [1, 512], F32, kind="Internal").ap() for i in range(4)]
    cc_out = [nc.dram_tensor(f"ccout{i}", [1, 512], F32, kind="Internal").ap() for i in range(4)]

    Cnp, Snp = _rope_tables()
    i_tabC = nc.inline_tensor(Cnp.astype(ml_dtypes.bfloat16), "tabC").ap()
    i_tabS = nc.inline_tensor(Snp.astype(ml_dtypes.bfloat16), "tabS").ap()
    swap_np = np.zeros((128, 128), np.float32)
    for p in range(128):
        swap_np[p ^ 32, p] = 1.0
    i_swap = nc.inline_tensor(swap_np.astype(ml_dtypes.bfloat16), "swapM").ap()
    i_eyeF = nc.inline_tensor(np.eye(128, dtype=np.float32), "eyeF").ap()
    i_onescolB = nc.inline_tensor(np.ones((128, 1), ml_dtypes.bfloat16), "onescolB").ap()
    i_onesrowF = nc.inline_tensor(np.ones((1, 128), np.float32), "onesrowF").ap()
    constv_np = np.tile(np.array([[EPS, CMAG, -CMAG, 0.0]], np.float32), (128, 1))
    i_constv = nc.inline_tensor(constv_np, "constv").ap()
    ind_np = np.zeros((D, H), np.float32)
    for c in range(CD):
        ind_np[128 * c: 128 * c + 64, 2 * c] = 1.0
        ind_np[128 * c + 64: 128 * c + 128, 2 * c + 1] = 1.0
    i_ind = nc.inline_tensor(ind_np.astype(ml_dtypes.bfloat16), "indnq").ap()
    ind32_np = np.zeros((QB, CD * 128), np.float32)
    for c in range(CD):
        for p in range(128):
            ind32_np[4 * c + p // 32, 128 * c + p] = 1.0
    i_ind32 = nc.inline_tensor(ind32_np[:32].astype(np.float32), "ind32").ap()

    b = KB(nc)
    import contextlib
    with contextlib.ExitStack() as ctx:
        def sb(name, shape, dtype):
            return ctx.enter_context(nc.sbuf_tensor(name, shape, dtype))
        B_X = sb("B_X", [128, 16384], BF16)
        B_Q = sb("B_Q", [128, 16384], BF16)
        B_K = sb("B_K", [128, 16384], BF16)
        B_V = sb("B_V", [128, 16640], BF16)
        B_W = sb("B_W", [128, 16384], BF16)
        B_M = sb("B_M", [128, 12288], BF16)
        B_P = B_M
        V_S = sb("V_S", [128, 16], F32)
        V_C = sb("V_C", [128, 48], F32)
        V_R = sb("V_R", [1, 2048], F32)
        V_R2 = sb("V_R2", [1, 1024], F32)
        V_SC = sb("V_SC", [1, 32], F32)
        C_T = sb("C_T", [128, 2052], BF16)
        C_T2 = sb("C_T2", [128, 2048], BF16)
        C_SW = sb("C_SW", [128, 128], BF16)
        C_EYE = sb("C_EYE", [128, 128], F32)
        C_IND = sb("C_IND", [128, 128], BF16)
        C_OR = sb("C_OR", [1, 128], F32)
        CONSTV = sb("CONSTV", [128, 4], F32)
        nc.const_aps.register(F32, EPS, CONSTV[:, 0:1])
        nc.const_aps.register(F32, CMAG, CONSTV[:, 1:2])
        nc.const_aps.register(F32, -CMAG, CONSTV[:, 2:3])
        C_I32 = None
        PS = ctx.enter_context(nc.psum_tensor("PS", [128, 4096], F32))
        block = ctx.enter_context(nc.Block())

        B_Mf = B_M.ap().bitcast(F32)
        B_Wf = B_W.ap().bitcast(F32)
        B_Vf = B_V.ap().bitcast(F32)

        def bank(i, n=512):
            return PS[:, 512 * i: 512 * i + n]

        def dma(q, out_ap, in_ap, sem="ld", waits=()):
            return b.emit(q, lambda e: e.dma_start(out=out_ap, in_=in_ap),
                          waits=waits, inc=sem, inc_k=16)

        dma("sp", C_T[:, 0:2048], i_tabC, "ldc")
        dma("sp", C_T2[:, :], i_tabS, "ldc")
        dma("sp", C_T[:, 2048:2049], i_onescolB, "ldc")
        dma("sp", C_SW[:, :], i_swap, "ldc")
        dma("sp", C_EYE[:, :], i_eyeF, "ldc")
        dma("sp", C_IND.ap()[:, 0:128].rearrange("p (c h) -> p c h", c=CD),
            i_ind.rearrange("(c p) h -> p c h", p=128), "ldc")
        dma("sp", CONSTV[:, :], i_constv, "ldc")
        tok_consts = dma("sp", C_OR[:, :], i_onesrowF, "ldc")
        t_ones = b.emit("dve", lambda e: e.memset(V_C[:, 47:48], 1.0), inc="dve")
        CCST = V_R[0:1, 0:512]
        GMX = V_R[0:1, 512:1024]

        def load_xseq(j, waits):
            tk = None
            for c in range(CD):
                tk = dma("pool", B_X[:, 2048 * c: 2048 * (c + 1)],
                         d_x[128 * c: 128 * (c + 1), L * j: L * (j + 1)],
                         "gx", waits)
            return tk

        def load_w(dr, slot, waits):
            return dma("pool",
                       B_W.ap()[:, 8192 * slot: 8192 * (slot + 1)].rearrange(
                           "p (c n) -> p c n", c=CD),
                       dr.rearrange("(c p) n -> p c n", p=128), f"gw{slot}", waits)

        tok_x = load_xseq(0, ())
        tok_wq = load_w(d_wq, 0, ())
        tok_wk = load_w(d_wk, 1, ())

        bank_free = [None] * 8
        tok_oproj_done = None
        tok_vproj_done = None
        pv_free = [None, None]
        tok_dbg_last = None

        if dbg == "simA":
            tok_rope = None
            tok_oproj_done = b.tok("pe")
            tok_vproj_done = b.tok("pe")
            b.emit("dve", lambda e: e.memset(V_S[:, :], 1.0), inc="dve")
            b.emit("dve", lambda e: e.memset(B_Q[:, :], 0.0), inc="dve")
            tz = b.emit("dve", lambda e: e.memset(B_K[:, :], 0.0), inc="dve")
            for c_ in range(CD):
                dma("sp", d_xln[128 * c_: 128 * (c_ + 1), :],
                    B_Q.ap().bitcast(F32)[:, 0:T], "sxln", waits=[tz])
        for j in range([] if dbg == "simA" else range(NSEQ) and NSEQ or NSEQ) if False else range(0 if dbg == "simA" else NSEQ):
            if j == 1:
                tok_x = load_xseq(1, [tok_oproj_done])
                tok_wq = load_w(d_wq, 0, [tok_vproj_done, b.tok("dve")])
                tok_wk = load_w(d_wk, 1, [tok_oproj_done])

            def proj_qk(wtok, slot, dst):
                last_mm = None
                last_ev = None
                for cq in range(CD):
                    for t4 in range(4):
                        bki = (cq * 4 + t4) % 4
                        w = [wtok, tok_x, tok_consts, bank_free[bki]]
                        for c in range(CD):
                            def mm(e, bki=bki, c=c, cq=cq, t4=t4, slot=slot):
                                return e.matmul(
                                    bank(bki),
                                    B_W[:, 8192 * slot + 1024 * c + 128 * cq:
                                        8192 * slot + 1024 * c + 128 * cq + 128],
                                    B_X[:, 2048 * c + 512 * t4: 2048 * c + 512 * (t4 + 1)],
                                    start=(c == 0), stop=(c == CD - 1))
                            last_mm = b.emit("pe", mm, waits=w if c == 0 else (),
                                             inc="pe" if c == CD - 1 else None)
                        def ev(e, bki=bki, cq=cq, t4=t4, dst=dst):
                            return e.copy(dst[:, 2048 * cq + 512 * t4: 2048 * cq + 512 * (t4 + 1)],
                                          bank(bki))
                        last_ev = b.emit("act", ev, waits=[last_mm], inc="act")
                        bank_free[bki] = last_ev
                return last_ev, last_mm

            tok_qe, tok_qmm = proj_qk(tok_wq, 0, B_Q)
            tok_ke, tok_kmm = proj_qk(tok_wk, 1, B_K)
            tok_wv = load_w(d_wv, 0, [tok_qmm])
            if dbg == "qraw" and j == 0:
                tdl = None
                for c in range(CD):
                    tdl = dma("pool", d_out[128 * c: 128 * (c + 1), 0:L],
                              B_Q[:, 2048 * c: 2048 * (c + 1)], "gst", waits=[tok_qe])
                b.emit("sp", lambda e: e.nop(), waits=[tdl])
                b.finalize(block)
                return nc

            prev_r1 = None
            prev_r3 = None
            for (src, etok) in ((B_Q, tok_qe), (B_K, tok_ke)):
                for cq in range(CD):
                    sw_last = None
                    for t4 in range(4):
                        bki = 4 + (t4 % 2)
                        w = [etok, tok_consts, bank_free[bki], prev_r1]
                        def mmsw(e, bki=bki, src=src, cq=cq, t4=t4):
                            return e.matmul(bank(bki), C_SW[:, :],
                                            src[:, 2048 * cq + 512 * t4: 2048 * cq + 512 * (t4 + 1)],
                                            start=True, stop=True)
                        tmm = b.emit("pe", mmsw, waits=w, inc="pe")
                        def evsw(e, bki=bki, t4=t4):
                            return e.copy(B_M[:, 512 * t4: 512 * (t4 + 1)], bank(bki))
                        sw_last = b.emit("act", evsw, waits=[tmm, prev_r1], inc="act")
                        bank_free[bki] = sw_last
                    def r1(e, src=src, cq=cq):
                        return e.tensor_tensor(B_M[:, 2048:4096],
                                               src[:, 2048 * cq: 2048 * (cq + 1)],
                                               C_T[:, 0:2048], ALU.mult)
                    b.emit("dve", r1, waits=[sw_last], inc="dve")
                    def r2(e):
                        return e.tensor_tensor(B_M[:, 4096:6144], B_M[:, 0:2048],
                                               C_T2[:, :], ALU.mult)
                    prev_r1 = b.emit("dve", r2, inc="dve")
                    def r3(e, src=src, cq=cq):
                        return e.tensor_tensor(src[:, 2048 * cq: 2048 * (cq + 1)],
                                               B_M[:, 2048:4096], B_M[:, 4096:6144],
                                               ALU.add)
                    prev_r3 = b.emit("dve", r3, inc="dve")
            tok_rope = prev_r3

            def vones(e):
                v = B_V.ap().rearrange("p (s w) -> p s w", w=65)
                return e.memset(v[:, :, 64:65], 1.0)
            tok_vones = b.emit("pool", vones,
                               waits=[b.tok("pe") if j == 1 else None], inc="pool")
            vlast_ev = None
            for st in range(16):
                for d2 in range(2):
                    bki = (st * 2 + d2) % 4
                    w = [tok_wv, bank_free[bki]]
                    for c in range(CD):
                        def mmv(e, bki=bki, c=c, st=st, d2=d2):
                            return e.matmul(
                                bank(bki),
                                B_X[:, 2048 * c + 128 * st: 2048 * c + 128 * (st + 1)],
                                B_W[:, 1024 * c + 512 * d2: 1024 * c + 512 * (d2 + 1)],
                                start=(c == 0), stop=(c == CD - 1))
                        last = b.emit("pe", mmv, waits=w if c == 0 else (),
                                      inc="pe" if c == CD - 1 else None)
                    def evv(e, bki=bki, st=st, d2=d2):
                        dst = B_V.ap()[:, 1040 * st: 1040 * (st + 1)].rearrange(
                            "p (h w) -> p h w", w=65)[:, 8 * d2: 8 * (d2 + 1), 0:64]
                        src = bank(bki).rearrange("p (h w) -> p h w", w=64)
                        return e.copy(dst, src)
                    vlast_ev = b.emit("act", evv, waits=[last, tok_vones], inc="act")
                    bank_free[bki] = vlast_ev
            tok_vproj_done = b.tok("pe")
            tok_wo = load_w(d_wo, 1, [tok_kmm])

            prev_mmn = None
            nsc = B_W[:, 0:512]
            for (src, base_col) in ((B_Q, 0), (B_K, 4)):
                for t4 in range(4):
                    w6 = [bank_free[6]]
                    last = None
                    for cq in range(CD):
                        def sqr(e, src=src, cq=cq, t4=t4):
                            return e.tensor_tensor(
                                nsc,
                                src[:, 2048 * cq + 512 * t4: 2048 * cq + 512 * (t4 + 1)],
                                src[:, 2048 * cq + 512 * t4: 2048 * cq + 512 * (t4 + 1)],
                                ALU.mult)
                        tsq = b.emit("dve", sqr, waits=[tok_rope, prev_mmn, tok_vproj_done], inc="dve")
                        def mmn(e, cq=cq):
                            return e.matmul(PS[0:16, 6 * 512: 6 * 512 + 512],
                                            C_IND[:, 16 * cq: 16 * (cq + 1)],
                                            nsc,
                                            start=(cq == 0), stop=(cq == CD - 1))
                        last = b.emit("pe", mmn, waits=[tsq] + (w6 if cq == 0 else []),
                                      inc="pe")
                        prev_mmn = last
                    col = base_col + t4
                    def red(e, col=col):
                        return e.tensor_reduce(V_S[0:16, col:col + 1],
                                               PS[0:16, 6 * 512: 6 * 512 + 512],
                                               AX.X, ALU.max, apply_absolute_value=True)
                    rt = b.emit("dve", red, waits=[last], inc="dve")
                    if dbg == "ind2" and j == 0 and src is B_Q and t4 == 0:
                        rt = b.emit("dve", lambda e: e.tensor_copy(
                            B_Mf[:, 512:1024][0:16, 0:512],
                            PS[0:16, 6 * 512: 6 * 512 + 512]), inc="dve")
                    bank_free[6] = rt
            tn1 = b.emit("dve", lambda e: e.tensor_reduce(V_S[0:16, 8:9], V_S[0:16, 0:4],
                                                          AX.X, ALU.max), inc="dve")
            tn2 = b.emit("dve", lambda e: e.tensor_reduce(V_S[0:16, 9:10], V_S[0:16, 4:8],
                                                          AX.X, ALU.max),
                         waits=[tn1], inc="dve")
            t3_ = b.emit("dve", lambda e: e.tensor_tensor(V_S[0:16, 10:11], V_S[0:16, 8:9],
                                                          V_S[0:16, 9:10], ALU.mult),
                         waits=[tn2], inc="dve")
            t4_ = b.emit("act", lambda e: e.activation(V_S[0:16, 11:12], V_S[0:16, 10:11],
                                                       AF.Sqrt), waits=[t3_], inc="act")
            for t4 in range(4):
                w6 = [bank_free[6]]
                last = None
                for cq in range(CD):
                    def dgm(e, cq=cq, t4=t4):
                        return e.tensor_tensor(
                            nsc,
                            B_Q[:, 2048 * cq + 512 * t4: 2048 * cq + 512 * (t4 + 1)],
                            B_K[:, 2048 * cq + 512 * t4: 2048 * cq + 512 * (t4 + 1)],
                            ALU.mult)
                    tsq = b.emit("dve", dgm, waits=[prev_mmn, tn2], inc="dve")
                    def mmn2(e, cq=cq):
                        return e.matmul(PS[0:16, 6 * 512: 6 * 512 + 512],
                                        C_IND[:, 16 * cq: 16 * (cq + 1)],
                                        nsc, start=(cq == 0), stop=(cq == CD - 1))
                    last = b.emit("pe", mmn2, waits=[tsq] + (w6 if cq == 0 else []),
                                  inc="pe")
                    prev_mmn = last
                def red2(e, t4=t4):
                    return e.tensor_reduce(V_S[0:16, t4:t4 + 1],
                                           PS[0:16, 6 * 512: 6 * 512 + 512],
                                           AX.X, ALU.max)
                rt = b.emit("dve", red2, waits=[last], inc="dve")
                bank_free[6] = rt
            tdg = b.emit("dve", lambda e: e.tensor_reduce(V_S[0:16, 13:14], V_S[0:16, 0:4],
                                                          AX.X, ALU.max),
                         waits=[bank_free[6]], inc="dve")
            tnA = b.emit("dve", lambda e: e.tensor_single_scalar(V_S[0:16, 14:15],
                         V_S[0:16, 13:14], 0.125, ALU.mult), waits=[tdg], inc="dve")
            tnB = b.emit("dve", lambda e: e.tensor_scalar(V_S[0:16, 15:16],
                         V_S[0:16, 11:12], SAFE, -85.0, ALU.mult, ALU.add),
                         waits=[t4_], inc="dve")
            tnC = b.emit("dve", lambda e: e.tensor_tensor(V_S[0:16, 10:11], V_S[0:16, 14:15],
                                                          V_S[0:16, 15:16], ALU.max),
                         waits=[tnA, tnB], inc="dve")
            t5_ = b.emit("dve", lambda e: e.tensor_single_scalar(V_S[0:16, 12:13],
                         V_S[0:16, 10:11], -1.0, ALU.mult),
                         waits=[tnC], inc="dve")
            w7 = [t5_, bank_free[7]]
            ttr = b.emit("pe", lambda e: e.transpose(PS[0:1, 7 * 512: 7 * 512 + 16],
                                                     V_S[0:16, 12:13], C_EYE[0:16, 0:16]),
                         waits=w7, inc="pe")
            tcp_ = b.emit("dve", lambda e: e.tensor_copy(V_SC[0:1, 0:16],
                                                         PS[0:1, 7 * 512: 7 * 512 + 16]),
                          waits=[ttr], inc="dve")
            tbc_ = b.emit("pe", lambda e: e.matmul(PS[:, 7 * 512: 7 * 512 + 16],
                                                   C_OR[:, :], V_SC[0:1, 0:16],
                                                   start=True, stop=True),
                          waits=[tcp_], inc="pe")
            tok_bias = b.emit("act", lambda e: e.copy(V_C[:, 0:16],
                                                      PS[:, 7 * 512: 7 * 512 + 16]),
                              waits=[tbc_], inc="act")
            bank_free[7] = tok_bias

            if dbg == "ind2" and j == 0:
                tc2 = dma("sp", d_out[0:16, 0:512], B_Mf[0:16, 512:1024], "st",
                          waits=[b.tok("dve")])
                tc3 = dma("sp", d_out[0:16, 512:528], V_S[0:16, 0:16], "st",
                          waits=[b.tok("dve")])
                b.emit("sp", lambda e: e.nop(), waits=[tc2, tc3])
                b.finalize(block)
                return nc

            if dbg == "ind" and j == 0:
                td1 = dma("pool", d_out[0:128, 0:128], C_IND[:, 0:128], "gst",
                          waits=[b.tok("dve"), b.tok("pe")])
                td2 = dma("pool", d_out[0:128, 128:640], B_M[:, 0:512], "gst",
                          waits=[b.tok("dve")])
                tcpy = b.emit("dve", lambda e: e.tensor_copy(
                    B_Mf[:, 512:1024][0:16, 0:512], PS[0:16, 6 * 512: 6 * 512 + 512]),
                    waits=[b.tok("pe")], inc="dve")
                td3 = dma("sp", d_out[0:16, 1024:1536], B_Mf[0:16, 512:1024], "st",
                          waits=[tcpy])
                b.emit("sp", lambda e: e.nop(), waits=[td1, td2, td3])
                b.finalize(block)
                return nc

            if dbg == "bias" and j == 0:
                tdbg1 = dma("sp", d_out[0:128, 0:48], V_C[:, :], "st", waits=[tok_bias])
                tdbg2 = dma("sp", d_out[0:16, 64:80], V_S[0:16, 0:16], "st",
                            waits=[tok_bias, b.tok("dve")])
                b.emit("sp", lambda e: e.nop(), waits=[tdbg1, tdbg2])
                b.finalize(block)
                return nc

            groups = [(0, 3), (3, 3), (6, 3), (9, 3), (12, 3), (15, 1)]
            prev_bc7 = tok_bias
            pv_free = [bank_free[6], None]
            sum_dma = [None, None]
            prev_rdist = [None, None]
            prev_rr = [None] * 4
            rdist_done = {}
            for h in range(H):
                p0 = 64 * (h % 2)
                cq = h // 2
                qoff = 2048 * cq
                for lc in range(4):
                    pv_last = None
                    pending = None

                    def emit_pv(pend):
                        gi, s0, glen, pslot, texp = pend
                        pv = None
                        for si in range(glen):
                            stile = s0 + si
                            wpv = [texp]
                            if gi == 0 and si == 0:
                                wpv += [vlast_ev, tok_vones] + pv_free
                            def mmpv(e, stile=stile, si=si, pslot=pslot, h=h, gi=gi, glen=glen):
                                return e.matmul(
                                    PS[0:65, 6 * 512: 6 * 512 + 512],
                                    B_V[:, 1040 * stile + 65 * h: 1040 * stile + 65 * (h + 1)],
                                    B_P[:, pslot + 512 * si: pslot + 512 * (si + 1)],
                                    start=(gi == 0 and si == 0),
                                    stop=(gi == len(groups) - 1 and si == glen - 1))
                            pv = b.emit("pe", mmpv, waits=wpv,
                                        inc="pe" if (gi == len(groups) - 1 and si == glen - 1) else None)
                        return pv

                    for gi, (s0, glen) in enumerate(groups):
                        gb = 0 if gi % 2 == 0 else 3
                        last_qk = None
                        for si in range(glen):
                            stile = s0 + si
                            wqk = [tok_rope, bank_free[gb]] if si == 0 else ()
                            def mmqk(e, gb=gb, si=si, stile=stile, p0=p0, qoff=qoff, lc=lc):
                                return e.matmul(
                                    PS[:, 512 * (gb + si): 512 * (gb + si) + 512],
                                    B_K[p0: p0 + 64, qoff + 128 * stile: qoff + 128 * (stile + 1)],
                                    B_Q[p0: p0 + 64, qoff + 512 * lc: qoff + 512 * (lc + 1)],
                                    start=True, stop=True)
                            last_qk = b.emit("pe", mmqk, waits=wqk,
                                             inc="pe" if si == glen - 1 else None)
                        pslot = (gi % 2) * 1536
                        def expg(e, gb=gb, glen=glen, pslot=pslot, h=h):
                            return e.activation(B_P[:, pslot: pslot + 512 * glen],
                                                PS[:, 512 * gb: 512 * gb + 512 * glen],
                                                AF.Exp, bias=V_C[:, h:h + 1], scale=0.125)
                        texp = b.emit("act", expg, waits=[last_qk, tok_bias], inc="act")
                        bank_free[gb] = texp
                        if pending is not None:
                            emit_pv(pending)
                        pending = (gi, s0, glen, pslot, texp)
                    pv_last = emit_pv(pending)
                    def evpv(e, h=h, lc=lc, cq=cq, p0=p0):
                        return e.copy(B_X[p0: p0 + 64, 2048 * cq + 512 * lc: 2048 * cq + 512 * (lc + 1)],
                                      PS[0:64, 6 * 512: 6 * 512 + 512])
                    tev = b.emit("act", evpv, waits=[pv_last], inc="act")
                    sslot = 512 * (lc % 2)
                    def evsum(e, sslot=sslot):
                        return e.tensor_copy(V_R2[0:1, sslot: sslot + 512],
                                             PS[64:65, 6 * 512: 6 * 512 + 512])
                    tsum = b.emit("dve", evsum, waits=[pv_last, sum_dma[lc % 2]], inc="dve")
                    sum_dma[lc % 2] = dma("sp",
                                          d_srow[0:1, 2048 * h + 512 * lc: 2048 * h + 512 * (lc + 1)],
                                          V_R2[0:1, sslot: sslot + 512], f"ss{lc % 2}", waits=[tsum])
                    pv_free = [tev, tsum]
                    if dbg == "attn2":
                        trc2 = b.emit("dve", lambda e, sslot=sslot: e.reciprocal(
                            V_R[0:1, 1536:2048], V_R2[0:1, sslot: sslot + 512]),
                            waits=[tsum, prev_bc7], inc="dve")
                        wb2 = [trc2, bank_free[7]]
                        def bcr2(e):
                            return e.matmul(PS[0:64, 7 * 512: 7 * 512 + 512], C_OR[0:1, 0:64],
                                            V_R[0:1, 1536:2048], start=True, stop=True)
                        prev_bc7 = b.emit("pe", bcr2, waits=wb2, inc="pe")
                        def nrm2_(e, h=h, lc=lc, cq=cq, p0=p0):
                            sl = B_X[p0: p0 + 64, 2048 * cq + 512 * lc: 2048 * cq + 512 * (lc + 1)]
                            return e.tensor_tensor(sl, sl, PS[0:64, 7 * 512: 7 * 512 + 512],
                                                   ALU.mult)
                        norm_tt = b.emit("dve", nrm2_, waits=[prev_bc7, tev], inc="dve")
                        bank_free[7] = norm_tt
                        pv_free = [tev, norm_tt]
                if dbg in ("attnraw", "attn2"):
                    continue
                hp = h % 2
                tok_sr = [b.tok("ss0"), b.tok("ss1")]
                sd = B_Wf[:, 256 + 32 * hp: 272 + 32 * hp]
                sd2 = B_Wf[:, 272 + 32 * hp: 288 + 32 * hp]
                tok_d1 = dma("sp", sd,
                             d_srow[0:1, 2048 * h: 2048 * (h + 1)].rearrange(
                                 "o (p n) -> (o p) n", p=128), f"lsd{hp}",
                             waits=tok_sr + [prev_rdist[hp]])
                tok_rc = b.emit("dve", lambda e, sd=sd, sd2=sd2: e.reciprocal(sd2, sd),
                                waits=[tok_d1], inc="dve")
                tok_d2 = dma("sp",
                             d_rrow[0:1, 2048 * h: 2048 * (h + 1)].rearrange(
                                 "o (p n) -> (o p) n", p=128), sd2, f"sdist{hp}",
                             waits=[tok_rc])
                prev_rdist[hp] = tok_d2
                rdist_done[h] = tok_d2
                for hn in ([h - 1] if h > 0 else []) + ([h] if h == H - 1 else []):
                    cqn = hn // 2
                    p0n = 64 * (hn % 2)
                    norm_tt = None
                    d3toks = []
                    for lc in range(4):
                        rstage = B_Wf[0:1, 512 + 512 * lc: 1024 + 512 * lc]
                        d3toks.append(dma("sp", rstage,
                                     d_rrow[0:1, 2048 * hn + 512 * lc: 2048 * hn + 512 * lc + 512],
                                     f"lrr{lc}", waits=[rdist_done[hn], prev_rr[lc]]))
                    for lc in range(4):
                        rstage = B_Wf[0:1, 512 + 512 * lc: 1024 + 512 * lc]
                        wb = [d3toks[lc], bank_free[7]]
                        def bcr(e, rstage=rstage):
                            return e.matmul(PS[0:64, 7 * 512: 7 * 512 + 512], C_OR[0:1, 0:64],
                                            rstage, start=True, stop=True)
                        prev_bc7 = b.emit("pe", bcr, waits=wb, inc="pe")
                        def nrm(e, hn=hn, lc=lc, cqn=cqn, p0n=p0n):
                            sl = B_X[p0n: p0n + 64, 2048 * cqn + 512 * lc: 2048 * cqn + 512 * (lc + 1)]
                            return e.tensor_tensor(sl, sl, PS[0:64, 7 * 512: 7 * 512 + 512],
                                                   ALU.mult)
                        norm_tt = b.emit("dve", nrm, waits=[prev_bc7, pv_free[0]], inc="dve")
                        bank_free[7] = norm_tt
                        prev_rr[lc] = norm_tt
            tok_attn_norm = b.tok("dve")

            if dbg == "attnraw" and j == 0:
                for c in range(CD):
                    tok_dbg_last = dma("pool", d_out[128 * c: 128 * (c + 1), 0:L],
                                       B_X[:, 2048 * c: 2048 * (c + 1)], "gst",
                                       waits=[b.tok("act"), b.tok("ss0"), b.tok("ss1")])
                tok_dbg_last = dma("sp",
                                   d_out.rearrange("d t -> (d t)")[D * L: D * L + H * L].rearrange("(o n) -> o n", o=1),
                                   d_srow, "st", waits=[b.tok("st"), tok_dbg_last])
                b.emit("sp", lambda e: e.nop(), waits=[tok_dbg_last])
                b.finalize(block)
                return nc

            if dbg in ("q", "attn", "attn2"):
                srcbuf = B_Q if dbg == "q" else B_X
                wtok = tok_rope if dbg == "q" else tok_attn_norm
                for c in range(CD):
                    tok_dbg_last = dma("pool", d_out[128 * c: 128 * (c + 1), L * j: L * (j + 1)],
                                       srcbuf[:, 2048 * c: 2048 * (c + 1)], "gst", waits=[wtok])
                tok_oproj_done = b.tok("pe")
                if j == NSEQ - 1:
                    b.emit("sp", lambda e: e.nop(), waits=[tok_dbg_last])
                    b.finalize(block)
                    return nc
                continue

            s1ring = B_Wf
            prev_xr = [None] * 4
            prev_ap1 = [None, None]
            prev_xlndma = [None, None]
            lsum = lsq = None
            prev_sq1 = [None, None]

            def emit_stats1(dm, ts1, tsq):
                nonlocal lsum, lsq, prev_sq1
                wst = [ts1] + (ssum_w if dm == 0 else [])
                lsum = b.emit("pe", lambda e, dm=dm: e.matmul(
                    PS[0:1, 4 * 512: 4 * 512 + 512], V_C[:, 47:48],
                    s1ring[:, 512 * dm: 512 * (dm + 1)],
                    start=(dm == 0), stop=(dm == CD - 1)), waits=wst, inc="pe")
                lsq = b.emit("pe", lambda e, dm=dm: e.matmul(
                    PS[0:1, 5 * 512: 5 * 512 + 512], V_C[:, 47:48],
                    B_Mf[:, 4096 + 512 * (dm % 2): 4608 + 512 * (dm % 2)],
                    start=(dm == 0), stop=(dm == CD - 1)), waits=[tsq], inc="pe")
                prev_sq1[dm % 2] = lsq

            for t4 in range(4):
                ssum_w = [bank_free[4], bank_free[5]]
                ring_toks = []
                stats_pend = None
                for dm in range(CD):
                    bki = dm % 3
                    w = [tok_wo, tok_attn_norm, bank_free[bki]]
                    for ca in range(CD):
                        def mmo(e, bki=bki, ca=ca, dm=dm, t4=t4):
                            return e.matmul(
                                bank(bki),
                                B_W[:, 8192 + 1024 * ca + 128 * dm:
                                    8192 + 1024 * ca + 128 * dm + 128],
                                B_X[:, 2048 * ca + 512 * t4: 2048 * ca + 512 * (t4 + 1)],
                                start=(ca == 0), stop=(ca == CD - 1))
                        last = b.emit("pe", mmo, waits=w if ca == 0 else (),
                                      inc="pe" if ca == CD - 1 else None)
                    if stats_pend is not None:
                        emit_stats1(*stats_pend)
                    r = dm % 4
                    xr_f = B_Mf[:, 512 * r: 512 * (r + 1)]
                    tok_xr = dma("act", xr_f,
                                 d_x[128 * dm: 128 * (dm + 1),
                                     L * j + 512 * t4: L * j + 512 * (t4 + 1)],
                                 f"lxr{r}", waits=[prev_xr[r]])
                    def adds1(e, bki=bki, dm=dm, xr_f=xr_f):
                        return e.tensor_tensor(s1ring[:, 512 * dm: 512 * (dm + 1)],
                                               bank(bki), xr_f, ALU.add)
                    ts1 = b.emit("dve", adds1, waits=[last, tok_xr, t_ones], inc="dve")
                    prev_xr[r] = ts1
                    bank_free[bki] = ts1
                    ring_toks.append(ts1)
                    dsq = B_Mf[:, 4096 + 512 * (dm % 2): 4608 + 512 * (dm % 2)]
                    tsq = b.emit("dve", lambda e, dm=dm, dsq=dsq: e.tensor_tensor(
                        dsq, s1ring[:, 512 * dm: 512 * (dm + 1)],
                        s1ring[:, 512 * dm: 512 * (dm + 1)], ALU.mult),
                        waits=[prev_sq1[dm % 2]], inc="dve")
                    stats_pend = (dm, ts1, tsq)
                emit_stats1(*stats_pend)
                trm = b.emit("dve", lambda e: e.tensor_single_scalar(V_R[0:1, 0:512],
                             PS[0:1, 4 * 512: 4 * 512 + 512], 1.0 / D, ALU.mult),
                             waits=[lsum], inc="dve")
                trq = b.emit("dve", lambda e: e.tensor_single_scalar(V_R[0:1, 512:1024],
                             PS[0:1, 5 * 512: 5 * 512 + 512], 1.0 / D, ALU.mult),
                             waits=[lsq], inc="dve")
                bank_free[4] = trm
                bank_free[5] = trq
                trm2 = b.emit("dve", lambda e: e.tensor_tensor(V_R[0:1, 1024:1536], V_R[0:1, 0:512],
                                                        V_R[0:1, 0:512], ALU.mult),
                              waits=[trm], inc="dve")
                tvar = b.emit("dve", lambda e: e.tensor_tensor(V_R[0:1, 512:1024],
                                                               V_R[0:1, 512:1024],
                                                               V_R[0:1, 1024:1536],
                                                               ALU.subtract),
                              waits=[trq, trm2], inc="dve")
                tln = b.emit("act", lambda e: e.activation(V_R[0:1, 1024:1536],
                                                           V_R[0:1, 512:1024], AF.Ln,
                                                           bias=CONSTV[0:1, 0:1]),
                             waits=[tvar], inc="act")
                trstd = b.emit("act", lambda e: e.activation(V_R[0:1, 1024:1536],
                                                             V_R[0:1, 1024:1536], AF.Exp,
                                                             scale=-0.5), inc="act")
                tmr = b.emit("dve", lambda e: e.tensor_tensor(V_R[0:1, 1536:2048],
                                                              V_R[0:1, 0:512],
                                                              V_R[0:1, 1024:1536], ALU.mult),
                             waits=[trstd], inc="dve")
                tb6 = b.emit("pe", lambda e: e.matmul(PS[:, 6 * 512: 7 * 512], C_OR[:, :],
                                                      V_R[0:1, 1024:1536],
                                                      start=True, stop=True),
                             waits=[trstd, bank_free[6]], inc="pe")
                tb7 = b.emit("pe", lambda e: e.matmul(PS[:, 7 * 512: 8 * 512], C_OR[:, :],
                                                      V_R[0:1, 1536:2048],
                                                      start=True, stop=True),
                             waits=[tmr, bank_free[7]], inc="pe")
                last_ap = None
                for dm in range(CD):
                    pp = dm % 2
                    t_f = B_Mf[:, 2048 + 512 * pp: 2560 + 512 * pp]
                    xln_f = B_Mf[:, 3072 + 512 * pp: 3584 + 512 * pp]
                    def ap1(e, dm=dm, t_f=t_f):
                        return e.tensor_tensor(t_f, s1ring[:, 512 * dm: 512 * (dm + 1)],
                                               PS[:, 6 * 512: 7 * 512], ALU.mult)
                    ta1 = b.emit("dve", ap1, waits=[tb6, prev_ap1[pp]], inc="dve")
                    def ap2(e, t_f=t_f, xln_f=xln_f):
                        return e.tensor_tensor(xln_f, t_f, PS[:, 7 * 512: 8 * 512],
                                               ALU.subtract)
                    ta2 = b.emit("dve", ap2, waits=[tb7, prev_xlndma[pp]], inc="dve")
                    prev_xlndma[pp] = dma("sp", d_xln[128 * dm: 128 * (dm + 1),
                                          L * j + 512 * t4: L * j + 512 * (t4 + 1)],
                                          xln_f, f"sxln{pp}", waits=[ta2])
                    first = (j == 0 and t4 == 0 and dm == 0)
                    col = 13 if first else 14
                    def redx(e, col=col, xln_f=xln_f):
                        return e.tensor_reduce(V_S[:, col:col + 1], xln_f,
                                               AX.X, ALU.max, apply_absolute_value=True)
                    trd = b.emit("dve", redx, inc="dve")
                    prev_ap1[pp] = trd
                    if not first:
                        b.emit("dve", lambda e: e.tensor_tensor(V_S[:, 13:14], V_S[:, 13:14],
                                                                V_S[:, 14:15], ALU.max),
                               waits=[trd], inc="dve")
                    last_ap = trd
                bank_free[6] = last_ap
                bank_free[7] = last_ap
            tok_oproj_done = b.tok("pe")

        if dbg == "xln":
            tok = dma("sp", d_out, d_xln, "st", waits=[b.tok("sxln0"), b.tok("sxln1"), b.tok("dve")])
            b.emit("sp", lambda e: e.nop(), waits=[tok])
            b.finalize(block)
            return nc

        C_I32 = C_T.ap().bitcast(F32)[0:32, 0:1024]
        tok_i32 = dma("sp", C_I32, i_ind32, "ldc", waits=[b.tok("dve"), b.tok("act"), tok_consts])
        SN = {0: B_Mf[:, 2048:2816], 1: B_Mf[:, 2816:3584], 2: B_Mf[:, 3584:4352]}
        tok_vfree = b.tok("pe")
        tok_vfree_a = b.tok("act")
        mats = ((d_wupN, CI, 1), (d_wgN, CI, 1), (d_wdN, CD, 3))
        prev_nst = [None] * 4
        sn_tok = {}
        qrot = ("sp", "act", "pool")
        ldi = 0
        for mi, (dN, nt, nw) in enumerate(mats):
            tr_ = None
            for ot in range(nt):
                for w in range(nw):
                    sl = ldi % 4
                    nst = B_Vf[:, 1024 * sl: 1024 * sl + 1024]
                    tok_n = dma(qrot[ldi % 3], nst,
                                dN[128 * ot: 128 * (ot + 1), 1024 * w: 1024 * (w + 1)],
                                f"lsn{sl}", waits=[tok_vfree, tok_vfree_a, prev_nst[sl]])
                    ldi += 1
                    nb0 = (32 * ot) if nw == 1 else (96 * ot + 32 * w)
                    def redn(e, nst=nst, nb0=nb0, mi=mi):
                        return e.tensor_reduce(
                            SN[mi][:, nb0: nb0 + 32],
                            nst.rearrange("p (b w) -> p b w", w=QB),
                            AX.X, ALU.max, apply_absolute_value=True)
                    tr_ = b.emit("dve", redn, waits=[tok_n], inc="dve")
                    prev_nst[sl] = tr_
            ncols = 32 * nt * nw
            sn_tok[mi] = b.emit("dve", lambda e, mi=mi, ncols=ncols: e.tensor_scalar(
                SN[mi][:, 0:ncols], SN[mi][:, 0:ncols], 1.0 / 7.0, 1e-12,
                ALU.mult, ALU.max), waits=[tr_], inc="dve")
        tok_scales = sn_tok[2]

        def jit_scale_tiles(mi, src_slice, w_extra, par=0):
            pc = 4 * 512 + 512 * par
            sT_st = B_Vf[0:32, 6272 + 256 * par: 6400 + 256 * par]
            rT_st = B_Vf[0:32, 6400 + 256 * par: 6528 + 256 * par]
            wtr = [sn_tok[mi], bank_free[4 + par]] + w_extra
            ttr2 = b.emit("pe", lambda e: e.transpose(
                PS[0:32, pc + 256: pc + 384], src_slice, C_EYE[:, :]),
                waits=wtr, inc="pe")
            tcp2 = b.emit("act", lambda e: e.copy(sT_st, PS[0:32, pc + 256: pc + 384]),
                          waits=[ttr2], inc="act")
            trc2 = b.emit("dve", lambda e: e.reciprocal(rT_st, sT_st),
                          waits=[tcp2], inc="dve")
            bank_free[4 + par] = tcp2
            return sT_st, rT_st, trc2

        def barrier(idx, cols, after_toks):
            ctoks = []
            prev_gst = b.tok("gci")
            prev_gmx = b.tok("dve")
            tok_ccz = b.emit("dve", lambda e: e.memset(CCST, 0.0),
                             waits=after_toks + [prev_gst], inc="dve")
            for si, col in enumerate(cols):
                w = [bank_free[5]] + after_toks
                def trp(e, col=col):
                    return e.transpose(PS[0:1, 5 * 512: 5 * 512 + 128],
                                       V_S[:, col:col + 1], C_EYE[:, :])
                tt = b.emit("pe", trp, waits=w, inc="pe")
                def rd(e, si=si, idx=idx):
                    return e.tensor_reduce(CCST[0:1, si:si + 1],
                                           PS[0:1, 5 * 512: 5 * 512 + 128],
                                           AX.X, ALU.max)
                tr_ = b.emit("dve", rd, waits=[tt, tok_ccz], inc="dve")
                bank_free[5] = tr_
                ctoks.append(tr_)
            tin = dma("pool", cc_in[idx], CCST, "gci", waits=ctoks)
            tcc = b.emit("pool", lambda e, idx=idx: e.collective_compute(
                "AllReduce", ALU.max, replica_groups=[list(range(8))],
                ins=[cc_in[idx]], outs=[cc_out[idx]]), waits=[tin], inc="cc")
            tg = dma("pool", GMX, cc_out[idx], "gco", waits=[tcc, prev_gmx])
            return tg

        def bcast_consts(row_lo, n, vc_lo, wtok):
            w = [wtok, bank_free[5]]
            tb = b.emit("pe", lambda e: e.matmul(PS[:, 5 * 512: 5 * 512 + n], C_OR[:, :],
                                                 V_SC[0:1, row_lo: row_lo + n],
                                                 start=True, stop=True),
                        waits=w, inc="pe")
            tc = b.emit("act", lambda e: e.copy(V_C[:, vc_lo: vc_lo + n],
                                                PS[:, 5 * 512: 5 * 512 + n]),
                        waits=[tb], inc="act")
            bank_free[5] = tc
            return tc

        tg1 = barrier(0, [13], [b.tok("dve")])
        t_a = b.emit("dve", lambda e: e.tensor_scalar(V_SC[0:1, 0:1], GMX[:, 0:1],
                                                      1.0 / QMA, 1e-12, ALU.mult, ALU.max),
                     waits=[tg1], inc="dve")
        t_b = b.emit("dve", lambda e: e.reciprocal(V_SC[0:1, 1:2], V_SC[0:1, 0:1]),
                     waits=[t_a], inc="dve")
        tok_c1 = bcast_consts(0, 2, 32, t_b)

        def xq_slab(c, t0, n):
            if c < 4:
                return B_Q[:, 4096 * c + t0: 4096 * c + t0 + n]
            return B_K[:, 4096 * (c - 4) + t0: 4096 * (c - 4) + t0 + n]

        xq_tok = None
        prev_ld = [None] * 4
        tok_ln1_done = (b.tok("dve"), b.tok("pe"))
        for c in range(CD):
            for t2 in range(2):
                sl = (c * 2 + t2) % 4
                stage = B_Wf[:, 2048 * sl: 2048 * sl + 2048]
                tok_l = dma("sp" if sl % 2 == 0 else "act", stage,
                            d_xln[128 * c: 128 * (c + 1), 2048 * t2: 2048 * (t2 + 1)],
                            f"lxq{sl}", waits=[prev_ld[sl], tok_scales, b.tok("sxln0"), b.tok("sxln1"), b.tok("gw0"), b.tok("gw1"), b.tok("gx"),
                                          tok_ln1_done[0], tok_ln1_done[1]])
                def q1(e, stage=stage):
                    return e.tensor_scalar(stage, stage, V_C[:, 33:34], CMAG,
                                           ALU.mult, ALU.add)
                tq1 = b.emit("dve", q1, waits=[tok_l, tok_c1], inc="dve")
                def q2(e, stage=stage, c=c, t2=t2):
                    return e.tensor_scalar(xq_slab(c, 2048 * t2, 2048), stage,
                                           -CMAG, V_C[:, 32:33], ALU.add, ALU.mult)
                xq_tok = b.emit("dve", q2, waits=[tq1], inc="dve")
                prev_ld[sl] = xq_tok

        if dbg == "xq":
            tdl = None
            for c in range(CD):
                buf = B_Q if c < 4 else B_K
                off = 4096 * (c % 4)
                tdl = dma("pool", d_out[128 * c: 128 * (c + 1), :],
                          buf[:, off: off + 4096], "gst", waits=[xq_tok])
            b.emit("sp", lambda e: e.nop(), waits=[tdl])
            b.finalize(block)
            return nc

        dq_base = 4096
        sp_base = 8192
        run_up, run_gate = 1, 2
        first_red = {1: True, 2: True}
        prev_w = [None, None]
        prev_dq = [None, None]
        prev_sp = [None] * 4
        prev_rd = [None] * 4
        spill_n = 0
        gtoks = {}
        tok_dq_done = {}
        a_last_mm = None
        tiles = [(it, mi) for it in range(A_LIMIT) for mi in range(M_LIMIT)]
        tinfo = {}
        bank6_free = [None]

        def a_wload(X):
            it, mi = tiles[X]
            dT = d_wup if mi == 0 else d_wg
            sl = X % 2
            wst = B_Vf[:, 1024 * sl: 1024 * sl + 1024]
            tok_w = dma("sp", wst.rearrange("p (c n) -> p c n", c=CD),
                        dT[:, 128 * it: 128 * (it + 1)].rearrange(
                            "(c p) n -> p c n", p=128),
                        f"lwa{sl}", waits=[prev_w[sl], xq_tok])
            tinfo[X] = {"sl": sl, "wst": wst, "tok_w": tok_w, "dq": []}

        def a_jit(X):
            it, mi = tiles[X]
            info = tinfo[X]
            tp_ = X % 2
            pc6 = 6 * 512 + 128 * tp_
            srT = B_Vf[0:32, 6272 + 256 * tp_: 6528 + 256 * tp_]
            src = SN[mi][:, 32 * it: 32 * (it + 1)]
            ttr2 = b.emit("pe", lambda e, src=src, pc6=pc6: e.transpose(
                PS[0:32, pc6: pc6 + 128], src, C_EYE[:, :]),
                waits=[sn_tok[mi], tok_i32, bank6_free[0]], inc="pe")
            tcp2 = b.emit("act", lambda e, srT=srT, pc6=pc6: e.copy(
                srT[0:32, 0:128], PS[0:32, pc6: pc6 + 128]), waits=[ttr2], inc="act")
            trc2 = b.emit("dve", lambda e, srT=srT: e.reciprocal(
                srT[0:32, 128:256], srT[0:32, 0:128]), waits=[tcp2], inc="dve")
            bank6_free[0] = tcp2
            info["jit"] = trc2
            info["srT"] = srT

        def a_bs(X, c):
            info = tinfo[X]
            sl = info["sl"]
            bpar = c % 2
            pc = 4 * 512 + 512 * bpar
            scr = B_Vf[:, 6784 + 128 * bpar: 6912 + 128 * bpar]
            def bs1(e, c=c, srT=info["srT"], pc=pc):
                return e.matmul(PS[:, pc: pc + 256],
                                C_I32[:, 128 * c: 128 * (c + 1)],
                                srT, start=True, stop=True)
            tb2 = b.emit("pe", bs1, waits=[info["jit"], bank_free[4 + bpar]], inc="pe")
            def dq1(e, wst=info["wst"], c=c, scr=scr, pc=pc):
                return e.tensor_tensor(scr, wst[:, 128 * c: 128 * (c + 1)],
                                       PS[:, pc + 128: pc + 256], ALU.mult)
            td1 = b.emit("dve", dq1, waits=[tb2, info["tok_w"]], inc="dve")
            td2 = b.emit("dve", lambda e, scr=scr: e.tensor_scalar(
                scr, scr, CMAG, -CMAG, ALU.add, ALU.add), inc="dve")
            def dq3(e, c=c, sl=sl, scr=scr, pc=pc):
                return e.tensor_tensor(B_V[:, dq_base + 2048 * sl + 128 * c:
                                            dq_base + 2048 * sl + 128 * (c + 1)],
                                       scr, PS[:, pc: pc + 128], ALU.mult)
            td3 = b.emit("dve", dq3, inc="dve")
            bank_free[4 + bpar] = td3
            info["dq"].append(td3)
            if c == CD - 1:
                prev_w[sl] = td3

        a_wload(0)
        a_jit(0)
        for c in range(CD):
            a_bs(0, c)
        spill_n = 0
        for X in range(len(tiles)):
            it, mi = tiles[X]
            info = tinfo[X]
            sl = info["sl"]
            ddst = d_up if mi == 0 else d_gate
            if X + 1 < len(tiles):
                a_wload(X + 1)
                a_jit(X + 1)
            for tp in range(4):
                pb = 0 if (X * 4 + tp) % 2 == 0 else 2
                wa = [info["dq"][-1], bank_free[pb], bank_free[pb + 1]]
                for c in range(CD):
                    for half in range(2):
                        def mma(e, pb=pb, c=c, half=half, sl=sl, tp=tp):
                            return e.matmul(
                                PS[:, 512 * (pb + half): 512 * (pb + half) + 512],
                                B_V[:, dq_base + 2048 * sl + 128 * c:
                                    dq_base + 2048 * sl + 128 * (c + 1)],
                                xq_slab(c, 1024 * tp + 512 * half, 512),
                                start=(c == 0), stop=(c == CD - 1))
                        a_last_mm = b.emit("pe", mma, waits=wa if (c == 0 and half == 0) else (),
                                           inc="pe" if (c == CD - 1 and half == 1) else None)
                if X + 1 < len(tiles):
                    a_bs(X + 1, 2 * tp)
                    a_bs(X + 1, 2 * tp + 1)
                ssl = spill_n % 4
                spill_n += 1
                spst = B_V[:, sp_base + 1024 * ssl: sp_base + 1024 * (ssl + 1)]
                def eva(e, pb=pb, spst=spst):
                    return e.copy(spst, PS[:, 512 * pb: 512 * pb + 1024])
                tev = b.emit("act", eva, waits=[a_last_mm, prev_sp[ssl], prev_rd[ssl]],
                             inc="act")
                run = run_up if mi == 0 else run_gate
                if first_red[run]:
                    def redA(e, spst=spst, run=run):
                        return e.tensor_reduce(V_S[:, run:run + 1], spst,
                                               AX.X, ALU.max, apply_absolute_value=True)
                    trd = b.emit("dve", redA, waits=[tev], inc="dve")
                    first_red[run] = False
                else:
                    def redA(e, spst=spst):
                        return e.tensor_reduce(V_S[:, 15:16], spst,
                                               AX.X, ALU.max, apply_absolute_value=True)
                    trd0 = b.emit("dve", redA, waits=[tev], inc="dve")
                    def acc(e, run=run):
                        return e.tensor_tensor(V_S[:, run:run + 1], V_S[:, run:run + 1],
                                               V_S[:, 15:16], ALU.max)
                    trd = b.emit("dve", acc, waits=[trd0], inc="dve")
                prev_rd[ssl] = trd
                bank_free[pb] = tev
                bank_free[pb + 1] = tev
                prev_sp[ssl] = dma("sp", ddst[128 * it: 128 * (it + 1),
                                            1024 * tp: 1024 * (tp + 1)],
                                   spst, f"ssp{ssl}", waits=[tev])

        if dbg == "ffnA":
            dob = d_out.bitcast(BF16)
            t1d = dma("sp", dob[0:512, 0:T], d_up[0:512, :], "st",
                      waits=[b.tok(f"ssp{0}"), b.tok("ssp1"), b.tok("ssp2"), b.tok("ssp3"), b.tok("dve")])
            t2d = dma("sp", dob[512:1024, 0:T], d_gate[0:512, :], "st",
                      waits=[b.tok(f"ssp{0}"), b.tok("ssp1"), b.tok("ssp2"), b.tok("ssp3"), b.tok("dve")])
            b.emit("sp", lambda e: e.nop(), waits=[t1d, t2d])
            b.finalize(block)
            return nc

        tg2 = barrier(1, [run_up, run_gate], [b.tok("dve"), b.tok("act")])
        tq1 = b.emit("dve", lambda e: e.tensor_scalar(V_SC[0:1, 2:3], GMX[:, 0:1],
                                                1.0 / QMA, 1e-12, ALU.mult, ALU.max),
               waits=[tg2], inc="dve")
        tq2 = b.emit("dve", lambda e: e.reciprocal(V_SC[0:1, 3:4], V_SC[0:1, 2:3]),
                     waits=[tq1], inc="dve")
        tq3 = b.emit("dve", lambda e: e.tensor_scalar(V_SC[0:1, 4:5], GMX[:, 1:2],
                                                1.0 / QMA, 1e-12, ALU.mult, ALU.max),
               waits=[tq2], inc="dve")
        tq4 = b.emit("dve", lambda e: e.reciprocal(V_SC[0:1, 5:6], V_SC[0:1, 4:5]),
                     waits=[tq3], inc="dve")
        t_ct = b.emit("dve", lambda e: e.tensor_single_scalar(V_SC[0:1, 6:7], V_SC[0:1, 4:5],
                                                       float(2 ** -16), ALU.mult),
                      waits=[tq4], inc="dve")
        tok_c2 = bcast_consts(2, 5, 34, t_ct)

        def ddq_slab(it, dm):
            base = 1024 * it + 128 * dm
            if it < 16:
                return B_Q[:, base: base + 128]
            return B_K[:, base - 16384: base - 16384 + 128]

        def ddq_tile(it):
            if it < 16:
                return B_Q[:, 1024 * it: 1024 * (it + 1)]
            return B_K[:, 1024 * (it - 16): 1024 * (it - 15)]

        dscr = B_Vf[:, 4352:5376]
        prev_wd = [b.tok("dve"), b.tok("dve")]
        prev_dd = None
        sb_prev = [tok_c2, tok_c2]
        tr_ev = {}
        ddq_last = None
        for g in range(3):
            sTg = B_Vf[0:32, 5376 + 0: 5376 + 1024]
            rTg = B_Vf[0:32, 6400 + 0: 6400 + 1024]
            ttr = None
            for dm in range(CD):
                def trg(e, dm=dm, g=g):
                    return e.transpose(
                        PS[0:32, 6 * 512 + 128 * dm: 6 * 512 + 128 * (dm + 1)],
                        SN[2][:, 96 * dm + 32 * g: 96 * dm + 32 * g + 32],
                        C_EYE[:, :])
                w0 = [tok_scales, sb_prev[1], prev_dd]
                if g == 0 and dm == 0:
                    w0 += [b.tok(f"ssp{i}") for i in range(4)]
                ttr = b.emit("pe", trg,
                             waits=w0 if dm == 0 else (),
                             inc="pe" if dm == CD - 1 else None)
            tev = b.emit("act", lambda e, sTg=sTg: e.copy(sTg, PS[0:32, 6 * 512: 6 * 512 + 1024]),
                         waits=[ttr], inc="act")
            trc = b.emit("dve", lambda e, sTg=sTg, rTg=rTg: e.reciprocal(rTg, sTg),
                         waits=[tev], inc="dve")
            for li in range(8):
                it = 8 * g + li
                sl = it % 2
                crow = it % 8
                wst = B_Vf[:, 1024 * sl: 1024 * sl + 1024]
                tok_w = dma("sp", wst, d_wd[128 * it: 128 * (it + 1), :],
                            f"lwd{sl}", waits=[prev_wd[sl]])
                pbase = 0 if it % 2 == 0 else 4
                s_b = PS[:, 512 * pbase: 512 * pbase + 1024]
                r_b = PS[:, 512 * pbase + 1024: 512 * pbase + 2048]
                wmm = [trc, sb_prev[0 if pbase == 0 else 1]]
                tmm = None
                for h2 in range(2):
                    def bs(e, crow=crow, h2=h2, pbase=pbase, sTg=sTg):
                        return e.matmul(PS[:, 512 * pbase + 512 * h2: 512 * pbase + 512 * (h2 + 1)],
                                        C_I32[:, 128 * crow: 128 * (crow + 1)],
                                        sTg[0:32, 512 * h2: 512 * (h2 + 1)],
                                        start=True, stop=True)
                    tmm = b.emit("pe", bs, waits=wmm if h2 == 0 else (), inc="pe")
                for h2 in range(2):
                    def br(e, crow=crow, h2=h2, pbase=pbase, rTg=rTg):
                        return e.matmul(PS[:, 512 * pbase + 1024 + 512 * h2: 512 * pbase + 1024 + 512 * (h2 + 1)],
                                        C_I32[:, 128 * crow: 128 * (crow + 1)],
                                        rTg[0:32, 512 * h2: 512 * (h2 + 1)],
                                        start=True, stop=True)
                    tmm = b.emit("pe", br, inc="pe")
                d1 = b.emit("dve", lambda e, wst=wst, r_b=r_b: e.tensor_tensor(
                    dscr, wst, r_b, ALU.mult), waits=[tmm, tok_w], inc="dve")
                d2 = b.emit("dve", lambda e: e.tensor_scalar(
                    dscr, dscr, CMAG, -CMAG, ALU.add, ALU.add), inc="dve")
                d3 = b.emit("dve", lambda e, it=it, s_b=s_b: e.tensor_tensor(
                    ddq_tile(it), dscr, s_b, ALU.mult), inc="dve")
                sb_prev[0 if pbase == 0 else 1] = d3
                prev_wd[sl] = d1
                prev_dd = d3
                ddq_last = d3
        for _bi in range(8):
            bank_free[_bi] = sb_prev[0] if _bi < 4 else sb_prev[1]

        B_Xf = B_X.ap().bitcast(F32)
        prev_g = [None] * 4
        prev_fa = [None] * 4
        prev_fb = [None] * 4
        prev_fc = [None] * 4
        prev_tst = [None] * 4
        tmax_tok = None
        NTB = I // 128 * (T // 1024)
        for k in range(NTB):
            it, th = k // 4, k % 4
            sst = k % 4
            gst = B_W[:, 1024 * sst: 1024 * (sst + 1)]
            tst = B_M[:, 1024 * sst: 1024 * (sst + 1)]
            fA = B_Wf[:, 4096 + 1024 * sst: 4096 + 1024 * sst + 1024]
            fB = B_Xf[:, 1024 * sst: 1024 * sst + 1024]
            fC = B_Xf[:, 4096 + 1024 * sst: 4096 + 1024 * sst + 1024]
            tok_g = dma("sp", gst, d_gate[128 * it: 128 * (it + 1),
                                          1024 * th: 1024 * (th + 1)],
                        f"lgb{sst}",
                        waits=[prev_g[sst]] + ([b.tok(f"ssp{i}") for i in range(4)] if k == 0 else []))
            a1 = b.emit("act", lambda e, gst=gst, fA=fA: e.activation(
                fA, gst, AF.Copy, bias=CMAG, scale=V_C[:, 37:38]),
                waits=[tok_g, tok_c2, prev_fa[sst]], inc="act")
            v1 = b.emit("dve", lambda e, fA=fA, fB=fB: e.tensor_single_scalar(
                fB, fA, -CMAG, ALU.add), waits=[a1, prev_fb[sst]], inc="dve")
            a3 = b.emit("act", lambda e, fB=fB, fC=fC: e.activation(
                fC, fB, AF.Sigmoid, scale=V_C[:, 36:37]),
                waits=[v1, prev_fc[sst]], inc="act")
            v2 = b.emit("dve", lambda e, fC=fC, fA=fA: e.tensor_scalar(
                fA, fC, 65536.0, CMAG, ALU.mult, ALU.add), waits=[a3], inc="dve")
            v3 = b.emit("dve", lambda e, fA=fA: e.tensor_scalar(
                fA, fA, -CMAG, 65535.0, ALU.add, ALU.min), inc="dve")
            v4 = b.emit("dve", lambda e, fA=fA, fB=fB, tst=tst: e.tensor_tensor(
                tst, fB, fA, ALU.mult), waits=[prev_tst[sst]], inc="dve")
            if tmax_tok is None:
                v5 = b.emit("dve", lambda e, tst=tst: e.tensor_reduce(
                    V_S[:, 3:4], tst, AX.X, ALU.max, apply_absolute_value=True), inc="dve")
            else:
                tv5 = b.emit("dve", lambda e, tst=tst: e.tensor_reduce(
                    V_S[:, 15:16], tst, AX.X, ALU.max, apply_absolute_value=True), inc="dve")
                v5 = b.emit("dve", lambda e: e.tensor_tensor(
                    V_S[:, 3:4], V_S[:, 3:4], V_S[:, 15:16], ALU.max),
                    waits=[tv5], inc="dve")
            tmax_tok = v5
            prev_g[sst] = a1
            prev_fa[sst] = v4
            prev_fb[sst] = v4
            prev_fc[sst] = v2
            prev_tst[sst] = dma("pool", d_t[128 * it: 128 * (it + 1),
                                            1024 * th: 1024 * (th + 1)],
                                tst, f"stb{sst}", waits=[v5])

        tg3 = barrier(2, [3], [b.tok("dve")])
        tr1 = b.emit("dve", lambda e: e.tensor_tensor(V_SC[0:1, 7:8], GMX[:, 0:1],
                                                V_SC[0:1, 6:7], ALU.mult),
               waits=[tg3], inc="dve")
        tr2 = b.emit("dve", lambda e: e.tensor_scalar(V_SC[0:1, 8:9], V_SC[0:1, 7:8],
                                                1.0 / QMA, 1e-12, ALU.mult, ALU.max),
               waits=[tr1], inc="dve")
        t_rt = b.emit("dve", lambda e: e.reciprocal(V_SC[0:1, 9:10], V_SC[0:1, 8:9]),
                      waits=[tr2], inc="dve")
        t_rtc = b.emit("dve", lambda e: e.tensor_tensor(V_SC[0:1, 16:17], V_SC[0:1, 9:10],
                                                        V_SC[0:1, 6:7], ALU.mult),
                       waits=[t_rt], inc="dve")
        tok_c3 = bcast_consts(16, 1, 39, t_rtc)

        prev_tl = [None] * 4
        prev_ul = [None] * 4
        prev_fa = [None] * 4
        prev_fb = [None] * 4
        prev_pst = [None] * 4
        pmax_tok = None
        for k in range(NTB):
            it, th = k // 4, k % 4
            sst = k % 4
            tstg = B_W[:, 1024 * sst: 1024 * (sst + 1)]
            ustg = B_W[:, 4096 + 1024 * sst: 4096 + 1024 * (sst + 1)]
            fA = B_Wf[:, 4096 + 1024 * sst: 4096 + 1024 * sst + 1024]
            fB = B_Xf[:, 1024 * sst: 1024 * sst + 1024]
            pst = B_M[:, 1024 * sst: 1024 * (sst + 1)]
            tok_t = dma("sp", tstg, d_t[128 * it: 128 * (it + 1),
                                        1024 * th: 1024 * (th + 1)],
                        f"ltc{sst}",
                        waits=[prev_tl[sst]] + ([b.tok(f"stb{i}") for i in range(4)] if k == 0 else []))
            tok_u = dma("act", ustg, d_up[128 * it: 128 * (it + 1),
                                          1024 * th: 1024 * (th + 1)],
                        f"luc{sst}", waits=[prev_ul[sst]])
            a1 = b.emit("act", lambda e, tstg=tstg, fA=fA: e.activation(
                fA, tstg, AF.Copy, bias=CMAG, scale=V_C[:, 39:40]),
                waits=[tok_t, tok_c3, prev_fa[sst]], inc="act")
            v1 = b.emit("dve", lambda e, fA=fA: e.tensor_single_scalar(
                fA, fA, -CMAG, ALU.add), waits=[a1], inc="dve")
            a2 = b.emit("act", lambda e, ustg=ustg, fB=fB: e.activation(
                fB, ustg, AF.Copy, bias=CMAG, scale=V_C[:, 35:36]),
                waits=[tok_u, tok_c3, prev_fb[sst]], inc="act")
            v2 = b.emit("dve", lambda e, fB=fB: e.tensor_single_scalar(
                fB, fB, -CMAG, ALU.add), waits=[a2], inc="dve")
            v3 = b.emit("dve", lambda e, fA=fA, fB=fB, pst=pst: e.tensor_tensor(
                pst, fA, fB, ALU.mult), waits=[prev_pst[sst]], inc="dve")
            if pmax_tok is None:
                v4 = b.emit("dve", lambda e, pst=pst: e.tensor_reduce(
                    V_S[:, 4:5], pst, AX.X, ALU.max, apply_absolute_value=True), inc="dve")
            else:
                tv4 = b.emit("dve", lambda e, pst=pst: e.tensor_reduce(
                    V_S[:, 15:16], pst, AX.X, ALU.max, apply_absolute_value=True), inc="dve")
                v4 = b.emit("dve", lambda e: e.tensor_tensor(
                    V_S[:, 4:5], V_S[:, 4:5], V_S[:, 15:16], ALU.max),
                    waits=[tv4], inc="dve")
            pmax_tok = v4
            prev_tl[sst] = a1
            prev_ul[sst] = a2
            prev_fa[sst] = v3
            prev_fb[sst] = v3
            prev_pst[sst] = dma("pool", d_P[128 * it: 128 * (it + 1),
                                            1024 * th: 1024 * (th + 1)],
                                pst, f"stp{sst}", waits=[v4])

        tg4 = barrier(3, [4], [b.tok("dve")])
        ts1_ = b.emit("dve", lambda e: e.tensor_tensor(V_SC[0:1, 10:11], V_SC[0:1, 8:9],
                                                V_SC[0:1, 2:3], ALU.mult),
               waits=[tg4], inc="dve")
        ts2_ = b.emit("dve", lambda e: e.tensor_tensor(V_SC[0:1, 11:12], V_SC[0:1, 10:11],
                                                GMX[:, 0:1], ALU.mult),
               waits=[ts1_], inc="dve")
        ts3_ = b.emit("dve", lambda e: e.tensor_scalar(V_SC[0:1, 12:13], V_SC[0:1, 11:12],
                                                1.0 / QMA, 1e-12, ALU.mult, ALU.max),
               waits=[ts2_], inc="dve")
        ts4_ = b.emit("dve", lambda e: e.reciprocal(V_SC[0:1, 13:14], V_SC[0:1, 12:13]),
               waits=[ts3_], inc="dve")
        t_co = b.emit("dve", lambda e: e.tensor_tensor(V_SC[0:1, 14:15], V_SC[0:1, 10:11],
                                                       V_SC[0:1, 13:14], ALU.mult),
                      waits=[ts4_], inc="dve")
        tok_c4 = bcast_consts(12, 3, 40, t_co)

        def oq_slab(set_, it2):
            if set_ == 0:
                return B_X[:, 512 * it2: 512 * (it2 + 1)]
            if it2 < 8:
                return B_X[:, 12288 + 512 * it2: 12288 + 512 * (it2 + 1)]
            return B_M[:, 512 * (it2 - 8): 512 * (it2 - 7)]

        s2ring = B_Wf[:, 0:4096]
        prev_ost = [None] * 4
        prev_fd = [None] * 4
        prev_oq = [None, None]
        prep_done = [None, None]
        prev_xr = [None] * 4
        prev_ap = [None, None]
        out_toks = [None, None]
        lsum_d = lsq_d = None

        def prep_oq(tc, extra_w=()):
            set_ = tc % 2
            last = None
            for it2 in range(CI):
                r = it2 % 4
                ost = B_W[:, 8192 + 512 * r: 8192 + 512 * (r + 1)]
                fD = B_Wf[:, 5120 + 512 * r: 5120 + 512 * (r + 1)]
                tl = dma("sp", ost, d_P[128 * it2: 128 * (it2 + 1),
                                        512 * tc: 512 * (tc + 1)],
                         f"lpd{r}", waits=[prev_ost[r]] + list(extra_w))
                pa = b.emit("act", lambda e, ost=ost, fD=fD: e.activation(
                    fD, ost, AF.Copy, bias=CMAG, scale=V_C[:, 42:43]),
                    waits=[tl, tok_c4, prev_fd[r]], inc="act")
                pv = b.emit("dve", lambda e, fD=fD, set_=set_, it2=it2: e.tensor_scalar(
                    oq_slab(set_, it2), fD, -CMAG, V_C[:, 40:41], ALU.add, ALU.mult),
                    waits=[pa, prev_oq[set_]], inc="dve")
                prev_ost[r] = pa
                prev_fd[r] = pv
                last = pv
            prep_done[set_] = last

        prep_oq(0, extra_w=[b.tok(f"stp{i}") for i in range(4)] + [b.tok("dve"), ddq_last])
        for tc in range(8):
            set_ = tc % 2
            ring_toks = []
            lsum_d = lsq_d = None
            prev_sq = [None, None]

            def emit_stats(dm, ts2, tsq):
                nonlocal lsum_d, lsq_d
                wst2 = [ts2] + ([bank_free[4], bank_free[5]] if dm == 0 else [])
                lsum_d = b.emit("pe", lambda e, dm=dm: e.matmul(
                    PS[0:1, 4 * 512: 4 * 512 + 512], V_C[:, 47:48],
                    s2ring[:, 512 * dm: 512 * (dm + 1)],
                    start=(dm == 0), stop=(dm == CD - 1)),
                    waits=wst2, inc="pe")
                lsq_d = b.emit("pe", lambda e, dm=dm, tsq=tsq: e.matmul(
                    PS[0:1, 5 * 512: 5 * 512 + 512], V_C[:, 47:48],
                    B_Vf[:, 4352 + 512 * (dm % 2): 4864 + 512 * (dm % 2)],
                    start=(dm == 0), stop=(dm == CD - 1)),
                    waits=[tsq], inc="pe")
                prev_sq[dm % 2] = lsq_d

            stats_pend = None
            for dm in range(CD):
                bki = dm % 4
                w = [ddq_last, prep_done[set_], bank_free[bki]]
                for it2 in range(CI):
                    def mmd(e, bki=bki, it2=it2, dm=dm, set_=set_):
                        return e.matmul(bank(bki), ddq_slab(it2, dm),
                                        oq_slab(set_, it2),
                                        start=(it2 == 0), stop=(it2 == CI - 1))
                    last = b.emit("pe", mmd, waits=w if it2 == 0 else (),
                                  inc="pe" if it2 == CI - 1 else None)
                if stats_pend is not None:
                    emit_stats(*stats_pend)
                xr = B_Vf[:, 512 * (dm % 4): 512 * (dm % 4) + 512]
                tok_xr = dma("act", xr,
                             d_xln[128 * dm: 128 * (dm + 1), 512 * tc: 512 * (tc + 1)],
                             f"lx2{dm % 4}",
                             waits=[prev_xr[dm % 4]] + ([ddq_last] if tc == 0 and dm < 4 else []))
                def adds2(e, bki=bki, dm=dm, xr=xr):
                    return e.tensor_tensor(s2ring[:, 512 * dm: 512 * (dm + 1)],
                                           bank(bki), xr, ALU.add)
                ts2 = b.emit("dve", adds2, waits=[last, tok_xr, t_ones], inc="dve")
                prev_xr[dm % 4] = ts2
                bank_free[bki] = ts2
                ring_toks.append(ts2)
                dsq = B_Vf[:, 4352 + 512 * (dm % 2): 4864 + 512 * (dm % 2)]
                tsq = b.emit("dve", lambda e, dm=dm, dsq=dsq: e.tensor_tensor(
                    dsq, s2ring[:, 512 * dm: 512 * (dm + 1)],
                    s2ring[:, 512 * dm: 512 * (dm + 1)], ALU.mult),
                    waits=[prev_sq[dm % 2]], inc="dve")
                prev_sq[dm % 2] = tsq
                stats_pend = (dm, ts2, tsq)
            emit_stats(*stats_pend)
            if tc < 7:
                prep_oq(tc + 1)
            trm = b.emit("dve", lambda e: e.tensor_single_scalar(V_R[0:1, 0:512],
                         PS[0:1, 4 * 512: 4 * 512 + 512], 1.0 / D, ALU.mult),
                         waits=[lsum_d], inc="dve")
            trq = b.emit("dve", lambda e: e.tensor_single_scalar(V_R[0:1, 512:1024],
                         PS[0:1, 5 * 512: 5 * 512 + 512], 1.0 / D, ALU.mult),
                         waits=[lsq_d], inc="dve")
            bank_free[4] = trm
            bank_free[5] = trq
            trm2 = b.emit("dve", lambda e: e.tensor_tensor(V_R[0:1, 1024:1536], V_R[0:1, 0:512],
                                                    V_R[0:1, 0:512], ALU.mult),
                          waits=[trm], inc="dve")
            tvar = b.emit("dve", lambda e: e.tensor_tensor(V_R[0:1, 512:1024],
                                                           V_R[0:1, 512:1024],
                                                           V_R[0:1, 1024:1536], ALU.subtract),
                          waits=[trq, trm2], inc="dve")
            tln = b.emit("act", lambda e: e.activation(V_R[0:1, 1024:1536],
                                                       V_R[0:1, 512:1024], AF.Ln, bias=CONSTV[0:1, 0:1]),
                         waits=[tvar], inc="act")
            trstd = b.emit("act", lambda e: e.activation(V_R[0:1, 1024:1536],
                                                         V_R[0:1, 1024:1536], AF.Exp,
                                                         scale=-0.5), inc="act")
            tmr = b.emit("dve", lambda e: e.tensor_tensor(V_R[0:1, 1536:2048],
                                                          V_R[0:1, 0:512],
                                                          V_R[0:1, 1024:1536], ALU.mult),
                         waits=[trstd], inc="dve")
            tb6 = b.emit("pe", lambda e: e.matmul(PS[:, 6 * 512: 7 * 512], C_OR[:, :],
                                                  V_R[0:1, 1024:1536], start=True, stop=True),
                         waits=[trstd, bank_free[6]], inc="pe")
            tb7 = b.emit("pe", lambda e: e.matmul(PS[:, 7 * 512: 8 * 512], C_OR[:, :],
                                                  V_R[0:1, 1536:2048], start=True, stop=True),
                         waits=[tmr, bank_free[7]], inc="pe")
            last_ap = None
            for dm in range(CD):
                pp = dm % 2
                ap1 = B_Vf[:, 2048 + 512 * pp: 2048 + 512 * (pp + 1)]
                ap2 = B_Vf[:, 3072 + 512 * pp: 3072 + 512 * (pp + 1)]
                ta1 = b.emit("dve", lambda e, dm=dm, ap1=ap1: e.tensor_tensor(
                    ap1, s2ring[:, 512 * dm: 512 * (dm + 1)],
                    PS[:, 6 * 512: 7 * 512], ALU.mult),
                    waits=[tb6, ring_toks[dm], prev_ap[pp]], inc="dve")
                ta2 = b.emit("dve", lambda e, ap1=ap1, ap2=ap2: e.tensor_tensor(
                    ap2, ap1, PS[:, 7 * 512: 8 * 512],
                    ALU.subtract), waits=[tb7, out_toks[pp]], inc="dve")
                out_toks[pp] = dma("pool", d_out[128 * dm: 128 * (dm + 1),
                                                 512 * tc: 512 * (tc + 1)],
                                   ap2, f"sout{pp}", waits=[ta2])
                prev_ap[pp] = ta2
                last_ap = ta2
            bank_free[6] = last_ap
            bank_free[7] = last_ap

        b.emit("sp", lambda e: e.nop(), waits=[out_toks[0], out_toks[1]])
        b.finalize(block)
        return nc


def _prep_core_inputs(inputs, c):
    x = np.asarray(inputs["L0_attn_x"])
    Wq = np.asarray(inputs["L0_attn_W_q"])
    Wk = np.asarray(inputs["L0_attn_W_k"])
    Wv = np.asarray(inputs["L0_attn_W_v"])
    Wo = np.asarray(inputs["L0_attn_W_o"])
    wg = np.asarray(inputs["L0_ffn_w_gate"])
    wu = np.asarray(inputs["L0_ffn_w_up"])
    wd = np.asarray(inputs["L0_ffn_w_down"])
    perm = np.zeros(D, np.int64)
    for h in range(H):
        perm[DH * h: DH * h + 32] = DH * h + 2 * np.arange(32)
        perm[DH * h + 32: DH * h + 64] = DH * h + 2 * np.arange(32) + 1
    xc = x[2 * c: 2 * c + 2].reshape(T, D)
    return {
        "xT": np.ascontiguousarray(xc.T),
        "wqT": np.ascontiguousarray(Wq[perm].T),
        "wkT": np.ascontiguousarray(Wk[perm].T),
        "wvT": np.ascontiguousarray(Wv.T),
        "woT": np.ascontiguousarray(Wo.T),
        "wupT": np.ascontiguousarray(wu.T),
        "wgT": np.ascontiguousarray(wg.T),
        "wdT": np.ascontiguousarray(wd.T),
        "wupN": np.ascontiguousarray(wu),
        "wgN": np.ascontiguousarray(wg),
        "wdN": np.ascontiguousarray(wd),
    }


def kernel(**inputs):
    nc = build_program(DBG_STAGE)
    in_maps = [_prep_core_inputs(inputs, c) for c in range(8)]
    res = run_bass_kernel_spmd(nc, in_maps, core_ids=list(range(8)))
    outs = []
    for c in range(8):
        o = np.asarray(res.results[c]["outT"])
        outs.append(o.T.reshape(2, L, D))
    return np.concatenate(outs, axis=0).astype(np.float32)


# version 06
# speedup vs baseline: 1.3905x; 1.0255x over previous
import math
import numpy as np
import ml_dtypes
import concourse.bass as bass
import concourse.mybir as mybir
from concourse.bass_utils import run_bass_kernel_spmd

F32 = mybir.dt.float32
BF16 = mybir.dt.bfloat16
AF = mybir.ActivationFunctionType
ALU = mybir.AluOpType
AX = mybir.AxisListType

D, H, DH, L, NSEQ = 1024, 16, 64, 2048, 2
T = NSEQ * L
I = 3072
QB = 32
CD = D // 128
CI = I // 128
CMAG = float(3 * 2 ** 22)
QMA = 32767.0
EPS = 1e-5
ROPE_BASE = 10000.0
SAFE = 1.02 / 8.0

ENGS = ("pe", "act", "dve", "pool", "sp")
A_LIMIT = CI
M_LIMIT = 2
DBG_STAGE = None


class KB:
    def __init__(self, nc):
        self.nc = nc
        self.ops = {e: [] for e in ENGS}
        self.sems = {}
        self.cnt = {}
        self.waited = {e: {} for e in ENGS}

    def sem(self, name):
        if name not in self.sems:
            self.sems[name] = self.nc.alloc_semaphore(name)
            self.cnt[name] = 0
        return self.sems[name]

    def __init_dma(self):
        if not hasattr(self, "dma_sems"):
            self.dma_sems = set()

    def emit(self, eng, fn, waits=(), inc=None, inc_k=1, chain=True):
        self.__init_dma()
        if chain and eng in ("dve", "act", "pool"):
            waits = list(waits) + [self.tok(eng)]
        real = []
        for tokv in waits:
            if tokv is None:
                continue
            s, v = tokv
            if s in self.dma_sems:
                v = self.cnt[s]
            assert s in self.cnt and v <= self.cnt[s], f"future token {s}:{v}"
            if v > self.waited[eng].get(s, 0):
                real.append((self.sems[s], v))
                self.waited[eng][s] = v
        tok = None
        semh = None
        if inc is not None:
            self.sem(inc)
            if inc_k == 16:
                self.dma_sems.add(inc)
            self.cnt[inc] += inc_k
            tok = (inc, self.cnt[inc])
            semh = self.sems[inc]

        def run(e, real=real, fn=fn, semh=semh, k=inc_k):
            for (sh, v) in real:
                e.wait_ge(sh, v)
            ins = fn(e)
            if semh is not None:
                ins.then_inc(semh, k)
        self.ops[eng].append(run)
        return tok

    def tok(self, name):
        n = self.cnt.get(name, 0)
        return (name, n) if n > 0 else None

    def finalize(self, block):
        mp = {"pe": block.tensor, "act": block.scalar, "dve": block.vector,
              "pool": block.gpsimd, "sp": block.sync}
        for e in ENGS:
            ops = self.ops[e]

            def body(eng, ops=ops):
                for f in ops:
                    f(eng)
            mp[e](body)


def _rope_tables():
    pos = np.arange(L, dtype=np.float32)
    idx = np.arange(0, DH, 2, dtype=np.float32)
    freqs = np.exp(np.float32(-math.log(ROPE_BASE)) * idx / np.float32(DH)).astype(np.float32)
    ang = pos[:, None] * freqs[None, :]
    sin = np.sin(ang).astype(np.float32)
    cos = np.cos(ang).astype(np.float32)
    C = np.zeros((128, L), np.float32)
    S = np.zeros((128, L), np.float32)
    for p in range(128):
        i = p % 32
        C[p] = cos[:, i]
        S[p] = sin[:, i] * (-1.0 if (p % 64) < 32 else 1.0)
    return C, S


def build_program(dbg=None):
    nc = bass.Bass("TRN2", target_bir_lowering=False, debug=False)

    d_x = nc.dram_tensor("xT", [D, T], F32, kind="ExternalInput").ap()
    d_wq = nc.dram_tensor("wqT", [D, D], F32, kind="ExternalInput").ap()
    d_wk = nc.dram_tensor("wkT", [D, D], F32, kind="ExternalInput").ap()
    d_wv = nc.dram_tensor("wvT", [D, D], F32, kind="ExternalInput").ap()
    d_wo = nc.dram_tensor("woT", [D, D], F32, kind="ExternalInput").ap()
    d_wup = nc.dram_tensor("wupT", [D, I], F32, kind="ExternalInput").ap()
    d_wg = nc.dram_tensor("wgT", [D, I], F32, kind="ExternalInput").ap()
    d_wd = nc.dram_tensor("wdT", [I, D], F32, kind="ExternalInput").ap()
    d_wupN = nc.dram_tensor("wupN", [I, D], F32, kind="ExternalInput").ap()
    d_wgN = nc.dram_tensor("wgN", [I, D], F32, kind="ExternalInput").ap()
    d_wdN = nc.dram_tensor("wdN", [D, I], F32, kind="ExternalInput").ap()
    d_out = nc.dram_tensor("outT", [D, T], F32, kind="ExternalOutput").ap()

    d_xln = nc.dram_tensor("xlnd", [D, T], F32, kind="Internal").ap()
    d_up = nc.dram_tensor("upd", [I, T], BF16, kind="Internal").ap()
    d_gate = nc.dram_tensor("gated", [I, T], BF16, kind="Internal").ap()
    d_t = nc.dram_tensor("td", [I, T], BF16, kind="Internal").ap()
    d_P = nc.dram_tensor("Pd", [I, T], BF16, kind="Internal").ap()
    d_o = nc.dram_tensor("od", [I, T], BF16, kind="Internal").ap()
    d_srow = nc.dram_tensor("srowd", [1, H * L], F32, kind="Internal").ap()
    d_rrow = nc.dram_tensor("rrowd", [1, H * L], F32, kind="Internal").ap()
    cc_in = [nc.dram_tensor(f"ccin{i}", [1, 512], F32, kind="Internal").ap() for i in range(4)]
    cc_out = [nc.dram_tensor(f"ccout{i}", [1, 512], F32, kind="Internal").ap() for i in range(4)]

    Cnp, Snp = _rope_tables()
    i_tabC = nc.inline_tensor(Cnp.astype(ml_dtypes.bfloat16), "tabC").ap()
    i_tabS = nc.inline_tensor(Snp.astype(ml_dtypes.bfloat16), "tabS").ap()
    swap_np = np.zeros((128, 128), np.float32)
    for p in range(128):
        swap_np[p ^ 32, p] = 1.0
    i_swap = nc.inline_tensor(swap_np.astype(ml_dtypes.bfloat16), "swapM").ap()
    i_eyeF = nc.inline_tensor(np.eye(128, dtype=np.float32), "eyeF").ap()
    i_onescolB = nc.inline_tensor(np.ones((128, 1), ml_dtypes.bfloat16), "onescolB").ap()
    i_onesrowF = nc.inline_tensor(np.ones((1, 128), np.float32), "onesrowF").ap()
    constv_np = np.tile(np.array([[EPS, CMAG, -CMAG, 0.0]], np.float32), (128, 1))
    i_constv = nc.inline_tensor(constv_np, "constv").ap()
    ind_np = np.zeros((D, H), np.float32)
    for c in range(CD):
        ind_np[128 * c: 128 * c + 64, 2 * c] = 1.0
        ind_np[128 * c + 64: 128 * c + 128, 2 * c + 1] = 1.0
    i_ind = nc.inline_tensor(ind_np.astype(ml_dtypes.bfloat16), "indnq").ap()
    ind32_np = np.zeros((QB, CD * 128), np.float32)
    for c in range(CD):
        for p in range(128):
            ind32_np[4 * c + p // 32, 128 * c + p] = 1.0
    i_ind32 = nc.inline_tensor(ind32_np[:32].astype(np.float32), "ind32").ap()

    b = KB(nc)
    import contextlib
    with contextlib.ExitStack() as ctx:
        def sb(name, shape, dtype):
            return ctx.enter_context(nc.sbuf_tensor(name, shape, dtype))
        B_X = sb("B_X", [128, 16384], BF16)
        B_Q = sb("B_Q", [128, 16384], BF16)
        B_K = sb("B_K", [128, 16384], BF16)
        B_V = sb("B_V", [128, 16640], BF16)
        B_W = sb("B_W", [128, 16384], BF16)
        B_M = sb("B_M", [128, 12288], BF16)
        B_P = B_M
        V_S = sb("V_S", [128, 16], F32)
        V_C = sb("V_C", [128, 48], F32)
        V_R = sb("V_R", [1, 2048], F32)
        V_R2 = sb("V_R2", [1, 1024], F32)
        V_SC = sb("V_SC", [1, 32], F32)
        C_T = sb("C_T", [128, 2052], BF16)
        C_T2 = sb("C_T2", [128, 2048], BF16)
        C_SW = sb("C_SW", [128, 128], BF16)
        C_EYE = sb("C_EYE", [128, 128], F32)
        C_IND = sb("C_IND", [128, 128], BF16)
        C_OR = sb("C_OR", [1, 128], F32)
        CONSTV = sb("CONSTV", [128, 4], F32)
        nc.const_aps.register(F32, EPS, CONSTV[:, 0:1])
        nc.const_aps.register(F32, CMAG, CONSTV[:, 1:2])
        nc.const_aps.register(F32, -CMAG, CONSTV[:, 2:3])
        C_I32 = None
        PS = ctx.enter_context(nc.psum_tensor("PS", [128, 4096], F32))
        block = ctx.enter_context(nc.Block())

        B_Mf = B_M.ap().bitcast(F32)
        B_Wf = B_W.ap().bitcast(F32)
        B_Vf = B_V.ap().bitcast(F32)

        def bank(i, n=512):
            return PS[:, 512 * i: 512 * i + n]

        def dma(q, out_ap, in_ap, sem="ld", waits=()):
            return b.emit(q, lambda e: e.dma_start(out=out_ap, in_=in_ap),
                          waits=waits, inc=sem, inc_k=16)

        dma("sp", C_T[:, 0:2048], i_tabC, "ldc")
        dma("sp", C_T2[:, :], i_tabS, "ldc")
        dma("sp", C_T[:, 2048:2049], i_onescolB, "ldc")
        dma("sp", C_SW[:, :], i_swap, "ldc")
        dma("sp", C_EYE[:, :], i_eyeF, "ldc")
        dma("sp", C_IND.ap()[:, 0:128].rearrange("p (c h) -> p c h", c=CD),
            i_ind.rearrange("(c p) h -> p c h", p=128), "ldc")
        dma("sp", CONSTV[:, :], i_constv, "ldc")
        tok_consts = dma("sp", C_OR[:, :], i_onesrowF, "ldc")
        t_ones = b.emit("dve", lambda e: e.memset(V_C[:, 47:48], 1.0), inc="dve")
        CCST = V_R[0:1, 0:512]
        GMX = V_R[0:1, 512:1024]

        def load_xseq(j, waits):
            tk = None
            for c in range(CD):
                tk = dma("pool", B_X[:, 2048 * c: 2048 * (c + 1)],
                         d_x[128 * c: 128 * (c + 1), L * j: L * (j + 1)],
                         "gx", waits)
            return tk

        def load_w(dr, slot, waits):
            return dma("pool",
                       B_W.ap()[:, 8192 * slot: 8192 * (slot + 1)].rearrange(
                           "p (c n) -> p c n", c=CD),
                       dr.rearrange("(c p) n -> p c n", p=128), f"gw{slot}", waits)

        tok_x = load_xseq(0, ())
        tok_wq = load_w(d_wq, 0, ())
        tok_wk = load_w(d_wk, 1, ())

        bank_free = [None] * 8
        tok_oproj_done = None
        tok_vproj_done = None
        pv_free = [None, None]
        tok_dbg_last = None

        if dbg == "simA":
            tok_rope = None
            tok_oproj_done = b.tok("pe")
            tok_vproj_done = b.tok("pe")
            b.emit("dve", lambda e: e.memset(V_S[:, :], 1.0), inc="dve")
            b.emit("dve", lambda e: e.memset(B_Q[:, :], 0.0), inc="dve")
            tz = b.emit("dve", lambda e: e.memset(B_K[:, :], 0.0), inc="dve")
            for c_ in range(CD):
                dma("sp", d_xln[128 * c_: 128 * (c_ + 1), :],
                    B_Q.ap().bitcast(F32)[:, 0:T], "sxln", waits=[tz])
        for j in range([] if dbg == "simA" else range(NSEQ) and NSEQ or NSEQ) if False else range(0 if dbg == "simA" else NSEQ):
            if j == 1:
                tok_x = load_xseq(1, [tok_oproj_done])
                tok_wq = load_w(d_wq, 0, [tok_vproj_done, b.tok("dve")])
                tok_wk = load_w(d_wk, 1, [tok_oproj_done])

            def proj_qk(wtok, slot, dst):
                last_mm = None
                last_ev = None
                for cq in range(CD):
                    for t4 in range(4):
                        bki = (cq * 4 + t4) % 4
                        w = [wtok, tok_x, tok_consts, bank_free[bki]]
                        for c in range(CD):
                            def mm(e, bki=bki, c=c, cq=cq, t4=t4, slot=slot):
                                return e.matmul(
                                    bank(bki),
                                    B_W[:, 8192 * slot + 1024 * c + 128 * cq:
                                        8192 * slot + 1024 * c + 128 * cq + 128],
                                    B_X[:, 2048 * c + 512 * t4: 2048 * c + 512 * (t4 + 1)],
                                    start=(c == 0), stop=(c == CD - 1))
                            last_mm = b.emit("pe", mm, waits=w if c == 0 else (),
                                             inc="pe" if c == CD - 1 else None)
                        def ev(e, bki=bki, cq=cq, t4=t4, dst=dst):
                            return e.copy(dst[:, 2048 * cq + 512 * t4: 2048 * cq + 512 * (t4 + 1)],
                                          bank(bki))
                        last_ev = b.emit("act", ev, waits=[last_mm], inc="act")
                        bank_free[bki] = last_ev
                return last_ev, last_mm

            tok_qe, tok_qmm = proj_qk(tok_wq, 0, B_Q)
            tok_ke, tok_kmm = proj_qk(tok_wk, 1, B_K)
            tok_wv = load_w(d_wv, 0, [tok_qmm])
            if dbg == "qraw" and j == 0:
                tdl = None
                for c in range(CD):
                    tdl = dma("pool", d_out[128 * c: 128 * (c + 1), 0:L],
                              B_Q[:, 2048 * c: 2048 * (c + 1)], "gst", waits=[tok_qe])
                b.emit("sp", lambda e: e.nop(), waits=[tdl])
                b.finalize(block)
                return nc

            prev_r1 = None
            prev_r3 = None
            for (src, etok) in ((B_Q, tok_qe), (B_K, tok_ke)):
                for cq in range(CD):
                    sw_last = None
                    for t4 in range(4):
                        bki = 4 + (t4 % 2)
                        w = [etok, tok_consts, bank_free[bki], prev_r1]
                        def mmsw(e, bki=bki, src=src, cq=cq, t4=t4):
                            return e.matmul(bank(bki), C_SW[:, :],
                                            src[:, 2048 * cq + 512 * t4: 2048 * cq + 512 * (t4 + 1)],
                                            start=True, stop=True)
                        tmm = b.emit("pe", mmsw, waits=w, inc="pe")
                        def evsw(e, bki=bki, t4=t4):
                            return e.copy(B_M[:, 512 * t4: 512 * (t4 + 1)], bank(bki))
                        sw_last = b.emit("act", evsw, waits=[tmm, prev_r1], inc="act")
                        bank_free[bki] = sw_last
                    def r1(e, src=src, cq=cq):
                        return e.tensor_tensor(B_M[:, 2048:4096],
                                               src[:, 2048 * cq: 2048 * (cq + 1)],
                                               C_T[:, 0:2048], ALU.mult)
                    b.emit("dve", r1, waits=[sw_last], inc="dve")
                    def r2(e):
                        return e.tensor_tensor(B_M[:, 4096:6144], B_M[:, 0:2048],
                                               C_T2[:, :], ALU.mult)
                    prev_r1 = b.emit("dve", r2, inc="dve")
                    def r3(e, src=src, cq=cq):
                        return e.tensor_tensor(src[:, 2048 * cq: 2048 * (cq + 1)],
                                               B_M[:, 2048:4096], B_M[:, 4096:6144],
                                               ALU.add)
                    prev_r3 = b.emit("dve", r3, inc="dve")
            tok_rope = prev_r3

            def vones(e):
                v = B_V.ap().rearrange("p (s w) -> p s w", w=65)
                return e.memset(v[:, :, 64:65], 1.0)
            tok_vones = b.emit("pool", vones,
                               waits=[b.tok("pe") if j == 1 else None], inc="pool")
            vlast_ev = None
            for st in range(16):
                for d2 in range(2):
                    bki = (st * 2 + d2) % 4
                    w = [tok_wv, bank_free[bki]]
                    for c in range(CD):
                        def mmv(e, bki=bki, c=c, st=st, d2=d2):
                            return e.matmul(
                                bank(bki),
                                B_X[:, 2048 * c + 128 * st: 2048 * c + 128 * (st + 1)],
                                B_W[:, 1024 * c + 512 * d2: 1024 * c + 512 * (d2 + 1)],
                                start=(c == 0), stop=(c == CD - 1))
                        last = b.emit("pe", mmv, waits=w if c == 0 else (),
                                      inc="pe" if c == CD - 1 else None)
                    def evv(e, bki=bki, st=st, d2=d2):
                        dst = B_V.ap()[:, 1040 * st: 1040 * (st + 1)].rearrange(
                            "p (h w) -> p h w", w=65)[:, 8 * d2: 8 * (d2 + 1), 0:64]
                        src = bank(bki).rearrange("p (h w) -> p h w", w=64)
                        return e.copy(dst, src)
                    vlast_ev = b.emit("act", evv, waits=[last, tok_vones], inc="act")
                    bank_free[bki] = vlast_ev
            tok_vproj_done = b.tok("pe")
            tok_wo = load_w(d_wo, 1, [tok_kmm])

            prev_mmn = None
            nsc = B_W[:, 0:512]
            for (src, base_col) in ((B_Q, 0), (B_K, 4)):
                for t4 in range(4):
                    w6 = [bank_free[6]]
                    last = None
                    for cq in range(CD):
                        def sqr(e, src=src, cq=cq, t4=t4):
                            return e.tensor_tensor(
                                nsc,
                                src[:, 2048 * cq + 512 * t4: 2048 * cq + 512 * (t4 + 1)],
                                src[:, 2048 * cq + 512 * t4: 2048 * cq + 512 * (t4 + 1)],
                                ALU.mult)
                        tsq = b.emit("dve", sqr, waits=[tok_rope, prev_mmn, tok_vproj_done], inc="dve")
                        def mmn(e, cq=cq):
                            return e.matmul(PS[0:16, 6 * 512: 6 * 512 + 512],
                                            C_IND[:, 16 * cq: 16 * (cq + 1)],
                                            nsc,
                                            start=(cq == 0), stop=(cq == CD - 1))
                        last = b.emit("pe", mmn, waits=[tsq] + (w6 if cq == 0 else []),
                                      inc="pe")
                        prev_mmn = last
                    col = base_col + t4
                    def red(e, col=col):
                        return e.tensor_reduce(V_S[0:16, col:col + 1],
                                               PS[0:16, 6 * 512: 6 * 512 + 512],
                                               AX.X, ALU.max, apply_absolute_value=True)
                    rt = b.emit("dve", red, waits=[last], inc="dve")
                    if dbg == "ind2" and j == 0 and src is B_Q and t4 == 0:
                        rt = b.emit("dve", lambda e: e.tensor_copy(
                            B_Mf[:, 512:1024][0:16, 0:512],
                            PS[0:16, 6 * 512: 6 * 512 + 512]), inc="dve")
                    bank_free[6] = rt
            tn1 = b.emit("dve", lambda e: e.tensor_reduce(V_S[0:16, 8:9], V_S[0:16, 0:4],
                                                          AX.X, ALU.max), inc="dve")
            tn2 = b.emit("dve", lambda e: e.tensor_reduce(V_S[0:16, 9:10], V_S[0:16, 4:8],
                                                          AX.X, ALU.max),
                         waits=[tn1], inc="dve")
            t3_ = b.emit("dve", lambda e: e.tensor_tensor(V_S[0:16, 10:11], V_S[0:16, 8:9],
                                                          V_S[0:16, 9:10], ALU.mult),
                         waits=[tn2], inc="dve")
            t4_ = b.emit("act", lambda e: e.activation(V_S[0:16, 11:12], V_S[0:16, 10:11],
                                                       AF.Sqrt), waits=[t3_], inc="act")
            for t4 in range(4):
                w6 = [bank_free[6]]
                last = None
                for cq in range(CD):
                    def dgm(e, cq=cq, t4=t4):
                        return e.tensor_tensor(
                            nsc,
                            B_Q[:, 2048 * cq + 512 * t4: 2048 * cq + 512 * (t4 + 1)],
                            B_K[:, 2048 * cq + 512 * t4: 2048 * cq + 512 * (t4 + 1)],
                            ALU.mult)
                    tsq = b.emit("dve", dgm, waits=[prev_mmn, tn2], inc="dve")
                    def mmn2(e, cq=cq):
                        return e.matmul(PS[0:16, 6 * 512: 6 * 512 + 512],
                                        C_IND[:, 16 * cq: 16 * (cq + 1)],
                                        nsc, start=(cq == 0), stop=(cq == CD - 1))
                    last = b.emit("pe", mmn2, waits=[tsq] + (w6 if cq == 0 else []),
                                  inc="pe")
                    prev_mmn = last
                def red2(e, t4=t4):
                    return e.tensor_reduce(V_S[0:16, t4:t4 + 1],
                                           PS[0:16, 6 * 512: 6 * 512 + 512],
                                           AX.X, ALU.max)
                rt = b.emit("dve", red2, waits=[last], inc="dve")
                bank_free[6] = rt
            tdg = b.emit("dve", lambda e: e.tensor_reduce(V_S[0:16, 13:14], V_S[0:16, 0:4],
                                                          AX.X, ALU.max),
                         waits=[bank_free[6]], inc="dve")
            tnA = b.emit("dve", lambda e: e.tensor_single_scalar(V_S[0:16, 14:15],
                         V_S[0:16, 13:14], 0.125, ALU.mult), waits=[tdg], inc="dve")
            tnB = b.emit("dve", lambda e: e.tensor_scalar(V_S[0:16, 15:16],
                         V_S[0:16, 11:12], SAFE, -85.0, ALU.mult, ALU.add),
                         waits=[t4_], inc="dve")
            tnC = b.emit("dve", lambda e: e.tensor_tensor(V_S[0:16, 10:11], V_S[0:16, 14:15],
                                                          V_S[0:16, 15:16], ALU.max),
                         waits=[tnA, tnB], inc="dve")
            t5_ = b.emit("dve", lambda e: e.tensor_single_scalar(V_S[0:16, 12:13],
                         V_S[0:16, 10:11], -1.0, ALU.mult),
                         waits=[tnC], inc="dve")
            w7 = [t5_, bank_free[7]]
            ttr = b.emit("pe", lambda e: e.transpose(PS[0:1, 7 * 512: 7 * 512 + 16],
                                                     V_S[0:16, 12:13], C_EYE[0:16, 0:16]),
                         waits=w7, inc="pe")
            tcp_ = b.emit("dve", lambda e: e.tensor_copy(V_SC[0:1, 0:16],
                                                         PS[0:1, 7 * 512: 7 * 512 + 16]),
                          waits=[ttr], inc="dve")
            tbc_ = b.emit("pe", lambda e: e.matmul(PS[:, 7 * 512: 7 * 512 + 16],
                                                   C_OR[:, :], V_SC[0:1, 0:16],
                                                   start=True, stop=True),
                          waits=[tcp_], inc="pe")
            tok_bias = b.emit("act", lambda e: e.copy(V_C[:, 0:16],
                                                      PS[:, 7 * 512: 7 * 512 + 16]),
                              waits=[tbc_], inc="act")
            bank_free[7] = tok_bias

            if dbg == "ind2" and j == 0:
                tc2 = dma("sp", d_out[0:16, 0:512], B_Mf[0:16, 512:1024], "st",
                          waits=[b.tok("dve")])
                tc3 = dma("sp", d_out[0:16, 512:528], V_S[0:16, 0:16], "st",
                          waits=[b.tok("dve")])
                b.emit("sp", lambda e: e.nop(), waits=[tc2, tc3])
                b.finalize(block)
                return nc

            if dbg == "ind" and j == 0:
                td1 = dma("pool", d_out[0:128, 0:128], C_IND[:, 0:128], "gst",
                          waits=[b.tok("dve"), b.tok("pe")])
                td2 = dma("pool", d_out[0:128, 128:640], B_M[:, 0:512], "gst",
                          waits=[b.tok("dve")])
                tcpy = b.emit("dve", lambda e: e.tensor_copy(
                    B_Mf[:, 512:1024][0:16, 0:512], PS[0:16, 6 * 512: 6 * 512 + 512]),
                    waits=[b.tok("pe")], inc="dve")
                td3 = dma("sp", d_out[0:16, 1024:1536], B_Mf[0:16, 512:1024], "st",
                          waits=[tcpy])
                b.emit("sp", lambda e: e.nop(), waits=[td1, td2, td3])
                b.finalize(block)
                return nc

            if dbg == "bias" and j == 0:
                tdbg1 = dma("sp", d_out[0:128, 0:48], V_C[:, :], "st", waits=[tok_bias])
                tdbg2 = dma("sp", d_out[0:16, 64:80], V_S[0:16, 0:16], "st",
                            waits=[tok_bias, b.tok("dve")])
                b.emit("sp", lambda e: e.nop(), waits=[tdbg1, tdbg2])
                b.finalize(block)
                return nc

            groups = [(0, 3), (3, 3), (6, 3), (9, 3), (12, 3), (15, 1)]
            prev_bc7 = tok_bias
            pv_free = [bank_free[6], None]
            sum_dma = [None, None]
            prev_rdist = [None, None]
            prev_rr = [None] * 4
            rdist_done = {}
            for h in range(H):
                p0 = 64 * (h % 2)
                cq = h // 2
                qoff = 2048 * cq
                for lc in range(4):
                    pv_last = None
                    pending = None

                    def emit_pv(pend):
                        gi, s0, glen, pslot, texp = pend
                        pv = None
                        for si in range(glen):
                            stile = s0 + si
                            wpv = [texp]
                            if gi == 0 and si == 0:
                                wpv += [vlast_ev, tok_vones] + pv_free
                            def mmpv(e, stile=stile, si=si, pslot=pslot, h=h, gi=gi, glen=glen):
                                return e.matmul(
                                    PS[0:65, 6 * 512: 6 * 512 + 512],
                                    B_V[:, 1040 * stile + 65 * h: 1040 * stile + 65 * (h + 1)],
                                    B_P[:, pslot + 512 * si: pslot + 512 * (si + 1)],
                                    start=(gi == 0 and si == 0),
                                    stop=(gi == len(groups) - 1 and si == glen - 1))
                            pv = b.emit("pe", mmpv, waits=wpv,
                                        inc="pe" if (gi == len(groups) - 1 and si == glen - 1) else None)
                        return pv

                    for gi, (s0, glen) in enumerate(groups):
                        gb = 0 if gi % 2 == 0 else 3
                        last_qk = None
                        for si in range(glen):
                            stile = s0 + si
                            wqk = [tok_rope, bank_free[gb]] if si == 0 else ()
                            def mmqk(e, gb=gb, si=si, stile=stile, p0=p0, qoff=qoff, lc=lc):
                                return e.matmul(
                                    PS[:, 512 * (gb + si): 512 * (gb + si) + 512],
                                    B_K[p0: p0 + 64, qoff + 128 * stile: qoff + 128 * (stile + 1)],
                                    B_Q[p0: p0 + 64, qoff + 512 * lc: qoff + 512 * (lc + 1)],
                                    start=True, stop=True)
                            last_qk = b.emit("pe", mmqk, waits=wqk,
                                             inc="pe" if si == glen - 1 else None)
                        pslot = (gi % 2) * 1536
                        def expg(e, gb=gb, glen=glen, pslot=pslot, h=h):
                            return e.activation(B_P[:, pslot: pslot + 512 * glen],
                                                PS[:, 512 * gb: 512 * gb + 512 * glen],
                                                AF.Exp, bias=V_C[:, h:h + 1], scale=0.125)
                        texp = b.emit("act", expg, waits=[last_qk, tok_bias], inc="act")
                        bank_free[gb] = texp
                        if pending is not None:
                            emit_pv(pending)
                        pending = (gi, s0, glen, pslot, texp)
                    pv_last = emit_pv(pending)
                    def evpv(e, h=h, lc=lc, cq=cq, p0=p0):
                        return e.copy(B_X[p0: p0 + 64, 2048 * cq + 512 * lc: 2048 * cq + 512 * (lc + 1)],
                                      PS[0:64, 6 * 512: 6 * 512 + 512])
                    tev = b.emit("act", evpv, waits=[pv_last], inc="act")
                    sslot = 512 * (lc % 2)
                    def evsum(e, sslot=sslot):
                        return e.tensor_copy(V_R2[0:1, sslot: sslot + 512],
                                             PS[64:65, 6 * 512: 6 * 512 + 512])
                    tsum = b.emit("dve", evsum, waits=[pv_last, sum_dma[lc % 2]], inc="dve")
                    sum_dma[lc % 2] = dma("sp",
                                          d_srow[0:1, 2048 * h + 512 * lc: 2048 * h + 512 * (lc + 1)],
                                          V_R2[0:1, sslot: sslot + 512], f"ss{lc % 2}", waits=[tsum])
                    pv_free = [tev, tsum]
                    if dbg == "attn2":
                        trc2 = b.emit("dve", lambda e, sslot=sslot: e.reciprocal(
                            V_R[0:1, 1536:2048], V_R2[0:1, sslot: sslot + 512]),
                            waits=[tsum, prev_bc7], inc="dve")
                        wb2 = [trc2, bank_free[7]]
                        def bcr2(e):
                            return e.matmul(PS[0:64, 7 * 512: 7 * 512 + 512], C_OR[0:1, 0:64],
                                            V_R[0:1, 1536:2048], start=True, stop=True)
                        prev_bc7 = b.emit("pe", bcr2, waits=wb2, inc="pe")
                        def nrm2_(e, h=h, lc=lc, cq=cq, p0=p0):
                            sl = B_X[p0: p0 + 64, 2048 * cq + 512 * lc: 2048 * cq + 512 * (lc + 1)]
                            return e.tensor_tensor(sl, sl, PS[0:64, 7 * 512: 7 * 512 + 512],
                                                   ALU.mult)
                        norm_tt = b.emit("dve", nrm2_, waits=[prev_bc7, tev], inc="dve")
                        bank_free[7] = norm_tt
                        pv_free = [tev, norm_tt]
                if dbg in ("attnraw", "attn2"):
                    continue
                hp = h % 2
                tok_sr = [b.tok("ss0"), b.tok("ss1")]
                sd = B_Wf[:, 256 + 32 * hp: 272 + 32 * hp]
                sd2 = B_Wf[:, 272 + 32 * hp: 288 + 32 * hp]
                tok_d1 = dma("sp", sd,
                             d_srow[0:1, 2048 * h: 2048 * (h + 1)].rearrange(
                                 "o (p n) -> (o p) n", p=128), f"lsd{hp}",
                             waits=tok_sr + [prev_rdist[hp]])
                tok_rc = b.emit("dve", lambda e, sd=sd, sd2=sd2: e.reciprocal(sd2, sd),
                                waits=[tok_d1], inc="dve")
                tok_d2 = dma("sp",
                             d_rrow[0:1, 2048 * h: 2048 * (h + 1)].rearrange(
                                 "o (p n) -> (o p) n", p=128), sd2, f"sdist{hp}",
                             waits=[tok_rc])
                prev_rdist[hp] = tok_d2
                rdist_done[h] = tok_d2
                for hn in ([h - 1] if h > 0 else []) + ([h] if h == H - 1 else []):
                    cqn = hn // 2
                    p0n = 64 * (hn % 2)
                    norm_tt = None
                    d3toks = []
                    for lc in range(4):
                        rstage = B_Wf[0:1, 512 + 512 * lc: 1024 + 512 * lc]
                        d3toks.append(dma("sp", rstage,
                                     d_rrow[0:1, 2048 * hn + 512 * lc: 2048 * hn + 512 * lc + 512],
                                     f"lrr{lc}", waits=[rdist_done[hn], prev_rr[lc]]))
                    for lc in range(4):
                        rstage = B_Wf[0:1, 512 + 512 * lc: 1024 + 512 * lc]
                        wb = [d3toks[lc], bank_free[7]]
                        def bcr(e, rstage=rstage):
                            return e.matmul(PS[0:64, 7 * 512: 7 * 512 + 512], C_OR[0:1, 0:64],
                                            rstage, start=True, stop=True)
                        prev_bc7 = b.emit("pe", bcr, waits=wb, inc="pe")
                        def nrm(e, hn=hn, lc=lc, cqn=cqn, p0n=p0n):
                            sl = B_X[p0n: p0n + 64, 2048 * cqn + 512 * lc: 2048 * cqn + 512 * (lc + 1)]
                            return e.tensor_tensor(sl, sl, PS[0:64, 7 * 512: 7 * 512 + 512],
                                                   ALU.mult)
                        norm_tt = b.emit("dve", nrm, waits=[prev_bc7, pv_free[0]], inc="dve")
                        bank_free[7] = norm_tt
                        prev_rr[lc] = norm_tt
            tok_attn_norm = b.tok("dve")

            if dbg == "attnraw" and j == 0:
                for c in range(CD):
                    tok_dbg_last = dma("pool", d_out[128 * c: 128 * (c + 1), 0:L],
                                       B_X[:, 2048 * c: 2048 * (c + 1)], "gst",
                                       waits=[b.tok("act"), b.tok("ss0"), b.tok("ss1")])
                tok_dbg_last = dma("sp",
                                   d_out.rearrange("d t -> (d t)")[D * L: D * L + H * L].rearrange("(o n) -> o n", o=1),
                                   d_srow, "st", waits=[b.tok("st"), tok_dbg_last])
                b.emit("sp", lambda e: e.nop(), waits=[tok_dbg_last])
                b.finalize(block)
                return nc

            if dbg in ("q", "attn", "attn2"):
                srcbuf = B_Q if dbg == "q" else B_X
                wtok = tok_rope if dbg == "q" else tok_attn_norm
                for c in range(CD):
                    tok_dbg_last = dma("pool", d_out[128 * c: 128 * (c + 1), L * j: L * (j + 1)],
                                       srcbuf[:, 2048 * c: 2048 * (c + 1)], "gst", waits=[wtok])
                tok_oproj_done = b.tok("pe")
                if j == NSEQ - 1:
                    b.emit("sp", lambda e: e.nop(), waits=[tok_dbg_last])
                    b.finalize(block)
                    return nc
                continue

            s1ring = B_Wf
            prev_xr = [None] * 4
            prev_ap1 = [None, None]
            prev_xlndma = [None, None]
            lsum = lsq = None
            prev_sq1 = [None, None]

            def emit_stats1(dm, ts1, tsq):
                nonlocal lsum, lsq, prev_sq1
                wst = [ts1] + (ssum_w if dm == 0 else [])
                lsum = b.emit("pe", lambda e, dm=dm: e.matmul(
                    PS[0:1, 4 * 512: 4 * 512 + 512], V_C[:, 47:48],
                    s1ring[:, 512 * dm: 512 * (dm + 1)],
                    start=(dm == 0), stop=(dm == CD - 1)), waits=wst, inc="pe")
                lsq = b.emit("pe", lambda e, dm=dm: e.matmul(
                    PS[0:1, 5 * 512: 5 * 512 + 512], V_C[:, 47:48],
                    B_Mf[:, 4096 + 512 * (dm % 2): 4608 + 512 * (dm % 2)],
                    start=(dm == 0), stop=(dm == CD - 1)), waits=[tsq], inc="pe")
                prev_sq1[dm % 2] = lsq

            for t4 in range(4):
                ssum_w = [bank_free[4], bank_free[5]]
                ring_toks = []
                stats_pend = None
                for dm in range(CD):
                    bki = dm % 3
                    w = [tok_wo, tok_attn_norm, bank_free[bki]]
                    for ca in range(CD):
                        def mmo(e, bki=bki, ca=ca, dm=dm, t4=t4):
                            return e.matmul(
                                bank(bki),
                                B_W[:, 8192 + 1024 * ca + 128 * dm:
                                    8192 + 1024 * ca + 128 * dm + 128],
                                B_X[:, 2048 * ca + 512 * t4: 2048 * ca + 512 * (t4 + 1)],
                                start=(ca == 0), stop=(ca == CD - 1))
                        last = b.emit("pe", mmo, waits=w if ca == 0 else (),
                                      inc="pe" if ca == CD - 1 else None)
                    if stats_pend is not None:
                        emit_stats1(*stats_pend)
                    r = dm % 4
                    xr_f = B_Mf[:, 512 * r: 512 * (r + 1)]
                    tok_xr = dma("act", xr_f,
                                 d_x[128 * dm: 128 * (dm + 1),
                                     L * j + 512 * t4: L * j + 512 * (t4 + 1)],
                                 f"lxr{r}", waits=[prev_xr[r]])
                    def adds1(e, bki=bki, dm=dm, xr_f=xr_f):
                        return e.tensor_tensor(s1ring[:, 512 * dm: 512 * (dm + 1)],
                                               bank(bki), xr_f, ALU.add)
                    ts1 = b.emit("dve", adds1, waits=[last, tok_xr, t_ones], inc="dve", chain=False)
                    prev_xr[r] = ts1
                    bank_free[bki] = ts1
                    ring_toks.append(ts1)
                    dsq = B_Mf[:, 4096 + 512 * (dm % 2): 4608 + 512 * (dm % 2)]
                    tsq = b.emit("dve", lambda e, dm=dm, dsq=dsq: e.tensor_tensor(
                        dsq, s1ring[:, 512 * dm: 512 * (dm + 1)],
                        s1ring[:, 512 * dm: 512 * (dm + 1)], ALU.mult),
                        waits=[prev_sq1[dm % 2]], inc="dve", chain=False)
                    stats_pend = (dm, ts1, tsq)
                emit_stats1(*stats_pend)
                trm = b.emit("dve", lambda e: e.tensor_single_scalar(V_R[0:1, 0:512],
                             PS[0:1, 4 * 512: 4 * 512 + 512], 1.0 / D, ALU.mult),
                             waits=[lsum], inc="dve")
                trq = b.emit("dve", lambda e: e.tensor_single_scalar(V_R[0:1, 512:1024],
                             PS[0:1, 5 * 512: 5 * 512 + 512], 1.0 / D, ALU.mult),
                             waits=[lsq], inc="dve")
                bank_free[4] = trm
                bank_free[5] = trq
                trm2 = b.emit("dve", lambda e: e.tensor_tensor(V_R[0:1, 1024:1536], V_R[0:1, 0:512],
                                                        V_R[0:1, 0:512], ALU.mult),
                              waits=[trm], inc="dve")
                tvar = b.emit("dve", lambda e: e.tensor_tensor(V_R[0:1, 512:1024],
                                                               V_R[0:1, 512:1024],
                                                               V_R[0:1, 1024:1536],
                                                               ALU.subtract),
                              waits=[trq, trm2], inc="dve")
                tln = b.emit("act", lambda e: e.activation(V_R[0:1, 1024:1536],
                                                           V_R[0:1, 512:1024], AF.Ln,
                                                           bias=CONSTV[0:1, 0:1]),
                             waits=[tvar], inc="act")
                trstd = b.emit("act", lambda e: e.activation(V_R[0:1, 1024:1536],
                                                             V_R[0:1, 1024:1536], AF.Exp,
                                                             scale=-0.5), inc="act")
                tmr = b.emit("dve", lambda e: e.tensor_tensor(V_R[0:1, 1536:2048],
                                                              V_R[0:1, 0:512],
                                                              V_R[0:1, 1024:1536], ALU.mult),
                             waits=[trstd], inc="dve")
                tb6 = b.emit("pe", lambda e: e.matmul(PS[:, 6 * 512: 7 * 512], C_OR[:, :],
                                                      V_R[0:1, 1024:1536],
                                                      start=True, stop=True),
                             waits=[trstd, bank_free[6]], inc="pe")
                tb7 = b.emit("pe", lambda e: e.matmul(PS[:, 7 * 512: 8 * 512], C_OR[:, :],
                                                      V_R[0:1, 1536:2048],
                                                      start=True, stop=True),
                             waits=[tmr, bank_free[7]], inc="pe")
                last_ap = None
                for dm in range(CD):
                    pp = dm % 2
                    t_f = B_Mf[:, 2048 + 512 * pp: 2560 + 512 * pp]
                    xln_f = B_Mf[:, 3072 + 512 * pp: 3584 + 512 * pp]
                    def ap1(e, dm=dm, t_f=t_f):
                        return e.tensor_tensor(t_f, s1ring[:, 512 * dm: 512 * (dm + 1)],
                                               PS[:, 6 * 512: 7 * 512], ALU.mult)
                    ta1 = b.emit("dve", ap1, waits=[tb6, prev_ap1[pp]], inc="dve")
                    def ap2(e, t_f=t_f, xln_f=xln_f):
                        return e.tensor_tensor(xln_f, t_f, PS[:, 7 * 512: 8 * 512],
                                               ALU.subtract)
                    ta2 = b.emit("dve", ap2, waits=[tb7, prev_xlndma[pp]], inc="dve")
                    prev_xlndma[pp] = dma("sp", d_xln[128 * dm: 128 * (dm + 1),
                                          L * j + 512 * t4: L * j + 512 * (t4 + 1)],
                                          xln_f, f"sxln{pp}", waits=[ta2])
                    first = (j == 0 and t4 == 0 and dm == 0)
                    col = 13 if first else 14
                    def redx(e, col=col, xln_f=xln_f):
                        return e.tensor_reduce(V_S[:, col:col + 1], xln_f,
                                               AX.X, ALU.max, apply_absolute_value=True)
                    trd = b.emit("dve", redx, inc="dve")
                    prev_ap1[pp] = trd
                    if not first:
                        b.emit("dve", lambda e: e.tensor_tensor(V_S[:, 13:14], V_S[:, 13:14],
                                                                V_S[:, 14:15], ALU.max),
                               waits=[trd], inc="dve")
                    last_ap = trd
                bank_free[6] = last_ap
                bank_free[7] = last_ap
            tok_oproj_done = b.tok("pe")

        if dbg == "xln":
            tok = dma("sp", d_out, d_xln, "st", waits=[b.tok("sxln0"), b.tok("sxln1"), b.tok("dve")])
            b.emit("sp", lambda e: e.nop(), waits=[tok])
            b.finalize(block)
            return nc

        C_I32 = C_T.ap().bitcast(F32)[0:32, 0:1024]
        tok_i32 = dma("sp", C_I32, i_ind32, "ldc", waits=[b.tok("dve"), b.tok("act"), tok_consts])
        SN = {0: B_Mf[:, 2048:2816], 1: B_Mf[:, 2816:3584], 2: B_Mf[:, 3584:4352]}
        tok_vfree = b.tok("pe")
        tok_vfree_a = b.tok("act")
        mats = ((d_wupN, CI, 1), (d_wgN, CI, 1), (d_wdN, CD, 3))
        prev_nst = [None] * 4
        sn_tok = {}
        qrot = ("sp", "act", "pool")
        ldi = 0
        for mi, (dN, nt, nw) in enumerate(mats):
            tr_ = None
            for ot in range(nt):
                for w in range(nw):
                    sl = ldi % 4
                    nst = B_Vf[:, 1024 * sl: 1024 * sl + 1024]
                    tok_n = dma(qrot[ldi % 3], nst,
                                dN[128 * ot: 128 * (ot + 1), 1024 * w: 1024 * (w + 1)],
                                f"lsn{sl}", waits=[tok_vfree, tok_vfree_a, prev_nst[sl]])
                    ldi += 1
                    nb0 = (32 * ot) if nw == 1 else (96 * ot + 32 * w)
                    def redn(e, nst=nst, nb0=nb0, mi=mi):
                        return e.tensor_reduce(
                            SN[mi][:, nb0: nb0 + 32],
                            nst.rearrange("p (b w) -> p b w", w=QB),
                            AX.X, ALU.max, apply_absolute_value=True)
                    tr_ = b.emit("dve", redn, waits=[tok_n], inc="dve")
                    prev_nst[sl] = tr_
            ncols = 32 * nt * nw
            sn_tok[mi] = b.emit("dve", lambda e, mi=mi, ncols=ncols: e.tensor_scalar(
                SN[mi][:, 0:ncols], SN[mi][:, 0:ncols], 1.0 / 7.0, 1e-12,
                ALU.mult, ALU.max), waits=[tr_], inc="dve")
        tok_scales = sn_tok[2]

        def jit_scale_tiles(mi, src_slice, w_extra, par=0):
            pc = 4 * 512 + 512 * par
            sT_st = B_Vf[0:32, 6272 + 256 * par: 6400 + 256 * par]
            rT_st = B_Vf[0:32, 6400 + 256 * par: 6528 + 256 * par]
            wtr = [sn_tok[mi], bank_free[4 + par]] + w_extra
            ttr2 = b.emit("pe", lambda e: e.transpose(
                PS[0:32, pc + 256: pc + 384], src_slice, C_EYE[:, :]),
                waits=wtr, inc="pe")
            tcp2 = b.emit("act", lambda e: e.copy(sT_st, PS[0:32, pc + 256: pc + 384]),
                          waits=[ttr2], inc="act")
            trc2 = b.emit("dve", lambda e: e.reciprocal(rT_st, sT_st),
                          waits=[tcp2], inc="dve")
            bank_free[4 + par] = tcp2
            return sT_st, rT_st, trc2

        def barrier(idx, cols, after_toks):
            ctoks = []
            prev_gst = b.tok("gci")
            prev_gmx = b.tok("dve")
            tok_ccz = b.emit("dve", lambda e: e.memset(CCST, 0.0),
                             waits=after_toks + [prev_gst], inc="dve")
            for si, col in enumerate(cols):
                w = [bank_free[5]] + after_toks
                def trp(e, col=col):
                    return e.transpose(PS[0:1, 5 * 512: 5 * 512 + 128],
                                       V_S[:, col:col + 1], C_EYE[:, :])
                tt = b.emit("pe", trp, waits=w, inc="pe")
                def rd(e, si=si, idx=idx):
                    return e.tensor_reduce(CCST[0:1, si:si + 1],
                                           PS[0:1, 5 * 512: 5 * 512 + 128],
                                           AX.X, ALU.max)
                tr_ = b.emit("dve", rd, waits=[tt, tok_ccz], inc="dve")
                bank_free[5] = tr_
                ctoks.append(tr_)
            tin = dma("pool", cc_in[idx], CCST, "gci", waits=ctoks)
            tcc = b.emit("pool", lambda e, idx=idx: e.collective_compute(
                "AllReduce", ALU.max, replica_groups=[list(range(8))],
                ins=[cc_in[idx]], outs=[cc_out[idx]]), waits=[tin], inc="cc")
            tg = dma("pool", GMX, cc_out[idx], "gco", waits=[tcc, prev_gmx])
            return tg

        def bcast_consts(row_lo, n, vc_lo, wtok):
            w = [wtok, bank_free[5]]
            tb = b.emit("pe", lambda e: e.matmul(PS[:, 5 * 512: 5 * 512 + n], C_OR[:, :],
                                                 V_SC[0:1, row_lo: row_lo + n],
                                                 start=True, stop=True),
                        waits=w, inc="pe")
            tc = b.emit("act", lambda e: e.copy(V_C[:, vc_lo: vc_lo + n],
                                                PS[:, 5 * 512: 5 * 512 + n]),
                        waits=[tb], inc="act")
            bank_free[5] = tc
            return tc

        tg1 = barrier(0, [13], [b.tok("dve")])
        t_a = b.emit("dve", lambda e: e.tensor_scalar(V_SC[0:1, 0:1], GMX[:, 0:1],
                                                      1.0 / QMA, 1e-12, ALU.mult, ALU.max),
                     waits=[tg1], inc="dve")
        t_b = b.emit("dve", lambda e: e.reciprocal(V_SC[0:1, 1:2], V_SC[0:1, 0:1]),
                     waits=[t_a], inc="dve")
        tok_c1 = bcast_consts(0, 2, 32, t_b)

        def xq_slab(c, t0, n):
            if c < 4:
                return B_Q[:, 4096 * c + t0: 4096 * c + t0 + n]
            return B_K[:, 4096 * (c - 4) + t0: 4096 * (c - 4) + t0 + n]

        xq_tok = None
        prev_ld = [None] * 4
        tok_ln1_done = (b.tok("dve"), b.tok("pe"))
        for c in range(CD):
            for t2 in range(2):
                sl = (c * 2 + t2) % 4
                stage = B_Wf[:, 2048 * sl: 2048 * sl + 2048]
                tok_l = dma("sp" if sl % 2 == 0 else "act", stage,
                            d_xln[128 * c: 128 * (c + 1), 2048 * t2: 2048 * (t2 + 1)],
                            f"lxq{sl}", waits=[prev_ld[sl], tok_scales, b.tok("sxln0"), b.tok("sxln1"), b.tok("gw0"), b.tok("gw1"), b.tok("gx"),
                                          tok_ln1_done[0], tok_ln1_done[1]])
                def q1(e, stage=stage):
                    return e.tensor_scalar(stage, stage, V_C[:, 33:34], CMAG,
                                           ALU.mult, ALU.add)
                tq1 = b.emit("dve", q1, waits=[tok_l, tok_c1], inc="dve")
                def q2(e, stage=stage, c=c, t2=t2):
                    return e.tensor_scalar(xq_slab(c, 2048 * t2, 2048), stage,
                                           -CMAG, V_C[:, 32:33], ALU.add, ALU.mult)
                xq_tok = b.emit("dve", q2, waits=[tq1], inc="dve")
                prev_ld[sl] = xq_tok

        if dbg == "xq":
            tdl = None
            for c in range(CD):
                buf = B_Q if c < 4 else B_K
                off = 4096 * (c % 4)
                tdl = dma("pool", d_out[128 * c: 128 * (c + 1), :],
                          buf[:, off: off + 4096], "gst", waits=[xq_tok])
            b.emit("sp", lambda e: e.nop(), waits=[tdl])
            b.finalize(block)
            return nc

        dq_base = 4096
        sp_base = 8192
        run_up, run_gate = 1, 2
        first_red = {1: True, 2: True}
        prev_w = [None, None]
        prev_dq = [None, None]
        prev_sp = [None] * 4
        prev_rd = [None] * 4
        spill_n = 0
        gtoks = {}
        tok_dq_done = {}
        a_last_mm = None
        tiles = [(it, mi) for it in range(A_LIMIT) for mi in range(M_LIMIT)]
        tinfo = {}
        bank6_free = [None]

        def a_wload(X):
            it, mi = tiles[X]
            dT = d_wup if mi == 0 else d_wg
            sl = X % 2
            wst = B_Vf[:, 1024 * sl: 1024 * sl + 1024]
            tok_w = dma("sp", wst.rearrange("p (c n) -> p c n", c=CD),
                        dT[:, 128 * it: 128 * (it + 1)].rearrange(
                            "(c p) n -> p c n", p=128),
                        f"lwa{sl}", waits=[prev_w[sl], xq_tok])
            tinfo[X] = {"sl": sl, "wst": wst, "tok_w": tok_w, "dq": []}

        def a_jit(X):
            it, mi = tiles[X]
            info = tinfo[X]
            tp_ = X % 2
            pc6 = 6 * 512 + 128 * tp_
            srT = B_Vf[0:32, 6272 + 256 * tp_: 6528 + 256 * tp_]
            src = SN[mi][:, 32 * it: 32 * (it + 1)]
            ttr2 = b.emit("pe", lambda e, src=src, pc6=pc6: e.transpose(
                PS[0:32, pc6: pc6 + 128], src, C_EYE[:, :]),
                waits=[sn_tok[mi], tok_i32, bank6_free[0]], inc="pe")
            tcp2 = b.emit("act", lambda e, srT=srT, pc6=pc6: e.copy(
                srT[0:32, 0:128], PS[0:32, pc6: pc6 + 128]), waits=[ttr2], inc="act")
            trc2 = b.emit("dve", lambda e, srT=srT: e.reciprocal(
                srT[0:32, 128:256], srT[0:32, 0:128]), waits=[tcp2], inc="dve")
            bank6_free[0] = tcp2
            info["jit"] = trc2
            info["srT"] = srT

        def a_bs(X, c):
            info = tinfo[X]
            sl = info["sl"]
            bpar = c % 2
            pc = 4 * 512 + 512 * bpar
            scr = B_Vf[:, 6784 + 128 * bpar: 6912 + 128 * bpar]
            def bs1(e, c=c, srT=info["srT"], pc=pc):
                return e.matmul(PS[:, pc: pc + 256],
                                C_I32[:, 128 * c: 128 * (c + 1)],
                                srT, start=True, stop=True)
            tb2 = b.emit("pe", bs1, waits=[info["jit"], bank_free[4 + bpar]], inc="pe")
            def dq1(e, wst=info["wst"], c=c, scr=scr, pc=pc):
                return e.tensor_tensor(scr, wst[:, 128 * c: 128 * (c + 1)],
                                       PS[:, pc + 128: pc + 256], ALU.mult)
            td1 = b.emit("dve", dq1, waits=[tb2, info["tok_w"]], inc="dve", chain=False)
            td2 = b.emit("dve", lambda e, scr=scr: e.tensor_scalar(
                scr, scr, CMAG, -CMAG, ALU.add, ALU.add), inc="dve", chain=False)
            def dq3(e, c=c, sl=sl, scr=scr, pc=pc):
                return e.tensor_tensor(B_V[:, dq_base + 2048 * sl + 128 * c:
                                            dq_base + 2048 * sl + 128 * (c + 1)],
                                       scr, PS[:, pc: pc + 128], ALU.mult)
            td3 = b.emit("dve", dq3, inc="dve", chain=False)
            bank_free[4 + bpar] = td3
            info["dq"].append(td3)
            if c == CD - 1:
                prev_w[sl] = td3

        a_wload(0)
        a_jit(0)
        for c in range(CD):
            a_bs(0, c)
        spill_n = 0
        for X in range(len(tiles)):
            it, mi = tiles[X]
            info = tinfo[X]
            sl = info["sl"]
            ddst = d_up if mi == 0 else d_gate
            if X + 1 < len(tiles):
                a_wload(X + 1)
                a_jit(X + 1)
            for tp in range(4):
                pb = 0 if (X * 4 + tp) % 2 == 0 else 2
                wa = [info["dq"][-1], bank_free[pb], bank_free[pb + 1]]
                for c in range(CD):
                    for half in range(2):
                        def mma(e, pb=pb, c=c, half=half, sl=sl, tp=tp):
                            return e.matmul(
                                PS[:, 512 * (pb + half): 512 * (pb + half) + 512],
                                B_V[:, dq_base + 2048 * sl + 128 * c:
                                    dq_base + 2048 * sl + 128 * (c + 1)],
                                xq_slab(c, 1024 * tp + 512 * half, 512),
                                start=(c == 0), stop=(c == CD - 1))
                        a_last_mm = b.emit("pe", mma, waits=wa if (c == 0 and half == 0) else (),
                                           inc="pe" if (c == CD - 1 and half == 1) else None)
                if X + 1 < len(tiles):
                    a_bs(X + 1, 2 * tp)
                    a_bs(X + 1, 2 * tp + 1)
                ssl = spill_n % 4
                spill_n += 1
                spst = B_V[:, sp_base + 1024 * ssl: sp_base + 1024 * (ssl + 1)]
                def eva(e, pb=pb, spst=spst):
                    return e.copy(spst, PS[:, 512 * pb: 512 * pb + 1024])
                tev = b.emit("act", eva, waits=[a_last_mm, prev_sp[ssl], prev_rd[ssl]],
                             inc="act")
                run = run_up if mi == 0 else run_gate
                if first_red[run]:
                    def redA(e, spst=spst, run=run):
                        return e.tensor_reduce(V_S[:, run:run + 1], spst,
                                               AX.X, ALU.max, apply_absolute_value=True)
                    trd = b.emit("dve", redA, waits=[tev], inc="dve")
                    first_red[run] = False
                else:
                    def redA(e, spst=spst):
                        return e.tensor_reduce(V_S[:, 15:16], spst,
                                               AX.X, ALU.max, apply_absolute_value=True)
                    trd0 = b.emit("dve", redA, waits=[tev], inc="dve")
                    def acc(e, run=run):
                        return e.tensor_tensor(V_S[:, run:run + 1], V_S[:, run:run + 1],
                                               V_S[:, 15:16], ALU.max)
                    trd = b.emit("dve", acc, waits=[trd0], inc="dve")
                prev_rd[ssl] = trd
                bank_free[pb] = tev
                bank_free[pb + 1] = tev
                prev_sp[ssl] = dma("sp", ddst[128 * it: 128 * (it + 1),
                                            1024 * tp: 1024 * (tp + 1)],
                                   spst, f"ssp{ssl}", waits=[tev])

        if dbg == "ffnA":
            dob = d_out.bitcast(BF16)
            t1d = dma("sp", dob[0:512, 0:T], d_up[0:512, :], "st",
                      waits=[b.tok(f"ssp{0}"), b.tok("ssp1"), b.tok("ssp2"), b.tok("ssp3"), b.tok("dve")])
            t2d = dma("sp", dob[512:1024, 0:T], d_gate[0:512, :], "st",
                      waits=[b.tok(f"ssp{0}"), b.tok("ssp1"), b.tok("ssp2"), b.tok("ssp3"), b.tok("dve")])
            b.emit("sp", lambda e: e.nop(), waits=[t1d, t2d])
            b.finalize(block)
            return nc

        tg2 = barrier(1, [run_up, run_gate], [b.tok("dve"), b.tok("act")])
        tq1 = b.emit("dve", lambda e: e.tensor_scalar(V_SC[0:1, 2:3], GMX[:, 0:1],
                                                1.0 / QMA, 1e-12, ALU.mult, ALU.max),
               waits=[tg2], inc="dve")
        tq2 = b.emit("dve", lambda e: e.reciprocal(V_SC[0:1, 3:4], V_SC[0:1, 2:3]),
                     waits=[tq1], inc="dve")
        tq3 = b.emit("dve", lambda e: e.tensor_scalar(V_SC[0:1, 4:5], GMX[:, 1:2],
                                                1.0 / QMA, 1e-12, ALU.mult, ALU.max),
               waits=[tq2], inc="dve")
        tq4 = b.emit("dve", lambda e: e.reciprocal(V_SC[0:1, 5:6], V_SC[0:1, 4:5]),
                     waits=[tq3], inc="dve")
        t_ct = b.emit("dve", lambda e: e.tensor_single_scalar(V_SC[0:1, 6:7], V_SC[0:1, 4:5],
                                                       float(2 ** -16), ALU.mult),
                      waits=[tq4], inc="dve")
        tok_c2 = bcast_consts(2, 5, 34, t_ct)

        def ddq_slab(it, dm):
            base = 1024 * it + 128 * dm
            if it < 16:
                return B_Q[:, base: base + 128]
            return B_K[:, base - 16384: base - 16384 + 128]

        def ddq_tile(it):
            if it < 16:
                return B_Q[:, 1024 * it: 1024 * (it + 1)]
            return B_K[:, 1024 * (it - 16): 1024 * (it - 15)]

        dscr = B_Vf[:, 4352:5376]
        prev_wd = [b.tok("dve"), b.tok("dve")]
        prev_dd = None
        sb_prev = [tok_c2, tok_c2]
        tr_ev = {}
        ddq_last = None
        for g in range(3):
            sTg = B_Vf[0:32, 5376 + 0: 5376 + 1024]
            rTg = B_Vf[0:32, 6400 + 0: 6400 + 1024]
            ttr = None
            for dm in range(CD):
                def trg(e, dm=dm, g=g):
                    return e.transpose(
                        PS[0:32, 6 * 512 + 128 * dm: 6 * 512 + 128 * (dm + 1)],
                        SN[2][:, 96 * dm + 32 * g: 96 * dm + 32 * g + 32],
                        C_EYE[:, :])
                w0 = [tok_scales, sb_prev[1], prev_dd]
                if g == 0 and dm == 0:
                    w0 += [b.tok(f"ssp{i}") for i in range(4)]
                ttr = b.emit("pe", trg,
                             waits=w0 if dm == 0 else (),
                             inc="pe" if dm == CD - 1 else None)
            tev = b.emit("act", lambda e, sTg=sTg: e.copy(sTg, PS[0:32, 6 * 512: 6 * 512 + 1024]),
                         waits=[ttr], inc="act")
            trc = b.emit("dve", lambda e, sTg=sTg, rTg=rTg: e.reciprocal(rTg, sTg),
                         waits=[tev], inc="dve")
            for li in range(8):
                it = 8 * g + li
                sl = it % 2
                crow = it % 8
                wst = B_Vf[:, 1024 * sl: 1024 * sl + 1024]
                tok_w = dma("sp", wst, d_wd[128 * it: 128 * (it + 1), :],
                            f"lwd{sl}", waits=[prev_wd[sl]])
                pbase = 0 if it % 2 == 0 else 4
                s_b = PS[:, 512 * pbase: 512 * pbase + 1024]
                r_b = PS[:, 512 * pbase + 1024: 512 * pbase + 2048]
                wmm = [trc, sb_prev[0 if pbase == 0 else 1]]
                tmm = None
                for h2 in range(2):
                    def bs(e, crow=crow, h2=h2, pbase=pbase, sTg=sTg):
                        return e.matmul(PS[:, 512 * pbase + 512 * h2: 512 * pbase + 512 * (h2 + 1)],
                                        C_I32[:, 128 * crow: 128 * (crow + 1)],
                                        sTg[0:32, 512 * h2: 512 * (h2 + 1)],
                                        start=True, stop=True)
                    tmm = b.emit("pe", bs, waits=wmm if h2 == 0 else (), inc="pe")
                for h2 in range(2):
                    def br(e, crow=crow, h2=h2, pbase=pbase, rTg=rTg):
                        return e.matmul(PS[:, 512 * pbase + 1024 + 512 * h2: 512 * pbase + 1024 + 512 * (h2 + 1)],
                                        C_I32[:, 128 * crow: 128 * (crow + 1)],
                                        rTg[0:32, 512 * h2: 512 * (h2 + 1)],
                                        start=True, stop=True)
                    tmm = b.emit("pe", br, inc="pe")
                d1 = b.emit("dve", lambda e, wst=wst, r_b=r_b: e.tensor_tensor(
                    dscr, wst, r_b, ALU.mult), waits=[tmm, tok_w], inc="dve")
                d2 = b.emit("dve", lambda e: e.tensor_scalar(
                    dscr, dscr, CMAG, -CMAG, ALU.add, ALU.add), inc="dve")
                d3 = b.emit("dve", lambda e, it=it, s_b=s_b: e.tensor_tensor(
                    ddq_tile(it), dscr, s_b, ALU.mult), inc="dve")
                sb_prev[0 if pbase == 0 else 1] = d3
                prev_wd[sl] = d1
                prev_dd = d3
                ddq_last = d3
        for _bi in range(8):
            bank_free[_bi] = sb_prev[0] if _bi < 4 else sb_prev[1]

        B_Xf = B_X.ap().bitcast(F32)
        prev_g = [None] * 4
        prev_fa = [None] * 4
        prev_fb = [None] * 4
        prev_fc = [None] * 4
        prev_tst = [None] * 4
        tmax_tok = None
        NTB = I // 128 * (T // 1024)
        for k in range(NTB):
            it, th = k // 4, k % 4
            sst = k % 4
            gst = B_W[:, 1024 * sst: 1024 * (sst + 1)]
            tst = B_M[:, 1024 * sst: 1024 * (sst + 1)]
            fA = B_Wf[:, 4096 + 1024 * sst: 4096 + 1024 * sst + 1024]
            fB = B_Xf[:, 1024 * sst: 1024 * sst + 1024]
            fC = B_Xf[:, 4096 + 1024 * sst: 4096 + 1024 * sst + 1024]
            tok_g = dma("sp", gst, d_gate[128 * it: 128 * (it + 1),
                                          1024 * th: 1024 * (th + 1)],
                        f"lgb{sst}",
                        waits=[prev_g[sst]] + ([b.tok(f"ssp{i}") for i in range(4)] if k == 0 else []))
            a1 = b.emit("act", lambda e, gst=gst, fA=fA: e.activation(
                fA, gst, AF.Copy, bias=CMAG, scale=V_C[:, 37:38]),
                waits=[tok_g, tok_c2, prev_fa[sst]], inc="act")
            v1 = b.emit("dve", lambda e, fA=fA, fB=fB: e.tensor_single_scalar(
                fB, fA, -CMAG, ALU.add), waits=[a1, prev_fb[sst]], inc="dve", chain=False)
            a3 = b.emit("act", lambda e, fB=fB, fC=fC: e.activation(
                fC, fB, AF.Sigmoid, scale=V_C[:, 36:37]),
                waits=[v1, prev_fc[sst]], inc="act")
            v2 = b.emit("dve", lambda e, fC=fC, fA=fA: e.tensor_scalar(
                fA, fC, 65536.0, CMAG, ALU.mult, ALU.add), waits=[a3], inc="dve", chain=False)
            v3 = b.emit("dve", lambda e, fA=fA: e.tensor_scalar(
                fA, fA, -CMAG, 65535.0, ALU.add, ALU.min), inc="dve", chain=False)
            v4 = b.emit("dve", lambda e, fA=fA, fB=fB, tst=tst: e.tensor_tensor(
                tst, fB, fA, ALU.mult), waits=[prev_tst[sst]], inc="dve", chain=False)
            if tmax_tok is None:
                v5 = b.emit("dve", lambda e, tst=tst: e.tensor_reduce(
                    V_S[:, 3:4], tst, AX.X, ALU.max, apply_absolute_value=True), inc="dve")
            else:
                tv5 = b.emit("dve", lambda e, tst=tst: e.tensor_reduce(
                    V_S[:, 15:16], tst, AX.X, ALU.max, apply_absolute_value=True), inc="dve")
                v5 = b.emit("dve", lambda e: e.tensor_tensor(
                    V_S[:, 3:4], V_S[:, 3:4], V_S[:, 15:16], ALU.max),
                    waits=[tv5], inc="dve")
            tmax_tok = v5
            prev_g[sst] = a1
            prev_fa[sst] = v4
            prev_fb[sst] = v4
            prev_fc[sst] = v2
            prev_tst[sst] = dma("pool", d_t[128 * it: 128 * (it + 1),
                                            1024 * th: 1024 * (th + 1)],
                                tst, f"stb{sst}", waits=[v5])

        tg3 = barrier(2, [3], [b.tok("dve")])
        tr1 = b.emit("dve", lambda e: e.tensor_tensor(V_SC[0:1, 7:8], GMX[:, 0:1],
                                                V_SC[0:1, 6:7], ALU.mult),
               waits=[tg3], inc="dve")
        tr2 = b.emit("dve", lambda e: e.tensor_scalar(V_SC[0:1, 8:9], V_SC[0:1, 7:8],
                                                1.0 / QMA, 1e-12, ALU.mult, ALU.max),
               waits=[tr1], inc="dve")
        t_rt = b.emit("dve", lambda e: e.reciprocal(V_SC[0:1, 9:10], V_SC[0:1, 8:9]),
                      waits=[tr2], inc="dve")
        t_rtc = b.emit("dve", lambda e: e.tensor_tensor(V_SC[0:1, 16:17], V_SC[0:1, 9:10],
                                                        V_SC[0:1, 6:7], ALU.mult),
                       waits=[t_rt], inc="dve")
        tok_c3 = bcast_consts(16, 1, 39, t_rtc)

        prev_tl = [None] * 4
        prev_ul = [None] * 4
        prev_fa = [None] * 4
        prev_fb = [None] * 4
        prev_pst = [None] * 4
        pmax_tok = None
        for k in range(NTB):
            it, th = k // 4, k % 4
            sst = k % 4
            tstg = B_W[:, 1024 * sst: 1024 * (sst + 1)]
            ustg = B_W[:, 4096 + 1024 * sst: 4096 + 1024 * (sst + 1)]
            fA = B_Wf[:, 4096 + 1024 * sst: 4096 + 1024 * sst + 1024]
            fB = B_Xf[:, 1024 * sst: 1024 * sst + 1024]
            pst = B_M[:, 1024 * sst: 1024 * (sst + 1)]
            tok_t = dma("sp", tstg, d_t[128 * it: 128 * (it + 1),
                                        1024 * th: 1024 * (th + 1)],
                        f"ltc{sst}",
                        waits=[prev_tl[sst]] + ([b.tok(f"stb{i}") for i in range(4)] if k == 0 else []))
            tok_u = dma("act", ustg, d_up[128 * it: 128 * (it + 1),
                                          1024 * th: 1024 * (th + 1)],
                        f"luc{sst}", waits=[prev_ul[sst]])
            a1 = b.emit("act", lambda e, tstg=tstg, fA=fA: e.activation(
                fA, tstg, AF.Copy, bias=CMAG, scale=V_C[:, 39:40]),
                waits=[tok_t, tok_c3, prev_fa[sst]], inc="act")
            v1 = b.emit("dve", lambda e, fA=fA: e.tensor_single_scalar(
                fA, fA, -CMAG, ALU.add), waits=[a1], inc="dve", chain=False)
            a2 = b.emit("act", lambda e, ustg=ustg, fB=fB: e.activation(
                fB, ustg, AF.Copy, bias=CMAG, scale=V_C[:, 35:36]),
                waits=[tok_u, tok_c3, prev_fb[sst]], inc="act")
            v2 = b.emit("dve", lambda e, fB=fB: e.tensor_single_scalar(
                fB, fB, -CMAG, ALU.add), waits=[a2], inc="dve", chain=False)
            v3 = b.emit("dve", lambda e, fA=fA, fB=fB, pst=pst: e.tensor_tensor(
                pst, fA, fB, ALU.mult), waits=[prev_pst[sst]], inc="dve", chain=False)
            if pmax_tok is None:
                v4 = b.emit("dve", lambda e, pst=pst: e.tensor_reduce(
                    V_S[:, 4:5], pst, AX.X, ALU.max, apply_absolute_value=True), inc="dve")
            else:
                tv4 = b.emit("dve", lambda e, pst=pst: e.tensor_reduce(
                    V_S[:, 15:16], pst, AX.X, ALU.max, apply_absolute_value=True), inc="dve")
                v4 = b.emit("dve", lambda e: e.tensor_tensor(
                    V_S[:, 4:5], V_S[:, 4:5], V_S[:, 15:16], ALU.max),
                    waits=[tv4], inc="dve")
            pmax_tok = v4
            prev_tl[sst] = a1
            prev_ul[sst] = a2
            prev_fa[sst] = v3
            prev_fb[sst] = v3
            prev_pst[sst] = dma("pool", d_P[128 * it: 128 * (it + 1),
                                            1024 * th: 1024 * (th + 1)],
                                pst, f"stp{sst}", waits=[v4])

        tg4 = barrier(3, [4], [b.tok("dve")])
        ts1_ = b.emit("dve", lambda e: e.tensor_tensor(V_SC[0:1, 10:11], V_SC[0:1, 8:9],
                                                V_SC[0:1, 2:3], ALU.mult),
               waits=[tg4], inc="dve")
        ts2_ = b.emit("dve", lambda e: e.tensor_tensor(V_SC[0:1, 11:12], V_SC[0:1, 10:11],
                                                GMX[:, 0:1], ALU.mult),
               waits=[ts1_], inc="dve")
        ts3_ = b.emit("dve", lambda e: e.tensor_scalar(V_SC[0:1, 12:13], V_SC[0:1, 11:12],
                                                1.0 / QMA, 1e-12, ALU.mult, ALU.max),
               waits=[ts2_], inc="dve")
        ts4_ = b.emit("dve", lambda e: e.reciprocal(V_SC[0:1, 13:14], V_SC[0:1, 12:13]),
               waits=[ts3_], inc="dve")
        t_co = b.emit("dve", lambda e: e.tensor_tensor(V_SC[0:1, 14:15], V_SC[0:1, 10:11],
                                                       V_SC[0:1, 13:14], ALU.mult),
                      waits=[ts4_], inc="dve")
        tok_c4 = bcast_consts(12, 3, 40, t_co)

        def oq_slab(set_, it2):
            if set_ == 0:
                return B_X[:, 512 * it2: 512 * (it2 + 1)]
            if it2 < 8:
                return B_X[:, 12288 + 512 * it2: 12288 + 512 * (it2 + 1)]
            return B_M[:, 512 * (it2 - 8): 512 * (it2 - 7)]

        s2ring = B_Wf[:, 0:4096]
        prev_ost = [None] * 4
        prev_fd = [None] * 4
        prev_oq = [None, None]
        prep_done = [None, None]
        prev_xr = [None] * 4
        prev_ap = [None, None]
        out_toks = [None, None]
        lsum_d = lsq_d = None

        def prep_oq(tc, extra_w=()):
            set_ = tc % 2
            last = None
            for it2 in range(CI):
                r = it2 % 4
                ost = B_W[:, 8192 + 512 * r: 8192 + 512 * (r + 1)]
                fD = B_Wf[:, 5120 + 512 * r: 5120 + 512 * (r + 1)]
                tl = dma("sp", ost, d_P[128 * it2: 128 * (it2 + 1),
                                        512 * tc: 512 * (tc + 1)],
                         f"lpd{r}", waits=[prev_ost[r]] + list(extra_w))
                pa = b.emit("act", lambda e, ost=ost, fD=fD: e.activation(
                    fD, ost, AF.Copy, bias=CMAG, scale=V_C[:, 42:43]),
                    waits=[tl, tok_c4, prev_fd[r]], inc="act")
                pv = b.emit("dve", lambda e, fD=fD, set_=set_, it2=it2: e.tensor_scalar(
                    oq_slab(set_, it2), fD, -CMAG, V_C[:, 40:41], ALU.add, ALU.mult),
                    waits=[pa, prev_oq[set_]], inc="dve", chain=False)
                prev_ost[r] = pa
                prev_fd[r] = pv
                last = pv
            prep_done[set_] = last

        prep_oq(0, extra_w=[b.tok(f"stp{i}") for i in range(4)] + [b.tok("dve"), ddq_last])
        for tc in range(8):
            set_ = tc % 2
            ring_toks = []
            lsum_d = lsq_d = None
            prev_sq = [None, None]

            def emit_stats(dm, ts2, tsq):
                nonlocal lsum_d, lsq_d
                wst2 = [ts2] + ([bank_free[4], bank_free[5]] if dm == 0 else [])
                lsum_d = b.emit("pe", lambda e, dm=dm: e.matmul(
                    PS[0:1, 4 * 512: 4 * 512 + 512], V_C[:, 47:48],
                    s2ring[:, 512 * dm: 512 * (dm + 1)],
                    start=(dm == 0), stop=(dm == CD - 1)),
                    waits=wst2, inc="pe")
                lsq_d = b.emit("pe", lambda e, dm=dm, tsq=tsq: e.matmul(
                    PS[0:1, 5 * 512: 5 * 512 + 512], V_C[:, 47:48],
                    B_Vf[:, 4352 + 512 * (dm % 2): 4864 + 512 * (dm % 2)],
                    start=(dm == 0), stop=(dm == CD - 1)),
                    waits=[tsq], inc="pe")
                prev_sq[dm % 2] = lsq_d

            stats_pend = None
            for dm in range(CD):
                bki = dm % 4
                w = [ddq_last, prep_done[set_], bank_free[bki]]
                for it2 in range(CI):
                    def mmd(e, bki=bki, it2=it2, dm=dm, set_=set_):
                        return e.matmul(bank(bki), ddq_slab(it2, dm),
                                        oq_slab(set_, it2),
                                        start=(it2 == 0), stop=(it2 == CI - 1))
                    last = b.emit("pe", mmd, waits=w if it2 == 0 else (),
                                  inc="pe" if it2 == CI - 1 else None)
                if stats_pend is not None:
                    emit_stats(*stats_pend)
                xr = B_Vf[:, 512 * (dm % 4): 512 * (dm % 4) + 512]
                tok_xr = dma("act", xr,
                             d_xln[128 * dm: 128 * (dm + 1), 512 * tc: 512 * (tc + 1)],
                             f"lx2{dm % 4}",
                             waits=[prev_xr[dm % 4]] + ([ddq_last] if tc == 0 and dm < 4 else []))
                def adds2(e, bki=bki, dm=dm, xr=xr):
                    return e.tensor_tensor(s2ring[:, 512 * dm: 512 * (dm + 1)],
                                           bank(bki), xr, ALU.add)
                ts2 = b.emit("dve", adds2, waits=[last, tok_xr, t_ones], inc="dve", chain=False)
                prev_xr[dm % 4] = ts2
                bank_free[bki] = ts2
                ring_toks.append(ts2)
                dsq = B_Vf[:, 4352 + 512 * (dm % 2): 4864 + 512 * (dm % 2)]
                tsq = b.emit("dve", lambda e, dm=dm, dsq=dsq: e.tensor_tensor(
                    dsq, s2ring[:, 512 * dm: 512 * (dm + 1)],
                    s2ring[:, 512 * dm: 512 * (dm + 1)], ALU.mult),
                    waits=[prev_sq[dm % 2]], inc="dve", chain=False)
                prev_sq[dm % 2] = tsq
                stats_pend = (dm, ts2, tsq)
            emit_stats(*stats_pend)
            if tc < 7:
                prep_oq(tc + 1)
            trm = b.emit("dve", lambda e: e.tensor_single_scalar(V_R[0:1, 0:512],
                         PS[0:1, 4 * 512: 4 * 512 + 512], 1.0 / D, ALU.mult),
                         waits=[lsum_d], inc="dve")
            trq = b.emit("dve", lambda e: e.tensor_single_scalar(V_R[0:1, 512:1024],
                         PS[0:1, 5 * 512: 5 * 512 + 512], 1.0 / D, ALU.mult),
                         waits=[lsq_d], inc="dve")
            bank_free[4] = trm
            bank_free[5] = trq
            trm2 = b.emit("dve", lambda e: e.tensor_tensor(V_R[0:1, 1024:1536], V_R[0:1, 0:512],
                                                    V_R[0:1, 0:512], ALU.mult),
                          waits=[trm], inc="dve")
            tvar = b.emit("dve", lambda e: e.tensor_tensor(V_R[0:1, 512:1024],
                                                           V_R[0:1, 512:1024],
                                                           V_R[0:1, 1024:1536], ALU.subtract),
                          waits=[trq, trm2], inc="dve")
            tln = b.emit("act", lambda e: e.activation(V_R[0:1, 1024:1536],
                                                       V_R[0:1, 512:1024], AF.Ln, bias=CONSTV[0:1, 0:1]),
                         waits=[tvar], inc="act")
            trstd = b.emit("act", lambda e: e.activation(V_R[0:1, 1024:1536],
                                                         V_R[0:1, 1024:1536], AF.Exp,
                                                         scale=-0.5), inc="act")
            tmr = b.emit("dve", lambda e: e.tensor_tensor(V_R[0:1, 1536:2048],
                                                          V_R[0:1, 0:512],
                                                          V_R[0:1, 1024:1536], ALU.mult),
                         waits=[trstd], inc="dve")
            tb6 = b.emit("pe", lambda e: e.matmul(PS[:, 6 * 512: 7 * 512], C_OR[:, :],
                                                  V_R[0:1, 1024:1536], start=True, stop=True),
                         waits=[trstd, bank_free[6]], inc="pe")
            tb7 = b.emit("pe", lambda e: e.matmul(PS[:, 7 * 512: 8 * 512], C_OR[:, :],
                                                  V_R[0:1, 1536:2048], start=True, stop=True),
                         waits=[tmr, bank_free[7]], inc="pe")
            last_ap = None
            for dm in range(CD):
                pp = dm % 2
                ap1 = B_Vf[:, 2048 + 512 * pp: 2048 + 512 * (pp + 1)]
                ap2 = B_Vf[:, 3072 + 512 * pp: 3072 + 512 * (pp + 1)]
                ta1 = b.emit("dve", lambda e, dm=dm, ap1=ap1: e.tensor_tensor(
                    ap1, s2ring[:, 512 * dm: 512 * (dm + 1)],
                    PS[:, 6 * 512: 7 * 512], ALU.mult),
                    waits=[tb6, ring_toks[dm], prev_ap[pp]], inc="dve")
                ta2 = b.emit("dve", lambda e, ap1=ap1, ap2=ap2: e.tensor_tensor(
                    ap2, ap1, PS[:, 7 * 512: 8 * 512],
                    ALU.subtract), waits=[tb7, out_toks[pp]], inc="dve")
                out_toks[pp] = dma("pool", d_out[128 * dm: 128 * (dm + 1),
                                                 512 * tc: 512 * (tc + 1)],
                                   ap2, f"sout{pp}", waits=[ta2])
                prev_ap[pp] = ta2
                last_ap = ta2
            bank_free[6] = last_ap
            bank_free[7] = last_ap

        b.emit("sp", lambda e: e.nop(), waits=[out_toks[0], out_toks[1]])
        b.finalize(block)
        return nc


def _prep_core_inputs(inputs, c):
    x = np.asarray(inputs["L0_attn_x"])
    Wq = np.asarray(inputs["L0_attn_W_q"])
    Wk = np.asarray(inputs["L0_attn_W_k"])
    Wv = np.asarray(inputs["L0_attn_W_v"])
    Wo = np.asarray(inputs["L0_attn_W_o"])
    wg = np.asarray(inputs["L0_ffn_w_gate"])
    wu = np.asarray(inputs["L0_ffn_w_up"])
    wd = np.asarray(inputs["L0_ffn_w_down"])
    perm = np.zeros(D, np.int64)
    for h in range(H):
        perm[DH * h: DH * h + 32] = DH * h + 2 * np.arange(32)
        perm[DH * h + 32: DH * h + 64] = DH * h + 2 * np.arange(32) + 1
    xc = x[2 * c: 2 * c + 2].reshape(T, D)
    return {
        "xT": np.ascontiguousarray(xc.T),
        "wqT": np.ascontiguousarray(Wq[perm].T),
        "wkT": np.ascontiguousarray(Wk[perm].T),
        "wvT": np.ascontiguousarray(Wv.T),
        "woT": np.ascontiguousarray(Wo.T),
        "wupT": np.ascontiguousarray(wu.T),
        "wgT": np.ascontiguousarray(wg.T),
        "wdT": np.ascontiguousarray(wd.T),
        "wupN": np.ascontiguousarray(wu),
        "wgN": np.ascontiguousarray(wg),
        "wdN": np.ascontiguousarray(wd),
    }


def kernel(**inputs):
    nc = build_program(DBG_STAGE)
    in_maps = [_prep_core_inputs(inputs, c) for c in range(8)]
    res = run_bass_kernel_spmd(nc, in_maps, core_ids=list(range(8)))
    outs = []
    for c in range(8):
        o = np.asarray(res.results[c]["outT"])
        outs.append(o.T.reshape(2, L, D))
    return np.concatenate(outs, axis=0).astype(np.float32)


# version 07
# speedup vs baseline: 1.4257x; 1.0253x over previous
import math
import numpy as np
import ml_dtypes
import concourse.bass as bass
import concourse.mybir as mybir
from concourse.bass_utils import run_bass_kernel_spmd

F32 = mybir.dt.float32
BF16 = mybir.dt.bfloat16
AF = mybir.ActivationFunctionType
ALU = mybir.AluOpType
AX = mybir.AxisListType

D, H, DH, L, NSEQ = 1024, 16, 64, 2048, 2
T = NSEQ * L
I = 3072
QB = 32
CD = D // 128
CI = I // 128
CMAG = float(3 * 2 ** 22)
QMA = 32767.0
EPS = 1e-5
ROPE_BASE = 10000.0
SAFE = 1.02 / 8.0

ENGS = ("pe", "act", "dve", "pool", "sp")
A_LIMIT = CI
M_LIMIT = 2
DBG_STAGE = None


class KB:
    def __init__(self, nc):
        self.nc = nc
        self.ops = {e: [] for e in ENGS}
        self.sems = {}
        self.cnt = {}
        self.waited = {e: {} for e in ENGS}

    def sem(self, name):
        if name not in self.sems:
            self.sems[name] = self.nc.alloc_semaphore(name)
            self.cnt[name] = 0
        return self.sems[name]

    def __init_dma(self):
        if not hasattr(self, "dma_sems"):
            self.dma_sems = set()

    def emit(self, eng, fn, waits=(), inc=None, inc_k=1, chain=True):
        self.__init_dma()
        if chain and eng in ("dve", "act", "pool"):
            waits = list(waits) + [self.tok(eng)]
        real = []
        for tokv in waits:
            if tokv is None:
                continue
            s, v = tokv
            if s in self.dma_sems:
                v = self.cnt[s]
            assert s in self.cnt and v <= self.cnt[s], f"future token {s}:{v}"
            if v > self.waited[eng].get(s, 0):
                real.append((self.sems[s], v))
                self.waited[eng][s] = v
        tok = None
        semh = None
        if inc is not None:
            self.sem(inc)
            if inc_k == 16:
                self.dma_sems.add(inc)
            self.cnt[inc] += inc_k
            tok = (inc, self.cnt[inc])
            semh = self.sems[inc]

        def run(e, real=real, fn=fn, semh=semh, k=inc_k):
            for (sh, v) in real:
                e.wait_ge(sh, v)
            ins = fn(e)
            if semh is not None:
                ins.then_inc(semh, k)
        self.ops[eng].append(run)
        return tok

    def tok(self, name):
        n = self.cnt.get(name, 0)
        return (name, n) if n > 0 else None

    def finalize(self, block):
        mp = {"pe": block.tensor, "act": block.scalar, "dve": block.vector,
              "pool": block.gpsimd, "sp": block.sync}
        for e in ENGS:
            ops = self.ops[e]

            def body(eng, ops=ops):
                for f in ops:
                    f(eng)
            mp[e](body)


def _rope_tables():
    pos = np.arange(L, dtype=np.float32)
    idx = np.arange(0, DH, 2, dtype=np.float32)
    freqs = np.exp(np.float32(-math.log(ROPE_BASE)) * idx / np.float32(DH)).astype(np.float32)
    ang = pos[:, None] * freqs[None, :]
    sin = np.sin(ang).astype(np.float32)
    cos = np.cos(ang).astype(np.float32)
    C = np.zeros((128, L), np.float32)
    S = np.zeros((128, L), np.float32)
    for p in range(128):
        i = p % 32
        C[p] = cos[:, i]
        S[p] = sin[:, i] * (-1.0 if (p % 64) < 32 else 1.0)
    return C, S


def build_program(dbg=None):
    nc = bass.Bass("TRN2", target_bir_lowering=False, debug=False)

    d_x = nc.dram_tensor("xT", [D, T], F32, kind="ExternalInput").ap()
    d_wq = nc.dram_tensor("wqT", [D, D], F32, kind="ExternalInput").ap()
    d_wk = nc.dram_tensor("wkT", [D, D], F32, kind="ExternalInput").ap()
    d_wv = nc.dram_tensor("wvT", [D, D], F32, kind="ExternalInput").ap()
    d_wo = nc.dram_tensor("woT", [D, D], F32, kind="ExternalInput").ap()
    d_wup = nc.dram_tensor("wupT", [D, I], F32, kind="ExternalInput").ap()
    d_wg = nc.dram_tensor("wgT", [D, I], F32, kind="ExternalInput").ap()
    d_wd = nc.dram_tensor("wdT", [I, D], F32, kind="ExternalInput").ap()
    d_wupN = nc.dram_tensor("wupN", [I, D], F32, kind="ExternalInput").ap()
    d_wgN = nc.dram_tensor("wgN", [I, D], F32, kind="ExternalInput").ap()
    d_wdN = nc.dram_tensor("wdN", [D, I], F32, kind="ExternalInput").ap()
    d_out = nc.dram_tensor("outT", [D, T], F32, kind="ExternalOutput").ap()

    d_xln = nc.dram_tensor("xlnd", [D, T], F32, kind="Internal").ap()
    d_up = nc.dram_tensor("upd", [I, T], BF16, kind="Internal").ap()
    d_gate = nc.dram_tensor("gated", [I, T], BF16, kind="Internal").ap()
    d_t = nc.dram_tensor("td", [I, T], BF16, kind="Internal").ap()
    d_P = nc.dram_tensor("Pd", [I, T], BF16, kind="Internal").ap()
    d_o = nc.dram_tensor("od", [I, T], BF16, kind="Internal").ap()
    d_srow = nc.dram_tensor("srowd", [1, H * L], F32, kind="Internal").ap()
    d_rrow = nc.dram_tensor("rrowd", [1, H * L], F32, kind="Internal").ap()
    cc_in = [nc.dram_tensor(f"ccin{i}", [1, 512], F32, kind="Internal").ap() for i in range(4)]
    cc_out = [nc.dram_tensor(f"ccout{i}", [1, 512], F32, kind="Internal").ap() for i in range(4)]

    Cnp, Snp = _rope_tables()
    i_tabC = nc.inline_tensor(Cnp.astype(ml_dtypes.bfloat16), "tabC").ap()
    i_tabS = nc.inline_tensor(Snp.astype(ml_dtypes.bfloat16), "tabS").ap()
    swap_np = np.zeros((128, 128), np.float32)
    for p in range(128):
        swap_np[p ^ 32, p] = 1.0
    i_swap = nc.inline_tensor(swap_np.astype(ml_dtypes.bfloat16), "swapM").ap()
    i_eyeF = nc.inline_tensor(np.eye(128, dtype=np.float32), "eyeF").ap()
    i_onescolB = nc.inline_tensor(np.ones((128, 1), ml_dtypes.bfloat16), "onescolB").ap()
    i_onesrowF = nc.inline_tensor(np.ones((1, 128), np.float32), "onesrowF").ap()
    constv_np = np.tile(np.array([[EPS, CMAG, -CMAG, 0.0]], np.float32), (128, 1))
    i_constv = nc.inline_tensor(constv_np, "constv").ap()
    ind_np = np.zeros((D, H), np.float32)
    for c in range(CD):
        ind_np[128 * c: 128 * c + 64, 2 * c] = 1.0
        ind_np[128 * c + 64: 128 * c + 128, 2 * c + 1] = 1.0
    i_ind = nc.inline_tensor(ind_np.astype(ml_dtypes.bfloat16), "indnq").ap()
    ind32_np = np.zeros((QB, CD * 128), np.float32)
    for c in range(CD):
        for p in range(128):
            ind32_np[4 * c + p // 32, 128 * c + p] = 1.0
    i_ind32 = nc.inline_tensor(ind32_np[:32].astype(np.float32), "ind32").ap()

    b = KB(nc)
    import contextlib
    with contextlib.ExitStack() as ctx:
        def sb(name, shape, dtype):
            return ctx.enter_context(nc.sbuf_tensor(name, shape, dtype))
        B_X = sb("B_X", [128, 16384], BF16)
        B_Q = sb("B_Q", [128, 16384], BF16)
        B_K = sb("B_K", [128, 16384], BF16)
        B_V = sb("B_V", [128, 16640], BF16)
        B_W = sb("B_W", [128, 16384], BF16)
        B_M = sb("B_M", [128, 12288], BF16)
        B_P = B_M
        V_S = sb("V_S", [128, 16], F32)
        V_C = sb("V_C", [128, 48], F32)
        V_R = sb("V_R", [1, 2048], F32)
        V_R2 = sb("V_R2", [1, 1024], F32)
        V_SC = sb("V_SC", [1, 32], F32)
        C_T = sb("C_T", [128, 2052], BF16)
        C_T2 = sb("C_T2", [128, 2048], BF16)
        C_SW = sb("C_SW", [128, 128], BF16)
        C_EYE = sb("C_EYE", [128, 128], F32)
        C_IND = sb("C_IND", [128, 128], BF16)
        C_OR = sb("C_OR", [1, 128], F32)
        CONSTV = sb("CONSTV", [128, 4], F32)
        nc.const_aps.register(F32, EPS, CONSTV[:, 0:1])
        nc.const_aps.register(F32, CMAG, CONSTV[:, 1:2])
        nc.const_aps.register(F32, -CMAG, CONSTV[:, 2:3])
        C_I32 = None
        PS = ctx.enter_context(nc.psum_tensor("PS", [128, 4096], F32))
        block = ctx.enter_context(nc.Block())

        B_Mf = B_M.ap().bitcast(F32)
        B_Wf = B_W.ap().bitcast(F32)
        B_Vf = B_V.ap().bitcast(F32)

        def bank(i, n=512):
            return PS[:, 512 * i: 512 * i + n]

        def dma(q, out_ap, in_ap, sem="ld", waits=()):
            return b.emit(q, lambda e: e.dma_start(out=out_ap, in_=in_ap),
                          waits=waits, inc=sem, inc_k=16)

        dma("sp", C_T[:, 0:2048], i_tabC, "ldc")
        dma("sp", C_T2[:, :], i_tabS, "ldc")
        dma("sp", C_T[:, 2048:2049], i_onescolB, "ldc")
        dma("sp", C_SW[:, :], i_swap, "ldc")
        dma("sp", C_EYE[:, :], i_eyeF, "ldc")
        dma("sp", C_IND.ap()[:, 0:128].rearrange("p (c h) -> p c h", c=CD),
            i_ind.rearrange("(c p) h -> p c h", p=128), "ldc")
        dma("sp", CONSTV[:, :], i_constv, "ldc")
        tok_consts = dma("sp", C_OR[:, :], i_onesrowF, "ldc")
        t_ones = b.emit("dve", lambda e: e.memset(V_C[:, 47:48], 1.0), inc="dve")
        CCST = V_R[0:1, 0:512]
        GMX = V_R[0:1, 512:1024]

        def load_xseq(j, waits):
            tk = None
            for c in range(CD):
                tk = dma("pool", B_X[:, 2048 * c: 2048 * (c + 1)],
                         d_x[128 * c: 128 * (c + 1), L * j: L * (j + 1)],
                         "gx", waits)
            return tk

        def load_w(dr, slot, waits):
            return dma("pool",
                       B_W.ap()[:, 8192 * slot: 8192 * (slot + 1)].rearrange(
                           "p (c n) -> p c n", c=CD),
                       dr.rearrange("(c p) n -> p c n", p=128), f"gw{slot}", waits)

        tok_x = load_xseq(0, ())
        tok_wq = load_w(d_wq, 0, ())
        tok_wk = load_w(d_wk, 1, ())

        bank_free = [None] * 8
        tok_oproj_done = None
        tok_vproj_done = None
        pv_free = [None, None]
        tok_dbg_last = None

        if dbg == "simA":
            tok_rope = None
            tok_oproj_done = b.tok("pe")
            tok_vproj_done = b.tok("pe")
            b.emit("dve", lambda e: e.memset(V_S[:, :], 1.0), inc="dve")
            b.emit("dve", lambda e: e.memset(B_Q[:, :], 0.0), inc="dve")
            tz = b.emit("dve", lambda e: e.memset(B_K[:, :], 0.0), inc="dve")
            for c_ in range(CD):
                dma("sp", d_xln[128 * c_: 128 * (c_ + 1), :],
                    B_Q.ap().bitcast(F32)[:, 0:T], "sxln", waits=[tz])
        for j in range([] if dbg == "simA" else range(NSEQ) and NSEQ or NSEQ) if False else range(0 if dbg == "simA" else NSEQ):
            if j == 1:
                tok_x = load_xseq(1, [tok_oproj_done])
                tok_wq = load_w(d_wq, 0, [tok_vproj_done, b.tok("dve")])
                tok_wk = load_w(d_wk, 1, [tok_oproj_done])

            def proj_qk(wtok, slot, dst):
                last_mm = None
                last_ev = None
                for cq in range(CD):
                    for t4 in range(4):
                        bki = (cq * 4 + t4) % 4
                        w = [wtok, tok_x, tok_consts, bank_free[bki]]
                        for c in range(CD):
                            def mm(e, bki=bki, c=c, cq=cq, t4=t4, slot=slot):
                                return e.matmul(
                                    bank(bki),
                                    B_W[:, 8192 * slot + 1024 * c + 128 * cq:
                                        8192 * slot + 1024 * c + 128 * cq + 128],
                                    B_X[:, 2048 * c + 512 * t4: 2048 * c + 512 * (t4 + 1)],
                                    start=(c == 0), stop=(c == CD - 1))
                            last_mm = b.emit("pe", mm, waits=w if c == 0 else (),
                                             inc="pe" if c == CD - 1 else None)
                        def ev(e, bki=bki, cq=cq, t4=t4, dst=dst):
                            return e.copy(dst[:, 2048 * cq + 512 * t4: 2048 * cq + 512 * (t4 + 1)],
                                          bank(bki))
                        last_ev = b.emit("act", ev, waits=[last_mm], inc="act")
                        bank_free[bki] = last_ev
                return last_ev, last_mm

            tok_qe, tok_qmm = proj_qk(tok_wq, 0, B_Q)
            tok_ke, tok_kmm = proj_qk(tok_wk, 1, B_K)
            tok_wv = load_w(d_wv, 0, [tok_qmm])
            if dbg == "qraw" and j == 0:
                tdl = None
                for c in range(CD):
                    tdl = dma("pool", d_out[128 * c: 128 * (c + 1), 0:L],
                              B_Q[:, 2048 * c: 2048 * (c + 1)], "gst", waits=[tok_qe])
                b.emit("sp", lambda e: e.nop(), waits=[tdl])
                b.finalize(block)
                return nc

            prev_r1 = None
            prev_r3 = None
            for (src, etok) in ((B_Q, tok_qe), (B_K, tok_ke)):
                for cq in range(CD):
                    sw_last = None
                    for t4 in range(4):
                        bki = 4 + (t4 % 2)
                        w = [etok, tok_consts, bank_free[bki], prev_r1]
                        def mmsw(e, bki=bki, src=src, cq=cq, t4=t4):
                            return e.matmul(bank(bki), C_SW[:, :],
                                            src[:, 2048 * cq + 512 * t4: 2048 * cq + 512 * (t4 + 1)],
                                            start=True, stop=True)
                        tmm = b.emit("pe", mmsw, waits=w, inc="pe")
                        def evsw(e, bki=bki, t4=t4):
                            return e.copy(B_M[:, 512 * t4: 512 * (t4 + 1)], bank(bki))
                        sw_last = b.emit("act", evsw, waits=[tmm, prev_r1], inc="act")
                        bank_free[bki] = sw_last
                    def r1(e, src=src, cq=cq):
                        return e.tensor_tensor(B_M[:, 2048:4096],
                                               src[:, 2048 * cq: 2048 * (cq + 1)],
                                               C_T[:, 0:2048], ALU.mult)
                    b.emit("dve", r1, waits=[sw_last], inc="dve")
                    def r2(e):
                        return e.tensor_tensor(B_M[:, 4096:6144], B_M[:, 0:2048],
                                               C_T2[:, :], ALU.mult)
                    prev_r1 = b.emit("dve", r2, inc="dve")
                    def r3(e, src=src, cq=cq):
                        return e.tensor_tensor(src[:, 2048 * cq: 2048 * (cq + 1)],
                                               B_M[:, 2048:4096], B_M[:, 4096:6144],
                                               ALU.add)
                    prev_r3 = b.emit("dve", r3, inc="dve")
            tok_rope = prev_r3

            def vones(e):
                v = B_V.ap().rearrange("p (s w) -> p s w", w=65)
                return e.memset(v[:, :, 64:65], 1.0)
            tok_vones = b.emit("pool", vones,
                               waits=[b.tok("pe") if j == 1 else None], inc="pool")
            vlast_ev = None
            for st in range(16):
                for d2 in range(2):
                    bki = (st * 2 + d2) % 4
                    w = [tok_wv, bank_free[bki]]
                    for c in range(CD):
                        def mmv(e, bki=bki, c=c, st=st, d2=d2):
                            return e.matmul(
                                bank(bki),
                                B_X[:, 2048 * c + 128 * st: 2048 * c + 128 * (st + 1)],
                                B_W[:, 1024 * c + 512 * d2: 1024 * c + 512 * (d2 + 1)],
                                start=(c == 0), stop=(c == CD - 1))
                        last = b.emit("pe", mmv, waits=w if c == 0 else (),
                                      inc="pe" if c == CD - 1 else None)
                    def evv(e, bki=bki, st=st, d2=d2):
                        dst = B_V.ap()[:, 1040 * st: 1040 * (st + 1)].rearrange(
                            "p (h w) -> p h w", w=65)[:, 8 * d2: 8 * (d2 + 1), 0:64]
                        src = bank(bki).rearrange("p (h w) -> p h w", w=64)
                        return e.copy(dst, src)
                    vlast_ev = b.emit("act", evv, waits=[last, tok_vones], inc="act")
                    bank_free[bki] = vlast_ev
            tok_vproj_done = b.tok("pe")
            tok_wo = load_w(d_wo, 1, [tok_kmm])

            prev_mmn = None
            nsc = B_W[:, 0:512]
            for (src, base_col) in ((B_Q, 0), (B_K, 4)):
                for t4 in range(4):
                    w6 = [bank_free[6]]
                    last = None
                    for cq in range(CD):
                        def sqr(e, src=src, cq=cq, t4=t4):
                            return e.tensor_tensor(
                                nsc,
                                src[:, 2048 * cq + 512 * t4: 2048 * cq + 512 * (t4 + 1)],
                                src[:, 2048 * cq + 512 * t4: 2048 * cq + 512 * (t4 + 1)],
                                ALU.mult)
                        tsq = b.emit("dve", sqr, waits=[tok_rope, prev_mmn, tok_vproj_done], inc="dve")
                        def mmn(e, cq=cq):
                            return e.matmul(PS[0:16, 6 * 512: 6 * 512 + 512],
                                            C_IND[:, 16 * cq: 16 * (cq + 1)],
                                            nsc,
                                            start=(cq == 0), stop=(cq == CD - 1))
                        last = b.emit("pe", mmn, waits=[tsq] + (w6 if cq == 0 else []),
                                      inc="pe")
                        prev_mmn = last
                    col = base_col + t4
                    def red(e, col=col):
                        return e.tensor_reduce(V_S[0:16, col:col + 1],
                                               PS[0:16, 6 * 512: 6 * 512 + 512],
                                               AX.X, ALU.max, apply_absolute_value=True)
                    rt = b.emit("dve", red, waits=[last], inc="dve")
                    if dbg == "ind2" and j == 0 and src is B_Q and t4 == 0:
                        rt = b.emit("dve", lambda e: e.tensor_copy(
                            B_Mf[:, 512:1024][0:16, 0:512],
                            PS[0:16, 6 * 512: 6 * 512 + 512]), inc="dve")
                    bank_free[6] = rt
            tn1 = b.emit("dve", lambda e: e.tensor_reduce(V_S[0:16, 8:9], V_S[0:16, 0:4],
                                                          AX.X, ALU.max), inc="dve")
            tn2 = b.emit("dve", lambda e: e.tensor_reduce(V_S[0:16, 9:10], V_S[0:16, 4:8],
                                                          AX.X, ALU.max),
                         waits=[tn1], inc="dve")
            t3_ = b.emit("dve", lambda e: e.tensor_tensor(V_S[0:16, 10:11], V_S[0:16, 8:9],
                                                          V_S[0:16, 9:10], ALU.mult),
                         waits=[tn2], inc="dve")
            t4_ = b.emit("act", lambda e: e.activation(V_S[0:16, 11:12], V_S[0:16, 10:11],
                                                       AF.Sqrt), waits=[t3_], inc="act")
            for t4 in range(4):
                w6 = [bank_free[6]]
                last = None
                for cq in range(CD):
                    def dgm(e, cq=cq, t4=t4):
                        return e.tensor_tensor(
                            nsc,
                            B_Q[:, 2048 * cq + 512 * t4: 2048 * cq + 512 * (t4 + 1)],
                            B_K[:, 2048 * cq + 512 * t4: 2048 * cq + 512 * (t4 + 1)],
                            ALU.mult)
                    tsq = b.emit("dve", dgm, waits=[prev_mmn, tn2], inc="dve")
                    def mmn2(e, cq=cq):
                        return e.matmul(PS[0:16, 6 * 512: 6 * 512 + 512],
                                        C_IND[:, 16 * cq: 16 * (cq + 1)],
                                        nsc, start=(cq == 0), stop=(cq == CD - 1))
                    last = b.emit("pe", mmn2, waits=[tsq] + (w6 if cq == 0 else []),
                                  inc="pe")
                    prev_mmn = last
                def red2(e, t4=t4):
                    return e.tensor_reduce(V_S[0:16, t4:t4 + 1],
                                           PS[0:16, 6 * 512: 6 * 512 + 512],
                                           AX.X, ALU.max)
                rt = b.emit("dve", red2, waits=[last], inc="dve")
                bank_free[6] = rt
            tdg = b.emit("dve", lambda e: e.tensor_reduce(V_S[0:16, 13:14], V_S[0:16, 0:4],
                                                          AX.X, ALU.max),
                         waits=[bank_free[6]], inc="dve")
            tnA = b.emit("dve", lambda e: e.tensor_single_scalar(V_S[0:16, 14:15],
                         V_S[0:16, 13:14], 0.125, ALU.mult), waits=[tdg], inc="dve")
            tnB = b.emit("dve", lambda e: e.tensor_scalar(V_S[0:16, 15:16],
                         V_S[0:16, 11:12], SAFE, -85.0, ALU.mult, ALU.add),
                         waits=[t4_], inc="dve")
            tnC = b.emit("dve", lambda e: e.tensor_tensor(V_S[0:16, 10:11], V_S[0:16, 14:15],
                                                          V_S[0:16, 15:16], ALU.max),
                         waits=[tnA, tnB], inc="dve")
            t5_ = b.emit("dve", lambda e: e.tensor_single_scalar(V_S[0:16, 12:13],
                         V_S[0:16, 10:11], -1.0, ALU.mult),
                         waits=[tnC], inc="dve")
            w7 = [t5_, bank_free[7]]
            ttr = b.emit("pe", lambda e: e.transpose(PS[0:1, 7 * 512: 7 * 512 + 16],
                                                     V_S[0:16, 12:13], C_EYE[0:16, 0:16]),
                         waits=w7, inc="pe")
            tcp_ = b.emit("dve", lambda e: e.tensor_copy(V_SC[0:1, 0:16],
                                                         PS[0:1, 7 * 512: 7 * 512 + 16]),
                          waits=[ttr], inc="dve")
            tbc_ = b.emit("pe", lambda e: e.matmul(PS[:, 7 * 512: 7 * 512 + 16],
                                                   C_OR[:, :], V_SC[0:1, 0:16],
                                                   start=True, stop=True),
                          waits=[tcp_], inc="pe")
            tok_bias = b.emit("act", lambda e: e.copy(V_C[:, 0:16],
                                                      PS[:, 7 * 512: 7 * 512 + 16]),
                              waits=[tbc_], inc="act")
            bank_free[7] = tok_bias

            if dbg == "ind2" and j == 0:
                tc2 = dma("sp", d_out[0:16, 0:512], B_Mf[0:16, 512:1024], "st",
                          waits=[b.tok("dve")])
                tc3 = dma("sp", d_out[0:16, 512:528], V_S[0:16, 0:16], "st",
                          waits=[b.tok("dve")])
                b.emit("sp", lambda e: e.nop(), waits=[tc2, tc3])
                b.finalize(block)
                return nc

            if dbg == "ind" and j == 0:
                td1 = dma("pool", d_out[0:128, 0:128], C_IND[:, 0:128], "gst",
                          waits=[b.tok("dve"), b.tok("pe")])
                td2 = dma("pool", d_out[0:128, 128:640], B_M[:, 0:512], "gst",
                          waits=[b.tok("dve")])
                tcpy = b.emit("dve", lambda e: e.tensor_copy(
                    B_Mf[:, 512:1024][0:16, 0:512], PS[0:16, 6 * 512: 6 * 512 + 512]),
                    waits=[b.tok("pe")], inc="dve")
                td3 = dma("sp", d_out[0:16, 1024:1536], B_Mf[0:16, 512:1024], "st",
                          waits=[tcpy])
                b.emit("sp", lambda e: e.nop(), waits=[td1, td2, td3])
                b.finalize(block)
                return nc

            if dbg == "bias" and j == 0:
                tdbg1 = dma("sp", d_out[0:128, 0:48], V_C[:, :], "st", waits=[tok_bias])
                tdbg2 = dma("sp", d_out[0:16, 64:80], V_S[0:16, 0:16], "st",
                            waits=[tok_bias, b.tok("dve")])
                b.emit("sp", lambda e: e.nop(), waits=[tdbg1, tdbg2])
                b.finalize(block)
                return nc

            groups = [(0, 3), (3, 3), (6, 3), (9, 3), (12, 3), (15, 1)]
            prev_bc7 = tok_bias
            pv_free = [bank_free[6], None]
            sum_dma = [None, None]
            prev_rdist = [None, None]
            prev_rr = [None] * 4
            rdist_done = {}
            for h in range(H):
                p0 = 64 * (h % 2)
                cq = h // 2
                qoff = 2048 * cq
                for lc in range(4):
                    pv_last = None
                    pending = None

                    def emit_pv(pend):
                        gi, s0, glen, pslot, texp = pend
                        pv = None
                        for si in range(glen):
                            stile = s0 + si
                            wpv = [texp]
                            if gi == 0 and si == 0:
                                wpv += [vlast_ev, tok_vones] + pv_free
                            def mmpv(e, stile=stile, si=si, pslot=pslot, h=h, gi=gi, glen=glen):
                                return e.matmul(
                                    PS[0:65, 6 * 512: 6 * 512 + 512],
                                    B_V[:, 1040 * stile + 65 * h: 1040 * stile + 65 * (h + 1)],
                                    B_P[:, pslot + 512 * si: pslot + 512 * (si + 1)],
                                    start=(gi == 0 and si == 0),
                                    stop=(gi == len(groups) - 1 and si == glen - 1))
                            pv = b.emit("pe", mmpv, waits=wpv,
                                        inc="pe" if (gi == len(groups) - 1 and si == glen - 1) else None)
                        return pv

                    for gi, (s0, glen) in enumerate(groups):
                        gb = 0 if gi % 2 == 0 else 3
                        last_qk = None
                        for si in range(glen):
                            stile = s0 + si
                            wqk = [tok_rope, bank_free[gb]] if si == 0 else ()
                            def mmqk(e, gb=gb, si=si, stile=stile, p0=p0, qoff=qoff, lc=lc):
                                return e.matmul(
                                    PS[:, 512 * (gb + si): 512 * (gb + si) + 512],
                                    B_K[p0: p0 + 64, qoff + 128 * stile: qoff + 128 * (stile + 1)],
                                    B_Q[p0: p0 + 64, qoff + 512 * lc: qoff + 512 * (lc + 1)],
                                    start=True, stop=True)
                            last_qk = b.emit("pe", mmqk, waits=wqk,
                                             inc="pe" if si == glen - 1 else None)
                        pslot = (gi % 2) * 1536
                        def expg(e, gb=gb, glen=glen, pslot=pslot, h=h):
                            return e.activation(B_P[:, pslot: pslot + 512 * glen],
                                                PS[:, 512 * gb: 512 * gb + 512 * glen],
                                                AF.Exp, bias=V_C[:, h:h + 1], scale=0.125)
                        texp = b.emit("act", expg, waits=[last_qk, tok_bias], inc="act")
                        bank_free[gb] = texp
                        if pending is not None:
                            emit_pv(pending)
                        pending = (gi, s0, glen, pslot, texp)
                    pv_last = emit_pv(pending)
                    def evpv(e, h=h, lc=lc, cq=cq, p0=p0):
                        return e.copy(B_X[p0: p0 + 64, 2048 * cq + 512 * lc: 2048 * cq + 512 * (lc + 1)],
                                      PS[0:64, 6 * 512: 6 * 512 + 512])
                    tev = b.emit("act", evpv, waits=[pv_last], inc="act")
                    sslot = 512 * (lc % 2)
                    def evsum(e, sslot=sslot):
                        return e.tensor_copy(V_R2[0:1, sslot: sslot + 512],
                                             PS[64:65, 6 * 512: 6 * 512 + 512])
                    tsum = b.emit("dve", evsum, waits=[pv_last, sum_dma[lc % 2]], inc="dve")
                    sum_dma[lc % 2] = dma("sp",
                                          d_srow[0:1, 2048 * h + 512 * lc: 2048 * h + 512 * (lc + 1)],
                                          V_R2[0:1, sslot: sslot + 512], f"ss{lc % 2}", waits=[tsum])
                    pv_free = [tev, tsum]
                    if dbg == "attn2":
                        trc2 = b.emit("dve", lambda e, sslot=sslot: e.reciprocal(
                            V_R[0:1, 1536:2048], V_R2[0:1, sslot: sslot + 512]),
                            waits=[tsum, prev_bc7], inc="dve")
                        wb2 = [trc2, bank_free[7]]
                        def bcr2(e):
                            return e.matmul(PS[0:64, 7 * 512: 7 * 512 + 512], C_OR[0:1, 0:64],
                                            V_R[0:1, 1536:2048], start=True, stop=True)
                        prev_bc7 = b.emit("pe", bcr2, waits=wb2, inc="pe")
                        def nrm2_(e, h=h, lc=lc, cq=cq, p0=p0):
                            sl = B_X[p0: p0 + 64, 2048 * cq + 512 * lc: 2048 * cq + 512 * (lc + 1)]
                            return e.tensor_tensor(sl, sl, PS[0:64, 7 * 512: 7 * 512 + 512],
                                                   ALU.mult)
                        norm_tt = b.emit("dve", nrm2_, waits=[prev_bc7, tev], inc="dve")
                        bank_free[7] = norm_tt
                        pv_free = [tev, norm_tt]
                if dbg in ("attnraw", "attn2"):
                    continue
                hp = h % 2
                tok_sr = [b.tok("ss0"), b.tok("ss1")]
                sd = B_Wf[:, 256 + 32 * hp: 272 + 32 * hp]
                sd2 = B_Wf[:, 272 + 32 * hp: 288 + 32 * hp]
                tok_d1 = dma("sp", sd,
                             d_srow[0:1, 2048 * h: 2048 * (h + 1)].rearrange(
                                 "o (p n) -> (o p) n", p=128), f"lsd{hp}",
                             waits=tok_sr + [prev_rdist[hp]])
                tok_rc = b.emit("dve", lambda e, sd=sd, sd2=sd2: e.reciprocal(sd2, sd),
                                waits=[tok_d1], inc="dve")
                tok_d2 = dma("sp",
                             d_rrow[0:1, 2048 * h: 2048 * (h + 1)].rearrange(
                                 "o (p n) -> (o p) n", p=128), sd2, f"sdist{hp}",
                             waits=[tok_rc])
                prev_rdist[hp] = tok_d2
                rdist_done[h] = tok_d2
                for hn in ([h - 1] if h > 0 else []) + ([h] if h == H - 1 else []):
                    cqn = hn // 2
                    p0n = 64 * (hn % 2)
                    norm_tt = None
                    d3toks = []
                    for lc in range(4):
                        rstage = B_Wf[0:1, 512 + 512 * lc: 1024 + 512 * lc]
                        d3toks.append(dma("sp", rstage,
                                     d_rrow[0:1, 2048 * hn + 512 * lc: 2048 * hn + 512 * lc + 512],
                                     f"lrr{lc}", waits=[rdist_done[hn], prev_rr[lc]]))
                    for lc in range(4):
                        rstage = B_Wf[0:1, 512 + 512 * lc: 1024 + 512 * lc]
                        wb = [d3toks[lc], bank_free[7]]
                        def bcr(e, rstage=rstage):
                            return e.matmul(PS[0:64, 7 * 512: 7 * 512 + 512], C_OR[0:1, 0:64],
                                            rstage, start=True, stop=True)
                        prev_bc7 = b.emit("pe", bcr, waits=wb, inc="pe")
                        def nrm(e, hn=hn, lc=lc, cqn=cqn, p0n=p0n):
                            sl = B_X[p0n: p0n + 64, 2048 * cqn + 512 * lc: 2048 * cqn + 512 * (lc + 1)]
                            return e.tensor_tensor(sl, sl, PS[0:64, 7 * 512: 7 * 512 + 512],
                                                   ALU.mult)
                        norm_tt = b.emit("dve", nrm, waits=[prev_bc7, pv_free[0]], inc="dve")
                        bank_free[7] = norm_tt
                        prev_rr[lc] = norm_tt
            tok_attn_norm = b.tok("dve")

            if dbg == "attnraw" and j == 0:
                for c in range(CD):
                    tok_dbg_last = dma("pool", d_out[128 * c: 128 * (c + 1), 0:L],
                                       B_X[:, 2048 * c: 2048 * (c + 1)], "gst",
                                       waits=[b.tok("act"), b.tok("ss0"), b.tok("ss1")])
                tok_dbg_last = dma("sp",
                                   d_out.rearrange("d t -> (d t)")[D * L: D * L + H * L].rearrange("(o n) -> o n", o=1),
                                   d_srow, "st", waits=[b.tok("st"), tok_dbg_last])
                b.emit("sp", lambda e: e.nop(), waits=[tok_dbg_last])
                b.finalize(block)
                return nc

            if dbg in ("q", "attn", "attn2"):
                srcbuf = B_Q if dbg == "q" else B_X
                wtok = tok_rope if dbg == "q" else tok_attn_norm
                for c in range(CD):
                    tok_dbg_last = dma("pool", d_out[128 * c: 128 * (c + 1), L * j: L * (j + 1)],
                                       srcbuf[:, 2048 * c: 2048 * (c + 1)], "gst", waits=[wtok])
                tok_oproj_done = b.tok("pe")
                if j == NSEQ - 1:
                    b.emit("sp", lambda e: e.nop(), waits=[tok_dbg_last])
                    b.finalize(block)
                    return nc
                continue

            s1ring = B_Wf
            prev_xr = [None] * 4
            prev_ap1 = [None, None]
            prev_xlndma = [None, None]
            lsum = lsq = None
            prev_sq1 = [None, None]

            def emit_stats1(dm, ts1, tsq):
                nonlocal lsum, lsq, prev_sq1
                wst = [ts1] + (ssum_w if dm == 0 else [])
                lsum = b.emit("pe", lambda e, dm=dm: e.matmul(
                    PS[0:1, 4 * 512: 4 * 512 + 512], V_C[:, 47:48],
                    s1ring[:, 512 * dm: 512 * (dm + 1)],
                    start=(dm == 0), stop=(dm == CD - 1)), waits=wst, inc="pe")
                lsq = b.emit("pe", lambda e, dm=dm: e.matmul(
                    PS[0:1, 5 * 512: 5 * 512 + 512], V_C[:, 47:48],
                    B_Mf[:, 4096 + 512 * (dm % 2): 4608 + 512 * (dm % 2)],
                    start=(dm == 0), stop=(dm == CD - 1)), waits=[tsq], inc="pe")
                prev_sq1[dm % 2] = lsq

            for t4 in range(4):
                ssum_w = [bank_free[4], bank_free[5]]
                ring_toks = []
                stats_pend = None
                for dm in range(CD):
                    bki = dm % 3
                    w = [tok_wo, tok_attn_norm, bank_free[bki]]
                    for ca in range(CD):
                        def mmo(e, bki=bki, ca=ca, dm=dm, t4=t4):
                            return e.matmul(
                                bank(bki),
                                B_W[:, 8192 + 1024 * ca + 128 * dm:
                                    8192 + 1024 * ca + 128 * dm + 128],
                                B_X[:, 2048 * ca + 512 * t4: 2048 * ca + 512 * (t4 + 1)],
                                start=(ca == 0), stop=(ca == CD - 1))
                        last = b.emit("pe", mmo, waits=w if ca == 0 else (),
                                      inc="pe" if ca == CD - 1 else None)
                    if stats_pend is not None:
                        emit_stats1(*stats_pend)
                    r = dm % 4
                    xr_f = B_Mf[:, 512 * r: 512 * (r + 1)]
                    tok_xr = dma("act", xr_f,
                                 d_x[128 * dm: 128 * (dm + 1),
                                     L * j + 512 * t4: L * j + 512 * (t4 + 1)],
                                 f"lxr{r}", waits=[prev_xr[r]])
                    def adds1(e, bki=bki, dm=dm, xr_f=xr_f):
                        return e.tensor_tensor(s1ring[:, 512 * dm: 512 * (dm + 1)],
                                               bank(bki), xr_f, ALU.add)
                    ts1 = b.emit("dve", adds1, waits=[last, tok_xr, t_ones], inc="dve", chain=False)
                    prev_xr[r] = ts1
                    bank_free[bki] = ts1
                    ring_toks.append(ts1)
                    dsq = B_Mf[:, 4096 + 512 * (dm % 2): 4608 + 512 * (dm % 2)]
                    tsq = b.emit("dve", lambda e, dm=dm, dsq=dsq: e.tensor_tensor(
                        dsq, s1ring[:, 512 * dm: 512 * (dm + 1)],
                        s1ring[:, 512 * dm: 512 * (dm + 1)], ALU.mult),
                        waits=[prev_sq1[dm % 2]], inc="dve", chain=False)
                    stats_pend = (dm, ts1, tsq)
                emit_stats1(*stats_pend)
                trm = b.emit("dve", lambda e: e.tensor_single_scalar(V_R[0:1, 0:512],
                             PS[0:1, 4 * 512: 4 * 512 + 512], 1.0 / D, ALU.mult),
                             waits=[lsum], inc="dve")
                trq = b.emit("dve", lambda e: e.tensor_single_scalar(V_R[0:1, 512:1024],
                             PS[0:1, 5 * 512: 5 * 512 + 512], 1.0 / D, ALU.mult),
                             waits=[lsq], inc="dve")
                bank_free[4] = trm
                bank_free[5] = trq
                trm2 = b.emit("dve", lambda e: e.tensor_tensor(V_R[0:1, 1024:1536], V_R[0:1, 0:512],
                                                        V_R[0:1, 0:512], ALU.mult),
                              waits=[trm], inc="dve")
                tvar = b.emit("dve", lambda e: e.tensor_tensor(V_R[0:1, 512:1024],
                                                               V_R[0:1, 512:1024],
                                                               V_R[0:1, 1024:1536],
                                                               ALU.subtract),
                              waits=[trq, trm2], inc="dve")
                tln = b.emit("act", lambda e: e.activation(V_R[0:1, 1024:1536],
                                                           V_R[0:1, 512:1024], AF.Ln,
                                                           bias=CONSTV[0:1, 0:1]),
                             waits=[tvar], inc="act")
                trstd = b.emit("act", lambda e: e.activation(V_R[0:1, 1024:1536],
                                                             V_R[0:1, 1024:1536], AF.Exp,
                                                             scale=-0.5), inc="act")
                tmr = b.emit("dve", lambda e: e.tensor_tensor(V_R[0:1, 1536:2048],
                                                              V_R[0:1, 0:512],
                                                              V_R[0:1, 1024:1536], ALU.mult),
                             waits=[trstd], inc="dve")
                tb6 = b.emit("pe", lambda e: e.matmul(PS[:, 6 * 512: 7 * 512], C_OR[:, :],
                                                      V_R[0:1, 1024:1536],
                                                      start=True, stop=True),
                             waits=[trstd, bank_free[6]], inc="pe")
                tb7 = b.emit("pe", lambda e: e.matmul(PS[:, 7 * 512: 8 * 512], C_OR[:, :],
                                                      V_R[0:1, 1536:2048],
                                                      start=True, stop=True),
                             waits=[tmr, bank_free[7]], inc="pe")
                last_ap = None
                for dm in range(CD):
                    pp = dm % 2
                    t_f = B_Mf[:, 2048 + 512 * pp: 2560 + 512 * pp]
                    xln_f = B_Mf[:, 3072 + 512 * pp: 3584 + 512 * pp]
                    def ap1(e, dm=dm, t_f=t_f):
                        return e.tensor_tensor(t_f, s1ring[:, 512 * dm: 512 * (dm + 1)],
                                               PS[:, 6 * 512: 7 * 512], ALU.mult)
                    ta1 = b.emit("dve", ap1, waits=[tb6, prev_ap1[pp]], inc="dve")
                    def ap2(e, t_f=t_f, xln_f=xln_f):
                        return e.tensor_tensor(xln_f, t_f, PS[:, 7 * 512: 8 * 512],
                                               ALU.subtract)
                    ta2 = b.emit("dve", ap2, waits=[tb7, prev_xlndma[pp]], inc="dve")
                    prev_xlndma[pp] = dma("sp", d_xln[128 * dm: 128 * (dm + 1),
                                          L * j + 512 * t4: L * j + 512 * (t4 + 1)],
                                          xln_f, f"sxln{pp}", waits=[ta2])
                    first = (j == 0 and t4 == 0 and dm == 0)
                    col = 13 if first else 14
                    def redx(e, col=col, xln_f=xln_f):
                        return e.tensor_reduce(V_S[:, col:col + 1], xln_f,
                                               AX.X, ALU.max, apply_absolute_value=True)
                    trd = b.emit("dve", redx, inc="dve")
                    prev_ap1[pp] = trd
                    if not first:
                        b.emit("dve", lambda e: e.tensor_tensor(V_S[:, 13:14], V_S[:, 13:14],
                                                                V_S[:, 14:15], ALU.max),
                               waits=[trd], inc="dve")
                    last_ap = trd
                bank_free[6] = last_ap
                bank_free[7] = last_ap
            tok_oproj_done = b.tok("pe")

        if dbg == "xln":
            tok = dma("sp", d_out, d_xln, "st", waits=[b.tok("sxln0"), b.tok("sxln1"), b.tok("dve")])
            b.emit("sp", lambda e: e.nop(), waits=[tok])
            b.finalize(block)
            return nc

        C_I32 = C_T.ap().bitcast(F32)[0:32, 0:1024]
        tok_i32 = dma("sp", C_I32, i_ind32, "ldc", waits=[b.tok("dve"), b.tok("act"), tok_consts])
        SN = {0: B_Mf[:, 2048:2816], 1: B_Mf[:, 2816:3584], 2: B_Mf[:, 3584:4352]}
        tok_vfree = b.tok("pe")
        tok_vfree_a = b.tok("act")
        mats = ((d_wupN, CI, 1), (d_wgN, CI, 1), (d_wdN, CD, 3))
        prev_nst = [None] * 4
        sn_tok = {}
        qrot = ("sp", "act", "pool")
        ldi = 0
        for mi, (dN, nt, nw) in enumerate(mats):
            tr_ = None
            for ot in range(nt):
                for w in range(nw):
                    sl = ldi % 4
                    nst = B_Vf[:, 1024 * sl: 1024 * sl + 1024]
                    tok_n = dma(qrot[ldi % 3], nst,
                                dN[128 * ot: 128 * (ot + 1), 1024 * w: 1024 * (w + 1)],
                                f"lsn{sl}", waits=[tok_vfree, tok_vfree_a, prev_nst[sl]])
                    ldi += 1
                    nb0 = (32 * ot) if nw == 1 else (96 * ot + 32 * w)
                    def redn(e, nst=nst, nb0=nb0, mi=mi):
                        return e.tensor_reduce(
                            SN[mi][:, nb0: nb0 + 32],
                            nst.rearrange("p (b w) -> p b w", w=QB),
                            AX.X, ALU.max, apply_absolute_value=True)
                    tr_ = b.emit("dve", redn, waits=[tok_n], inc="dve")
                    prev_nst[sl] = tr_
            ncols = 32 * nt * nw
            sn_tok[mi] = b.emit("dve", lambda e, mi=mi, ncols=ncols: e.tensor_scalar(
                SN[mi][:, 0:ncols], SN[mi][:, 0:ncols], 1.0 / 7.0, 1e-12,
                ALU.mult, ALU.max), waits=[tr_], inc="dve")
        tok_scales = sn_tok[2]

        def jit_scale_tiles(mi, src_slice, w_extra, par=0):
            pc = 4 * 512 + 512 * par
            sT_st = B_Vf[0:32, 6272 + 256 * par: 6400 + 256 * par]
            rT_st = B_Vf[0:32, 6400 + 256 * par: 6528 + 256 * par]
            wtr = [sn_tok[mi], bank_free[4 + par]] + w_extra
            ttr2 = b.emit("pe", lambda e: e.transpose(
                PS[0:32, pc + 256: pc + 384], src_slice, C_EYE[:, :]),
                waits=wtr, inc="pe")
            tcp2 = b.emit("act", lambda e: e.copy(sT_st, PS[0:32, pc + 256: pc + 384]),
                          waits=[ttr2], inc="act")
            trc2 = b.emit("dve", lambda e: e.reciprocal(rT_st, sT_st),
                          waits=[tcp2], inc="dve")
            bank_free[4 + par] = tcp2
            return sT_st, rT_st, trc2

        def barrier(idx, cols, after_toks):
            ctoks = []
            prev_gst = b.tok("gci")
            prev_gmx = b.tok("dve")
            tok_ccz = b.emit("dve", lambda e: e.memset(CCST, 0.0),
                             waits=after_toks + [prev_gst], inc="dve")
            for si, col in enumerate(cols):
                w = [bank_free[5]] + after_toks
                def trp(e, col=col):
                    return e.transpose(PS[0:1, 5 * 512: 5 * 512 + 128],
                                       V_S[:, col:col + 1], C_EYE[:, :])
                tt = b.emit("pe", trp, waits=w, inc="pe")
                def rd(e, si=si, idx=idx):
                    return e.tensor_reduce(CCST[0:1, si:si + 1],
                                           PS[0:1, 5 * 512: 5 * 512 + 128],
                                           AX.X, ALU.max)
                tr_ = b.emit("dve", rd, waits=[tt, tok_ccz], inc="dve")
                bank_free[5] = tr_
                ctoks.append(tr_)
            tin = dma("pool", cc_in[idx], CCST, "gci", waits=ctoks)
            tcc = b.emit("pool", lambda e, idx=idx: e.collective_compute(
                "AllReduce", ALU.max, replica_groups=[list(range(8))],
                ins=[cc_in[idx]], outs=[cc_out[idx]]), waits=[tin], inc="cc")
            tg = dma("pool", GMX, cc_out[idx], "gco", waits=[tcc, prev_gmx])
            return tg

        def bcast_consts(row_lo, n, vc_lo, wtok):
            w = [wtok, bank_free[5]]
            tb = b.emit("pe", lambda e: e.matmul(PS[:, 5 * 512: 5 * 512 + n], C_OR[:, :],
                                                 V_SC[0:1, row_lo: row_lo + n],
                                                 start=True, stop=True),
                        waits=w, inc="pe")
            tc = b.emit("act", lambda e: e.copy(V_C[:, vc_lo: vc_lo + n],
                                                PS[:, 5 * 512: 5 * 512 + n]),
                        waits=[tb], inc="act")
            bank_free[5] = tc
            return tc

        tg1 = barrier(0, [13], [b.tok("dve")])
        t_a = b.emit("dve", lambda e: e.tensor_scalar(V_SC[0:1, 0:1], GMX[:, 0:1],
                                                      1.0 / QMA, 1e-12, ALU.mult, ALU.max),
                     waits=[tg1], inc="dve")
        t_b = b.emit("dve", lambda e: e.reciprocal(V_SC[0:1, 1:2], V_SC[0:1, 0:1]),
                     waits=[t_a], inc="dve")
        tok_c1 = bcast_consts(0, 2, 32, t_b)

        def xq_slab(c, t0, n):
            if c < 4:
                return B_Q[:, 4096 * c + t0: 4096 * c + t0 + n]
            return B_K[:, 4096 * (c - 4) + t0: 4096 * (c - 4) + t0 + n]

        xq_tok = None
        prev_ld = [None] * 4
        tok_ln1_done = (b.tok("dve"), b.tok("pe"))
        for c in range(CD):
            for t2 in range(2):
                sl = (c * 2 + t2) % 4
                stage = B_Wf[:, 2048 * sl: 2048 * sl + 2048]
                tok_l = dma("sp" if sl % 2 == 0 else "act", stage,
                            d_xln[128 * c: 128 * (c + 1), 2048 * t2: 2048 * (t2 + 1)],
                            f"lxq{sl}", waits=[prev_ld[sl], tok_scales, b.tok("sxln0"), b.tok("sxln1"), b.tok("gw0"), b.tok("gw1"), b.tok("gx"),
                                          tok_ln1_done[0], tok_ln1_done[1]])
                def q1(e, stage=stage):
                    return e.tensor_scalar(stage, stage, V_C[:, 33:34], CMAG,
                                           ALU.mult, ALU.add)
                tq1 = b.emit("dve", q1, waits=[tok_l, tok_c1], inc="dve")
                def q2(e, stage=stage, c=c, t2=t2):
                    return e.tensor_scalar(xq_slab(c, 2048 * t2, 2048), stage,
                                           -CMAG, V_C[:, 32:33], ALU.add, ALU.mult)
                xq_tok = b.emit("dve", q2, waits=[tq1], inc="dve")
                prev_ld[sl] = xq_tok

        if dbg == "xq":
            tdl = None
            for c in range(CD):
                buf = B_Q if c < 4 else B_K
                off = 4096 * (c % 4)
                tdl = dma("pool", d_out[128 * c: 128 * (c + 1), :],
                          buf[:, off: off + 4096], "gst", waits=[xq_tok])
            b.emit("sp", lambda e: e.nop(), waits=[tdl])
            b.finalize(block)
            return nc

        dq_base = 4096
        sp_base = 8192
        run_up, run_gate = 1, 2
        first_red = {1: True, 2: True}
        prev_w = [None, None]
        prev_dq = [None, None]
        prev_sp = [None] * 4
        prev_rd = [None] * 4
        spill_n = 0
        gtoks = {}
        tok_dq_done = {}
        a_last_mm = None
        tiles = [(it, mi) for it in range(A_LIMIT) for mi in range(M_LIMIT)]
        tinfo = {}
        bank6_free = [None]

        def a_wload(X):
            it, mi = tiles[X]
            dT = d_wup if mi == 0 else d_wg
            sl = X % 2
            wst = B_Vf[:, 1024 * sl: 1024 * sl + 1024]
            tok_w = dma("sp", wst.rearrange("p (c n) -> p c n", c=CD),
                        dT[:, 128 * it: 128 * (it + 1)].rearrange(
                            "(c p) n -> p c n", p=128),
                        f"lwa{sl}", waits=[prev_w[sl], xq_tok])
            tinfo[X] = {"sl": sl, "wst": wst, "tok_w": tok_w, "dq": []}

        def a_jit(X):
            it, mi = tiles[X]
            info = tinfo[X]
            tp_ = X % 2
            pc6 = 6 * 512 + 128 * tp_
            srT = B_Vf[0:32, 6272 + 256 * tp_: 6528 + 256 * tp_]
            src = SN[mi][:, 32 * it: 32 * (it + 1)]
            ttr2 = b.emit("pe", lambda e, src=src, pc6=pc6: e.transpose(
                PS[0:32, pc6: pc6 + 128], src, C_EYE[:, :]),
                waits=[sn_tok[mi], tok_i32, bank6_free[0]], inc="pe")
            tcp2 = b.emit("act", lambda e, srT=srT, pc6=pc6: e.copy(
                srT[0:32, 0:128], PS[0:32, pc6: pc6 + 128]), waits=[ttr2], inc="act")
            trc2 = b.emit("dve", lambda e, srT=srT: e.reciprocal(
                srT[0:32, 128:256], srT[0:32, 0:128]), waits=[tcp2], inc="dve")
            bank6_free[0] = tcp2
            info["jit"] = trc2
            info["srT"] = srT

        def a_bs(X, c):
            info = tinfo[X]
            sl = info["sl"]
            bpar = c % 2
            pc = 4 * 512 + 512 * bpar
            scr = B_Vf[:, 6784 + 128 * bpar: 6912 + 128 * bpar]
            def bs1(e, c=c, srT=info["srT"], pc=pc):
                return e.matmul(PS[:, pc: pc + 256],
                                C_I32[:, 128 * c: 128 * (c + 1)],
                                srT, start=True, stop=True)
            tb2 = b.emit("pe", bs1, waits=[info["jit"], bank_free[4 + bpar]], inc="pe")
            def dq1(e, wst=info["wst"], c=c, scr=scr, pc=pc):
                return e.tensor_tensor(scr, wst[:, 128 * c: 128 * (c + 1)],
                                       PS[:, pc + 128: pc + 256], ALU.mult)
            td1 = b.emit("dve", dq1, waits=[tb2, info["tok_w"]], inc="dve", chain=False)
            td2 = b.emit("dve", lambda e, scr=scr: e.tensor_scalar(
                scr, scr, CMAG, -CMAG, ALU.add, ALU.add), inc="dve", chain=False)
            def dq3(e, c=c, sl=sl, scr=scr, pc=pc):
                return e.tensor_tensor(B_V[:, dq_base + 2048 * sl + 128 * c:
                                            dq_base + 2048 * sl + 128 * (c + 1)],
                                       scr, PS[:, pc: pc + 128], ALU.mult)
            td3 = b.emit("dve", dq3, inc="dve", chain=False)
            bank_free[4 + bpar] = td3
            info["dq"].append(td3)
            if c == CD - 1:
                prev_w[sl] = td3

        a_wload(0)
        a_jit(0)
        for c in range(CD):
            a_bs(0, c)
        spill_n = 0
        for X in range(len(tiles)):
            it, mi = tiles[X]
            info = tinfo[X]
            sl = info["sl"]
            ddst = d_up if mi == 0 else d_gate
            if X + 1 < len(tiles):
                a_wload(X + 1)
                a_jit(X + 1)
            for tp in range(4):
                pb = 0 if (X * 4 + tp) % 2 == 0 else 2
                wa = [info["dq"][-1], bank_free[pb], bank_free[pb + 1]]
                for c in range(CD):
                    for half in range(2):
                        def mma(e, pb=pb, c=c, half=half, sl=sl, tp=tp):
                            return e.matmul(
                                PS[:, 512 * (pb + half): 512 * (pb + half) + 512],
                                B_V[:, dq_base + 2048 * sl + 128 * c:
                                    dq_base + 2048 * sl + 128 * (c + 1)],
                                xq_slab(c, 1024 * tp + 512 * half, 512),
                                start=(c == 0), stop=(c == CD - 1))
                        a_last_mm = b.emit("pe", mma, waits=wa if (c == 0 and half == 0) else (),
                                           inc="pe" if (c == CD - 1 and half == 1) else None)
                if X + 1 < len(tiles):
                    a_bs(X + 1, 2 * tp)
                    a_bs(X + 1, 2 * tp + 1)
                ssl = spill_n % 4
                spill_n += 1
                spst = B_V[:, sp_base + 1024 * ssl: sp_base + 1024 * (ssl + 1)]
                def eva(e, pb=pb, spst=spst):
                    return e.copy(spst, PS[:, 512 * pb: 512 * pb + 1024])
                tev = b.emit("act", eva, waits=[a_last_mm, prev_sp[ssl], prev_rd[ssl]],
                             inc="act")
                run = run_up if mi == 0 else run_gate
                if first_red[run]:
                    def redA(e, spst=spst, run=run):
                        return e.tensor_reduce(V_S[:, run:run + 1], spst,
                                               AX.X, ALU.max, apply_absolute_value=True)
                    trd = b.emit("dve", redA, waits=[tev], inc="dve")
                    first_red[run] = False
                else:
                    def redA(e, spst=spst):
                        return e.tensor_reduce(V_S[:, 15:16], spst,
                                               AX.X, ALU.max, apply_absolute_value=True)
                    trd0 = b.emit("dve", redA, waits=[tev], inc="dve")
                    def acc(e, run=run):
                        return e.tensor_tensor(V_S[:, run:run + 1], V_S[:, run:run + 1],
                                               V_S[:, 15:16], ALU.max)
                    trd = b.emit("dve", acc, waits=[trd0], inc="dve")
                prev_rd[ssl] = trd
                bank_free[pb] = tev
                bank_free[pb + 1] = tev
                prev_sp[ssl] = dma("sp", ddst[128 * it: 128 * (it + 1),
                                            1024 * tp: 1024 * (tp + 1)],
                                   spst, f"ssp{ssl}", waits=[tev])

        if dbg == "ffnA":
            dob = d_out.bitcast(BF16)
            t1d = dma("sp", dob[0:512, 0:T], d_up[0:512, :], "st",
                      waits=[b.tok(f"ssp{0}"), b.tok("ssp1"), b.tok("ssp2"), b.tok("ssp3"), b.tok("dve")])
            t2d = dma("sp", dob[512:1024, 0:T], d_gate[0:512, :], "st",
                      waits=[b.tok(f"ssp{0}"), b.tok("ssp1"), b.tok("ssp2"), b.tok("ssp3"), b.tok("dve")])
            b.emit("sp", lambda e: e.nop(), waits=[t1d, t2d])
            b.finalize(block)
            return nc

        tg2 = barrier(1, [run_up, run_gate], [b.tok("dve"), b.tok("act")])
        tq1 = b.emit("dve", lambda e: e.tensor_scalar(V_SC[0:1, 2:3], GMX[:, 0:1],
                                                1.0 / QMA, 1e-12, ALU.mult, ALU.max),
               waits=[tg2], inc="dve")
        tq2 = b.emit("dve", lambda e: e.reciprocal(V_SC[0:1, 3:4], V_SC[0:1, 2:3]),
                     waits=[tq1], inc="dve")
        tq3 = b.emit("dve", lambda e: e.tensor_scalar(V_SC[0:1, 4:5], GMX[:, 1:2],
                                                1.0 / QMA, 1e-12, ALU.mult, ALU.max),
               waits=[tq2], inc="dve")
        tq4 = b.emit("dve", lambda e: e.reciprocal(V_SC[0:1, 5:6], V_SC[0:1, 4:5]),
                     waits=[tq3], inc="dve")
        t_ct = b.emit("dve", lambda e: e.tensor_single_scalar(V_SC[0:1, 6:7], V_SC[0:1, 4:5],
                                                       float(2 ** -16), ALU.mult),
                      waits=[tq4], inc="dve")
        tok_c2 = bcast_consts(2, 5, 34, t_ct)

        def ddq_slab(it, dm):
            base = 1024 * it + 128 * dm
            if it < 16:
                return B_Q[:, base: base + 128]
            return B_K[:, base - 16384: base - 16384 + 128]

        def ddq_tile(it):
            if it < 16:
                return B_Q[:, 1024 * it: 1024 * (it + 1)]
            return B_K[:, 1024 * (it - 16): 1024 * (it - 15)]

        dscr = B_Vf[:, 4352:5376]
        prev_wd = [b.tok("dve"), b.tok("dve")]
        prev_dd = None
        sb_prev = [tok_c2, tok_c2]
        tr_ev = {}
        ddq_last = None
        for g in range(3):
            sTg = B_Vf[0:32, 5376 + 0: 5376 + 1024]
            rTg = B_Vf[0:32, 6400 + 0: 6400 + 1024]
            ttr = None
            for dm in range(CD):
                def trg(e, dm=dm, g=g):
                    return e.transpose(
                        PS[0:32, 6 * 512 + 128 * dm: 6 * 512 + 128 * (dm + 1)],
                        SN[2][:, 96 * dm + 32 * g: 96 * dm + 32 * g + 32],
                        C_EYE[:, :])
                w0 = [tok_scales, sb_prev[1], prev_dd]
                if g == 0 and dm == 0:
                    w0 += [b.tok(f"ssp{i}") for i in range(4)]
                ttr = b.emit("pe", trg,
                             waits=w0 if dm == 0 else (),
                             inc="pe" if dm == CD - 1 else None)
            tev = b.emit("act", lambda e, sTg=sTg: e.copy(sTg, PS[0:32, 6 * 512: 6 * 512 + 1024]),
                         waits=[ttr], inc="act")
            trc = b.emit("dve", lambda e, sTg=sTg, rTg=rTg: e.reciprocal(rTg, sTg),
                         waits=[tev], inc="dve")
            for li in range(8):
                it = 8 * g + li
                sl = it % 2
                crow = it % 8
                wst = B_Vf[:, 1024 * sl: 1024 * sl + 1024]
                tok_w = dma("sp", wst, d_wd[128 * it: 128 * (it + 1), :],
                            f"lwd{sl}", waits=[prev_wd[sl]])
                pbase = 0 if it % 2 == 0 else 4
                s_b = PS[:, 512 * pbase: 512 * pbase + 1024]
                r_b = PS[:, 512 * pbase + 1024: 512 * pbase + 2048]
                wmm = [trc, sb_prev[0 if pbase == 0 else 1]]
                tmm = None
                for h2 in range(2):
                    def bs(e, crow=crow, h2=h2, pbase=pbase, sTg=sTg):
                        return e.matmul(PS[:, 512 * pbase + 512 * h2: 512 * pbase + 512 * (h2 + 1)],
                                        C_I32[:, 128 * crow: 128 * (crow + 1)],
                                        sTg[0:32, 512 * h2: 512 * (h2 + 1)],
                                        start=True, stop=True)
                    tmm = b.emit("pe", bs, waits=wmm if h2 == 0 else (), inc="pe")
                for h2 in range(2):
                    def br(e, crow=crow, h2=h2, pbase=pbase, rTg=rTg):
                        return e.matmul(PS[:, 512 * pbase + 1024 + 512 * h2: 512 * pbase + 1024 + 512 * (h2 + 1)],
                                        C_I32[:, 128 * crow: 128 * (crow + 1)],
                                        rTg[0:32, 512 * h2: 512 * (h2 + 1)],
                                        start=True, stop=True)
                    tmm = b.emit("pe", br, inc="pe")
                d1 = b.emit("dve", lambda e, wst=wst, r_b=r_b: e.tensor_tensor(
                    dscr, wst, r_b, ALU.mult), waits=[tmm, tok_w], inc="dve")
                d2 = b.emit("dve", lambda e: e.tensor_scalar(
                    dscr, dscr, CMAG, -CMAG, ALU.add, ALU.add), inc="dve")
                d3 = b.emit("dve", lambda e, it=it, s_b=s_b: e.tensor_tensor(
                    ddq_tile(it), dscr, s_b, ALU.mult), inc="dve")
                sb_prev[0 if pbase == 0 else 1] = d3
                prev_wd[sl] = d1
                prev_dd = d3
                ddq_last = d3
        for _bi in range(8):
            bank_free[_bi] = sb_prev[0] if _bi < 4 else sb_prev[1]

        B_Xf = B_X.ap().bitcast(F32)
        prev_g = [None] * 4
        prev_fa = [None] * 4
        prev_fb = [None] * 4
        prev_fc = [None] * 4
        prev_tst = [None] * 4
        tmax_tok = None
        NTB = I // 128 * (T // 1024)
        for k in range(NTB):
            it, th = k // 4, k % 4
            sst = k % 4
            gst = B_W[:, 1024 * sst: 1024 * (sst + 1)]
            tst = B_M[:, 1024 * sst: 1024 * (sst + 1)]
            fA = B_Wf[:, 4096 + 1024 * sst: 4096 + 1024 * sst + 1024]
            fB = B_Xf[:, 1024 * sst: 1024 * sst + 1024]
            fC = B_Xf[:, 4096 + 1024 * sst: 4096 + 1024 * sst + 1024]
            tok_g = dma("sp", gst, d_gate[128 * it: 128 * (it + 1),
                                          1024 * th: 1024 * (th + 1)],
                        f"lgb{sst}",
                        waits=[prev_g[sst]] + ([b.tok(f"ssp{i}") for i in range(4)] if k == 0 else []))
            a1 = b.emit("act", lambda e, gst=gst, fA=fA: e.activation(
                fA, gst, AF.Copy, bias=CMAG, scale=V_C[:, 37:38]),
                waits=[tok_g, tok_c2, prev_fa[sst]], inc="act")
            v1 = b.emit("act", lambda e, fA=fA, fB=fB: e.activation(
                fB, fA, AF.Copy, bias=-CMAG), waits=[a1, prev_fb[sst]], inc="act")
            a3 = b.emit("act", lambda e, fB=fB, fC=fC: e.activation(
                fC, fB, AF.Sigmoid, scale=V_C[:, 36:37]),
                waits=[v1, prev_fc[sst]], inc="act")
            v2 = b.emit("dve", lambda e, fC=fC, fA=fA: e.tensor_scalar(
                fA, fC, 65536.0, CMAG, ALU.mult, ALU.add), waits=[a3], inc="dve", chain=False)
            v3 = b.emit("dve", lambda e, fA=fA: e.tensor_scalar(
                fA, fA, -CMAG, 65535.0, ALU.add, ALU.min), inc="dve", chain=False)
            v4 = b.emit("dve", lambda e, fA=fA, fB=fB, tst=tst: e.tensor_tensor(
                tst, fB, fA, ALU.mult), waits=[prev_tst[sst]], inc="dve", chain=False)
            if tmax_tok is None:
                v5 = b.emit("dve", lambda e, tst=tst: e.tensor_reduce(
                    V_S[:, 3:4], tst, AX.X, ALU.max, apply_absolute_value=True), inc="dve")
            else:
                tv5 = b.emit("dve", lambda e, tst=tst: e.tensor_reduce(
                    V_S[:, 15:16], tst, AX.X, ALU.max, apply_absolute_value=True), inc="dve")
                v5 = b.emit("dve", lambda e: e.tensor_tensor(
                    V_S[:, 3:4], V_S[:, 3:4], V_S[:, 15:16], ALU.max),
                    waits=[tv5], inc="dve")
            tmax_tok = v5
            prev_g[sst] = a1
            prev_fa[sst] = v4
            prev_fb[sst] = v4
            prev_fc[sst] = v2
            prev_tst[sst] = dma("pool", d_t[128 * it: 128 * (it + 1),
                                            1024 * th: 1024 * (th + 1)],
                                tst, f"stb{sst}", waits=[v5])

        tg3 = barrier(2, [3], [b.tok("dve")])
        tr1 = b.emit("dve", lambda e: e.tensor_tensor(V_SC[0:1, 7:8], GMX[:, 0:1],
                                                V_SC[0:1, 6:7], ALU.mult),
               waits=[tg3], inc="dve")
        tr2 = b.emit("dve", lambda e: e.tensor_scalar(V_SC[0:1, 8:9], V_SC[0:1, 7:8],
                                                1.0 / QMA, 1e-12, ALU.mult, ALU.max),
               waits=[tr1], inc="dve")
        t_rt = b.emit("dve", lambda e: e.reciprocal(V_SC[0:1, 9:10], V_SC[0:1, 8:9]),
                      waits=[tr2], inc="dve")
        t_rtc = b.emit("dve", lambda e: e.tensor_tensor(V_SC[0:1, 16:17], V_SC[0:1, 9:10],
                                                        V_SC[0:1, 6:7], ALU.mult),
                       waits=[t_rt], inc="dve")
        tok_c3 = bcast_consts(16, 1, 39, t_rtc)

        prev_tl = [None] * 4
        prev_ul = [None] * 4
        prev_fa = [None] * 4
        prev_fb = [None] * 4
        prev_fc2 = [None] * 4
        prev_pst = [None] * 4
        pmax_tok = None
        for k in range(NTB):
            it, th = k // 4, k % 4
            sst = k % 4
            tstg = B_W[:, 1024 * sst: 1024 * (sst + 1)]
            ustg = B_W[:, 4096 + 1024 * sst: 4096 + 1024 * (sst + 1)]
            fA = B_Wf[:, 4096 + 1024 * sst: 4096 + 1024 * sst + 1024]
            fB = B_Xf[:, 1024 * sst: 1024 * sst + 1024]
            pst = B_M[:, 1024 * sst: 1024 * (sst + 1)]
            tok_t = dma("sp", tstg, d_t[128 * it: 128 * (it + 1),
                                        1024 * th: 1024 * (th + 1)],
                        f"ltc{sst}",
                        waits=[prev_tl[sst]] + ([b.tok(f"stb{i}") for i in range(4)] if k == 0 else []))
            tok_u = dma("act", ustg, d_up[128 * it: 128 * (it + 1),
                                          1024 * th: 1024 * (th + 1)],
                        f"luc{sst}", waits=[prev_ul[sst]])
            a1 = b.emit("act", lambda e, tstg=tstg, fA=fA: e.activation(
                fA, tstg, AF.Copy, bias=CMAG, scale=V_C[:, 39:40]),
                waits=[tok_t, tok_c3, prev_fa[sst]], inc="act")
            fC2 = B_Xf[:, 4096 + 1024 * sst: 4096 + 1024 * sst + 1024]
            v1 = b.emit("act", lambda e, fA=fA, fC2=fC2: e.activation(
                fC2, fA, AF.Copy, bias=-CMAG), waits=[a1, prev_fc2[sst]], inc="act")
            a2 = b.emit("act", lambda e, ustg=ustg, fB=fB: e.activation(
                fB, ustg, AF.Copy, bias=CMAG, scale=V_C[:, 35:36]),
                waits=[tok_u, tok_c3, prev_fb[sst]], inc="act")
            v2 = b.emit("dve", lambda e, fB=fB: e.tensor_single_scalar(
                fB, fB, -CMAG, ALU.add), waits=[a2], inc="dve", chain=False)
            v3 = b.emit("dve", lambda e, fC2=fC2, fB=fB, pst=pst: e.tensor_tensor(
                pst, fC2, fB, ALU.mult), waits=[v1, prev_pst[sst]], inc="dve", chain=False)
            if pmax_tok is None:
                v4 = b.emit("dve", lambda e, pst=pst: e.tensor_reduce(
                    V_S[:, 4:5], pst, AX.X, ALU.max, apply_absolute_value=True), inc="dve")
            else:
                tv4 = b.emit("dve", lambda e, pst=pst: e.tensor_reduce(
                    V_S[:, 15:16], pst, AX.X, ALU.max, apply_absolute_value=True), inc="dve")
                v4 = b.emit("dve", lambda e: e.tensor_tensor(
                    V_S[:, 4:5], V_S[:, 4:5], V_S[:, 15:16], ALU.max),
                    waits=[tv4], inc="dve")
            pmax_tok = v4
            prev_tl[sst] = a1
            prev_ul[sst] = a2
            prev_fa[sst] = v1
            prev_fb[sst] = v3
            prev_fc2[sst] = v3
            prev_pst[sst] = dma("pool", d_P[128 * it: 128 * (it + 1),
                                            1024 * th: 1024 * (th + 1)],
                                pst, f"stp{sst}", waits=[v4])

        tg4 = barrier(3, [4], [b.tok("dve")])
        ts1_ = b.emit("dve", lambda e: e.tensor_tensor(V_SC[0:1, 10:11], V_SC[0:1, 8:9],
                                                V_SC[0:1, 2:3], ALU.mult),
               waits=[tg4], inc="dve")
        ts2_ = b.emit("dve", lambda e: e.tensor_tensor(V_SC[0:1, 11:12], V_SC[0:1, 10:11],
                                                GMX[:, 0:1], ALU.mult),
               waits=[ts1_], inc="dve")
        ts3_ = b.emit("dve", lambda e: e.tensor_scalar(V_SC[0:1, 12:13], V_SC[0:1, 11:12],
                                                1.0 / QMA, 1e-12, ALU.mult, ALU.max),
               waits=[ts2_], inc="dve")
        ts4_ = b.emit("dve", lambda e: e.reciprocal(V_SC[0:1, 13:14], V_SC[0:1, 12:13]),
               waits=[ts3_], inc="dve")
        t_co = b.emit("dve", lambda e: e.tensor_tensor(V_SC[0:1, 14:15], V_SC[0:1, 10:11],
                                                       V_SC[0:1, 13:14], ALU.mult),
                      waits=[ts4_], inc="dve")
        tok_c4 = bcast_consts(12, 3, 40, t_co)

        def oq_slab(set_, it2):
            if set_ == 0:
                return B_X[:, 512 * it2: 512 * (it2 + 1)]
            if it2 < 8:
                return B_X[:, 12288 + 512 * it2: 12288 + 512 * (it2 + 1)]
            return B_M[:, 512 * (it2 - 8): 512 * (it2 - 7)]

        s2ring = B_Wf[:, 0:4096]
        prev_ost = [None] * 4
        prev_fd = [None] * 4
        prev_oq = [None, None]
        prep_done = [None, None]
        prev_xr = [None] * 4
        prev_ap = [None, None]
        out_toks = [None, None]
        lsum_d = lsq_d = None

        def prep_oq(tc, extra_w=()):
            set_ = tc % 2
            last = None
            for it2 in range(CI):
                r = it2 % 4
                ost = B_W[:, 8192 + 512 * r: 8192 + 512 * (r + 1)]
                fD = B_Wf[:, 5120 + 512 * r: 5120 + 512 * (r + 1)]
                tl = dma("sp", ost, d_P[128 * it2: 128 * (it2 + 1),
                                        512 * tc: 512 * (tc + 1)],
                         f"lpd{r}", waits=[prev_ost[r]] + list(extra_w))
                pa = b.emit("act", lambda e, ost=ost, fD=fD: e.activation(
                    fD, ost, AF.Copy, bias=CMAG, scale=V_C[:, 42:43]),
                    waits=[tl, tok_c4, prev_fd[r]], inc="act")
                pv = b.emit("dve", lambda e, fD=fD, set_=set_, it2=it2: e.tensor_scalar(
                    oq_slab(set_, it2), fD, -CMAG, V_C[:, 40:41], ALU.add, ALU.mult),
                    waits=[pa, prev_oq[set_]], inc="dve", chain=False)
                prev_ost[r] = pa
                prev_fd[r] = pv
                last = pv
            prep_done[set_] = last

        prep_oq(0, extra_w=[b.tok(f"stp{i}") for i in range(4)] + [b.tok("dve"), ddq_last])
        for tc in range(8):
            set_ = tc % 2
            ring_toks = []
            lsum_d = lsq_d = None
            prev_sq = [None, None]

            def emit_stats(dm, ts2, tsq):
                nonlocal lsum_d, lsq_d
                wst2 = [ts2] + ([bank_free[4], bank_free[5]] if dm == 0 else [])
                lsum_d = b.emit("pe", lambda e, dm=dm: e.matmul(
                    PS[0:1, 4 * 512: 4 * 512 + 512], V_C[:, 47:48],
                    s2ring[:, 512 * dm: 512 * (dm + 1)],
                    start=(dm == 0), stop=(dm == CD - 1)),
                    waits=wst2, inc="pe")
                lsq_d = b.emit("pe", lambda e, dm=dm, tsq=tsq: e.matmul(
                    PS[0:1, 5 * 512: 5 * 512 + 512], V_C[:, 47:48],
                    B_Vf[:, 4352 + 512 * (dm % 2): 4864 + 512 * (dm % 2)],
                    start=(dm == 0), stop=(dm == CD - 1)),
                    waits=[tsq], inc="pe")
                prev_sq[dm % 2] = lsq_d

            stats_pend = None
            for dm in range(CD):
                bki = dm % 4
                w = [ddq_last, prep_done[set_], bank_free[bki]]
                for it2 in range(CI):
                    def mmd(e, bki=bki, it2=it2, dm=dm, set_=set_):
                        return e.matmul(bank(bki), ddq_slab(it2, dm),
                                        oq_slab(set_, it2),
                                        start=(it2 == 0), stop=(it2 == CI - 1))
                    last = b.emit("pe", mmd, waits=w if it2 == 0 else (),
                                  inc="pe" if it2 == CI - 1 else None)
                if stats_pend is not None:
                    emit_stats(*stats_pend)
                xr = B_Vf[:, 512 * (dm % 4): 512 * (dm % 4) + 512]
                tok_xr = dma("act", xr,
                             d_xln[128 * dm: 128 * (dm + 1), 512 * tc: 512 * (tc + 1)],
                             f"lx2{dm % 4}",
                             waits=[prev_xr[dm % 4]] + ([ddq_last] if tc == 0 and dm < 4 else []))
                def adds2(e, bki=bki, dm=dm, xr=xr):
                    return e.tensor_tensor(s2ring[:, 512 * dm: 512 * (dm + 1)],
                                           bank(bki), xr, ALU.add)
                ts2 = b.emit("dve", adds2, waits=[last, tok_xr, t_ones], inc="dve", chain=False)
                prev_xr[dm % 4] = ts2
                bank_free[bki] = ts2
                ring_toks.append(ts2)
                dsq = B_Vf[:, 4352 + 512 * (dm % 2): 4864 + 512 * (dm % 2)]
                tsq = b.emit("dve", lambda e, dm=dm, dsq=dsq: e.tensor_tensor(
                    dsq, s2ring[:, 512 * dm: 512 * (dm + 1)],
                    s2ring[:, 512 * dm: 512 * (dm + 1)], ALU.mult),
                    waits=[prev_sq[dm % 2]], inc="dve", chain=False)
                prev_sq[dm % 2] = tsq
                stats_pend = (dm, ts2, tsq)
            emit_stats(*stats_pend)
            if tc < 7:
                prep_oq(tc + 1)
            trm = b.emit("dve", lambda e: e.tensor_single_scalar(V_R[0:1, 0:512],
                         PS[0:1, 4 * 512: 4 * 512 + 512], 1.0 / D, ALU.mult),
                         waits=[lsum_d], inc="dve")
            trq = b.emit("dve", lambda e: e.tensor_single_scalar(V_R[0:1, 512:1024],
                         PS[0:1, 5 * 512: 5 * 512 + 512], 1.0 / D, ALU.mult),
                         waits=[lsq_d], inc="dve")
            bank_free[4] = trm
            bank_free[5] = trq
            trm2 = b.emit("dve", lambda e: e.tensor_tensor(V_R[0:1, 1024:1536], V_R[0:1, 0:512],
                                                    V_R[0:1, 0:512], ALU.mult),
                          waits=[trm], inc="dve")
            tvar = b.emit("dve", lambda e: e.tensor_tensor(V_R[0:1, 512:1024],
                                                           V_R[0:1, 512:1024],
                                                           V_R[0:1, 1024:1536], ALU.subtract),
                          waits=[trq, trm2], inc="dve")
            tln = b.emit("act", lambda e: e.activation(V_R[0:1, 1024:1536],
                                                       V_R[0:1, 512:1024], AF.Ln, bias=CONSTV[0:1, 0:1]),
                         waits=[tvar], inc="act")
            trstd = b.emit("act", lambda e: e.activation(V_R[0:1, 1024:1536],
                                                         V_R[0:1, 1024:1536], AF.Exp,
                                                         scale=-0.5), inc="act")
            tmr = b.emit("dve", lambda e: e.tensor_tensor(V_R[0:1, 1536:2048],
                                                          V_R[0:1, 0:512],
                                                          V_R[0:1, 1024:1536], ALU.mult),
                         waits=[trstd], inc="dve")
            tb6 = b.emit("pe", lambda e: e.matmul(PS[:, 6 * 512: 7 * 512], C_OR[:, :],
                                                  V_R[0:1, 1024:1536], start=True, stop=True),
                         waits=[trstd, bank_free[6]], inc="pe")
            tb7 = b.emit("pe", lambda e: e.matmul(PS[:, 7 * 512: 8 * 512], C_OR[:, :],
                                                  V_R[0:1, 1536:2048], start=True, stop=True),
                         waits=[tmr, bank_free[7]], inc="pe")
            last_ap = None
            for dm in range(CD):
                pp = dm % 2
                ap1 = B_Vf[:, 2048 + 512 * pp: 2048 + 512 * (pp + 1)]
                ap2 = B_Vf[:, 3072 + 512 * pp: 3072 + 512 * (pp + 1)]
                ta1 = b.emit("dve", lambda e, dm=dm, ap1=ap1: e.tensor_tensor(
                    ap1, s2ring[:, 512 * dm: 512 * (dm + 1)],
                    PS[:, 6 * 512: 7 * 512], ALU.mult),
                    waits=[tb6, ring_toks[dm], prev_ap[pp]], inc="dve")
                ta2 = b.emit("dve", lambda e, ap1=ap1, ap2=ap2: e.tensor_tensor(
                    ap2, ap1, PS[:, 7 * 512: 8 * 512],
                    ALU.subtract), waits=[tb7, out_toks[pp]], inc="dve")
                out_toks[pp] = dma("pool", d_out[128 * dm: 128 * (dm + 1),
                                                 512 * tc: 512 * (tc + 1)],
                                   ap2, f"sout{pp}", waits=[ta2])
                prev_ap[pp] = ta2
                last_ap = ta2
            bank_free[6] = last_ap
            bank_free[7] = last_ap

        b.emit("sp", lambda e: e.nop(), waits=[out_toks[0], out_toks[1]])
        b.finalize(block)
        return nc


def _prep_core_inputs(inputs, c):
    x = np.asarray(inputs["L0_attn_x"])
    Wq = np.asarray(inputs["L0_attn_W_q"])
    Wk = np.asarray(inputs["L0_attn_W_k"])
    Wv = np.asarray(inputs["L0_attn_W_v"])
    Wo = np.asarray(inputs["L0_attn_W_o"])
    wg = np.asarray(inputs["L0_ffn_w_gate"])
    wu = np.asarray(inputs["L0_ffn_w_up"])
    wd = np.asarray(inputs["L0_ffn_w_down"])
    perm = np.zeros(D, np.int64)
    for h in range(H):
        perm[DH * h: DH * h + 32] = DH * h + 2 * np.arange(32)
        perm[DH * h + 32: DH * h + 64] = DH * h + 2 * np.arange(32) + 1
    xc = x[2 * c: 2 * c + 2].reshape(T, D)
    return {
        "xT": np.ascontiguousarray(xc.T),
        "wqT": np.ascontiguousarray(Wq[perm].T),
        "wkT": np.ascontiguousarray(Wk[perm].T),
        "wvT": np.ascontiguousarray(Wv.T),
        "woT": np.ascontiguousarray(Wo.T),
        "wupT": np.ascontiguousarray(wu.T),
        "wgT": np.ascontiguousarray(wg.T),
        "wdT": np.ascontiguousarray(wd.T),
        "wupN": np.ascontiguousarray(wu),
        "wgN": np.ascontiguousarray(wg),
        "wdN": np.ascontiguousarray(wd),
    }


def kernel(**inputs):
    nc = build_program(DBG_STAGE)
    in_maps = [_prep_core_inputs(inputs, c) for c in range(8)]
    res = run_bass_kernel_spmd(nc, in_maps, core_ids=list(range(8)))
    outs = []
    for c in range(8):
        o = np.asarray(res.results[c]["outT"])
        outs.append(o.T.reshape(2, L, D))
    return np.concatenate(outs, axis=0).astype(np.float32)
